# version 01
# baseline (speedup 1.0000x reference)
import contextlib
import numpy as np
import ml_dtypes
import concourse.bass as bass
import concourse.mybir as mybir
from concourse.bass_utils import run_bass_kernel_spmd

F32 = mybir.dt.float32
BF16 = mybir.dt.bfloat16
AF = mybir.ActivationFunctionType
ALU = mybir.AluOpType
AX = mybir.AxisListType
BF = ml_dtypes.bfloat16

NCORES = 8
B_CORE = 2
S = 2048
T = B_CORE * S
D = 1024
NT = T // 128
KC = D // 128
CH = 512
NCH = S // CH
NEG = -1.0e30
DBG_SC = False

TENSORS = {
    "x0": ([T, D], F32),
    "L0_attn_Wq": ([D, D], F32), "L0_attn_bq": ([1, D], F32),
    "L0_attn_Wk": ([D, D], F32), "L0_attn_bk": ([1, D], F32),
    "L0_attn_Wv": ([D, D], F32), "L0_attn_bv": ([1, D], F32),
    "L0_attn_Wo": ([D, D], F32), "L0_attn_bo": ([1, D], F32),
    "L0_attn_decay_param": ([1, 1], F32),
    "L0_ffn_w_router": ([16, D], F32),
    "L0_ffn_gate_up_proj": ([16, D, 4096], F32),
    "L0_ffn_down_proj": ([16, 2048, D], F32),
    "L0_ffn_w_gate": ([2048, D], F32),
    "L0_ffn_w_up": ([2048, D], F32),
    "L0_ffn_w_down": ([D, 2048], F32),
    "L1_attn_Wq": ([D, D], F32), "L1_attn_Wk": ([D, D], F32),
    "L1_attn_Wv": ([D, D], F32), "L1_attn_Wo": ([D, D], F32),
    "L1_attn_bo": ([1, D], F32),
    "L1_ffn_ln1_w": ([1, D], F32), "L1_ffn_ln1_b": ([1, D], F32),
    "L1_ffn_w_qkv": ([D, 3 * D], F32),
    "L1_ffn_w_proj": ([D, D], F32),
    "L1_ffn_ln2_w": ([1, D], F32), "L1_ffn_ln2_b": ([1, D], F32),
    "L1_ffn_gate_w": ([D, 8], F32), "L1_ffn_gate_b": ([1, 8], F32),
    "L1_ffn_w1": ([8, D, 4096], F32),
    "L1_ffn_w2": ([8, 4096, D], F32),
    "c_ident_b": ([128, 128], BF16),
    "c_m1": ([CH, CH], F32),
    "c_m2neg": ([CH, CH], F32),
    "c_mask1": ([CH, CH], F32),
    "c_trel": ([128, CH // 128], F32),
    "c_trel2": ([128, CH // 128], F32),
    "xn_tm": ([T, D], BF16),
    "xn_fm": ([KC, 128, T], BF16),
    "scores_tm": ([T, 16], F32),
    "q_fm": ([KC, 128, T], BF16),
    "qs_fm": ([KC, 128, T], BF16),
    "kphi_fm": ([KC, 128, T], BF16),
    "khat_tm": ([T, D], BF16),
    "v_tm": ([T, D], BF16),
    "read_fm": ([KC, 128, T], BF16),
    "attn0_tm": ([T, D], F32),
    "ffn0_tm": ([T, D], F32),
    "ffn0e_tm": ([T, D], F32),
    "x1_tm": ([T, D], F32),
    "xn1f_tm": ([T, D], F32),
    "xn1_fm": ([KC, 128, T], BF16),
    "phiq_fm": ([KC, 128, T], BF16),
    "phik_fm": ([KC, 128, T], BF16),
    "phik_tm": ([T, D], BF16),
    "v1_tm": ([T, D], BF16),
    "y1_fm": ([KC, 128, T], BF16),
    "attn1_tm": ([T, D], F32),
    "xln_fm": ([KC, 128, T], BF16),
    "qk_fm": ([2 * KC, 128, T], BF16),
    "vm_tm": ([T, D], BF16),
    "ym_fm": ([KC, 128, T], BF16),
    "u_tm": ([T, D], F32),
    "xg_fm": ([KC, 128, T], BF16),
    "mask8_tm": ([T, 8], F32),
    "moe_tm": ([T, D], F32),
    "OUT": ([T, D], F32),
}

NP_DT = {F32: np.float32, BF16: BF}


class Sem:
    def __init__(self, h):
        self.h = h
        self.v = 0


class _DummyInstr:
    def then_inc(self, h, n):
        return self

    def annotate(self, a):
        return self


class _DummyEngine:
    def __getattr__(self, name):
        def f(*a, **k):
            return _DummyInstr()
        return f


class Ctx:
    NDSEM = 28

    def __init__(self, nc, es, kinds):
        self.nc = nc
        self.es = es
        self.kinds = kinds
        self._aps = {}
        self.sems = {}
        for nm in ["A", "Dv", "Tn", "Pl", "Sp", "xA", "xD", "xT"]:
            self.sems[nm] = Sem(es.enter_context(nc.semaphore(nm)))
        self.dsems = [Sem(es.enter_context(nc.semaphore(f"d{i}")))
                      for i in range(self.NDSEM)]
        self.identb = None
        self.marks = {}
        self._dry = True
        self.phase_id = 0

    def ap(self, name):
        if name not in self._aps:
            shape, dt = TENSORS[name]
            kind = {"in": "ExternalInput", "out": "ExternalOutput",
                    "tmp": "Internal"}[self.kinds[name]]
            self._aps[name] = self.nc.dram_tensor(name, shape, dt, kind=kind).ap()
        return self._aps[name]

    def sem(self, nm):
        return self.sems[nm]

    def dsem(self, i):
        return self.dsems[i]

    def inc(self, ins, sem, label=None, n=1):
        sem.v += n
        if label is not None:
            key = (self.phase_id, label)
            old = self.marks.get(key)
            assert old is None or old[0] is sem, f"mark {label} sem mismatch"
            self.marks[key] = (sem, sem.v)
        ins.then_inc(sem.h, n)
        return ins

    def wait(self, e, sem, label):
        key = (self.phase_id, label)
        if key not in self.marks:
            assert self._dry, f"unknown mark {label} in emit pass"
            v = 0
        else:
            msem, v = self.marks[key]
            assert msem is sem, f"wait on mark {label}: wrong semaphore"
        e.wait_ge(sem.h, v)

    def incflush(self, e, ins, which, sem, label, n=1):
        xs = self.sems["xA"] if which == "A" else self.sems["xD"]
        self.selfsync(e, ins, xs)
        if which == "A":
            f = e.activation(self.jA[:, 1:2], self.jA[:, 0:1], AF.Copy)
        else:
            f = e.tensor_copy(self.jD[:, 1:2], self.jD[:, 0:1])
        return self.inc(f, sem, label, n)

    def selfsync(self, e, ins, xsem):
        ins.then_inc(xsem.h, 1)
        xsem.v += 1
        e.wait_ge(xsem.h, xsem.v)
        return ins

    def twopass(self, bodies):
        allsems = list(self.sems.values()) + self.dsems
        snap = [(s, s.v) for s in allsems]
        finals = {id(s): v for s, v in snap}
        self._dry = True
        dummy = _DummyEngine()
        for _, fn in bodies:
            for s, v in snap:
                s.v = v
            fn(dummy)
            for s, v in snap:
                finals[id(s)] = max(finals[id(s)], s.v)
        self._dry = False
        with self.nc.Block() as blk:
            for name, fn in bodies:
                for s, v in snap:
                    s.v = v
                getattr(blk, name)(fn)
        for s, v in snap:
            s.v = finals[id(s)]


def _ph_prelude(P):
    nc = P.nc
    if P.identb is not None:
        return
    P.jA = P.es.enter_context(nc.sbuf_tensor("junkA", [128, 4], F32))[:, :]
    P.jD = P.es.enter_context(nc.sbuf_tensor("junkD", [128, 4], F32))[:, :]
    identb = P.es.enter_context(nc.sbuf_tensor("c_identb_sb", [128, 128], BF16))
    ld = P.dsem(27)
    with nc.Block() as blk:
        @blk.sync
        def _(e):
            e.dma_start(identb[:, :], P.ap("c_ident_b")).then_inc(ld.h, 16)

        @blk.vector
        def _(e):
            e.memset(P.jD, 0.0)

        @blk.scalar
        def _(e):
            e.memzero(P.jA)

        @blk.tensor
        def _(e):
            e.wait_ge(ld.h, ld.v + 16)
    ld.v += 16
    P.identb = identb[:, :]


def run_phase(P, fn, *args, **kwargs):
    _ph_prelude(P)
    P.phase_id += 1
    with contextlib.ExitStack() as es:
        fn(P, es, *args, **kwargs)
    with P.nc.Block() as blk:
        @blk.sync
        def _(e):
            for ds in P.dsems:
                e.wait_ge(ds.h, ds.v)



def load_cast_weights(P, es, specs, tag=""):
    nc = P.nc
    Dv = P.sem("Dv")
    lds = [P.dsem(22 + k) for k in range(4)]
    hes = contextlib.ExitStack()
    stage = hes.enter_context(nc.sbuf_tensor("wc_stage" + tag, [128, 4 * 512], F32))
    pieces = []
    for dram, sb in specs:
        K, N = dram.shape
        kp = min(K, 128)
        for kc in range(max(1, K // 128)):
            for j in range(0, N, 512):
                w = min(512, N - j)
                pieces.append((dram[kc * kp:kc * kp + kp, j:j + w],
                               sb[0:kp, kc * N + j: kc * N + j + w]))

    def b_sync(e):
        for i, (dsl, ssl) in enumerate(pieces):
            if i >= 4:
                P.wait(e, Dv, f"wcv{i - 4}")
            kp, w = dsl.shape
            P.inc(e.dma_start(stage[0:kp, (i % 4) * 512:(i % 4) * 512 + w], dsl),
                  lds[i % 4], f"wcl{i}", 16)

    def b_vec(e):
        for i, (dsl, ssl) in enumerate(pieces):
            P.wait(e, lds[i % 4], f"wcl{i}")
            kp, w = dsl.shape
            P.inc(e.tensor_copy(ssl, stage[0:kp, (i % 4) * 512:(i % 4) * 512 + w]),
                  Dv, f"wcv{i}")

    P.twopass([("sync", b_sync), ("vector", b_vec)])
    hes.close()


def load_cast_transpose_weights(P, es, specs, tag=""):
    nc = P.nc
    Dv, Tn = P.sem("Dv"), P.sem("Tn")
    lds = [P.dsem(22), P.dsem(23)]
    KMAX = max(d.shape[1] for d, _ in specs)
    hes = contextlib.ExitStack()
    stage = hes.enter_context(nc.sbuf_tensor("wt_stage" + tag, [128, 2 * 512], F32))
    stb = hes.enter_context(nc.sbuf_tensor("wt_stb" + tag, [128, 2 * KMAX], BF16))
    pst = hes.enter_context(nc.psum_tensor("wt_ps" + tag, [128, 2 * KMAX], BF16))
    rows = []
    for dram, sb in specs:
        M, K = dram.shape
        for m in range(0, M, 128):
            rows.append((dram[m:m + 128, :], sb, M, m, K))

    def b_sync(e):
        ip = 0
        for r, (dsl, sb, M, m, K) in enumerate(rows):
            for j in range(0, K, 512):
                if ip >= 2:
                    P.wait(e, Dv, f"wtc{ip - 2}")
                P.inc(e.dma_start(stage[:, (ip % 2) * 512:(ip % 2) * 512 + 512],
                                  dsl[:, j:j + 512]), lds[ip % 2], f"wtl{ip}", 16)
                ip += 1

    def b_tensor(e):
        ip = 0
        for r, (dsl, sb, M, m, K) in enumerate(rows):
            ip += K // 512
            P.wait(e, Dv, f"wtc{ip - 1}")
            if r >= 2:
                K2 = rows[r - 2][4]
                P.wait(e, Dv, f"wte{r - 2}.{K2 // 128 - 1}")
            ins = None
            for kc in range(K // 128):
                ins = e.transpose(pst[:, (r % 2) * KMAX + kc * 128:
                                      (r % 2) * KMAX + (kc + 1) * 128],
                                  stb[:, (r % 2) * KMAX + kc * 128:
                                      (r % 2) * KMAX + (kc + 1) * 128],
                                  P.identb)
            P.inc(ins, Tn, f"wtt{r}")

    def b_vec(e):
        ip = 0
        for r, (dsl, sb, M, m, K) in enumerate(rows):
            for j in range(0, K, 512):
                P.wait(e, lds[ip % 2], f"wtl{ip}")
                if r >= 2:
                    P.wait(e, Tn, f"wtt{r - 1}")
                P.inc(e.tensor_copy(stb[:, (r % 2) * KMAX + j:
                                        (r % 2) * KMAX + j + 512],
                                    stage[:, (ip % 2) * 512:(ip % 2) * 512 + 512]),
                      Dv, f"wtc{ip}")
                ip += 1
            P.wait(e, Tn, f"wtt{r}")
            for kc in range(K // 128):
                P.inc(e.tensor_copy(sb[:, kc * M + m: kc * M + m + 128],
                                    pst[:, (r % 2) * KMAX + kc * 128:
                                        (r % 2) * KMAX + (kc + 1) * 128]),
                      Dv, f"wte{r}.{kc}")

    P.twopass([("sync", b_sync), ("tensor", b_tensor), ("vector", b_vec)])
    hes.close()


def bias_bcast(P, es, names, width=None, tag=""):
    nc = P.nc
    Dv, Tn = P.sem("Dv"), P.sem("Tn")
    ld = P.dsem(22)
    widths = {nm: (width or TENSORS[nm][0][1]) for nm in names}
    tot = sum(widths.values())
    bbc = es.enter_context(nc.sbuf_tensor("bb_bc" + tag, [128, tot], F32))
    hes = contextlib.ExitStack()
    brow = hes.enter_context(nc.sbuf_tensor("bb_row" + tag, [1, tot], F32))
    ones = hes.enter_context(nc.sbuf_tensor("bb_ones" + tag, [1, 128], F32))
    psb = hes.enter_context(nc.psum_tensor("bb_ps" + tag, [128, 512], F32))
    offs = {}
    o = 0
    for nm in names:
        offs[nm] = o
        o += widths[nm]
    segs = []
    for nm in names:
        w = widths[nm]
        for j in range(0, w, 512):
            segs.append((offs[nm] + j, min(512, w - j)))

    def b_sync(e):
        for i, nm in enumerate(names):
            P.inc(e.dma_start(brow[:, offs[nm]:offs[nm] + widths[nm]], P.ap(nm)),
                  ld, f"bbl{i}", 16)

    def b_vec(e):
        P.inc(e.memset(ones[:, :], 1.0), Dv, "bbones")
        for i, (ro, w) in enumerate(segs):
            P.wait(e, Tn, f"bbm{i}")
            P.inc(e.tensor_copy(bbc[:, ro:ro + w], psb[:, 0:w]), Dv, f"bbe{i}")

    def b_tensor(e):
        P.wait(e, Dv, "bbones")
        for i, nm in enumerate(names):
            P.wait(e, ld, f"bbl{i}")
        for i, (ro, w) in enumerate(segs):
            if i >= 1:
                P.wait(e, Dv, f"bbe{i - 1}")
            P.inc(e.matmul(psb[:, 0:w], ones[:, :], brow[:, ro:ro + w],
                           start=True, stop=True), Tn, f"bbm{i}")

    P.twopass([("sync", b_sync), ("vector", b_vec), ("tensor", b_tensor)])
    hes.close()
    return {nm: bbc[:, offs[nm]:offs[nm] + widths[nm]] for nm in names}



def ph_ln(P, es, src, out_fm, out_tm=None, out_f32=None, lnw=None, router=None):
    nc = P.nc
    A, Dv, Tn, xA, xD = (P.sem(n) for n in ["A", "Dv", "Tn", "xA", "xD"])
    lds = [P.dsem(0), P.dsem(1)]
    s_tm = [P.dsem(10), P.dsem(11)]
    s_f32 = [P.dsem(12), P.dsem(13)]
    s_fm = [P.dsem(14), P.dsem(15)]
    s_sc = [P.dsem(16), P.dsem(17)]
    E = router[1] if router else 0
    use_sig = router[3] if router else False
    assert not (router and use_sig and router[5]), "sigmoid+bias router unsupported"

    xa = es.enter_context(nc.sbuf_tensor("ln_x", [128, 2 * D], F32))
    xc = es.enter_context(nc.sbuf_tensor("ln_xc", [128, D], F32))
    sq = es.enter_context(nc.sbuf_tensor("ln_sq", [128, D], F32))
    xnf = es.enter_context(nc.sbuf_tensor("ln_xnf", [128, 2 * D], F32))
    xnb = es.enter_context(nc.sbuf_tensor("ln_xnb", [128, 2 * D], BF16))
    lob = es.enter_context(nc.sbuf_tensor("ln_lob", [128, 2 * D], BF16))
    fmb = es.enter_context(nc.sbuf_tensor("ln_fmb", [128, 2 * D], BF16))
    lofm = es.enter_context(nc.sbuf_tensor("ln_lofm", [128, 2 * D], BF16))
    stats = es.enter_context(nc.sbuf_tensor("ln_stats", [128, 8], F32))
    pst = es.enter_context(nc.psum_tensor("ln_pst", [128, 2 * D], BF16))
    if router:
        wrt = es.enter_context(nc.sbuf_tensor("ln_wrt", [128, 2 * KC * E], BF16))
        psr = es.enter_context(nc.psum_tensor("ln_psr", [128, E], F32))
        sc = es.enter_context(nc.sbuf_tensor("ln_sc", [128, 2 * 4 * E], F32))
        lgb = es.enter_context(nc.sbuf_tensor("ln_lgb", [128, 2 * 8], F32)) \
            if router[5] else None
    wbc = bias_bcast(P, es, list(lnw)) if lnw else None
    gb = bias_bcast(P, es, [router[5]], width=E, tag="g")[router[5]] \
        if (router and router[5]) else None
    if router:
        _router_w_prep(P, es, router, wrt)
    xsrc = P.ap(src)

    def slot(i):
        return (i % 2) * D

    def emit_exp(e, j):
        P.wait(e, Tn, f"mmr{j}")
        if j >= 2:
            P.wait(e, s_sc[j % 2], f"ssc{j - 2}")
        P.selfsync(e, e.activation(sc[:, (j % 2) * 4 * E + 2 * E:
                                      (j % 2) * 4 * E + 3 * E],
                                   psr[:, 0:E], AF.Exp, scale=-1.0), xA)
        P.inc(e.activation(stats[:, 7:8], stats[:, 7:8], AF.Copy), A, f"ex{j}")

    def b_sync(e):
        for i in range(NT):
            if i >= 2:
                P.wait(e, A, f"ln{i - 2}")
            P.inc(e.dma_start(xa[:, slot(i):slot(i) + D],
                              xsrc[i * 128:(i + 1) * 128, :]),
                  lds[i % 2], f"lx{i}", 16)

    def b_scalar(e):
        for i in range(NT):
            P.wait(e, lds[i % 2], f"lx{i}")
            if i >= 2:
                P.wait(e, Tn, f"tr{i - 2}")
                if out_tm:
                    P.wait(e, s_tm[i % 2], f"stm{i - 2}")
                if out_f32:
                    P.wait(e, s_f32[i % 2], f"sf32{i - 2}")
                if lnw:
                    P.wait(e, Dv, f"lo{i - 2}")
            xi = xa[:, slot(i):slot(i) + D]
            P.selfsync(e, e.activation(sq[:, :], xi, AF.Copy,
                                       accum_out=stats[:, 0:1]), xA)
            P.selfsync(e, e.activation(stats[:, 1:2], stats[:, 0:1], AF.Copy,
                                       scale=-1.0 / D), xA)
            e.activation(xc[:, :], xi, AF.Identity, bias=stats[:, 1:2])
            P.selfsync(e, e.activation(sq[:, :], xc[:, :], AF.Square,
                                       accum_out=stats[:, 2:3]), xA)
            P.selfsync(e, e.activation(stats[:, 3:4], stats[:, 2:3], AF.Ln,
                                       scale=1.0 / D, bias=np.float32(1e-5)), xA)
            P.selfsync(e, e.activation(stats[:, 4:5], stats[:, 3:4], AF.Exp,
                                       scale=-0.5), xA)
            e.activation(xnf[:, slot(i):slot(i) + D], xc[:, :], AF.Copy,
                         scale=stats[:, 4:5])
            if lnw:
                P.inc(e.activation(xnf[:, slot(i):slot(i) + 1],
                                   xnf[:, slot(i):slot(i) + 1], AF.Copy),
                      A, f"ln{i}")
            else:
                P.inc(e.activation(xnb[:, slot(i):slot(i) + D],
                                   xnf[:, slot(i):slot(i) + D], AF.Copy),
                      A, f"ln{i}")
            if router and use_sig and i >= 1:
                emit_exp(e, i - 1)
        if router and use_sig:
            emit_exp(e, NT - 1)

    def b_vector(e):
        for i in range(NT + 1):
            if i < NT:
                P.wait(e, A, f"ln{i}")
                if lnw:
                    if i >= 2:
                        if out_tm:
                            P.wait(e, s_tm[i % 2], f"stm{i - 2}")
                        if out_f32:
                            P.wait(e, s_f32[i % 2], f"sf32{i - 2}")
                        P.wait(e, Tn, f"tr{i - 2}")
                    e.tensor_mul(xnf[:, slot(i):slot(i) + D],
                                 xnf[:, slot(i):slot(i) + D], wbc[lnw[0]])
                    e.tensor_add(xnf[:, slot(i):slot(i) + D],
                                 xnf[:, slot(i):slot(i) + D], wbc[lnw[1]])
                    e.tensor_copy(xnb[:, slot(i):slot(i) + D],
                                  xnf[:, slot(i):slot(i) + D])
                if router:
                    P.inc(e.tensor_sub(lob[:, slot(i):slot(i) + D],
                                       xnf[:, slot(i):slot(i) + D],
                                       xnb[:, slot(i):slot(i) + D]), Dv, f"lo{i}")
                else:
                    P.inc(e.tensor_copy(lob[:, 0:1], xnb[:, slot(i):slot(i) + 1]),
                          Dv, f"lo{i}")
            if i >= 1:
                j = i - 1
                P.wait(e, Tn, f"tr{j}")
                if j >= 2:
                    P.wait(e, s_fm[j % 2], f"sfm{j - 2}")
                if router:
                    e.tensor_copy(fmb[:, slot(j):slot(j) + D], pst[:, 0:D])
                    P.inc(e.tensor_copy(lofm[:, slot(j):slot(j) + D],
                                        pst[:, D:2 * D]), Dv, f"ev{j}")
                else:
                    P.inc(e.tensor_copy(fmb[:, slot(j):slot(j) + D], pst[:, 0:D]),
                          Dv, f"ev{j}")
            if router and i >= 1:
                j = i - 1
                P.wait(e, Tn, f"mmr{j}")
                if j >= 2:
                    P.wait(e, s_sc[j % 2], f"ssc{j - 2}")
                base = (j % 2) * 4 * E
                if router[5]:
                    P.selfsync(e, e.tensor_add(lgb[:, (j % 2) * 8:(j % 2) * 8 + E],
                                               psr[:, 0:E], gb), xD)
                    lsrc = lgb[:, (j % 2) * 8:(j % 2) * 8 + E]
                else:
                    lsrc = psr[:, 0:E]
                if DBG_SC:
                    e.tensor_copy(sc[:, base + 3 * E:base + 4 * E], psr[:, 0:E])
                P.selfsync(e, e.reduce_max(stats[:, 5:6], lsrc, axis=AX.X), xD)
                if use_sig:
                    P.wait(e, A, f"ex{j}")
                    P.selfsync(e, e.tensor_scalar(sc[:, base + E:base + 2 * E],
                                                  sc[:, base + 2 * E:base + 3 * E],
                                                  1.0, None, ALU.add), xD)
                    P.selfsync(e, e.reciprocal(sc[:, base + 3 * E:base + 4 * E],
                                               sc[:, base + E:base + 2 * E]), xD)
                    P.selfsync(e, e.tensor_scalar(sc[:, base:base + E], lsrc,
                                                  stats[:, 5:6], None,
                                                  ALU.is_equal), xD)
                    P.selfsync(e, e.tensor_mul(sc[:, base:base + E],
                                               sc[:, base:base + E],
                                               sc[:, base + 3 * E:base + 4 * E]),
                               xD)
                    P.inc(e.tensor_copy(stats[:, 6:7], stats[:, 5:6]),
                          Dv, f"sco{j}")
                else:
                    P.selfsync(e, e.tensor_scalar(sc[:, base:base + E], lsrc,
                                                  stats[:, 5:6], None,
                                                  ALU.is_equal), xD)
                    P.inc(e.tensor_copy(stats[:, 6:7], stats[:, 5:6]),
                          Dv, f"sco{j}")

    def b_tensor(e):
        for i in range(NT + 1):
            if i < NT:
                P.wait(e, Dv, f"lo{i}")
                if i >= 1:
                    P.wait(e, Dv, f"ev{i - 1}")
                ins = None
                for c in range(KC):
                    ins = e.transpose(pst[:, c * 128:(c + 1) * 128],
                                      xnb[:, slot(i) + c * 128:
                                          slot(i) + (c + 1) * 128], P.identb)
                if router:
                    for c in range(KC):
                        ins = e.transpose(pst[:, D + c * 128:D + (c + 1) * 128],
                                          lob[:, slot(i) + c * 128:
                                              slot(i) + (c + 1) * 128], P.identb)
                P.inc(ins, Tn, f"tr{i}")
            if router and i >= 1:
                j = i - 1
                P.wait(e, Dv, f"ev{j}")
                if j >= 1:
                    P.wait(e, Dv, f"sco{j - 1}")
                    if use_sig:
                        P.wait(e, A, f"ex{j - 1}")
                n = 0
                ins = None
                for kc in range(KC):
                    for hl in range(3):
                        lh = lofm[:, slot(j) + kc * 128:slot(j) + kc * 128 + 128] \
                            if hl == 1 else \
                            fmb[:, slot(j) + kc * 128:slot(j) + kc * 128 + 128]
                        rh = wrt[:, (2 * kc + (1 if hl == 2 else 0)) * E:
                                 (2 * kc + (1 if hl == 2 else 0) + 1) * E]
                        ins = e.matmul(psr[:, 0:E], lh, rh, start=(n == 0),
                                       stop=(n == 3 * KC - 1))
                        n += 1
                P.inc(ins, Tn, f"mmr{j}")

    def b_gpsimd(e):
        for i in range(NT):
            P.wait(e, A, f"ln{i}")
            if lnw:
                P.wait(e, Dv, f"lo{i}")
            if out_tm:
                P.inc(e.dma_start(P.ap(out_tm)[i * 128:(i + 1) * 128, :],
                                  xnb[:, slot(i):slot(i) + D]),
                      s_tm[i % 2], f"stm{i}", 16)
            if out_f32:
                P.inc(e.dma_start(P.ap(out_f32)[i * 128:(i + 1) * 128, :],
                                  xnf[:, slot(i):slot(i) + D]),
                      s_f32[i % 2], f"sf32{i}", 16)
            P.wait(e, Dv, f"ev{i}")
            P.inc(e.dma_start(P.ap(out_fm)[:, :, i * 128:(i + 1) * 128]
                              .rearrange("c p j -> p c j"),
                              fmb[:, slot(i):slot(i) + D]
                              .rearrange("p (c j) -> p c j", c=KC)),
                  s_fm[i % 2], f"sfm{i}", 16)
            if router:
                P.wait(e, Dv, f"sco{i}")
                wE = 4 * E if DBG_SC else E
                P.inc(e.dma_start(P.ap(router[2])[i * 128:(i + 1) * 128, :],
                                  sc[:, (i % 2) * 4 * E:(i % 2) * 4 * E + wE]),
                      s_sc[i % 2], f"ssc{i}", 16)

    P.twopass([("sync", b_sync), ("scalar", b_scalar), ("vector", b_vector),
               ("tensor", b_tensor), ("gpsimd", b_gpsimd)])


def _router_w_prep(P, es, router, wrt):
    nc = P.nc
    wname, E, w_is_ExD = router[0], router[1], router[4]
    Dv, Tn = P.sem("Dv"), P.sem("Tn")
    ld = P.dsem(22)
    if w_is_ExD:
        hes = contextlib.ExitStack()
        wst = hes.enter_context(nc.sbuf_tensor("rw_st", [E, D], F32))
        whi = hes.enter_context(nc.sbuf_tensor("rw_hi", [E, D], BF16))
        wlo = hes.enter_context(nc.sbuf_tensor("rw_lo", [E, D], BF16))
        pse = hes.enter_context(nc.psum_tensor("rw_ps", [128, 2 * E], BF16))

        def b_sync(e):
            P.inc(e.dma_start(wst[:, :], P.ap(wname)), ld, "rwl", 16)

        def b_vec(e):
            P.wait(e, ld, "rwl")
            e.tensor_copy(whi[:, :], wst[:, :])
            P.inc(e.tensor_sub(wlo[:, :], wst[:, :], whi[:, :]), Dv, "rws")
            for kc in range(KC):
                P.wait(e, Tn, f"rwt{kc}")
                e.tensor_copy(wrt[:, 2 * kc * E:(2 * kc + 1) * E], pse[:, 0:E])
                P.inc(e.tensor_copy(wrt[:, (2 * kc + 1) * E:(2 * kc + 2) * E],
                                    pse[:, E:2 * E]), Dv, f"rwe{kc}")

        def b_tensor(e):
            P.wait(e, Dv, "rws")
            for kc in range(KC):
                if kc >= 1:
                    P.wait(e, Dv, f"rwe{kc - 1}")
                e.transpose(pse[:, 0:E], whi[0:E, kc * 128:(kc + 1) * 128],
                            P.identb[0:E, 0:E])
                P.inc(e.transpose(pse[:, E:2 * E],
                                  wlo[0:E, kc * 128:(kc + 1) * 128],
                                  P.identb[0:E, 0:E]), Tn, f"rwt{kc}")

        P.twopass([("sync", b_sync), ("vector", b_vec), ("tensor", b_tensor)])
        hes.close()
    else:
        wst = es.enter_context(nc.sbuf_tensor("rw_st", [128, KC * E], F32))

        def b_sync(e):
            P.inc(e.dma_start(wst[:, :].rearrange("p (c e) -> p c e", c=KC),
                              P.ap(wname).rearrange("(c p) e -> p c e", p=128)),
                  ld, "rwl", 16)

        def b_vec(e):
            P.wait(e, ld, "rwl")
            for kc in range(KC):
                e.tensor_copy(wrt[:, 2 * kc * E:(2 * kc + 1) * E],
                              wst[:, kc * E:(kc + 1) * E])
            for kc in range(KC):
                e.tensor_sub(wrt[:, (2 * kc + 1) * E:(2 * kc + 2) * E],
                             wst[:, kc * E:(kc + 1) * E],
                             wrt[:, 2 * kc * E:(2 * kc + 1) * E])

        P.twopass([("sync", b_sync), ("vector", b_vec)])



def _ph_ln0(P, es):
    ph_ln(P, es, "x0", "xn_fm", out_tm="xn_tm",
          router=("L0_ffn_w_router", 16, "scores_tm", True, True, None))



def make_constants():
    s = np.arange(CH, dtype=np.float32)[:, None]
    t = np.arange(CH, dtype=np.float32)[None, :]
    c = {
        "c_ident_b": np.eye(128, dtype=BF),
        "c_m1": (t - 1.0 - s).astype(np.float32),
        "c_m2neg": np.where(s < t, 0.0, NEG).astype(np.float32),
        "c_mask1": (s <= t).astype(np.float32),
        "c_trel": (np.arange(128, dtype=np.float32)[:, None]
                   + 128.0 * np.arange(CH // 128, dtype=np.float32)[None, :]),
    }
    c["c_trel2"] = (CH - 1.0) - c["c_trel"]
    return c


def shard_inputs(inputs):
    vals = {}
    x = np.ascontiguousarray(np.asarray(inputs["L0_attn_x"], dtype=np.float32))
    vals["x0"] = [np.ascontiguousarray(x[2 * i:2 * i + 2].reshape(T, D))
                  for i in range(NCORES)]
    for name in TENSORS:
        if name in ("x0", "OUT") or name.startswith("c_"):
            continue
        if name in inputs:
            shape, dt = TENSORS[name]
            a = np.ascontiguousarray(np.asarray(inputs[name],
                                                dtype=NP_DT[dt]).reshape(shape))
            vals[name] = [a] * NCORES
    for name, a in make_constants().items():
        shape, dt = TENSORS[name]
        a = np.ascontiguousarray(a.astype(NP_DT[dt]).reshape(shape))
        vals[name] = [a] * NCORES
    return vals


def build_launch(phase_items, later_reads):
    all_writes = set()
    written = set()
    kinds = {}
    for _, _, rd, wr in phase_items:
        for r in rd:
            if r not in written:
                kinds[r] = "in"
        for w in wr:
            written.add(w)
            all_writes.add(w)
    for w in all_writes:
        if w == "OUT" or w in later_reads:
            kinds[w] = "out"
        elif kinds.get(w) != "in":
            kinds[w] = "tmp"
        else:
            raise RuntimeError(f"tensor {w} both external input and written")
    nc = bass.Bass("TRN2", target_bir_lowering=False)
    es = contextlib.ExitStack()
    P = Ctx(nc, es, kinds)
    for name, builder, rd, wr in phase_items:
        run_phase(P, builder)
    es.close()
    ext_in = [n for n, k in kinds.items() if k == "in"]
    ext_out = [n for n, k in kinds.items() if k == "out"]
    return nc, ext_in, ext_out


def run_pipeline(vals, groups, phase_defs=None, keep_all=False):
    defs = {p[0]: p for p in (phase_defs or PHASE_DEFS)}
    for gi, g in enumerate(groups):
        items = [defs[n] for n in g]
        later = set()
        for g2 in groups[gi + 1:]:
            for n2 in g2:
                later.update(defs[n2][2])
        later.add("OUT")
        if keep_all:
            later.update(TENSORS.keys())
        nc, ext_in, ext_out = build_launch(items, later)
        in_maps = [{n: vals[n][c] for n in ext_in} for c in range(NCORES)]
        res = run_bass_kernel_spmd(nc, in_maps, core_ids=list(range(NCORES)))
        for n in ext_out:
            vals[n] = [np.asarray(res.results[c][n]) for c in range(NCORES)]
    return vals


def kernel(**inputs):
    vals = shard_inputs(inputs)
    groups = [[p[0]] for p in PHASE_DEFS]
    run_pipeline(vals, groups)
    out = np.stack([vals["OUT"][c].reshape(B_CORE, S, D) for c in range(NCORES)])
    return np.ascontiguousarray(out.reshape(16, S, D))



def emit_decay_consts(P, es):
    nc = P.nc
    A, Dv, Tn, xA, xD = (P.sem(n) for n in ["A", "Dv", "Tn", "xA", "xD"])
    ld = P.dsem(22)
    ld2 = P.dsem(23)
    ld3 = P.dsem(24)
    NJ = CH // 128
    cols = es.enter_context(nc.sbuf_tensor("dc_cols", [128, 2 * NJ + 2], F32))
    hes = contextlib.ExitStack()
    dst = hes.enter_context(nc.sbuf_tensor("dc_st", [1, 8], F32))
    trl = hes.enter_context(nc.sbuf_tensor("dc_trl", [128, 2 * NJ], F32))
    ones1 = hes.enter_context(nc.sbuf_tensor("dc_ones", [1, 128], F32))
    psc = hes.enter_context(nc.psum_tensor("dc_ps", [128, 1], F32))

    def b_sync(e):
        P.inc(e.dma_start(dst[:, 0:1], P.ap("L0_attn_decay_param")), ld, "dcl", 16)
        P.inc(e.dma_start(trl[:, 0:NJ], P.ap("c_trel")), ld2, "dct1", 16)
        P.inc(e.dma_start(trl[:, NJ:2 * NJ], P.ap("c_trel2")), ld3, "dct2", 16)

    def b_scalar(e):
        P.wait(e, ld, "dcl")
        P.selfsync(e, e.activation(dst[:, 1:2], dst[:, 0:1], AF.Exp, scale=-1.0), xA)
        P.incflush(e, e.activation(dst[:, 2:3], dst[:, 1:2], AF.Identity,
                                   bias=np.float32(1.0)), "A", A, "dcs")
        P.wait(e, Dv, "dcr")
        P.incflush(e, e.activation(dst[:, 4:5], dst[:, 3:4], AF.Ln), "A", A, "dcln")
        P.wait(e, Dv, "dcb")
        P.wait(e, ld2, "dct1")
        P.wait(e, ld3, "dct2")
        P.selfsync(e, e.activation(cols[:, 1:1 + NJ], trl[:, 0:NJ], AF.Exp,
                                   scale=cols[:, 0:1]), xA)
        P.selfsync(e, e.activation(cols[:, 1 + NJ:1 + 2 * NJ], trl[:, NJ:2 * NJ],
                                   AF.Exp, scale=cols[:, 0:1]), xA)
        P.incflush(e, e.activation(cols[:, 1 + 2 * NJ:2 + 2 * NJ], cols[:, 0:1],
                                   AF.Exp, scale=float(CH)), "A", A, "dcd")

    def b_vector(e):
        P.wait(e, A, "dcs")
        P.incflush(e, e.reciprocal(dst[:, 3:4], dst[:, 2:3]), "D", Dv, "dcr")
        P.incflush(e, e.memset(ones1[:, :], 1.0), "D", Dv, "dco")
        P.wait(e, Tn, "dcm")
        P.incflush(e, e.tensor_copy(cols[:, 0:1], psc[:, 0:1]), "D", Dv, "dcb")

    def b_tensor(e):
        P.wait(e, A, "dcln")
        P.wait(e, Dv, "dco")
        P.inc(e.matmul(psc[:, 0:1], ones1[:, :], dst[:, 4:5],
                       start=True, stop=True), Tn, "dcm")

    P.twopass([("sync", b_sync), ("scalar", b_scalar), ("vector", b_vector),
               ("tensor", b_tensor)])
    hes.close()
    return {"ld_col": cols[:, 0:1], "dq": cols[:, 1:1 + NJ],
            "dk": cols[:, 1 + NJ:1 + 2 * NJ],
            "dC": cols[:, 1 + 2 * NJ:2 + 2 * NJ], "done": "dcd"}



def ph_qkv0(P, es):
    nc = P.nc
    A, Dv, Tn = (P.sem(n) for n in ["A", "Dv", "Tn"])
    lds = [P.dsem(0), P.dsem(1)]
    std = {"qs": [P.dsem(10), P.dsem(11)], "q": [P.dsem(12), P.dsem(13)],
           "kp": [P.dsem(14), P.dsem(15)], "kh": [P.dsem(16), P.dsem(17)],
           "v": [P.dsem(18), P.dsem(19)]}
    wq = es.enter_context(nc.sbuf_tensor("q0_wq", [128, KC * D], BF16))
    wk = es.enter_context(nc.sbuf_tensor("q0_wk", [128, KC * D], BF16))
    wv = es.enter_context(nc.sbuf_tensor("q0_wv", [128, KC * D], BF16))
    load_cast_weights(P, es, [(P.ap("L0_attn_Wq"), wq[:, :]),
                              (P.ap("L0_attn_Wk"), wk[:, :]),
                              (P.ap("L0_attn_Wv"), wv[:, :])])
    bb = bias_bcast(P, es, ["L0_attn_bq", "L0_attn_bk", "L0_attn_bv"])
    dc = emit_decay_consts(P, es)

    xfm = es.enter_context(nc.sbuf_tensor("q0_x", [128, 2 * D], BF16))
    qbf = es.enter_context(nc.sbuf_tensor("q0_qb", [128, 2 * D], BF16))
    qsb = es.enter_context(nc.sbuf_tensor("q0_qs", [128, 2 * D], BF16))
    kpb = es.enter_context(nc.sbuf_tensor("q0_kp", [128, 2 * D], BF16))
    khb = es.enter_context(nc.sbuf_tensor("q0_kh", [128, 2 * D], BF16))
    vbf = es.enter_context(nc.sbuf_tensor("q0_vb", [128, 2 * D], BF16))
    k1 = es.enter_context(nc.sbuf_tensor("q0_k1", [128, D], F32))
    km = es.enter_context(nc.sbuf_tensor("q0_km", [128, D], F32))
    kr = es.enter_context(nc.sbuf_tensor("q0_kr", [128, D], F32))
    keb = es.enter_context(nc.sbuf_tensor("q0_ke", [128, D], F32))
    qfmb = es.enter_context(nc.sbuf_tensor("q0_qf", [128, 2 * D], BF16))
    qsfmb = es.enter_context(nc.sbuf_tensor("q0_qsf", [128, 2 * D], BF16))
    kfmb = es.enter_context(nc.sbuf_tensor("q0_kf", [128, 2 * D], BF16))
    psP = es.enter_context(nc.psum_tensor("q0_ps", [128, 3 * D], F32))
    psT = es.enter_context(nc.psum_tensor("q0_pt", [128, 2 * D], BF16))

    def slot(i):
        return (i % 2) * D

    def jsub(i):
        return (i % (S // 128)) % NCH

    def b_sync(e):
        for i in range(NT):
            if i >= 2:
                P.wait(e, Tn, f"pjv{i - 2}")
            P.inc(e.dma_start(xfm[:, slot(i):slot(i) + D]
                              .rearrange("p (c j) -> p c j", c=KC),
                              P.ap("xn_fm")[:, :, i * 128:(i + 1) * 128]
                              .rearrange("c p j -> p c j")),
                  lds[i % 2], f"lx{i}", 16)

    def b_tensor(e):
        for i in range(NT):
            P.wait(e, lds[i % 2], f"lx{i}")
            if i >= 1:
                P.wait(e, Dv, f"vb{i - 1}")
                P.wait(e, Dv, f"km{i - 1}")
            for wi, (wsb, tag) in enumerate([(wq, "q"), (wk, "k"), (wv, "v")]):
                ins = None
                for n in range(2):
                    for kc in range(KC):
                        ins = e.matmul(psP[:, wi * D + n * 512:wi * D + n * 512 + 512],
                                       xfm[:, slot(i) + kc * 128:slot(i) + kc * 128 + 128],
                                       wsb[:, kc * D + n * 512:kc * D + n * 512 + 512],
                                       start=(kc == 0), stop=(kc == KC - 1))
                P.inc(ins, Tn, f"pj{tag}{i}")
            P.wait(e, Dv, f"qs{i}")
            if i >= 1:
                P.wait(e, Dv, f"eq{i - 1}")
            ins = None
            for c in range(KC):
                ins = e.transpose(psT[:, c * 128:(c + 1) * 128],
                                  qbf[:, slot(i) + c * 128:slot(i) + (c + 1) * 128],
                                  P.identb)
            P.inc(ins, Tn, f"tq{i}")
            if i >= 1:
                P.wait(e, Dv, f"eqs{i - 1}")
            ins = None
            for c in range(KC):
                ins = e.transpose(psT[:, D + c * 128:D + (c + 1) * 128],
                                  qsb[:, slot(i) + c * 128:slot(i) + (c + 1) * 128],
                                  P.identb)
            P.inc(ins, Tn, f"tqs{i}")
            P.wait(e, Dv, f"kp{i}")
            P.wait(e, Dv, f"eq{i}")
            ins = None
            for c in range(KC):
                ins = e.transpose(psT[:, c * 128:(c + 1) * 128],
                                  kpb[:, slot(i) + c * 128:slot(i) + (c + 1) * 128],
                                  P.identb)
            P.inc(ins, Tn, f"tkp{i}")

    def b_scalar(e):
        for i in range(NT):
            P.wait(e, Dv, f"km{i}")
            if i >= 1:
                P.wait(e, Dv, f"kp{i - 1}")
            P.inc(e.activation(keb[:, :], km[:, :], AF.Exp), A, f"ke{i}")

    def b_vector(e):
        j = None
        for i in range(NT):
            jj = jsub(i)
            P.wait(e, Tn, f"pjq{i}")
            if i >= 2:
                P.wait(e, Tn, f"tq{i - 2}")
                P.wait(e, Tn, f"tqs{i - 2}")
            e.tensor_add(qbf[:, slot(i):slot(i) + D], psP[:, 0:D],
                         bb["L0_attn_bq"])
            P.inc(e.tensor_scalar(qsb[:, slot(i):slot(i) + D],
                                  qbf[:, slot(i):slot(i) + D],
                                  dc["dq"][:, jj:jj + 1], None, ALU.mult),
                  Dv, f"qs{i}")
            P.wait(e, Tn, f"pjk{i}")
            e.tensor_add(k1[:, :], psP[:, D:2 * D], bb["L0_attn_bk"])
            e.tensor_relu(kr[:, :], k1[:, :])
            P.inc(e.tensor_scalar(km[:, :], k1[:, :], 0.0, None, ALU.min),
                  Dv, f"km{i}")
            P.wait(e, Tn, f"pjv{i}")
            if i >= 2:
                P.wait(e, std["v"][i % 2], f"sv{i - 2}")
                P.wait(e, std["kh"][i % 2], f"skh{i - 2}")
                P.wait(e, std["kp"][i % 2], f"skp{i - 2}")
            P.inc(e.tensor_add(vbf[:, slot(i):slot(i) + D], psP[:, 2 * D:3 * D],
                               bb["L0_attn_bv"]), Dv, f"vb{i}")
            P.wait(e, A, f"ke{i}")
            if i >= 2:
                P.wait(e, Tn, f"tkp{i - 2}")
            e.tensor_add(kpb[:, slot(i):slot(i) + D], keb[:, :], kr[:, :])
            P.inc(e.tensor_scalar(khb[:, slot(i):slot(i) + D],
                                  kpb[:, slot(i):slot(i) + D],
                                  dc["dk"][:, jj:jj + 1], None, ALU.mult),
                  Dv, f"kp{i}")
            P.wait(e, Tn, f"tq{i}")
            if i >= 2:
                P.wait(e, std["q"][i % 2], f"sq{i - 2}")
            P.inc(e.tensor_copy(qfmb[:, slot(i):slot(i) + D], psT[:, 0:D]),
                  Dv, f"eq{i}")
            P.wait(e, Tn, f"tqs{i}")
            P.inc(e.tensor_copy(qsfmb[:, slot(i):slot(i) + D], psT[:, D:2 * D]),
                  Dv, f"eqs{i}")
            P.wait(e, Tn, f"tkp{i}")
            P.inc(e.tensor_copy(kfmb[:, slot(i):slot(i) + D], psT[:, 0:D]),
                  Dv, f"ekp{i}")

    def b_gpsimd(e):
        for i in range(NT):
            P.wait(e, Dv, f"eq{i}")
            P.inc(e.dma_start(P.ap("q_fm")[:, :, i * 128:(i + 1) * 128]
                              .rearrange("c p j -> p c j"),
                              qfmb[:, slot(i):slot(i) + D]
                              .rearrange("p (c j) -> p c j", c=KC)),
                  std["q"][i % 2], f"sq{i}", 16)
            P.wait(e, Dv, f"eqs{i}")
            P.inc(e.dma_start(P.ap("qs_fm")[:, :, i * 128:(i + 1) * 128]
                              .rearrange("c p j -> p c j"),
                              qsfmb[:, slot(i):slot(i) + D]
                              .rearrange("p (c j) -> p c j", c=KC)),
                  std["qs"][i % 2], f"sqs{i}", 16)
            P.wait(e, Dv, f"ekp{i}")
            P.inc(e.dma_start(P.ap("kphi_fm")[:, :, i * 128:(i + 1) * 128]
                              .rearrange("c p j -> p c j"),
                              kfmb[:, slot(i):slot(i) + D]
                              .rearrange("p (c j) -> p c j", c=KC)),
                  std["kp"][i % 2], f"skp{i}", 16)
            P.wait(e, Dv, f"kp{i}")
            P.inc(e.dma_start(P.ap("khat_tm")[i * 128:(i + 1) * 128, :],
                              khb[:, slot(i):slot(i) + D]),
                  std["kh"][i % 2], f"skh{i}", 16)
            P.wait(e, Dv, f"vb{i}")
            P.inc(e.dma_start(P.ap("v_tm")[i * 128:(i + 1) * 128, :],
                              vbf[:, slot(i):slot(i) + D]),
                  std["v"][i % 2], f"sv{i}", 16)

    P.twopass([("sync", b_sync), ("scalar", b_scalar), ("vector", b_vector),
               ("tensor", b_tensor), ("gpsimd", b_gpsimd)])



def ph_scan0(P, es):
    nc = P.nc
    A, Dv, Tn = (P.sem(n) for n in ["A", "Dv", "Tn"])
    lds = [[P.dsem(2 * k), P.dsem(2 * k + 1)] for k in range(5)]
    strd = [P.dsem(10), P.dsem(11)]
    dc = emit_decay_consts(P, es)
    mT = es.enter_context(nc.sbuf_tensor("s0_mT", [128, 4 * 512], F32))
    M = es.enter_context(nc.sbuf_tensor("s0_M", [128, B_CORE * KC * D], F32))
    Mb = es.enter_context(nc.sbuf_tensor("s0_Mb", [128, B_CORE * KC * D], BF16))
    ld14 = P.dsem(22)
    ld13 = P.dsem(23)
    with contextlib.ExitStack() as hes:
        m1 = hes.enter_context(nc.sbuf_tensor("s0_m1", [128, 4 * 512], F32))
        m2 = hes.enter_context(nc.sbuf_tensor("s0_m2", [128, 4 * 512], F32))

        def mb_sync(e):
            P.inc(e.dma_start(m1[:, :].rearrange("p (j t) -> p j t", j=4),
                              P.ap("c_m1").rearrange("(j p) t -> p j t", p=128)),
                  ld14, "lm1", 16)
            P.inc(e.dma_start(m2[:, :].rearrange("p (j t) -> p j t", j=4),
                              P.ap("c_m2neg").rearrange("(j p) t -> p j t", p=128)),
                  ld13, "lm2", 16)

        def mb_vec(e):
            P.wait(e, ld14, "lm1")
            P.wait(e, ld13, "lm2")
            P.wait(e, A, dc["done"])
            e.tensor_scalar(m1[:, :], m1[:, :], dc["ld_col"], None, ALU.mult)
            P.inc(e.tensor_add(m1[:, :], m1[:, :], m2[:, :]), Dv, "marg")
            e.memset(M[:, :], 0.0)
            e.memset(Mb[:, :], 0.0)

        def mb_scalar(e):
            P.wait(e, Dv, "marg")
            e.activation(mT[:, :], m1[:, :], AF.Exp)

        P.twopass([("sync", mb_sync), ("vector", mb_vec), ("scalar", mb_scalar)])

    qsf = es.enter_context(nc.sbuf_tensor("s0_qs", [128, 2 * KC * CH], BF16))
    qf = es.enter_context(nc.sbuf_tensor("s0_q", [128, 2 * KC * CH], BF16))
    kf = es.enter_context(nc.sbuf_tensor("s0_k", [128, 2 * KC * CH], BF16))
    kht = es.enter_context(nc.sbuf_tensor("s0_kh", [128, 2 * 4 * D], BF16))
    vt = es.enter_context(nc.sbuf_tensor("s0_v", [128, 2 * 4 * D], BF16))
    absf = es.enter_context(nc.sbuf_tensor("s0_ab", [128, 4 * CH], BF16))
    rbf = es.enter_context(nc.sbuf_tensor("s0_rb", [128, 2 * D], BF16))
    rfm = es.enter_context(nc.sbuf_tensor("s0_rf", [128, 2 * D], BF16))
    psA = es.enter_context(nc.psum_tensor("s0_pa", [128, 2 * 512], F32))
    psR = es.enter_context(nc.psum_tensor("s0_pr", [128, 2 * D], F32))
    psT = es.enter_context(nc.psum_tensor("s0_pt", [128, D], BF16))
    NBC = B_CORE * NCH


    def b_sync(e):
        for idx in range(NBC):
            b, c = idx // NCH, idx % NCH
            off = b * S + c * CH
            sl = (idx % 2)
            if idx >= 2:
                P.wait(e, Tn, f"dm{idx - 2}.7")
            for k, (name, fm) in enumerate([("qs_fm", True), ("q_fm", True),
                                            ("kphi_fm", True), ("khat_tm", False),
                                            ("v_tm", False)]):
                if fm:
                    dst = [qsf, qf, kf][k][:, sl * KC * CH:(sl + 1) * KC * CH]
                    P.inc(e.dma_start(dst.rearrange("p (c j) -> p c j", c=KC),
                                      P.ap(name)[:, :, off:off + CH]
                                      .rearrange("c p j -> p c j")),
                          lds[k][idx % 2], f"l{k}.{idx}", 16)
                else:
                    dst = [None, None, None, kht, vt][k][:, sl * 4 * D:(sl + 1) * 4 * D]
                    P.inc(e.dma_start(dst.rearrange("p (s d) -> p s d", s=4),
                                      P.ap(name)[off:off + CH, :]
                                      .rearrange("(s p) d -> p s d", p=128)),
                          lds[k][idx % 2], f"l{k}.{idx}", 16)

    def b_vector(e):
        for idx in range(NBC):
            b, c = idx // NCH, idx % NCH
            sl = (idx % 2)
            for js in range(4):
                P.wait(e, Tn, f"aw{idx}.{js}")
                P.inc(e.tensor_mul(absf[:, js * CH:(js + 1) * CH],
                                   psA[:, (js % 2) * 512:(js % 2) * 512 + 512],
                                   mT[:, js * CH:(js + 1) * CH]), Dv, f"ab{idx}.{js}")
            for jt in range(4):
                P.wait(e, Tn, f"rd{idx}.{jt}")
                P.inc(e.tensor_copy(rbf[:, (jt % 2) * D:(jt % 2) * D + D],
                                    psR[:, (jt % 2) * D:(jt % 2) * D + D]),
                      Dv, f"re{idx}.{jt}")
                P.wait(e, Tn, f"rt{idx}.{jt}")
                if idx * 4 + jt >= 2:
                    pj = idx * 4 + jt - 2
                    P.wait(e, strd[jt % 2], f"sr{pj // 4}.{pj % 4}")
                P.inc(e.tensor_copy(rfm[:, (jt % 2) * D:(jt % 2) * D + D],
                                    psT[:, 0:D]), Dv, f"rf{idx}.{jt}")
            for di in range(KC):
                P.wait(e, Tn, f"dm{idx}.{di}")
                ins = None
                for n in range(2):
                    ins = e.scalar_tensor_tensor(
                        M[:, b * KC * D + di * D + n * 512:
                          b * KC * D + di * D + n * 512 + 512],
                        M[:, b * KC * D + di * D + n * 512:
                          b * KC * D + di * D + n * 512 + 512],
                        dc["dC"], psA[:, n * 512:n * 512 + 512],
                        ALU.mult, ALU.add)
                P.inc(ins, Dv, f"mu{idx}.{di}")
                P.inc(e.tensor_copy(Mb[:, b * KC * D + di * D:
                                       b * KC * D + (di + 1) * D],
                                    M[:, b * KC * D + di * D:
                                      b * KC * D + (di + 1) * D]),
                      Dv, f"mb{idx}.{di}")

    def b_tensor2(e):
        for idx in range(NBC):
            b, c = idx // NCH, idx % NCH
            sl = (idx % 2)
            qso = sl * KC * CH
            kho = sl * 4 * D
            for k in range(5):
                P.wait(e, lds[k][idx % 2], f"l{k}.{idx}")
            for js in range(4):
                if js >= 2:
                    P.wait(e, Dv, f"ab{idx}.{js - 2}")
                elif idx >= 1:
                    P.wait(e, Dv, f"mu{idx - 1}.7")
                ins = None
                for kc in range(KC):
                    ins = e.matmul(psA[:, (js % 2) * 512:(js % 2) * 512 + 512],
                                   kf[:, qso + kc * CH + js * 128:
                                      qso + kc * CH + js * 128 + 128],
                                   qf[:, qso + kc * CH:qso + kc * CH + CH],
                                   start=(kc == 0), stop=(kc == KC - 1))
                P.inc(ins, Tn, f"aw{idx}.{js}")
            if c != 0:
                P.wait(e, Dv, f"mb{idx - 1}.7")
            P.wait(e, Dv, f"ab{idx}.3")
            for jt in range(4):
                if jt >= 2:
                    P.wait(e, Dv, f"re{idx}.{jt - 2}")
                elif idx >= 1:
                    P.wait(e, Dv, f"re{idx - 1}.{jt + 2}")
                ins = None
                ngrp = 0
                for n in range(2):
                    for kc in range(KC):
                        ins = e.matmul(psR[:, (jt % 2) * D + n * 512:
                                           (jt % 2) * D + n * 512 + 512],
                                       qsf[:, qso + kc * CH + jt * 128:
                                           qso + kc * CH + jt * 128 + 128],
                                       Mb[:, b * KC * D + kc * D + n * 512:
                                          b * KC * D + kc * D + n * 512 + 512],
                                       start=(kc == 0), stop=False)
                    for js in range(4):
                        ins = e.matmul(psR[:, (jt % 2) * D + n * 512:
                                           (jt % 2) * D + n * 512 + 512],
                                       absf[:, js * CH + jt * 128:
                                            js * CH + jt * 128 + 128],
                                       vt[:, kho + js * D + n * 512:
                                          kho + js * D + n * 512 + 512],
                                       start=False, stop=(js == 3))
                P.inc(ins, Tn, f"rd{idx}.{jt}")
                P.wait(e, Dv, f"re{idx}.{jt}")
                if idx * 4 + jt >= 1:
                    pj = idx * 4 + jt - 1
                    P.wait(e, Dv, f"rf{pj // 4}.{pj % 4}")
                ins = None
                for cc in range(KC):
                    ins = e.transpose(psT[:, cc * 128:(cc + 1) * 128],
                                      rbf[:, (jt % 2) * D + cc * 128:
                                          (jt % 2) * D + (cc + 1) * 128],
                                      P.identb)
                P.inc(ins, Tn, f"rt{idx}.{jt}")
            for di in range(KC):
                if di >= 1:
                    P.wait(e, Dv, f"mu{idx}.{di - 1}")
                ins = None
                for n in range(2):
                    for js in range(4):
                        ins = e.matmul(psA[:, n * 512:n * 512 + 512],
                                       kht[:, kho + js * D + di * 128:
                                           kho + js * D + di * 128 + 128],
                                       vt[:, kho + js * D + n * 512:
                                          kho + js * D + n * 512 + 512],
                                       start=(js == 0), stop=(js == 3))
                P.inc(ins, Tn, f"dm{idx}.{di}")

    def b_gpsimd(e):
        for idx in range(NBC):
            b, c = idx // NCH, idx % NCH
            off = b * S + c * CH
            for jt in range(4):
                P.wait(e, Dv, f"rf{idx}.{jt}")
                P.inc(e.dma_start(P.ap("read_fm")[:, :, off + jt * 128:
                                                  off + (jt + 1) * 128]
                                  .rearrange("c p j -> p c j"),
                                  rfm[:, (jt % 2) * D:(jt % 2) * D + D]
                                  .rearrange("p (c j) -> p c j", c=KC)),
                      strd[jt % 2], f"sr{idx}.{jt}", 16)

    P.twopass([("sync", b_sync), ("vector", b_vector), ("tensor", b_tensor2),
               ("gpsimd", b_gpsimd)])



def ph_proj_tm(P, es, src_fm, wname, out_tm, bias=None, w_transposed=False,
               extra_tm=(), out_bf16=False):
    nc = P.nc
    A, Dv, Tn = (P.sem(n) for n in ["A", "Dv", "Tn"])
    lds = [P.dsem(0), P.dsem(1)]
    eld = [[P.dsem(2 + 2 * k), P.dsem(3 + 2 * k)] for k in range(len(extra_tm))]
    sto = [P.dsem(10), P.dsem(11)]
    wsb = es.enter_context(nc.sbuf_tensor("pj_w", [128, KC * D], BF16))
    if w_transposed:
        load_cast_transpose_weights(P, es, [(P.ap(wname), wsb[:, :])])
    else:
        load_cast_weights(P, es, [(P.ap(wname), wsb[:, :])])
    bb = bias_bcast(P, es, [bias])[bias] if bias else None
    xfm = es.enter_context(nc.sbuf_tensor("pj_x", [128, 2 * D], BF16))
    ext = [es.enter_context(nc.sbuf_tensor(f"pj_e{k}", [128, 2 * D], F32))
           for k in range(len(extra_tm))]
    ob = es.enter_context(nc.sbuf_tensor("pj_o", [128, 2 * D],
                                         BF16 if out_bf16 else F32))
    psP = es.enter_context(nc.psum_tensor("pj_ps", [128, 2 * D], F32))

    def slot(i):
        return (i % 2) * D

    def b_sync(e):
        for i in range(NT):
            if i >= 2:
                P.wait(e, Tn, f"pj{i - 2}")
            P.inc(e.dma_start(xfm[:, slot(i):slot(i) + D]
                              .rearrange("p (c j) -> p c j", c=KC),
                              P.ap(src_fm)[:, :, i * 128:(i + 1) * 128]
                              .rearrange("c p j -> p c j")),
                  lds[i % 2], f"lx{i}", 16)
            for k, nm in enumerate(extra_tm):
                if i >= 2:
                    P.wait(e, Dv, f"ev{i - 2}")
                P.inc(e.dma_start(ext[k][:, slot(i):slot(i) + D],
                                  P.ap(nm)[i * 128:(i + 1) * 128, :]),
                      eld[k][i % 2], f"le{k}.{i}", 16)

    def b_tensor(e):
        for i in range(NT):
            P.wait(e, lds[i % 2], f"lx{i}")
            if i >= 2:
                P.wait(e, Dv, f"ev{i - 2}")
            ins = None
            for n in range(2):
                for kc in range(KC):
                    ins = e.matmul(psP[:, slot(i) + n * 512:slot(i) + n * 512 + 512],
                                   xfm[:, slot(i) + kc * 128:slot(i) + kc * 128 + 128],
                                   wsb[:, kc * D + n * 512:kc * D + n * 512 + 512],
                                   start=(kc == 0), stop=(kc == KC - 1))
            P.inc(ins, Tn, f"pj{i}")

    def b_vector(e):
        for i in range(NT):
            P.wait(e, Tn, f"pj{i}")
            if i >= 2:
                P.wait(e, sto[i % 2], f"so{i - 2}")
            for k in range(len(extra_tm)):
                P.wait(e, eld[k][i % 2], f"le{k}.{i}")
            src = psP[:, slot(i):slot(i) + D]
            if bias is not None:
                e.tensor_add(ob[:, slot(i):slot(i) + D], src, bb)
                src = ob[:, slot(i):slot(i) + D]
            for k in range(len(extra_tm)):
                e.tensor_add(ob[:, slot(i):slot(i) + D], src,
                             ext[k][:, slot(i):slot(i) + D])
                src = ob[:, slot(i):slot(i) + D]
            if bias is None and not extra_tm:
                e.tensor_copy(ob[:, slot(i):slot(i) + D], src)
            P.inc(e.tensor_copy(P.jD[:, 2:3], P.jD[:, 3:4]), Dv, f"ev{i}")

    def b_gpsimd(e):
        for i in range(NT):
            P.wait(e, Dv, f"ev{i}")
            P.inc(e.dma_start(P.ap(out_tm)[i * 128:(i + 1) * 128, :],
                              ob[:, slot(i):slot(i) + D]),
                  sto[i % 2], f"so{i}", 16)

    P.twopass([("sync", b_sync), ("tensor", b_tensor), ("vector", b_vector),
               ("gpsimd", b_gpsimd)])



def ph_shared0(P, es):
    nc = P.nc
    A, Dv, Tn = (P.sem(n) for n in ["A", "Dv", "Tn"])
    lds = [P.dsem(0), P.dsem(1)]
    sto = [P.dsem(10), P.dsem(11)]
    F2 = 2048
    wg = es.enter_context(nc.sbuf_tensor("sh_wg", [128, KC * F2], BF16))
    wu = es.enter_context(nc.sbuf_tensor("sh_wu", [128, KC * F2], BF16))
    wd = es.enter_context(nc.sbuf_tensor("sh_wd", [128, (F2 // 128) * D], BF16))
    load_cast_transpose_weights(P, es, [(P.ap("L0_ffn_w_gate"), wg[:, :]),
                                        (P.ap("L0_ffn_w_up"), wu[:, :]),
                                        (P.ap("L0_ffn_w_down"), wd[:, :])])
    TC = 512
    NTC = T // TC
    xfm = es.enter_context(nc.sbuf_tensor("sh_x", [128, 2 * KC * TC], BF16))
    hg = es.enter_context(nc.sbuf_tensor("sh_hg", [128, TC], F32))
    hfm = es.enter_context(nc.sbuf_tensor("sh_h", [128, (F2 // 128) * TC], BF16))
    ob = es.enter_context(nc.sbuf_tensor("sh_o", [128, 2 * 2 * D], F32))
    psG = es.enter_context(nc.psum_tensor("sh_pg", [128, TC], F32))
    psU = es.enter_context(nc.psum_tensor("sh_pu", [128, TC], F32))
    psD = es.enter_context(nc.psum_tensor("sh_pd", [128, 2 * D], F32))
    NF = F2 // 128

    def b_sync(e):
        for i in range(NTC):
            if i >= 2:
                P.wait(e, Tn, f"gu{i - 2}.{NF - 1}")
            P.inc(e.dma_start(xfm[:, (i % 2) * KC * TC:(i % 2 + 1) * KC * TC]
                              .rearrange("p (c j) -> p c j", c=KC),
                              P.ap("xn_fm")[:, :, i * TC:(i + 1) * TC]
                              .rearrange("c p j -> p c j")),
                  lds[i % 2], f"lx{i}", 16)

    def b_tensor(e):
        for i in range(NTC):
            P.wait(e, lds[i % 2], f"lx{i}")
            xo = (i % 2) * KC * TC
            for j in range(NF):
                if j >= 1:
                    P.wait(e, Dv, f"hm{i}.{j - 1}")
                elif i >= 1:
                    P.wait(e, Dv, f"hm{i - 1}.{NF - 1}")
                ins = None
                for kc in range(KC):
                    ins = e.matmul(psG[:, :],
                                   wg[:, kc * F2 + j * 128:kc * F2 + (j + 1) * 128],
                                   xfm[:, xo + kc * TC:xo + kc * TC + TC],
                                   start=(kc == 0), stop=(kc == KC - 1))
                for kc in range(KC):
                    ins = e.matmul(psU[:, :],
                                   wu[:, kc * F2 + j * 128:kc * F2 + (j + 1) * 128],
                                   xfm[:, xo + kc * TC:xo + kc * TC + TC],
                                   start=(kc == 0), stop=(kc == KC - 1))
                P.inc(ins, Tn, f"gu{i}.{j}")
            for ts in range(4):
                P.wait(e, Dv, f"hm{i}.{NF - 1}")
                if ts >= 2:
                    P.wait(e, Dv, f"de{i}.{ts - 2}")
                elif i >= 1:
                    P.wait(e, Dv, f"de{i - 1}.{ts + 2}")
                ins = None
                for n in range(2):
                    for kc in range(NF):
                        ins = e.matmul(psD[:, (ts % 2) * D + n * 512:
                                           (ts % 2) * D + n * 512 + 512],
                                       hfm[:, kc * TC + ts * 128:
                                           kc * TC + ts * 128 + 128],
                                       wd[:, kc * D + n * 512:kc * D + n * 512 + 512],
                                       start=(kc == 0), stop=(kc == NF - 1))
                P.inc(ins, Tn, f"dn{i}.{ts}")

    def b_scalar(e):
        for i in range(NTC):
            for j in range(NF):
                P.wait(e, Tn, f"gu{i}.{j}")
                P.inc(e.activation(hg[:, :], psG[:, :], AF.Silu), A, f"sl{i}.{j}")

    def b_vector(e):
        for i in range(NTC):
            for j in range(NF):
                P.wait(e, A, f"sl{i}.{j}")
                P.inc(e.tensor_mul(hfm[:, j * TC:(j + 1) * TC], hg[:, :],
                                   psU[:, :]), Dv, f"hm{i}.{j}")
            for ts in range(4):
                P.wait(e, Tn, f"dn{i}.{ts}")
                gl = i * 4 + ts
                if gl >= 4:
                    P.wait(e, sto[gl % 2], f"so{(gl - 4) // 4}.{(gl - 4) % 4}")
                P.inc(e.tensor_copy(ob[:, (gl % 4) * D:(gl % 4) * D + D],
                                    psD[:, (ts % 2) * D:(ts % 2) * D + D]),
                      Dv, f"de{i}.{ts}")

    def b_gpsimd(e):
        for i in range(NTC):
            for ts in range(4):
                P.wait(e, Dv, f"de{i}.{ts}")
                gl = i * 4 + ts
                P.inc(e.dma_start(P.ap("ffn0_tm")[gl * 128:(gl + 1) * 128, :],
                                  ob[:, (gl % 4) * D:(gl % 4) * D + D]),
                      sto[gl % 2], f"so{i}.{ts}", 16)

    P.twopass([("sync", b_sync), ("tensor", b_tensor), ("scalar", b_scalar),
               ("vector", b_vector), ("gpsimd", b_gpsimd)])



def ph_ffn0e(P, es):
    nc = P.nc
    A, Dv, Tn = (P.sem(n) for n in ["A", "Dv", "Tn"])
    lds = [P.dsem(0), P.dsem(1)]
    sto = [P.dsem(10 + k) for k in range(8)]
    NE = 16
    F2 = 4096
    ND = 2048 // 128
    wgu = es.enter_context(nc.sbuf_tensor("fe_wgu", [128, KC * F2], BF16))
    wd = es.enter_context(nc.sbuf_tensor("fe_wd", [128, ND * D], BF16))
    ssc = es.enter_context(nc.sbuf_tensor("fe_sc", [128, NT * NE], F32))
    TC = 512
    NTC = T // TC
    xtm = es.enter_context(nc.sbuf_tensor("fe_x", [128, 2 * 4 * D], BF16))
    xsb = es.enter_context(nc.sbuf_tensor("fe_xs", [128, 4 * D], BF16))
    xfm = es.enter_context(nc.sbuf_tensor("fe_xf", [128, KC * TC], BF16))
    hg = es.enter_context(nc.sbuf_tensor("fe_hg", [128, TC], F32))
    hfm = es.enter_context(nc.sbuf_tensor("fe_h", [128, (F2 // 2 // 128) * TC], BF16))
    ob = es.enter_context(nc.sbuf_tensor("fe_o", [128, 4 * D], F32))
    psG = es.enter_context(nc.psum_tensor("fe_pg", [128, TC], F32))
    psU = es.enter_context(nc.psum_tensor("fe_pu", [128, TC], F32))
    psD = es.enter_context(nc.psum_tensor("fe_pd", [128, 2 * D], F32))
    psT = es.enter_context(nc.psum_tensor("fe_pt", [128, D], BF16))
    NF = F2 // 2 // 128

    ld22 = P.dsem(22)
    with nc.Block() as blk:
        @blk.sync
        def _(e):
            e.dma_start(ssc[:, :].rearrange("p (i e) -> p i e", i=NT),
                        P.ap("scores_tm").rearrange("(i p) e -> p i e", p=128))\
                .then_inc(ld22.h, 16)
            e.wait_ge(ld22.h, ld22.v + 16)
    ld22.v += 16

    for ei in range(NE):
        with contextlib.ExitStack() as wes:
            load_cast_weights(P, wes,
                              [(P.ap("L0_ffn_gate_up_proj")[ei], wgu[:, :]),
                               (P.ap("L0_ffn_down_proj")[ei], wd[:, :])],
                              tag=f"e{ei}")

        def b_sync(e, ei=ei):
            for i in range(NTC):
                if i >= 2:
                    P.wait(e, Tn, f"e{ei}gu{i - 2}.{NF - 1}")
                elif ei >= 1 and i == 0:
                    P.wait(e, Tn, f"e{ei - 1}gu{NTC - 2}.{NF - 1}")
                elif ei >= 1 and i == 1:
                    P.wait(e, Tn, f"e{ei - 1}gu{NTC - 1}.{NF - 1}")
                P.inc(e.dma_start(xtm[:, (i % 2) * 4 * D:(i % 2) * 4 * D + 4 * D]
                                  .rearrange("p (s d) -> p s d", s=4),
                                  P.ap("xn_tm")[i * TC:(i + 1) * TC, :]
                                  .rearrange("(s p) d -> p s d", p=128)),
                      lds[i % 2], f"e{ei}lx{i}", 16)

        def b_vector(e, ei=ei):
            for i in range(NTC):
                P.wait(e, lds[i % 2], f"e{ei}lx{i}")
                for su in range(4):
                    if i > 0:
                        P.wait(e, Tn, f"e{ei}tr{i - 1}.{su}")
                    elif ei > 0:
                        P.wait(e, Tn, f"e{ei - 1}tr{NTC - 1}.{su}")
                    P.inc(e.tensor_scalar(xsb[:, su * D:(su + 1) * D],
                                          xtm[:, (i % 2) * 4 * D + su * D:
                                              (i % 2) * 4 * D + (su + 1) * D],
                                          ssc[:, (i * 4 + su) * NE + ei:
                                              (i * 4 + su) * NE + ei + 1],
                                          None, ALU.mult), Dv, f"e{ei}xs{i}.{su}")
                    P.wait(e, Tn, f"e{ei}tr{i}.{su}")
                    P.inc(e.tensor_copy(
                        xfm[:, :].rearrange("p (c j) -> p c j", c=KC)
                        [:, :, su * 128:(su + 1) * 128],
                        psT[:, :].rearrange("p (c j) -> p c j", c=KC)),
                        Dv, f"e{ei}xf{i}.{su}")
                for j in range(NF):
                    P.wait(e, A, f"e{ei}sl{i}.{j}")
                    P.inc(e.tensor_mul(hfm[:, j * TC:(j + 1) * TC], hg[:, :],
                                       psU[:, :]), Dv, f"e{ei}hm{i}.{j}")
                for ts in range(4):
                    P.wait(e, Tn, f"e{ei}dn{i}.{ts}")
                    if i >= 1:
                        P.wait(e, sto[((i - 1) * 4 + ts) % 8],
                               f"e{ei}so{i - 1}.{ts}")
                    elif ei >= 1:
                        P.wait(e, sto[((NTC - 1) * 4 + ts) % 8],
                               f"e{ei - 1}so{NTC - 1}.{ts}")
                    P.inc(e.tensor_copy(ob[:, ts * D:(ts + 1) * D],
                                        psD[:, (ts % 2) * D:(ts % 2) * D + D]),
                          Dv, f"e{ei}de{i}.{ts}")

        def b_tensor(e, ei=ei):
            for i in range(NTC):
                for su in range(4):
                    P.wait(e, Dv, f"e{ei}xs{i}.{su}")
                    if i > 0 or su > 0 or ei > 0:
                        P.wait(e, Dv, f"e{ei}xf{i}.{su - 1}" if su > 0 else
                               (f"e{ei}xf{i - 1}.3" if i > 0 else
                                f"e{ei - 1}xf{NTC - 1}.3"))
                    ins = None
                    for c in range(KC):
                        ins = e.transpose(psT[:, c * 128:(c + 1) * 128],
                                          xsb[:, su * D + c * 128:
                                              su * D + (c + 1) * 128], P.identb)
                    P.inc(ins, Tn, f"e{ei}tr{i}.{su}")
                P.wait(e, Dv, f"e{ei}xf{i}.3")
                for j in range(NF):
                    if j >= 1:
                        P.wait(e, Dv, f"e{ei}hm{i}.{j - 1}")
                    elif i >= 1:
                        P.wait(e, Dv, f"e{ei}hm{i - 1}.{NF - 1}")
                    elif ei >= 1:
                        P.wait(e, Dv, f"e{ei - 1}hm{NTC - 1}.{NF - 1}")
                    ins = None
                    for kc in range(KC):
                        ins = e.matmul(psG[:, :],
                                       wgu[:, kc * F2 + j * 128:
                                           kc * F2 + (j + 1) * 128],
                                       xfm[:, kc * TC:(kc + 1) * TC],
                                       start=(kc == 0), stop=(kc == KC - 1))
                    for kc in range(KC):
                        ins = e.matmul(psU[:, :],
                                       wgu[:, kc * F2 + 2048 + j * 128:
                                           kc * F2 + 2048 + (j + 1) * 128],
                                       xfm[:, kc * TC:(kc + 1) * TC],
                                       start=(kc == 0), stop=(kc == KC - 1))
                    P.inc(ins, Tn, f"e{ei}gu{i}.{j}")
                for ts in range(4):
                    P.wait(e, Dv, f"e{ei}hm{i}.{NF - 1}")
                    if ts >= 2:
                        P.wait(e, Dv, f"e{ei}de{i}.{ts - 2}")
                    elif i >= 1:
                        P.wait(e, Dv, f"e{ei}de{i - 1}.{ts + 2}")
                    elif ei >= 1:
                        P.wait(e, Dv, f"e{ei - 1}de{NTC - 1}.{ts + 2}")
                    ins = None
                    for n in range(2):
                        for kc in range(ND):
                            ins = e.matmul(psD[:, (ts % 2) * D + n * 512:
                                               (ts % 2) * D + n * 512 + 512],
                                           hfm[:, kc * TC + ts * 128:
                                               kc * TC + ts * 128 + 128],
                                           wd[:, kc * D + n * 512:
                                              kc * D + n * 512 + 512],
                                           start=(kc == 0), stop=(kc == ND - 1))
                    P.inc(ins, Tn, f"e{ei}dn{i}.{ts}")

        def b_scalar(e, ei=ei):
            for i in range(NTC):
                for j in range(NF):
                    P.wait(e, Tn, f"e{ei}gu{i}.{j}")
                    P.inc(e.activation(hg[:, :], psG[:, :], AF.Silu),
                          A, f"e{ei}sl{i}.{j}")

        def b_gpsimd(e, ei=ei):
            for i in range(NTC):
                for ts in range(4):
                    P.wait(e, Dv, f"e{ei}de{i}.{ts}")
                    gl = i * 4 + ts
                    if ei >= 1:
                        P.wait(e, sto[gl % 8], f"e{ei - 1}so{i}.{ts}")
                    elif gl >= 8:
                        P.wait(e, sto[gl % 8], f"e{ei}so{i - 2}.{ts}")
                    if ei == 0:
                        ins = e.dma_start(
                            P.ap("ffn0e_tm")[gl * 128:(gl + 1) * 128, :],
                            ob[:, ts * D:(ts + 1) * D])
                    else:
                        ins = e.dma_start(
                            P.ap("ffn0e_tm")[gl * 128:(gl + 1) * 128, :],
                            ob[:, ts * D:(ts + 1) * D], accum_op=ALU.add)
                    P.inc(ins, sto[gl % 8], f"e{ei}so{i}.{ts}", 16)

        P.twopass([("sync", b_sync), ("vector", b_vector), ("tensor", b_tensor),
                   ("scalar", b_scalar), ("gpsimd", b_gpsimd)])



def ph_add(P, es, srcs, out):
    nc = P.nc
    Dv = P.sem("Dv")
    lds = [[P.dsem(2 * k), P.dsem(2 * k + 1)] for k in range(len(srcs))]
    sto = [P.dsem(10), P.dsem(11)]
    bufs = [es.enter_context(nc.sbuf_tensor(f"ad_{k}", [128, 2 * D], F32))
            for k in range(len(srcs))]
    ob = es.enter_context(nc.sbuf_tensor("ad_o", [128, 2 * D], F32))

    def slot(i):
        return (i % 2) * D

    def b_sync(e):
        for i in range(NT):
            for k, nm in enumerate(srcs):
                if i >= 2:
                    P.wait(e, Dv, f"ao{i - 2}")
                P.inc(e.dma_start(bufs[k][:, slot(i):slot(i) + D],
                                  P.ap(nm)[i * 128:(i + 1) * 128, :]),
                      lds[k][i % 2], f"l{k}.{i}", 16)

    def b_vector(e):
        for i in range(NT):
            for k in range(len(srcs)):
                P.wait(e, lds[k][i % 2], f"l{k}.{i}")
            if i >= 2:
                P.wait(e, sto[i % 2], f"s{i - 2}")
            e.tensor_add(ob[:, slot(i):slot(i) + D], bufs[0][:, slot(i):slot(i) + D],
                         bufs[1][:, slot(i):slot(i) + D])
            for k in range(2, len(srcs)):
                e.tensor_add(ob[:, slot(i):slot(i) + D],
                             ob[:, slot(i):slot(i) + D],
                             bufs[k][:, slot(i):slot(i) + D])
            P.inc(e.tensor_copy(P.jD[:, 2:3], P.jD[:, 3:4]), Dv, f"ao{i}")

    def b_gpsimd(e):
        for i in range(NT):
            P.wait(e, Dv, f"ao{i}")
            P.inc(e.dma_start(P.ap(out)[i * 128:(i + 1) * 128, :],
                              ob[:, slot(i):slot(i) + D]),
                  sto[i % 2], f"s{i}", 16)

    P.twopass([("sync", b_sync), ("vector", b_vector), ("gpsimd", b_gpsimd)])



def ph_qkv1(P, es):
    nc = P.nc
    A, Dv, Tn = (P.sem(n) for n in ["A", "Dv", "Tn"])
    lds = [P.dsem(0), P.dsem(1)]
    std = {"pq": [P.dsem(10), P.dsem(11)], "pkf": [P.dsem(12), P.dsem(13)],
           "pkt": [P.dsem(14), P.dsem(15)], "v": [P.dsem(16), P.dsem(17)]}
    wq = es.enter_context(nc.sbuf_tensor("q1_wq", [128, KC * D], BF16))
    wk = es.enter_context(nc.sbuf_tensor("q1_wk", [128, KC * D], BF16))
    wv = es.enter_context(nc.sbuf_tensor("q1_wv", [128, KC * D], BF16))
    load_cast_transpose_weights(P, es, [(P.ap("L1_attn_Wq"), wq[:, :]),
                                        (P.ap("L1_attn_Wk"), wk[:, :]),
                                        (P.ap("L1_attn_Wv"), wv[:, :])])
    xfm = es.enter_context(nc.sbuf_tensor("q1_x", [128, 2 * D], BF16))
    pqb = es.enter_context(nc.sbuf_tensor("q1_pq", [128, 2 * D], BF16))
    pkb = es.enter_context(nc.sbuf_tensor("q1_pk", [128, 2 * D], BF16))
    vbf = es.enter_context(nc.sbuf_tensor("q1_vb", [128, 2 * D], BF16))
    w1 = es.enter_context(nc.sbuf_tensor("q1_t1", [128, D], F32))
    wm = es.enter_context(nc.sbuf_tensor("q1_tm", [128, 2 * D], F32))
    wr = es.enter_context(nc.sbuf_tensor("q1_tr", [128, 2 * D], F32))
    we = es.enter_context(nc.sbuf_tensor("q1_te", [128, 2 * D], F32))
    qfmb = es.enter_context(nc.sbuf_tensor("q1_qf", [128, 2 * D], BF16))
    kfmb = es.enter_context(nc.sbuf_tensor("q1_kf", [128, 2 * D], BF16))
    psP = es.enter_context(nc.psum_tensor("q1_ps", [128, 3 * D], F32))
    psT = es.enter_context(nc.psum_tensor("q1_pt", [128, 2 * D], BF16))

    def slot(i):
        return (i % 2) * D

    def b_sync(e):
        for i in range(NT):
            if i >= 2:
                P.wait(e, Tn, f"pjv{i - 2}")
            P.inc(e.dma_start(xfm[:, slot(i):slot(i) + D]
                              .rearrange("p (c j) -> p c j", c=KC),
                              P.ap("xn1_fm")[:, :, i * 128:(i + 1) * 128]
                              .rearrange("c p j -> p c j")),
                  lds[i % 2], f"lx{i}", 16)

    def b_tensor(e):
        for i in range(NT):
            P.wait(e, lds[i % 2], f"lx{i}")
            if i >= 1:
                P.wait(e, Dv, f"vb{i - 1}")
            for wi, (wsb, tag) in enumerate([(wq, "q"), (wk, "k"), (wv, "v")]):
                ins = None
                for n in range(2):
                    for kc in range(KC):
                        ins = e.matmul(psP[:, wi * D + n * 512:wi * D + n * 512 + 512],
                                       xfm[:, slot(i) + kc * 128:slot(i) + kc * 128 + 128],
                                       wsb[:, kc * D + n * 512:kc * D + n * 512 + 512],
                                       start=(kc == 0), stop=(kc == KC - 1))
                P.inc(ins, Tn, f"pj{tag}{i}")
            P.wait(e, Dv, f"fq{i}")
            if i >= 1:
                P.wait(e, Dv, f"eq{i - 1}")
            ins = None
            for c in range(KC):
                ins = e.transpose(psT[:, c * 128:(c + 1) * 128],
                                  pqb[:, slot(i) + c * 128:slot(i) + (c + 1) * 128],
                                  P.identb)
            P.inc(ins, Tn, f"tq{i}")
            P.wait(e, Dv, f"fk{i}")
            if i >= 1:
                P.wait(e, Dv, f"ek{i - 1}")
            ins = None
            for c in range(KC):
                ins = e.transpose(psT[:, D + c * 128:D + (c + 1) * 128],
                                  pkb[:, slot(i) + c * 128:slot(i) + (c + 1) * 128],
                                  P.identb)
            P.inc(ins, Tn, f"tk{i}")

    def b_scalar(e):
        for i in range(NT):
            P.wait(e, Dv, f"mq{i}")
            P.inc(e.activation(we[:, 0:D], wm[:, 0:D], AF.Exp), A, f"xq{i}")
            P.wait(e, Dv, f"mk{i}")
            P.inc(e.activation(we[:, D:2 * D], wm[:, D:2 * D], AF.Exp), A, f"xk{i}")

    def b_vector(e):
        for i in range(NT):
            P.wait(e, Tn, f"pjq{i}")
            e.tensor_copy(w1[:, :], psP[:, 0:D])
            e.tensor_relu(wr[:, 0:D], w1[:, :])
            P.inc(e.tensor_scalar(wm[:, 0:D], w1[:, :], 0.0, None, ALU.min),
                  Dv, f"mq{i}")
            P.wait(e, Tn, f"pjk{i}")
            e.tensor_copy(w1[:, :], psP[:, D:2 * D])
            e.tensor_relu(wr[:, D:2 * D], w1[:, :])
            P.inc(e.tensor_scalar(wm[:, D:2 * D], w1[:, :], 0.0, None, ALU.min),
                  Dv, f"mk{i}")
            P.wait(e, Tn, f"pjv{i}")
            if i >= 2:
                P.wait(e, std["v"][i % 2], f"sv{i - 2}")
            P.inc(e.tensor_copy(vbf[:, slot(i):slot(i) + D], psP[:, 2 * D:3 * D]),
                  Dv, f"vb{i}")
            P.wait(e, A, f"xq{i}")
            if i >= 2:
                P.wait(e, Tn, f"tq{i - 2}")
            P.inc(e.tensor_add(pqb[:, slot(i):slot(i) + D], we[:, 0:D],
                               wr[:, 0:D]), Dv, f"fq{i}")
            P.wait(e, A, f"xk{i}")
            if i >= 2:
                P.wait(e, Tn, f"tk{i - 2}")
                P.wait(e, std["pkt"][i % 2], f"skt{i - 2}")
            P.inc(e.tensor_add(pkb[:, slot(i):slot(i) + D], we[:, D:2 * D],
                               wr[:, D:2 * D]), Dv, f"fk{i}")
            P.wait(e, Tn, f"tq{i}")
            if i >= 2:
                P.wait(e, std["pq"][i % 2], f"sq{i - 2}")
            P.inc(e.tensor_copy(qfmb[:, slot(i):slot(i) + D], psT[:, 0:D]),
                  Dv, f"eq{i}")
            P.wait(e, Tn, f"tk{i}")
            if i >= 2:
                P.wait(e, std["pkf"][i % 2], f"skf{i - 2}")
            P.inc(e.tensor_copy(kfmb[:, slot(i):slot(i) + D], psT[:, D:2 * D]),
                  Dv, f"ek{i}")

    def b_gpsimd(e):
        for i in range(NT):
            P.wait(e, Dv, f"eq{i}")
            P.inc(e.dma_start(P.ap("phiq_fm")[:, :, i * 128:(i + 1) * 128]
                              .rearrange("c p j -> p c j"),
                              qfmb[:, slot(i):slot(i) + D]
                              .rearrange("p (c j) -> p c j", c=KC)),
                  std["pq"][i % 2], f"sq{i}", 16)
            P.wait(e, Dv, f"ek{i}")
            P.inc(e.dma_start(P.ap("phik_fm")[:, :, i * 128:(i + 1) * 128]
                              .rearrange("c p j -> p c j"),
                              kfmb[:, slot(i):slot(i) + D]
                              .rearrange("p (c j) -> p c j", c=KC)),
                  std["pkf"][i % 2], f"skf{i}", 16)
            P.wait(e, Dv, f"fk{i}")
            P.inc(e.dma_start(P.ap("phik_tm")[i * 128:(i + 1) * 128, :],
                              pkb[:, slot(i):slot(i) + D]),
                  std["pkt"][i % 2], f"skt{i}", 16)
            P.wait(e, Dv, f"vb{i}")
            P.inc(e.dma_start(P.ap("v1_tm")[i * 128:(i + 1) * 128, :],
                              vbf[:, slot(i):slot(i) + D]),
                  std["v"][i % 2], f"sv{i}", 16)

    P.twopass([("sync", b_sync), ("scalar", b_scalar), ("vector", b_vector),
               ("tensor", b_tensor), ("gpsimd", b_gpsimd)])



def ph_scan1(P, es):
    nc = P.nc
    A, Dv, Tn, xD = (P.sem(n) for n in ["A", "Dv", "Tn", "xD"])
    lds = [[P.dsem(2 * k), P.dsem(2 * k + 1)] for k in range(4)]
    strd = [P.dsem(10), P.dsem(11)]
    ld22 = P.dsem(22)
    NH = 16
    DH = 64
    W = DH + 1
    msk = es.enter_context(nc.sbuf_tensor("s1_mk", [128, 4 * CH], F32))
    Sg = es.enter_context(nc.sbuf_tensor("s1_S", [128, B_CORE * NH * W], F32))
    Sb = es.enter_context(nc.sbuf_tensor("s1_Sb", [128, B_CORE * NH * W], BF16))
    pqf = es.enter_context(nc.sbuf_tensor("s1_pq", [128, 2 * KC * CH], BF16))
    pkf = es.enter_context(nc.sbuf_tensor("s1_pk", [128, 2 * KC * CH], BF16))
    pkt = es.enter_context(nc.sbuf_tensor("s1_kt", [128, 2 * 4 * D], BF16))
    vt = es.enter_context(nc.sbuf_tensor("s1_v", [128, 2 * 4 * D], BF16))
    vaug = es.enter_context(nc.sbuf_tensor("s1_va", [128, 4 * NH * W], BF16))
    absf = es.enter_context(nc.sbuf_tensor("s1_ab", [128, 4 * CH], BF16))
    ytm = es.enter_context(nc.sbuf_tensor("s1_y", [128, 4 * D], BF16))
    yfm = es.enter_context(nc.sbuf_tensor("s1_yf", [128, 2 * D], BF16))
    nrw = es.enter_context(nc.sbuf_tensor("s1_nw", [128, 8], F32))
    psA = es.enter_context(nc.psum_tensor("s1_pa", [128, 4 * 512], F32))
    psR = es.enter_context(nc.psum_tensor("s1_pr", [128, 2 * 512], F32))
    psD = es.enter_context(nc.psum_tensor("s1_pd", [128, W], F32))
    psT = es.enter_context(nc.psum_tensor("s1_pt", [128, D], BF16))
    NBC = B_CORE * NCH

    with nc.Block() as blk:
        @blk.sync
        def _(e):
            e.dma_start(msk[:, :].rearrange("p (j t) -> p j t", j=4),
                        P.ap("c_mask1").rearrange("(j p) t -> p j t", p=128))\
                .then_inc(ld22.h, 16)

        @blk.vector
        def _(e):
            e.wait_ge(ld22.h, ld22.v + 16)
            e.memset(Sg[:, :], 0.0)
            e.memset(Sb[:, :], 0.0)
            e.memset(vaug[:, :], 1.0)
    ld22.v += 16

    def hq(h):
        return h // 2, (h % 2) * DH

    def b_sync(e):
        for idx in range(NBC):
            if idx >= 2:
                P.wait(e, Tn, f"ds{idx - 2}.15")
            for k, (name, fm) in enumerate([("phiq_fm", True), ("phik_fm", True),
                                            ("phik_tm", False), ("v1_tm", False)]):
                b, c = idx // NCH, idx % NCH
                off = b * S + c * CH
                sl = idx % 2
                if fm:
                    dst = [pqf, pkf][k][:, sl * KC * CH:(sl + 1) * KC * CH]
                    P.inc(e.dma_start(dst.rearrange("p (c j) -> p c j", c=KC),
                                      P.ap(name)[:, :, off:off + CH]
                                      .rearrange("c p j -> p c j")),
                          lds[k][idx % 2], f"l{k}.{idx}", 16)
                else:
                    dst = [None, None, pkt, vt][k][:, sl * 4 * D:(sl + 1) * 4 * D]
                    P.inc(e.dma_start(dst.rearrange("p (s d) -> p s d", s=4),
                                      P.ap(name)[off:off + CH, :]
                                      .rearrange("(s p) d -> p s d", p=128)),
                          lds[k][idx % 2], f"l{k}.{idx}", 16)

    def b_vector(e):
        for idx in range(NBC):
            sl = idx % 2
            P.wait(e, lds[3][idx % 2], f"l3.{idx}")
            if idx >= 1:
                P.wait(e, Tn, f"ds{idx - 1}.15")
            ins = None
            for su in range(4):
                for h in range(NH):
                    ins = e.tensor_copy(vaug[:, su * NH * W + h * W:
                                            su * NH * W + h * W + DH],
                                        vt[:, sl * 4 * D + su * D + h * DH:
                                           sl * 4 * D + su * D + (h + 1) * DH])
            P.inc(ins, Dv, f"va{idx}")
            for h in range(NH):
                for js in range(4):
                    P.wait(e, Tn, f"aw{idx}.{h}.{js}")
                    P.inc(e.tensor_mul(absf[:, js * CH:(js + 1) * CH],
                                       psA[:, js * 512:(js + 1) * 512],
                                       msk[:, js * CH:(js + 1) * CH]),
                          Dv, f"ab{idx}.{h}.{js}")
                for jt in range(4):
                    P.wait(e, Tn, f"rd{idx}.{h}.{jt}")
                    ro = ((h * 4 + jt) % 2) * 512
                    P.selfsync(e, e.tensor_scalar(nrw[:, 0:1],
                                                  psR[:, ro + DH:ro + W],
                                                  1e-5, None, ALU.add), xD)
                    P.selfsync(e, e.reciprocal(nrw[:, 1:2], nrw[:, 0:1]), xD)
                    P.inc(e.tensor_scalar(ytm[:, jt * D + h * DH:jt * D + (h + 1) * DH],
                                          psR[:, ro:ro + DH], nrw[:, 1:2],
                                          None, ALU.mult), Dv, f"yn{idx}.{h}.{jt}")
                P.wait(e, Tn, f"ds{idx}.{h}")
                bo = (idx // NCH) * NH * W
                hb = (h % 2) * DH
                P.selfsync(e, e.tensor_add(Sg[hb:hb + DH, bo + h * W:bo + (h + 1) * W],
                                           Sg[hb:hb + DH, bo + h * W:bo + (h + 1) * W],
                                           psD[hb:hb + DH, :]), xD)
                P.selfsync(e, e.tensor_copy(Sb[hb:hb + DH, bo + h * W:bo + (h + 1) * W],
                                            Sg[hb:hb + DH, bo + h * W:bo + (h + 1) * W]),
                           xD)
                P.inc(e.tensor_copy(P.jD[:, 2:3], P.jD[:, 3:4]), Dv, f"su{idx}.{h}")
            P.selfsync(e, e.tensor_copy(P.jD[:, 2:3], P.jD[:, 3:4]), xD)
            P.inc(e.tensor_copy(P.jD[:, 1:2], P.jD[:, 0:1]), Dv, f"ynf{idx}")
            for jt in range(4):
                P.wait(e, Tn, f"yt{idx}.{jt}")
                gl = idx * 4 + jt
                if gl >= 2:
                    P.wait(e, strd[gl % 2], f"sy{(gl - 2) // 4}.{(gl - 2) % 4}")
                P.inc(e.tensor_copy(yfm[:, (gl % 2) * D:(gl % 2) * D + D],
                                    psT[:, :]), Dv, f"ye{idx}.{jt}")

    def b_tensor(e):
        for idx in range(NBC):
            b, c = idx // NCH, idx % NCH
            sl = idx % 2
            qo = sl * KC * CH
            ko = sl * 4 * D
            for k in range(4):
                P.wait(e, lds[k][idx % 2], f"l{k}.{idx}")
            P.wait(e, Dv, f"va{idx}")
            for h in range(NH):
                hc, hr = hq(h)
                for js in range(4):
                    if h > 0:
                        P.wait(e, Dv, f"ab{idx}.{h - 1}.{js}")
                    elif idx >= 1:
                        P.wait(e, Dv, f"ab{idx - 1}.{NH - 1}.{js}")
                    P.inc(e.matmul(psA[:, js * 512:(js + 1) * 512],
                                   pkf[hr:hr + DH, qo + hc * CH + js * 128:
                                       qo + hc * CH + js * 128 + 128],
                                   pqf[hr:hr + DH, qo + hc * CH:qo + hc * CH + CH],
                                   start=True, stop=True), Tn, f"aw{idx}.{h}.{js}")
                P.wait(e, Dv, f"ab{idx}.{h}.3")
                if c != 0:
                    P.wait(e, Dv, f"su{idx - 1}.{h}")
                for jt in range(4):
                    ro = ((h * 4 + jt) % 2) * 512
                    gl = idx * NH * 4 + h * 4 + jt
                    if gl >= 2:
                        pg = gl - 2
                        pidx, ph, pjt = pg // (NH * 4), \
                            (pg % (NH * 4)) // 4, pg % 4
                        P.wait(e, Dv, f"yn{pidx}.{ph}.{pjt}")
                    ins = e.matmul(psR[:, ro:ro + W],
                                   pqf[hr:hr + DH, qo + hc * CH + jt * 128:
                                       qo + hc * CH + jt * 128 + 128],
                                   Sb[hr:hr + DH, b * NH * W + h * W:
                                      b * NH * W + (h + 1) * W],
                                   start=True, stop=False)
                    for js in range(4):
                        ins = e.matmul(psR[:, ro:ro + W],
                                       absf[:, js * CH + jt * 128:
                                            js * CH + jt * 128 + 128],
                                       vaug[:, js * NH * W + h * W:
                                            js * NH * W + (h + 1) * W],
                                       start=False, stop=(js == 3))
                    P.inc(ins, Tn, f"rd{idx}.{h}.{jt}")
                if idx >= 1 or h >= 1:
                    P.wait(e, Dv, f"su{idx}.{h - 1}" if h >= 1 else
                           f"su{idx - 1}.{NH - 1}")
                ins = None
                hb2 = (h % 2) * DH
                for js in range(4):
                    ins = e.matmul(psD[hb2:hb2 + DH, :],
                                   pkt[:, ko + js * D + h * DH:
                                       ko + js * D + (h + 1) * DH],
                                   vaug[:, js * NH * W + h * W:
                                        js * NH * W + (h + 1) * W],
                                   start=(js == 0), stop=(js == 3),
                                   tile_position=(0, hb2))
                P.inc(ins, Tn, f"ds{idx}.{h}")
            for jt in range(4):
                P.wait(e, Dv, f"ynf{idx}")
                gl = idx * 4 + jt
                if gl >= 1:
                    P.wait(e, Dv, f"ye{(gl - 1) // 4}.{(gl - 1) % 4}")
                ins = None
                for cc in range(KC):
                    ins = e.transpose(psT[:, cc * 128:(cc + 1) * 128],
                                      ytm[:, jt * D + cc * 128:
                                          jt * D + (cc + 1) * 128],
                                      P.identb)
                P.inc(ins, Tn, f"yt{idx}.{jt}")

    def b_gpsimd(e):
        for idx in range(NBC):
            b, c = idx // NCH, idx % NCH
            off = b * S + c * CH
            for jt in range(4):
                P.wait(e, Dv, f"ye{idx}.{jt}")
                gl = idx * 4 + jt
                P.inc(e.dma_start(P.ap("y1_fm")[:, :, off + jt * 128:
                                                off + (jt + 1) * 128]
                                  .rearrange("c p j -> p c j"),
                                  yfm[:, (gl % 2) * D:(gl % 2) * D + D]
                                  .rearrange("p (c j) -> p c j", c=KC)),
                      strd[gl % 2], f"sy{idx}.{jt}", 16)

    P.twopass([("sync", b_sync), ("vector", b_vector), ("tensor", b_tensor),
               ("gpsimd", b_gpsimd)])



def ph_qkvmsa(P, es):
    nc = P.nc
    A, Dv, Tn = (P.sem(n) for n in ["A", "Dv", "Tn"])
    lds = [P.dsem(0), P.dsem(1)]
    stq = [P.dsem(10), P.dsem(11)]
    stv = [P.dsem(12), P.dsem(13)]
    D3 = 3 * D
    wq = es.enter_context(nc.sbuf_tensor("qm_w", [128, KC * D3], BF16))
    load_cast_weights(P, es, [(P.ap("L1_ffn_w_qkv"), wq[:, :])])
    TC = 512
    NTC = T // TC
    xfm = es.enter_context(nc.sbuf_tensor("qm_x", [128, 2 * KC * TC], BF16))
    qkb = es.enter_context(nc.sbuf_tensor("qm_qk", [128, 2 * 16 * TC], BF16))
    vbf = es.enter_context(nc.sbuf_tensor("qm_v", [128, 2 * 4 * D], BF16))
    psQ = es.enter_context(nc.psum_tensor("qm_pq", [128, 2 * TC], F32))
    psV = es.enter_context(nc.psum_tensor("qm_pv", [128, 2 * D], F32))

    def b_sync(e):
        for i in range(NTC):
            if i >= 2:
                P.wait(e, Tn, f"v{i - 2}.3")
            P.inc(e.dma_start(xfm[:, (i % 2) * KC * TC:(i % 2 + 1) * KC * TC]
                              .rearrange("p (c j) -> p c j", c=KC),
                              P.ap("xln_fm")[:, :, i * TC:(i + 1) * TC]
                              .rearrange("c p j -> p c j")),
                  lds[i % 2], f"lx{i}", 16)

    def b_tensor(e):
        for i in range(NTC):
            P.wait(e, lds[i % 2], f"lx{i}")
            xo = (i % 2) * KC * TC
            for f in range(16):
                if f >= 2:
                    P.wait(e, Dv, f"qe{i}.{f - 2}")
                elif i >= 1:
                    P.wait(e, Dv, f"qe{i - 1}.{f + 14}")
                ins = None
                for kc in range(KC):
                    ins = e.matmul(psQ[:, (f % 2) * TC:(f % 2) * TC + TC],
                                   wq[:, kc * D3 + f * 128:kc * D3 + (f + 1) * 128],
                                   xfm[:, xo + kc * TC:xo + kc * TC + TC],
                                   start=(kc == 0), stop=(kc == KC - 1))
                P.inc(ins, Tn, f"qk{i}.{f}")
            for su in range(4):
                if su >= 2:
                    P.wait(e, Dv, f"ve{i}.{su - 2}")
                elif i >= 1:
                    P.wait(e, Dv, f"ve{i - 1}.{su + 2}")
                ins = None
                for n in range(2):
                    for kc in range(KC):
                        ins = e.matmul(psV[:, (su % 2) * D + n * 512:
                                           (su % 2) * D + n * 512 + 512],
                                       xfm[:, xo + kc * TC + su * 128:
                                           xo + kc * TC + su * 128 + 128],
                                       wq[:, kc * D3 + 2048 + n * 512:
                                          kc * D3 + 2048 + n * 512 + 512],
                                       start=(kc == 0), stop=(kc == KC - 1))
                P.inc(ins, Tn, f"v{i}.{su}")

    def b_vector(e):
        for i in range(NTC):
            for f in range(16):
                P.wait(e, Tn, f"qk{i}.{f}")
                if i >= 2 and f == 0:
                    P.wait(e, stq[i % 2], f"sq{i - 2}")
                P.inc(e.tensor_copy(qkb[:, (i % 2) * 16 * TC + f * TC:
                                        (i % 2) * 16 * TC + (f + 1) * TC],
                                    psQ[:, (f % 2) * TC:(f % 2) * TC + TC]),
                      Dv, f"qe{i}.{f}")
            for su in range(4):
                P.wait(e, Tn, f"v{i}.{su}")
                gl = i * 4 + su
                if gl >= 8:
                    P.wait(e, stv[(gl // 4) % 2], f"sv{(gl - 8) // 4}.{(gl - 8) % 4}")
                P.inc(e.tensor_copy(vbf[:, (i % 2) * 4 * D + su * D:
                                        (i % 2) * 4 * D + (su + 1) * D],
                                    psV[:, (su % 2) * D:(su % 2) * D + D]),
                      Dv, f"ve{i}.{su}")

    def b_gpsimd(e):
        for i in range(NTC):
            P.wait(e, Dv, f"qe{i}.15")
            P.inc(e.dma_start(P.ap("qk_fm")[:, :, i * TC:(i + 1) * TC]
                              .rearrange("c p j -> p c j"),
                              qkb[:, (i % 2) * 16 * TC:(i % 2 + 1) * 16 * TC]
                              .rearrange("p (c j) -> p c j", c=16)),
                  stq[i % 2], f"sq{i}", 16)
            for su in range(4):
                P.wait(e, Dv, f"ve{i}.{su}")
                gl = i * 4 + su
                P.inc(e.dma_start(P.ap("vm_tm")[gl * 128:(gl + 1) * 128, :],
                                  vbf[:, (i % 2) * 4 * D + su * D:
                                      (i % 2) * 4 * D + (su + 1) * D]),
                      stv[(gl // 4) % 2], f"sv{i}.{su}", 16)

    P.twopass([("sync", b_sync), ("tensor", b_tensor), ("vector", b_vector),
               ("gpsimd", b_gpsimd)])



def ph_msa(P, es):
    nc = P.nc
    A, Dv, Tn, xA, xD = (P.sem(n) for n in ["A", "Dv", "Tn", "xA", "xD"])
    ldk = [P.dsem(0), P.dsem(1)]
    ldv = [P.dsem(2), P.dsem(3)]
    ldq = [P.dsem(4), P.dsem(5)]
    sty = [P.dsem(10), P.dsem(11)]
    NH, DH = 16, 64
    SK = S // 512
    NQ = S // 128
    ksb = es.enter_context(nc.sbuf_tensor("ms_k", [128, KC * S], BF16))
    vsb = es.enter_context(nc.sbuf_tensor("ms_v", [128, (S // 128) * D], BF16))
    qsb = es.enter_context(nc.sbuf_tensor("ms_q", [128, 2 * KC * 128], BF16))
    Pb = es.enter_context(nc.sbuf_tensor("ms_P", [128, S], BF16))
    PTb = es.enter_context(nc.sbuf_tensor("ms_PT", [128, S], BF16))
    ybuf = es.enter_context(nc.sbuf_tensor("ms_y", [128, D], BF16))
    yfm = es.enter_context(nc.sbuf_tensor("ms_yf", [128, 2 * D], BF16))
    nrw = es.enter_context(nc.sbuf_tensor("ms_nw", [128, 8], F32))
    psS = es.enter_context(nc.psum_tensor("ms_ps", [128, S], F32))
    psT = es.enter_context(nc.psum_tensor("ms_pt", [128, S], BF16))
    psV = es.enter_context(nc.psum_tensor("ms_pv", [128, DH], F32))
    psY = es.enter_context(nc.psum_tensor("ms_py", [128, D], BF16))

    def b_sync(e):
        for b in range(B_CORE):
            if b >= 1:
                P.wait(e, Tn, f"pv{b - 1}.{NQ - 1}.{NH - 1}")
            P.inc(e.dma_start(ksb[:, :].rearrange("p (c j) -> p c j", c=KC),
                              P.ap("qk_fm")[KC:2 * KC, :, b * S:(b + 1) * S]
                              .rearrange("c p j -> p c j")),
                  ldk[b % 2], f"lk{b}", 16)
            P.inc(e.dma_start(vsb[:, :].rearrange("p (s d) -> p s d", s=S // 128),
                              P.ap("vm_tm")[b * S:(b + 1) * S, :]
                              .rearrange("(s p) d -> p s d", p=128)),
                  ldv[b % 2], f"lv{b}", 16)
            for qt in range(NQ):
                g = b * NQ + qt
                if g >= 2:
                    P.wait(e, Tn, f"pv{(g - 2) // NQ}.{(g - 2) % NQ}.{NH - 1}")
                P.inc(e.dma_start(qsb[:, (g % 2) * KC * 128:(g % 2 + 1) * KC * 128]
                                  .rearrange("p (c j) -> p c j", c=KC),
                                  P.ap("qk_fm")[0:KC, :, b * S + qt * 128:
                                                b * S + (qt + 1) * 128]
                                  .rearrange("c p j -> p c j")),
                      ldq[g % 2], f"lq{b}.{qt}", 16)

    def b_tensor(e):
        for b in range(B_CORE):
            P.wait(e, ldk[b % 2], f"lk{b}")
            P.wait(e, ldv[b % 2], f"lv{b}")
            for qt in range(NQ):
                g = b * NQ + qt
                P.wait(e, ldq[g % 2], f"lq{b}.{qt}")
                qo = (g % 2) * KC * 128
                for h in range(NH):
                    hc, hr = h // 2, (h % 2) * DH
                    if h >= 1:
                        P.wait(e, A, f"ex{b}.{qt}.{h - 1}")
                    elif g >= 1:
                        pb, pq = (g - 1) // NQ, (g - 1) % NQ
                        P.wait(e, A, f"ex{pb}.{pq}.{NH - 1}")
                    ins = None
                    for kb in range(SK):
                        ins = e.matmul(psS[:, kb * 512:(kb + 1) * 512],
                                       qsb[hr:hr + DH, qo + hc * 128:
                                           qo + (hc + 1) * 128],
                                       ksb[hr:hr + DH, hc * S + kb * 512:
                                           hc * S + kb * 512 + 512],
                                       start=True, stop=True)
                    P.inc(ins, Tn, f"sc{b}.{qt}.{h}")
                    P.wait(e, A, f"ex{b}.{qt}.{h}")
                    if h >= 1:
                        P.wait(e, Dv, f"pte{b}.{qt}.{h - 1}")
                    elif g >= 1:
                        pb, pq = (g - 1) // NQ, (g - 1) % NQ
                        P.wait(e, Dv, f"pte{pb}.{pq}.{NH - 1}")
                    ins = None
                    for kb in range(S // 128):
                        ins = e.transpose(psT[:, kb * 128:(kb + 1) * 128],
                                          Pb[:, kb * 128:(kb + 1) * 128],
                                          P.identb)
                    P.inc(ins, Tn, f"pt{b}.{qt}.{h}")
                    P.wait(e, Dv, f"pte{b}.{qt}.{h}")
                    if h >= 1:
                        P.wait(e, Dv, f"yn{b}.{qt}.{h - 1}")
                    elif g >= 1:
                        P.wait(e, Dv, f"yn{(g - 1) // NQ}.{(g - 1) % NQ}.{NH - 1}")
                    ins = None
                    for kb in range(S // 128):
                        ins = e.matmul(psV[:, :],
                                       PTb[:, kb * 128:(kb + 1) * 128],
                                       vsb[:, kb * D + h * DH:kb * D + (h + 1) * DH],
                                       start=(kb == 0), stop=(kb == S // 128 - 1))
                    P.inc(ins, Tn, f"pv{b}.{qt}.{h}")
                P.wait(e, Dv, f"ynf{b}.{qt}")
                if g >= 1:
                    P.wait(e, Dv, f"ye{(g - 1) // NQ}.{(g - 1) % NQ}")
                ins = None
                for cc in range(KC):
                    ins = e.transpose(psY[:, cc * 128:(cc + 1) * 128],
                                      ybuf[:, cc * 128:(cc + 1) * 128], P.identb)
                P.inc(ins, Tn, f"yt{b}.{qt}")

    def b_scalar(e):
        for b in range(B_CORE):
            for qt in range(NQ):
                for h in range(NH):
                    g = b * NQ + qt
                    P.wait(e, Tn, f"sc{b}.{qt}.{h}")
                    P.wait(e, Dv, f"nb{b}.{qt}.{h}")
                    if h >= 1:
                        P.wait(e, Tn, f"pt{b}.{qt}.{h - 1}")
                    elif g >= 1:
                        P.wait(e, Tn, f"pt{(g - 1) // NQ}.{(g - 1) % NQ}.{NH - 1}")
                    P.selfsync(e, e.activation(Pb[:, :], psS[:, :], AF.Exp,
                                               scale=0.125, bias=nrw[:, 2:3],
                                               accum_out=nrw[:, 3:4]), xA)
                    P.incflush(e, e.activation(nrw[:, 4:5], nrw[:, 3:4], AF.Copy),
                               "A", A, f"ex{b}.{qt}.{h}")

    def b_vector(e):
        for b in range(B_CORE):
            for qt in range(NQ):
                g = b * NQ + qt
                for h in range(NH):
                    P.wait(e, Tn, f"sc{b}.{qt}.{h}")
                    P.selfsync(e, e.reduce_max(nrw[:, 0:1], psS[:, :], axis=AX.X),
                               xD)
                    P.selfsync(e, e.tensor_scalar(nrw[:, 2:3], nrw[:, 0:1], -0.125,
                                                  None, ALU.mult), xD)
                    P.inc(e.tensor_copy(P.jD[:, 2:3], P.jD[:, 3:4]),
                          Dv, f"nb{b}.{qt}.{h}")
                    P.wait(e, Tn, f"pt{b}.{qt}.{h}")
                    P.inc(e.tensor_copy(PTb[:, :], psT[:, :]),
                          Dv, f"pte{b}.{qt}.{h}")
                    P.wait(e, A, f"ex{b}.{qt}.{h}")
                    P.selfsync(e, e.reciprocal(nrw[:, 5:6], nrw[:, 4:5]), xD)
                    P.wait(e, Tn, f"pv{b}.{qt}.{h}")
                    P.inc(e.tensor_scalar(ybuf[:, h * DH:(h + 1) * DH], psV[:, :],
                                          nrw[:, 5:6], None, ALU.mult),
                          Dv, f"yn{b}.{qt}.{h}")
                P.selfsync(e, e.tensor_copy(P.jD[:, 2:3], P.jD[:, 3:4]), xD)
                P.inc(e.tensor_copy(P.jD[:, 1:2], P.jD[:, 0:1]), Dv, f"ynf{b}.{qt}")
                P.wait(e, Tn, f"yt{b}.{qt}")
                if g >= 2:
                    P.wait(e, sty[g % 2], f"sy{(g - 2) // NQ}.{(g - 2) % NQ}")
                P.inc(e.tensor_copy(yfm[:, (g % 2) * D:(g % 2) * D + D], psY[:, :]),
                      Dv, f"ye{b}.{qt}")

    def b_gpsimd(e):
        for b in range(B_CORE):
            for qt in range(NQ):
                g = b * NQ + qt
                P.wait(e, Dv, f"ye{b}.{qt}")
                P.inc(e.dma_start(P.ap("ym_fm")[:, :, g * 128:(g + 1) * 128]
                                  .rearrange("c p j -> p c j"),
                                  yfm[:, (g % 2) * D:(g % 2) * D + D]
                                  .rearrange("p (c j) -> p c j", c=KC)),
                      sty[g % 2], f"sy{b}.{qt}", 16)

    P.twopass([("sync", b_sync), ("tensor", b_tensor), ("scalar", b_scalar),
               ("vector", b_vector), ("gpsimd", b_gpsimd)])



def ph_moe8(P, es):
    nc = P.nc
    A, Dv, Tn = (P.sem(n) for n in ["A", "Dv", "Tn"])
    lds = [P.dsem(0), P.dsem(1)]
    sto = [P.dsem(10 + k) for k in range(8)]
    ld22 = P.dsem(22)
    NE = 8
    HID = 4096
    NHID = HID // 128
    w1 = es.enter_context(nc.sbuf_tensor("m8_w1", [128, KC * HID], BF16))
    w2 = es.enter_context(nc.sbuf_tensor("m8_w2", [128, NHID * D], BF16))
    msk = es.enter_context(nc.sbuf_tensor("m8_mk", [128, NT * NE], F32))
    TC = 512
    NTC = T // TC
    xfm = es.enter_context(nc.sbuf_tensor("m8_x", [128, 2 * KC * TC], BF16))
    hfm = es.enter_context(nc.sbuf_tensor("m8_h", [128, NHID * TC], BF16))
    ob = es.enter_context(nc.sbuf_tensor("m8_o", [128, 4 * D], F32))
    psH = es.enter_context(nc.psum_tensor("m8_ph", [128, 2 * TC], F32))
    psY = es.enter_context(nc.psum_tensor("m8_py", [128, 2 * D], F32))

    with nc.Block() as blk:
        @blk.sync
        def _(e):
            e.dma_start(msk[:, :].rearrange("p (i e) -> p i e", i=NT),
                        P.ap("mask8_tm").rearrange("(i p) e -> p i e", p=128))\
                .then_inc(ld22.h, 16)
            e.wait_ge(ld22.h, ld22.v + 16)
    ld22.v += 16

    for ei in range(NE):
        with contextlib.ExitStack() as wes:
            load_cast_weights(P, wes, [(P.ap("L1_ffn_w1")[ei], w1[:, :]),
                                       (P.ap("L1_ffn_w2")[ei], w2[:, :])],
                              tag=f"m{ei}")

        def b_sync(e, ei=ei):
            for i in range(NTC):
                if i >= 2:
                    P.wait(e, Tn, f"e{ei}h{i - 2}.{NHID - 1}")
                elif ei >= 1:
                    P.wait(e, Tn, f"e{ei - 1}h{NTC - 2 + i}.{NHID - 1}")
                P.inc(e.dma_start(xfm[:, (i % 2) * KC * TC:(i % 2 + 1) * KC * TC]
                                  .rearrange("p (c j) -> p c j", c=KC),
                                  P.ap("xg_fm")[:, :, i * TC:(i + 1) * TC]
                                  .rearrange("c p j -> p c j")),
                      lds[i % 2], f"e{ei}lx{i}", 16)

        def b_tensor(e, ei=ei):
            for i in range(NTC):
                P.wait(e, lds[i % 2], f"e{ei}lx{i}")
                xo = (i % 2) * KC * TC
                for j in range(NHID):
                    if j >= 2:
                        P.wait(e, A, f"e{ei}g{i}.{j - 2}")
                    elif i >= 1:
                        P.wait(e, A, f"e{ei}g{i - 1}.{NHID - 2 + j}")
                    elif ei >= 1:
                        P.wait(e, A, f"e{ei - 1}g{NTC - 1}.{NHID - 2 + j}")
                    ins = None
                    for kc in range(KC):
                        ins = e.matmul(psH[:, (j % 2) * TC:(j % 2) * TC + TC],
                                       w1[:, kc * HID + j * 128:
                                          kc * HID + (j + 1) * 128],
                                       xfm[:, xo + kc * TC:xo + kc * TC + TC],
                                       start=(kc == 0), stop=(kc == KC - 1))
                    P.inc(ins, Tn, f"e{ei}h{i}.{j}")
                for ts in range(4):
                    P.wait(e, A, f"e{ei}g{i}.{NHID - 1}")
                    if ts >= 2:
                        P.wait(e, A, f"e{ei}ye{i}.{ts - 2}")
                    elif i >= 1:
                        P.wait(e, A, f"e{ei}ye{i - 1}.{ts + 2}")
                    elif ei >= 1:
                        P.wait(e, A, f"e{ei - 1}ye{NTC - 1}.{ts + 2}")
                    ins = None
                    for n in range(2):
                        for kc in range(NHID):
                            ins = e.matmul(psY[:, (ts % 2) * D + n * 512:
                                               (ts % 2) * D + n * 512 + 512],
                                           hfm[:, kc * TC + ts * 128:
                                               kc * TC + ts * 128 + 128],
                                           w2[:, kc * D + n * 512:
                                              kc * D + n * 512 + 512],
                                           start=(kc == 0), stop=(kc == NHID - 1))
                    P.inc(ins, Tn, f"e{ei}y{i}.{ts}")

        def b_scalar(e, ei=ei):
            for i in range(NTC):
                for j in range(NHID):
                    P.wait(e, Tn, f"e{ei}h{i}.{j}")
                    P.inc(e.activation(hfm[:, j * TC:(j + 1) * TC],
                                       psH[:, (j % 2) * TC:(j % 2) * TC + TC],
                                       AF.Gelu_apprx_tanh), A, f"e{ei}g{i}.{j}")
                for ts in range(4):
                    P.wait(e, Tn, f"e{ei}y{i}.{ts}")
                    gl = i * 4 + ts
                    if i >= 1:
                        P.wait(e, sto[((i - 1) * 4 + ts) % 8],
                               f"e{ei}so{i - 1}.{ts}")
                    elif ei >= 1:
                        P.wait(e, sto[((NTC - 1) * 4 + ts) % 8],
                               f"e{ei - 1}so{NTC - 1}.{ts}")
                    P.inc(e.activation(ob[:, ts * D:(ts + 1) * D],
                                       psY[:, (ts % 2) * D:(ts % 2) * D + D],
                                       AF.Copy,
                                       scale=msk[:, (i * 4 + ts) * NE + ei:
                                                 (i * 4 + ts) * NE + ei + 1]),
                          A, f"e{ei}ye{i}.{ts}")

        def b_gpsimd(e, ei=ei):
            for i in range(NTC):
                for ts in range(4):
                    P.wait(e, A, f"e{ei}ye{i}.{ts}")
                    gl = i * 4 + ts
                    if ei >= 1:
                        P.wait(e, sto[gl % 8], f"e{ei - 1}so{i}.{ts}")
                    elif gl >= 8:
                        P.wait(e, sto[gl % 8], f"e{ei}so{i - 2}.{ts}")
                    if ei == 0:
                        ins = e.dma_start(
                            P.ap("moe_tm")[gl * 128:(gl + 1) * 128, :],
                            ob[:, ts * D:(ts + 1) * D])
                    else:
                        ins = e.dma_start(
                            P.ap("moe_tm")[gl * 128:(gl + 1) * 128, :],
                            ob[:, ts * D:(ts + 1) * D], accum_op=ALU.add)
                    P.inc(ins, sto[gl % 8], f"e{ei}so{i}.{ts}", 16)

        P.twopass([("sync", b_sync), ("tensor", b_tensor), ("scalar", b_scalar),
                   ("gpsimd", b_gpsimd)])


def _ph_qkv0(P, es):
    ph_qkv0(P, es)


def _ph_scan0(P, es):
    ph_scan0(P, es)


def _ph_wo0(P, es):
    ph_proj_tm(P, es, "read_fm", "L0_attn_Wo", "attn0_tm", bias="L0_attn_bo")


PHASE_DEFS = [
    ("ln0", _ph_ln0,
     ["x0", "L0_ffn_w_router", "c_ident_b"],
     ["xn_tm", "xn_fm", "scores_tm"]),
    ("qkv0", _ph_qkv0,
     ["xn_fm", "L0_attn_Wq", "L0_attn_bq", "L0_attn_Wk", "L0_attn_bk",
      "L0_attn_Wv", "L0_attn_bv", "L0_attn_decay_param", "c_trel", "c_trel2",
      "c_ident_b"],
     ["q_fm", "qs_fm", "kphi_fm", "khat_tm", "v_tm"]),
    ("scan0", _ph_scan0,
     ["q_fm", "qs_fm", "kphi_fm", "khat_tm", "v_tm", "L0_attn_decay_param",
      "c_m1", "c_m2neg", "c_trel", "c_trel2", "c_ident_b"],
     ["read_fm"]),
    ("wo0", _ph_wo0,
     ["read_fm", "L0_attn_Wo", "L0_attn_bo", "c_ident_b"],
     ["attn0_tm"]),
    ("shared0", ph_shared0,
     ["xn_fm", "L0_ffn_w_gate", "L0_ffn_w_up", "L0_ffn_w_down", "c_ident_b"],
     ["ffn0_tm"]),
    ("ffn0e", ph_ffn0e,
     ["xn_tm", "scores_tm", "L0_ffn_gate_up_proj", "L0_ffn_down_proj",
      "c_ident_b"],
     ["ffn0e_tm"]),
    ("resid0", lambda P, es: ph_add(P, es, ["x0", "attn0_tm", "ffn0_tm",
                                            "ffn0e_tm"], "x1_tm"),
     ["x0", "attn0_tm", "ffn0_tm", "ffn0e_tm", "c_ident_b"],
     ["x1_tm"]),
    ("ln1", lambda P, es: ph_ln(P, es, "x1_tm", "xn1_fm", out_f32="xn1f_tm"),
     ["x1_tm", "c_ident_b"],
     ["xn1_fm", "xn1f_tm"]),
    ("qkv1", ph_qkv1,
     ["xn1_fm", "L1_attn_Wq", "L1_attn_Wk", "L1_attn_Wv", "c_ident_b"],
     ["phiq_fm", "phik_fm", "phik_tm", "v1_tm"]),
    ("scan1", ph_scan1,
     ["phiq_fm", "phik_fm", "phik_tm", "v1_tm", "c_mask1", "c_ident_b"],
     ["y1_fm"]),
    ("wo1", lambda P, es: ph_proj_tm(P, es, "y1_fm", "L1_attn_Wo", "attn1_tm",
                                     bias="L1_attn_bo", w_transposed=True),
     ["y1_fm", "L1_attn_Wo", "L1_attn_bo", "c_ident_b"],
     ["attn1_tm"]),
    ("vitln1", lambda P, es: ph_ln(P, es, "xn1f_tm", "xln_fm",
                                   lnw=("L1_ffn_ln1_w", "L1_ffn_ln1_b")),
     ["xn1f_tm", "L1_ffn_ln1_w", "L1_ffn_ln1_b", "c_ident_b"],
     ["xln_fm"]),
    ("qkvmsa", ph_qkvmsa,
     ["xln_fm", "L1_ffn_w_qkv", "c_ident_b"],
     ["qk_fm", "vm_tm"]),
    ("msa", ph_msa,
     ["qk_fm", "vm_tm", "c_ident_b"],
     ["ym_fm"]),
    ("msaproj", lambda P, es: ph_proj_tm(P, es, "ym_fm", "L1_ffn_w_proj", "u_tm",
                                         extra_tm=("xn1f_tm",)),
     ["ym_fm", "L1_ffn_w_proj", "xn1f_tm", "c_ident_b"],
     ["u_tm"]),
    ("ln2g", lambda P, es: ph_ln(P, es, "u_tm", "xg_fm",
                                 lnw=("L1_ffn_ln2_w", "L1_ffn_ln2_b"),
                                 router=("L1_ffn_gate_w", 8, "mask8_tm", False,
                                         False, "L1_ffn_gate_b")),
     ["u_tm", "L1_ffn_ln2_w", "L1_ffn_ln2_b", "L1_ffn_gate_w", "L1_ffn_gate_b",
      "c_ident_b"],
     ["xg_fm", "mask8_tm"]),
    ("moe8", ph_moe8,
     ["xg_fm", "mask8_tm", "L1_ffn_w1", "L1_ffn_w2", "c_ident_b"],
     ["moe_tm"]),
    ("residout", lambda P, es: ph_add(P, es, ["x1_tm", "attn1_tm", "u_tm",
                                              "moe_tm"], "OUT"),
     ["x1_tm", "attn1_tm", "u_tm", "moe_tm", "c_ident_b"],
     ["OUT"]),
]







# version 03
# speedup vs baseline: 1.0092x; 1.0092x over previous
import contextlib
import numpy as np
import ml_dtypes
import concourse.bass as bass
import concourse.mybir as mybir
from concourse.bass_utils import run_bass_kernel_spmd

F32 = mybir.dt.float32
BF16 = mybir.dt.bfloat16
AF = mybir.ActivationFunctionType
ALU = mybir.AluOpType
AX = mybir.AxisListType
BF = ml_dtypes.bfloat16

NCORES = 8
B_CORE = 2
S = 2048
T = B_CORE * S
D = 1024
NT = T // 128
KC = D // 128
CH = 512
NCH = S // CH
NEG = -1.0e30
DBG_SC = False

TENSORS = {
    "x0": ([T, D], F32),
    "L0_attn_Wq": ([D, D], F32), "L0_attn_bq": ([1, D], F32),
    "L0_attn_Wk": ([D, D], F32), "L0_attn_bk": ([1, D], F32),
    "L0_attn_Wv": ([D, D], F32), "L0_attn_bv": ([1, D], F32),
    "L0_attn_Wo": ([D, D], F32), "L0_attn_bo": ([1, D], F32),
    "L0_attn_decay_param": ([1, 1], F32),
    "L0_ffn_w_router": ([16, D], F32),
    "L0_ffn_gate_up_proj": ([16, D, 4096], F32),
    "L0_ffn_down_proj": ([16, 2048, D], F32),
    "L0_ffn_w_gate": ([2048, D], F32),
    "L0_ffn_w_up": ([2048, D], F32),
    "L0_ffn_w_down": ([D, 2048], F32),
    "L1_attn_Wq": ([D, D], F32), "L1_attn_Wk": ([D, D], F32),
    "L1_attn_Wv": ([D, D], F32), "L1_attn_Wo": ([D, D], F32),
    "L1_attn_bo": ([1, D], F32),
    "L1_ffn_ln1_w": ([1, D], F32), "L1_ffn_ln1_b": ([1, D], F32),
    "L1_ffn_w_qkv": ([D, 3 * D], F32),
    "L1_ffn_w_proj": ([D, D], F32),
    "L1_ffn_ln2_w": ([1, D], F32), "L1_ffn_ln2_b": ([1, D], F32),
    "L1_ffn_gate_w": ([D, 8], F32), "L1_ffn_gate_b": ([1, 8], F32),
    "L1_ffn_w1": ([8, D, 4096], F32),
    "L1_ffn_w2": ([8, 4096, D], F32),
    "c_ident_b": ([128, 128], BF16),
    "c_m1": ([CH, CH], F32),
    "c_m2neg": ([CH, CH], F32),
    "c_mask1": ([CH, CH], F32),
    "c_trel": ([128, CH // 128], F32),
    "c_trel2": ([128, CH // 128], F32),
    "xn_tm": ([T, D], BF16),
    "xn_fm": ([KC, 128, T], BF16),
    "scores_tm": ([T, 16], F32),
    "q_fm": ([KC, 128, T], BF16),
    "qs_fm": ([KC, 128, T], BF16),
    "kphi_fm": ([KC, 128, T], BF16),
    "khat_tm": ([T, D], BF16),
    "v_tm": ([T, D], BF16),
    "read_fm": ([KC, 128, T], BF16),
    "attn0_tm": ([T, D], F32),
    "ffn0_tm": ([T, D], F32),
    "ffn0e_tm": ([T, D], F32),
    "x1_tm": ([T, D], F32),
    "xn1f_tm": ([T, D], F32),
    "xn1_fm": ([KC, 128, T], BF16),
    "phiq_fm": ([KC, 128, T], BF16),
    "phik_fm": ([KC, 128, T], BF16),
    "phik_tm": ([T, D], BF16),
    "v1_tm": ([T, D], BF16),
    "y1_fm": ([KC, 128, T], BF16),
    "attn1_tm": ([T, D], F32),
    "xln_fm": ([KC, 128, T], BF16),
    "qk_fm": ([2 * KC, 128, T], BF16),
    "vm_tm": ([T, D], BF16),
    "ym_fm": ([KC, 128, T], BF16),
    "u_tm": ([T, D], F32),
    "xg_fm": ([KC, 128, T], BF16),
    "mask8_tm": ([T, 8], F32),
    "moe_tm": ([T, D], F32),
    "OUT": ([T, D], F32),
}

NP_DT = {F32: np.float32, BF16: BF}


class Sem:
    def __init__(self, h):
        self.h = h
        self.v = 0


class _DummyInstr:
    def then_inc(self, h, n):
        return self

    def annotate(self, a):
        return self


class _DummyEngine:
    def __getattr__(self, name):
        def f(*a, **k):
            return _DummyInstr()
        return f


class Ctx:
    NDSEM = 28

    def __init__(self, nc, es, kinds):
        self.nc = nc
        self.es = es
        self.kinds = kinds
        self._aps = {}
        self.sems = {}
        for nm in ["A", "Dv", "Tn", "Pl", "Sp", "xA", "xD", "xT"]:
            self.sems[nm] = Sem(es.enter_context(nc.semaphore(nm)))
        self.dsems = [Sem(es.enter_context(nc.semaphore(f"d{i}")))
                      for i in range(self.NDSEM)]
        self.identb = None
        self.marks = {}
        self._dry = True
        self.phase_id = 0

    def ap(self, name):
        if name not in self._aps:
            shape, dt = TENSORS[name]
            kind = {"in": "ExternalInput", "out": "ExternalOutput",
                    "tmp": "Internal"}[self.kinds[name]]
            self._aps[name] = self.nc.dram_tensor(name, shape, dt, kind=kind).ap()
        return self._aps[name]

    def sem(self, nm):
        return self.sems[nm]

    def dsem(self, i):
        return self.dsems[i]

    def inc(self, ins, sem, label=None, n=1):
        sem.v += n
        if label is not None:
            key = (self.phase_id, label)
            old = self.marks.get(key)
            assert old is None or old[0] is sem, f"mark {label} sem mismatch"
            self.marks[key] = (sem, sem.v)
        ins.then_inc(sem.h, n)
        return ins

    def wait(self, e, sem, label):
        key = (self.phase_id, label)
        if key not in self.marks:
            assert self._dry, f"unknown mark {label} in emit pass"
            v = 0
        else:
            msem, v = self.marks[key]
            assert msem is sem, f"wait on mark {label}: wrong semaphore"
        e.wait_ge(sem.h, v)

    def incflush(self, e, ins, which, sem, label, n=1):
        xs = self.sems["xA"] if which == "A" else self.sems["xD"]
        self.selfsync(e, ins, xs)
        if which == "A":
            f = e.activation(self.jA[:, 1:2], self.jA[:, 0:1], AF.Copy)
        else:
            f = e.tensor_copy(self.jD[:, 1:2], self.jD[:, 0:1])
        return self.inc(f, sem, label, n)

    def selfsync(self, e, ins, xsem):
        ins.then_inc(xsem.h, 1)
        xsem.v += 1
        e.wait_ge(xsem.h, xsem.v)
        return ins

    def twopass(self, bodies):
        allsems = list(self.sems.values()) + self.dsems
        snap = [(s, s.v) for s in allsems]
        finals = {id(s): v for s, v in snap}
        self._dry = True
        dummy = _DummyEngine()
        for _, fn in bodies:
            for s, v in snap:
                s.v = v
            fn(dummy)
            for s, v in snap:
                finals[id(s)] = max(finals[id(s)], s.v)
        self._dry = False
        with self.nc.Block() as blk:
            for name, fn in bodies:
                for s, v in snap:
                    s.v = v
                getattr(blk, name)(fn)
        for s, v in snap:
            s.v = finals[id(s)]


def _ph_prelude(P):
    nc = P.nc
    if P.identb is not None:
        return
    P.jA = P.es.enter_context(nc.sbuf_tensor(f"junkA{P.phase_id}", [128, 4], F32))[:, :]
    P.jD = P.es.enter_context(nc.sbuf_tensor(f"junkD{P.phase_id}", [128, 4], F32))[:, :]
    identb = P.es.enter_context(nc.sbuf_tensor(f"c_identb_sb{P.phase_id}", [128, 128], BF16))
    ld = P.dsem(27)
    with nc.Block() as blk:
        @blk.sync
        def _(e):
            e.dma_start(identb[:, :], P.ap("c_ident_b")).then_inc(ld.h, 16)

        @blk.vector
        def _(e):
            e.memset(P.jD, 0.0)

        @blk.scalar
        def _(e):
            e.memzero(P.jA)

        @blk.tensor
        def _(e):
            e.wait_ge(ld.h, ld.v + 16)
    ld.v += 16
    P.identb = identb[:, :]


def run_phase(P, fn, *args, **kwargs):
    _ph_prelude(P)
    P.phase_id += 1
    with contextlib.ExitStack() as es:
        fn(P, es, *args, **kwargs)
    with P.nc.Block() as blk:
        @blk.sync
        def _(e):
            for ds in P.dsems:
                e.wait_ge(ds.h, ds.v)



def load_cast_weights(P, es, specs, tag=""):
    nc = P.nc
    Dv = P.sem("Dv")
    lds = [P.dsem(22 + k) for k in range(4)]
    hes = contextlib.ExitStack()
    stage = hes.enter_context(nc.sbuf_tensor(f"wc_stage{P.phase_id}{tag}", [128, 4 * 512], F32))
    pieces = []
    for dram, sb in specs:
        K, N = dram.shape
        kp = min(K, 128)
        for kc in range(max(1, K // 128)):
            for j in range(0, N, 512):
                w = min(512, N - j)
                pieces.append((dram[kc * kp:kc * kp + kp, j:j + w],
                               sb[0:kp, kc * N + j: kc * N + j + w]))

    def b_sync(e):
        for i, (dsl, ssl) in enumerate(pieces):
            if i >= 4:
                P.wait(e, Dv, f"wcv{i - 4}")
            kp, w = dsl.shape
            P.inc(e.dma_start(stage[0:kp, (i % 4) * 512:(i % 4) * 512 + w], dsl),
                  lds[i % 4], f"wcl{i}", 16)

    def b_vec(e):
        for i, (dsl, ssl) in enumerate(pieces):
            P.wait(e, lds[i % 4], f"wcl{i}")
            kp, w = dsl.shape
            P.inc(e.tensor_copy(ssl, stage[0:kp, (i % 4) * 512:(i % 4) * 512 + w]),
                  Dv, f"wcv{i}")

    P.twopass([("sync", b_sync), ("vector", b_vec)])
    hes.close()


def load_cast_transpose_weights(P, es, specs, tag=""):
    nc = P.nc
    Dv, Tn = P.sem("Dv"), P.sem("Tn")
    lds = [P.dsem(22), P.dsem(23)]
    KMAX = max(d.shape[1] for d, _ in specs)
    hes = contextlib.ExitStack()
    stage = hes.enter_context(nc.sbuf_tensor(f"wt_stage{P.phase_id}{tag}", [128, 2 * 512], F32))
    stb = hes.enter_context(nc.sbuf_tensor(f"wt_stb{P.phase_id}{tag}", [128, 2 * KMAX], BF16))
    pst = hes.enter_context(nc.psum_tensor(f"wt_ps{P.phase_id}{tag}", [128, 2 * KMAX], BF16))
    rows = []
    for dram, sb in specs:
        M, K = dram.shape
        for m in range(0, M, 128):
            rows.append((dram[m:m + 128, :], sb, M, m, K))

    def b_sync(e):
        ip = 0
        for r, (dsl, sb, M, m, K) in enumerate(rows):
            for j in range(0, K, 512):
                if ip >= 2:
                    P.wait(e, Dv, f"wtc{ip - 2}")
                P.inc(e.dma_start(stage[:, (ip % 2) * 512:(ip % 2) * 512 + 512],
                                  dsl[:, j:j + 512]), lds[ip % 2], f"wtl{ip}", 16)
                ip += 1

    def b_tensor(e):
        ip = 0
        for r, (dsl, sb, M, m, K) in enumerate(rows):
            ip += K // 512
            P.wait(e, Dv, f"wtc{ip - 1}")
            if r >= 2:
                K2 = rows[r - 2][4]
                P.wait(e, Dv, f"wte{r - 2}.{K2 // 128 - 1}")
            ins = None
            for kc in range(K // 128):
                ins = e.transpose(pst[:, (r % 2) * KMAX + kc * 128:
                                      (r % 2) * KMAX + (kc + 1) * 128],
                                  stb[:, (r % 2) * KMAX + kc * 128:
                                      (r % 2) * KMAX + (kc + 1) * 128],
                                  P.identb)
            P.inc(ins, Tn, f"wtt{r}")

    def b_vec(e):
        ip = 0
        for r, (dsl, sb, M, m, K) in enumerate(rows):
            for j in range(0, K, 512):
                P.wait(e, lds[ip % 2], f"wtl{ip}")
                if r >= 2:
                    P.wait(e, Tn, f"wtt{r - 1}")
                P.inc(e.tensor_copy(stb[:, (r % 2) * KMAX + j:
                                        (r % 2) * KMAX + j + 512],
                                    stage[:, (ip % 2) * 512:(ip % 2) * 512 + 512]),
                      Dv, f"wtc{ip}")
                ip += 1
            P.wait(e, Tn, f"wtt{r}")
            for kc in range(K // 128):
                P.inc(e.tensor_copy(sb[:, kc * M + m: kc * M + m + 128],
                                    pst[:, (r % 2) * KMAX + kc * 128:
                                        (r % 2) * KMAX + (kc + 1) * 128]),
                      Dv, f"wte{r}.{kc}")

    P.twopass([("sync", b_sync), ("tensor", b_tensor), ("vector", b_vec)])
    hes.close()


def bias_bcast(P, es, names, width=None, tag=""):
    nc = P.nc
    Dv, Tn = P.sem("Dv"), P.sem("Tn")
    ld = P.dsem(22)
    widths = {nm: (width or TENSORS[nm][0][1]) for nm in names}
    tot = sum(widths.values())
    bbc = es.enter_context(nc.sbuf_tensor(f"bb_bc{P.phase_id}{tag}", [128, tot], F32))
    hes = contextlib.ExitStack()
    brow = hes.enter_context(nc.sbuf_tensor(f"bb_row{P.phase_id}{tag}", [1, tot], F32))
    ones = hes.enter_context(nc.sbuf_tensor(f"bb_ones{P.phase_id}{tag}", [1, 128], F32))
    psb = hes.enter_context(nc.psum_tensor(f"bb_ps{P.phase_id}{tag}", [128, 512], F32))
    offs = {}
    o = 0
    for nm in names:
        offs[nm] = o
        o += widths[nm]
    segs = []
    for nm in names:
        w = widths[nm]
        for j in range(0, w, 512):
            segs.append((offs[nm] + j, min(512, w - j)))

    def b_sync(e):
        for i, nm in enumerate(names):
            P.inc(e.dma_start(brow[:, offs[nm]:offs[nm] + widths[nm]], P.ap(nm)),
                  ld, f"bbl{i}", 16)

    def b_vec(e):
        P.inc(e.memset(ones[:, :], 1.0), Dv, "bbones")
        for i, (ro, w) in enumerate(segs):
            P.wait(e, Tn, f"bbm{i}")
            P.inc(e.tensor_copy(bbc[:, ro:ro + w], psb[:, 0:w]), Dv, f"bbe{i}")

    def b_tensor(e):
        P.wait(e, Dv, "bbones")
        for i, nm in enumerate(names):
            P.wait(e, ld, f"bbl{i}")
        for i, (ro, w) in enumerate(segs):
            if i >= 1:
                P.wait(e, Dv, f"bbe{i - 1}")
            P.inc(e.matmul(psb[:, 0:w], ones[:, :], brow[:, ro:ro + w],
                           start=True, stop=True), Tn, f"bbm{i}")

    P.twopass([("sync", b_sync), ("vector", b_vec), ("tensor", b_tensor)])
    hes.close()
    return {nm: bbc[:, offs[nm]:offs[nm] + widths[nm]] for nm in names}



def ph_ln(P, es, src, out_fm, out_tm=None, out_f32=None, lnw=None, router=None):
    nc = P.nc
    A, Dv, Tn, xA, xD = (P.sem(n) for n in ["A", "Dv", "Tn", "xA", "xD"])
    lds = [P.dsem(0), P.dsem(1)]
    s_tm = [P.dsem(10), P.dsem(11)]
    s_f32 = [P.dsem(12), P.dsem(13)]
    s_fm = [P.dsem(14), P.dsem(15)]
    s_sc = [P.dsem(16), P.dsem(17)]
    E = router[1] if router else 0
    use_sig = router[3] if router else False
    assert not (router and use_sig and router[5]), "sigmoid+bias router unsupported"

    xa = es.enter_context(nc.sbuf_tensor(f"ln_x{P.phase_id}", [128, 2 * D], F32))
    xc = es.enter_context(nc.sbuf_tensor(f"ln_xc{P.phase_id}", [128, D], F32))
    sq = es.enter_context(nc.sbuf_tensor(f"ln_sq{P.phase_id}", [128, D], F32))
    xnf = es.enter_context(nc.sbuf_tensor(f"ln_xnf{P.phase_id}", [128, 2 * D], F32))
    xnb = es.enter_context(nc.sbuf_tensor(f"ln_xnb{P.phase_id}", [128, 2 * D], BF16))
    lob = es.enter_context(nc.sbuf_tensor(f"ln_lob{P.phase_id}", [128, 2 * D], BF16))
    fmb = es.enter_context(nc.sbuf_tensor(f"ln_fmb{P.phase_id}", [128, 2 * D], BF16))
    lofm = es.enter_context(nc.sbuf_tensor(f"ln_lofm{P.phase_id}", [128, 2 * D], BF16))
    stats = es.enter_context(nc.sbuf_tensor(f"ln_stats{P.phase_id}", [128, 8], F32))
    pst = es.enter_context(nc.psum_tensor(f"ln_pst{P.phase_id}", [128, 2 * D], BF16))
    if router:
        wrt = es.enter_context(nc.sbuf_tensor(f"ln_wrt{P.phase_id}", [128, 2 * KC * E], BF16))
        psr = es.enter_context(nc.psum_tensor(f"ln_psr{P.phase_id}", [128, E], F32))
        sc = es.enter_context(nc.sbuf_tensor(f"ln_sc{P.phase_id}", [128, 2 * 4 * E], F32))
        lgb = es.enter_context(nc.sbuf_tensor(f"ln_lgb{P.phase_id}", [128, 2 * 8], F32)) \
            if router[5] else None
    wbc = bias_bcast(P, es, list(lnw)) if lnw else None
    gb = bias_bcast(P, es, [router[5]], width=E, tag="g")[router[5]] \
        if (router and router[5]) else None
    if router:
        _router_w_prep(P, es, router, wrt)
    xsrc = P.ap(src)

    def slot(i):
        return (i % 2) * D

    def emit_exp(e, j):
        P.wait(e, Tn, f"mmr{j}")
        if j >= 2:
            P.wait(e, s_sc[j % 2], f"ssc{j - 2}")
        P.selfsync(e, e.activation(sc[:, (j % 2) * 4 * E + 2 * E:
                                      (j % 2) * 4 * E + 3 * E],
                                   psr[:, 0:E], AF.Exp, scale=-1.0), xA)
        P.inc(e.activation(stats[:, 7:8], stats[:, 7:8], AF.Copy), A, f"ex{j}")

    def b_sync(e):
        for i in range(NT):
            if i >= 2:
                P.wait(e, A, f"ln{i - 2}")
            P.inc(e.dma_start(xa[:, slot(i):slot(i) + D],
                              xsrc[i * 128:(i + 1) * 128, :]),
                  lds[i % 2], f"lx{i}", 16)

    def b_scalar(e):
        for i in range(NT):
            P.wait(e, lds[i % 2], f"lx{i}")
            if i >= 2:
                P.wait(e, Tn, f"tr{i - 2}")
                if out_tm:
                    P.wait(e, s_tm[i % 2], f"stm{i - 2}")
                if out_f32:
                    P.wait(e, s_f32[i % 2], f"sf32{i - 2}")
                if lnw:
                    P.wait(e, Dv, f"lo{i - 2}")
            xi = xa[:, slot(i):slot(i) + D]
            P.selfsync(e, e.activation(sq[:, :], xi, AF.Copy,
                                       accum_out=stats[:, 0:1]), xA)
            P.selfsync(e, e.activation(stats[:, 1:2], stats[:, 0:1], AF.Copy,
                                       scale=-1.0 / D), xA)
            e.activation(xc[:, :], xi, AF.Identity, bias=stats[:, 1:2])
            P.selfsync(e, e.activation(sq[:, :], xc[:, :], AF.Square,
                                       accum_out=stats[:, 2:3]), xA)
            P.selfsync(e, e.activation(stats[:, 3:4], stats[:, 2:3], AF.Ln,
                                       scale=1.0 / D, bias=np.float32(1e-5)), xA)
            P.selfsync(e, e.activation(stats[:, 4:5], stats[:, 3:4], AF.Exp,
                                       scale=-0.5), xA)
            e.activation(xnf[:, slot(i):slot(i) + D], xc[:, :], AF.Copy,
                         scale=stats[:, 4:5])
            if lnw:
                P.inc(e.activation(xnf[:, slot(i):slot(i) + 1],
                                   xnf[:, slot(i):slot(i) + 1], AF.Copy),
                      A, f"ln{i}")
            else:
                P.inc(e.activation(xnb[:, slot(i):slot(i) + D],
                                   xnf[:, slot(i):slot(i) + D], AF.Copy),
                      A, f"ln{i}")
            if router and use_sig and i >= 1:
                emit_exp(e, i - 1)
        if router and use_sig:
            emit_exp(e, NT - 1)

    def b_vector(e):
        for i in range(NT + 1):
            if i < NT:
                P.wait(e, A, f"ln{i}")
                if lnw:
                    if i >= 2:
                        if out_tm:
                            P.wait(e, s_tm[i % 2], f"stm{i - 2}")
                        if out_f32:
                            P.wait(e, s_f32[i % 2], f"sf32{i - 2}")
                        P.wait(e, Tn, f"tr{i - 2}")
                    e.tensor_mul(xnf[:, slot(i):slot(i) + D],
                                 xnf[:, slot(i):slot(i) + D], wbc[lnw[0]])
                    e.tensor_add(xnf[:, slot(i):slot(i) + D],
                                 xnf[:, slot(i):slot(i) + D], wbc[lnw[1]])
                    e.tensor_copy(xnb[:, slot(i):slot(i) + D],
                                  xnf[:, slot(i):slot(i) + D])
                if router:
                    P.inc(e.tensor_sub(lob[:, slot(i):slot(i) + D],
                                       xnf[:, slot(i):slot(i) + D],
                                       xnb[:, slot(i):slot(i) + D]), Dv, f"lo{i}")
                else:
                    P.inc(e.tensor_copy(lob[:, 0:1], xnb[:, slot(i):slot(i) + 1]),
                          Dv, f"lo{i}")
            if i >= 1:
                j = i - 1
                P.wait(e, Tn, f"tr{j}")
                if j >= 2:
                    P.wait(e, s_fm[j % 2], f"sfm{j - 2}")
                if router:
                    e.tensor_copy(fmb[:, slot(j):slot(j) + D], pst[:, 0:D])
                    P.inc(e.tensor_copy(lofm[:, slot(j):slot(j) + D],
                                        pst[:, D:2 * D]), Dv, f"ev{j}")
                else:
                    P.inc(e.tensor_copy(fmb[:, slot(j):slot(j) + D], pst[:, 0:D]),
                          Dv, f"ev{j}")
            if router and i >= 1:
                j = i - 1
                P.wait(e, Tn, f"mmr{j}")
                if j >= 2:
                    P.wait(e, s_sc[j % 2], f"ssc{j - 2}")
                base = (j % 2) * 4 * E
                if router[5]:
                    P.selfsync(e, e.tensor_add(lgb[:, (j % 2) * 8:(j % 2) * 8 + E],
                                               psr[:, 0:E], gb), xD)
                    lsrc = lgb[:, (j % 2) * 8:(j % 2) * 8 + E]
                else:
                    lsrc = psr[:, 0:E]
                if DBG_SC:
                    e.tensor_copy(sc[:, base + 3 * E:base + 4 * E], psr[:, 0:E])
                P.selfsync(e, e.reduce_max(stats[:, 5:6], lsrc, axis=AX.X), xD)
                if use_sig:
                    P.wait(e, A, f"ex{j}")
                    P.selfsync(e, e.tensor_scalar(sc[:, base + E:base + 2 * E],
                                                  sc[:, base + 2 * E:base + 3 * E],
                                                  1.0, None, ALU.add), xD)
                    P.selfsync(e, e.reciprocal(sc[:, base + 3 * E:base + 4 * E],
                                               sc[:, base + E:base + 2 * E]), xD)
                    P.selfsync(e, e.tensor_scalar(sc[:, base:base + E], lsrc,
                                                  stats[:, 5:6], None,
                                                  ALU.is_equal), xD)
                    P.selfsync(e, e.tensor_mul(sc[:, base:base + E],
                                               sc[:, base:base + E],
                                               sc[:, base + 3 * E:base + 4 * E]),
                               xD)
                    P.inc(e.tensor_copy(stats[:, 6:7], stats[:, 5:6]),
                          Dv, f"sco{j}")
                else:
                    P.selfsync(e, e.tensor_scalar(sc[:, base:base + E], lsrc,
                                                  stats[:, 5:6], None,
                                                  ALU.is_equal), xD)
                    P.inc(e.tensor_copy(stats[:, 6:7], stats[:, 5:6]),
                          Dv, f"sco{j}")

    def b_tensor(e):
        for i in range(NT + 1):
            if i < NT:
                P.wait(e, Dv, f"lo{i}")
                if i >= 1:
                    P.wait(e, Dv, f"ev{i - 1}")
                ins = None
                for c in range(KC):
                    ins = e.transpose(pst[:, c * 128:(c + 1) * 128],
                                      xnb[:, slot(i) + c * 128:
                                          slot(i) + (c + 1) * 128], P.identb)
                if router:
                    for c in range(KC):
                        ins = e.transpose(pst[:, D + c * 128:D + (c + 1) * 128],
                                          lob[:, slot(i) + c * 128:
                                              slot(i) + (c + 1) * 128], P.identb)
                P.inc(ins, Tn, f"tr{i}")
            if router and i >= 1:
                j = i - 1
                P.wait(e, Dv, f"ev{j}")
                if j >= 1:
                    P.wait(e, Dv, f"sco{j - 1}")
                    if use_sig:
                        P.wait(e, A, f"ex{j - 1}")
                n = 0
                ins = None
                for kc in range(KC):
                    for hl in range(3):
                        lh = lofm[:, slot(j) + kc * 128:slot(j) + kc * 128 + 128] \
                            if hl == 1 else \
                            fmb[:, slot(j) + kc * 128:slot(j) + kc * 128 + 128]
                        rh = wrt[:, (2 * kc + (1 if hl == 2 else 0)) * E:
                                 (2 * kc + (1 if hl == 2 else 0) + 1) * E]
                        ins = e.matmul(psr[:, 0:E], lh, rh, start=(n == 0),
                                       stop=(n == 3 * KC - 1))
                        n += 1
                P.inc(ins, Tn, f"mmr{j}")

    def b_gpsimd(e):
        for i in range(NT):
            P.wait(e, A, f"ln{i}")
            if lnw:
                P.wait(e, Dv, f"lo{i}")
            if out_tm:
                P.inc(e.dma_start(P.ap(out_tm)[i * 128:(i + 1) * 128, :],
                                  xnb[:, slot(i):slot(i) + D]),
                      s_tm[i % 2], f"stm{i}", 16)
            if out_f32:
                P.inc(e.dma_start(P.ap(out_f32)[i * 128:(i + 1) * 128, :],
                                  xnf[:, slot(i):slot(i) + D]),
                      s_f32[i % 2], f"sf32{i}", 16)
            P.wait(e, Dv, f"ev{i}")
            P.inc(e.dma_start(P.ap(out_fm)[:, :, i * 128:(i + 1) * 128]
                              .rearrange("c p j -> p c j"),
                              fmb[:, slot(i):slot(i) + D]
                              .rearrange("p (c j) -> p c j", c=KC)),
                  s_fm[i % 2], f"sfm{i}", 16)
            if router:
                P.wait(e, Dv, f"sco{i}")
                wE = 4 * E if DBG_SC else E
                P.inc(e.dma_start(P.ap(router[2])[i * 128:(i + 1) * 128, :],
                                  sc[:, (i % 2) * 4 * E:(i % 2) * 4 * E + wE]),
                      s_sc[i % 2], f"ssc{i}", 16)

    P.twopass([("sync", b_sync), ("scalar", b_scalar), ("vector", b_vector),
               ("tensor", b_tensor), ("gpsimd", b_gpsimd)])


def _router_w_prep(P, es, router, wrt):
    nc = P.nc
    wname, E, w_is_ExD = router[0], router[1], router[4]
    Dv, Tn = P.sem("Dv"), P.sem("Tn")
    ld = P.dsem(22)
    if w_is_ExD:
        hes = contextlib.ExitStack()
        wst = hes.enter_context(nc.sbuf_tensor(f"rw_st{P.phase_id}", [E, D], F32))
        whi = hes.enter_context(nc.sbuf_tensor(f"rw_hi{P.phase_id}", [E, D], BF16))
        wlo = hes.enter_context(nc.sbuf_tensor(f"rw_lo{P.phase_id}", [E, D], BF16))
        pse = hes.enter_context(nc.psum_tensor(f"rw_ps{P.phase_id}", [128, 2 * E], BF16))

        def b_sync(e):
            P.inc(e.dma_start(wst[:, :], P.ap(wname)), ld, "rwl", 16)

        def b_vec(e):
            P.wait(e, ld, "rwl")
            e.tensor_copy(whi[:, :], wst[:, :])
            P.inc(e.tensor_sub(wlo[:, :], wst[:, :], whi[:, :]), Dv, "rws")
            for kc in range(KC):
                P.wait(e, Tn, f"rwt{kc}")
                e.tensor_copy(wrt[:, 2 * kc * E:(2 * kc + 1) * E], pse[:, 0:E])
                P.inc(e.tensor_copy(wrt[:, (2 * kc + 1) * E:(2 * kc + 2) * E],
                                    pse[:, E:2 * E]), Dv, f"rwe{kc}")

        def b_tensor(e):
            P.wait(e, Dv, "rws")
            for kc in range(KC):
                if kc >= 1:
                    P.wait(e, Dv, f"rwe{kc - 1}")
                e.transpose(pse[:, 0:E], whi[0:E, kc * 128:(kc + 1) * 128],
                            P.identb[0:E, 0:E])
                P.inc(e.transpose(pse[:, E:2 * E],
                                  wlo[0:E, kc * 128:(kc + 1) * 128],
                                  P.identb[0:E, 0:E]), Tn, f"rwt{kc}")

        P.twopass([("sync", b_sync), ("vector", b_vec), ("tensor", b_tensor)])
        hes.close()
    else:
        wst = es.enter_context(nc.sbuf_tensor(f"rw_st{P.phase_id}", [128, KC * E], F32))

        def b_sync(e):
            P.inc(e.dma_start(wst[:, :].rearrange("p (c e) -> p c e", c=KC),
                              P.ap(wname).rearrange("(c p) e -> p c e", p=128)),
                  ld, "rwl", 16)

        def b_vec(e):
            P.wait(e, ld, "rwl")
            for kc in range(KC):
                e.tensor_copy(wrt[:, 2 * kc * E:(2 * kc + 1) * E],
                              wst[:, kc * E:(kc + 1) * E])
            for kc in range(KC):
                e.tensor_sub(wrt[:, (2 * kc + 1) * E:(2 * kc + 2) * E],
                             wst[:, kc * E:(kc + 1) * E],
                             wrt[:, 2 * kc * E:(2 * kc + 1) * E])

        P.twopass([("sync", b_sync), ("vector", b_vec)])



def _ph_ln0(P, es):
    ph_ln(P, es, "x0", "xn_fm", out_tm="xn_tm",
          router=("L0_ffn_w_router", 16, "scores_tm", True, True, None))



def make_constants():
    s = np.arange(CH, dtype=np.float32)[:, None]
    t = np.arange(CH, dtype=np.float32)[None, :]
    c = {
        "c_ident_b": np.eye(128, dtype=BF),
        "c_m1": (t - 1.0 - s).astype(np.float32),
        "c_m2neg": np.where(s < t, 0.0, NEG).astype(np.float32),
        "c_mask1": (s <= t).astype(np.float32),
        "c_trel": (np.arange(128, dtype=np.float32)[:, None]
                   + 128.0 * np.arange(CH // 128, dtype=np.float32)[None, :]),
    }
    c["c_trel2"] = (CH - 1.0) - c["c_trel"]
    return c


def shard_inputs(inputs):
    vals = {}
    x = np.ascontiguousarray(np.asarray(inputs["L0_attn_x"], dtype=np.float32))
    vals["x0"] = [np.ascontiguousarray(x[2 * i:2 * i + 2].reshape(T, D))
                  for i in range(NCORES)]
    for name in TENSORS:
        if name in ("x0", "OUT") or name.startswith("c_"):
            continue
        if name in inputs:
            shape, dt = TENSORS[name]
            a = np.ascontiguousarray(np.asarray(inputs[name],
                                                dtype=NP_DT[dt]).reshape(shape))
            vals[name] = [a] * NCORES
    for name, a in make_constants().items():
        shape, dt = TENSORS[name]
        a = np.ascontiguousarray(a.astype(NP_DT[dt]).reshape(shape))
        vals[name] = [a] * NCORES
    return vals


def build_launch(phase_items, later_reads):
    all_writes = set()
    written = set()
    kinds = {}
    for _, _, rd, wr in phase_items:
        for r in rd:
            if r not in written:
                kinds[r] = "in"
        for w in wr:
            written.add(w)
            all_writes.add(w)
    for w in all_writes:
        if w == "OUT" or w in later_reads:
            kinds[w] = "out"
        elif kinds.get(w) != "in":
            kinds[w] = "tmp"
        else:
            raise RuntimeError(f"tensor {w} both external input and written")
    nc = bass.Bass("TRN2", target_bir_lowering=False)
    es = contextlib.ExitStack()
    P = Ctx(nc, es, kinds)
    for name, builder, rd, wr in phase_items:
        run_phase(P, builder)
    es.close()
    ext_in = [n for n, k in kinds.items() if k == "in"]
    ext_out = [n for n, k in kinds.items() if k == "out"]
    return nc, ext_in, ext_out


def run_pipeline(vals, groups, phase_defs=None, keep_all=False):
    defs = {p[0]: p for p in (phase_defs or PHASE_DEFS)}
    for gi, g in enumerate(groups):
        items = [defs[n] for n in g]
        later = set()
        for g2 in groups[gi + 1:]:
            for n2 in g2:
                later.update(defs[n2][2])
        later.add("OUT")
        if keep_all:
            later.update(TENSORS.keys())
        nc, ext_in, ext_out = build_launch(items, later)
        in_maps = [{n: vals[n][c] for n in ext_in} for c in range(NCORES)]
        res = run_bass_kernel_spmd(nc, in_maps, core_ids=list(range(NCORES)))
        for n in ext_out:
            vals[n] = [np.asarray(res.results[c][n]) for c in range(NCORES)]
    return vals


FUSED = True


def kernel(**inputs):
    vals = shard_inputs(inputs)
    if FUSED:
        groups = [[p[0] for p in PHASE_DEFS]]
    else:
        groups = [[p[0]] for p in PHASE_DEFS]
    run_pipeline(vals, groups)
    out = np.stack([vals["OUT"][c].reshape(B_CORE, S, D) for c in range(NCORES)])
    return np.ascontiguousarray(out.reshape(16, S, D))



def emit_decay_consts(P, es):
    nc = P.nc
    A, Dv, Tn, xA, xD = (P.sem(n) for n in ["A", "Dv", "Tn", "xA", "xD"])
    ld = P.dsem(22)
    ld2 = P.dsem(23)
    ld3 = P.dsem(24)
    NJ = CH // 128
    cols = es.enter_context(nc.sbuf_tensor(f"dc_cols{P.phase_id}", [128, 2 * NJ + 2], F32))
    hes = contextlib.ExitStack()
    dst = hes.enter_context(nc.sbuf_tensor(f"dc_st{P.phase_id}", [1, 8], F32))
    trl = hes.enter_context(nc.sbuf_tensor(f"dc_trl{P.phase_id}", [128, 2 * NJ], F32))
    ones1 = hes.enter_context(nc.sbuf_tensor(f"dc_ones{P.phase_id}", [1, 128], F32))
    psc = hes.enter_context(nc.psum_tensor(f"dc_ps{P.phase_id}", [128, 1], F32))

    def b_sync(e):
        P.inc(e.dma_start(dst[:, 0:1], P.ap("L0_attn_decay_param")), ld, "dcl", 16)
        P.inc(e.dma_start(trl[:, 0:NJ], P.ap("c_trel")), ld2, "dct1", 16)
        P.inc(e.dma_start(trl[:, NJ:2 * NJ], P.ap("c_trel2")), ld3, "dct2", 16)

    def b_scalar(e):
        P.wait(e, ld, "dcl")
        P.selfsync(e, e.activation(dst[:, 1:2], dst[:, 0:1], AF.Exp, scale=-1.0), xA)
        P.incflush(e, e.activation(dst[:, 2:3], dst[:, 1:2], AF.Identity,
                                   bias=np.float32(1.0)), "A", A, "dcs")
        P.wait(e, Dv, "dcr")
        P.incflush(e, e.activation(dst[:, 4:5], dst[:, 3:4], AF.Ln), "A", A, "dcln")
        P.wait(e, Dv, "dcb")
        P.wait(e, ld2, "dct1")
        P.wait(e, ld3, "dct2")
        P.selfsync(e, e.activation(cols[:, 1:1 + NJ], trl[:, 0:NJ], AF.Exp,
                                   scale=cols[:, 0:1]), xA)
        P.selfsync(e, e.activation(cols[:, 1 + NJ:1 + 2 * NJ], trl[:, NJ:2 * NJ],
                                   AF.Exp, scale=cols[:, 0:1]), xA)
        P.incflush(e, e.activation(cols[:, 1 + 2 * NJ:2 + 2 * NJ], cols[:, 0:1],
                                   AF.Exp, scale=float(CH)), "A", A, "dcd")

    def b_vector(e):
        P.wait(e, A, "dcs")
        P.incflush(e, e.reciprocal(dst[:, 3:4], dst[:, 2:3]), "D", Dv, "dcr")
        P.incflush(e, e.memset(ones1[:, :], 1.0), "D", Dv, "dco")
        P.wait(e, Tn, "dcm")
        P.incflush(e, e.tensor_copy(cols[:, 0:1], psc[:, 0:1]), "D", Dv, "dcb")

    def b_tensor(e):
        P.wait(e, A, "dcln")
        P.wait(e, Dv, "dco")
        P.inc(e.matmul(psc[:, 0:1], ones1[:, :], dst[:, 4:5],
                       start=True, stop=True), Tn, "dcm")

    P.twopass([("sync", b_sync), ("scalar", b_scalar), ("vector", b_vector),
               ("tensor", b_tensor)])
    hes.close()
    return {"ld_col": cols[:, 0:1], "dq": cols[:, 1:1 + NJ],
            "dk": cols[:, 1 + NJ:1 + 2 * NJ],
            "dC": cols[:, 1 + 2 * NJ:2 + 2 * NJ], "done": "dcd"}



def ph_qkv0(P, es):
    nc = P.nc
    A, Dv, Tn = (P.sem(n) for n in ["A", "Dv", "Tn"])
    lds = [P.dsem(0), P.dsem(1)]
    std = {"qs": [P.dsem(10), P.dsem(11)], "q": [P.dsem(12), P.dsem(13)],
           "kp": [P.dsem(14), P.dsem(15)], "kh": [P.dsem(16), P.dsem(17)],
           "v": [P.dsem(18), P.dsem(19)]}
    wq = es.enter_context(nc.sbuf_tensor(f"q0_wq{P.phase_id}", [128, KC * D], BF16))
    wk = es.enter_context(nc.sbuf_tensor(f"q0_wk{P.phase_id}", [128, KC * D], BF16))
    wv = es.enter_context(nc.sbuf_tensor(f"q0_wv{P.phase_id}", [128, KC * D], BF16))
    load_cast_weights(P, es, [(P.ap("L0_attn_Wq"), wq[:, :]),
                              (P.ap("L0_attn_Wk"), wk[:, :]),
                              (P.ap("L0_attn_Wv"), wv[:, :])])
    bb = bias_bcast(P, es, ["L0_attn_bq", "L0_attn_bk", "L0_attn_bv"])
    dc = emit_decay_consts(P, es)

    xfm = es.enter_context(nc.sbuf_tensor(f"q0_x{P.phase_id}", [128, 2 * D], BF16))
    qbf = es.enter_context(nc.sbuf_tensor(f"q0_qb{P.phase_id}", [128, 2 * D], BF16))
    qsb = es.enter_context(nc.sbuf_tensor(f"q0_qs{P.phase_id}", [128, 2 * D], BF16))
    kpb = es.enter_context(nc.sbuf_tensor(f"q0_kp{P.phase_id}", [128, 2 * D], BF16))
    khb = es.enter_context(nc.sbuf_tensor(f"q0_kh{P.phase_id}", [128, 2 * D], BF16))
    vbf = es.enter_context(nc.sbuf_tensor(f"q0_vb{P.phase_id}", [128, 2 * D], BF16))
    k1 = es.enter_context(nc.sbuf_tensor(f"q0_k1{P.phase_id}", [128, D], F32))
    km = es.enter_context(nc.sbuf_tensor(f"q0_km{P.phase_id}", [128, D], F32))
    kr = es.enter_context(nc.sbuf_tensor(f"q0_kr{P.phase_id}", [128, D], F32))
    keb = es.enter_context(nc.sbuf_tensor(f"q0_ke{P.phase_id}", [128, D], F32))
    qfmb = es.enter_context(nc.sbuf_tensor(f"q0_qf{P.phase_id}", [128, 2 * D], BF16))
    qsfmb = es.enter_context(nc.sbuf_tensor(f"q0_qsf{P.phase_id}", [128, 2 * D], BF16))
    kfmb = es.enter_context(nc.sbuf_tensor(f"q0_kf{P.phase_id}", [128, 2 * D], BF16))
    psP = es.enter_context(nc.psum_tensor(f"q0_ps{P.phase_id}", [128, 3 * D], F32))
    psT = es.enter_context(nc.psum_tensor(f"q0_pt{P.phase_id}", [128, 2 * D], BF16))

    def slot(i):
        return (i % 2) * D

    def jsub(i):
        return (i % (S // 128)) % NCH

    def b_sync(e):
        for i in range(NT):
            if i >= 2:
                P.wait(e, Tn, f"pjv{i - 2}")
            P.inc(e.dma_start(xfm[:, slot(i):slot(i) + D]
                              .rearrange("p (c j) -> p c j", c=KC),
                              P.ap("xn_fm")[:, :, i * 128:(i + 1) * 128]
                              .rearrange("c p j -> p c j")),
                  lds[i % 2], f"lx{i}", 16)

    def b_tensor(e):
        for i in range(NT):
            P.wait(e, lds[i % 2], f"lx{i}")
            if i >= 1:
                P.wait(e, Dv, f"vb{i - 1}")
                P.wait(e, Dv, f"km{i - 1}")
            for wi, (wsb, tag) in enumerate([(wq, "q"), (wk, "k"), (wv, "v")]):
                ins = None
                for n in range(2):
                    for kc in range(KC):
                        ins = e.matmul(psP[:, wi * D + n * 512:wi * D + n * 512 + 512],
                                       xfm[:, slot(i) + kc * 128:slot(i) + kc * 128 + 128],
                                       wsb[:, kc * D + n * 512:kc * D + n * 512 + 512],
                                       start=(kc == 0), stop=(kc == KC - 1))
                P.inc(ins, Tn, f"pj{tag}{i}")
            P.wait(e, Dv, f"qs{i}")
            if i >= 1:
                P.wait(e, Dv, f"eq{i - 1}")
            ins = None
            for c in range(KC):
                ins = e.transpose(psT[:, c * 128:(c + 1) * 128],
                                  qbf[:, slot(i) + c * 128:slot(i) + (c + 1) * 128],
                                  P.identb)
            P.inc(ins, Tn, f"tq{i}")
            if i >= 1:
                P.wait(e, Dv, f"eqs{i - 1}")
            ins = None
            for c in range(KC):
                ins = e.transpose(psT[:, D + c * 128:D + (c + 1) * 128],
                                  qsb[:, slot(i) + c * 128:slot(i) + (c + 1) * 128],
                                  P.identb)
            P.inc(ins, Tn, f"tqs{i}")
            P.wait(e, Dv, f"kp{i}")
            P.wait(e, Dv, f"eq{i}")
            ins = None
            for c in range(KC):
                ins = e.transpose(psT[:, c * 128:(c + 1) * 128],
                                  kpb[:, slot(i) + c * 128:slot(i) + (c + 1) * 128],
                                  P.identb)
            P.inc(ins, Tn, f"tkp{i}")

    def b_scalar(e):
        for i in range(NT):
            P.wait(e, Dv, f"km{i}")
            if i >= 1:
                P.wait(e, Dv, f"kp{i - 1}")
            P.inc(e.activation(keb[:, :], km[:, :], AF.Exp), A, f"ke{i}")

    def b_vector(e):
        j = None
        for i in range(NT):
            jj = jsub(i)
            P.wait(e, Tn, f"pjq{i}")
            if i >= 2:
                P.wait(e, Tn, f"tq{i - 2}")
                P.wait(e, Tn, f"tqs{i - 2}")
            e.tensor_add(qbf[:, slot(i):slot(i) + D], psP[:, 0:D],
                         bb["L0_attn_bq"])
            P.inc(e.tensor_scalar(qsb[:, slot(i):slot(i) + D],
                                  qbf[:, slot(i):slot(i) + D],
                                  dc["dq"][:, jj:jj + 1], None, ALU.mult),
                  Dv, f"qs{i}")
            P.wait(e, Tn, f"pjk{i}")
            e.tensor_add(k1[:, :], psP[:, D:2 * D], bb["L0_attn_bk"])
            e.tensor_relu(kr[:, :], k1[:, :])
            P.inc(e.tensor_scalar(km[:, :], k1[:, :], 0.0, None, ALU.min),
                  Dv, f"km{i}")
            P.wait(e, Tn, f"pjv{i}")
            if i >= 2:
                P.wait(e, std["v"][i % 2], f"sv{i - 2}")
                P.wait(e, std["kh"][i % 2], f"skh{i - 2}")
                P.wait(e, std["kp"][i % 2], f"skp{i - 2}")
            P.inc(e.tensor_add(vbf[:, slot(i):slot(i) + D], psP[:, 2 * D:3 * D],
                               bb["L0_attn_bv"]), Dv, f"vb{i}")
            P.wait(e, A, f"ke{i}")
            if i >= 2:
                P.wait(e, Tn, f"tkp{i - 2}")
            e.tensor_add(kpb[:, slot(i):slot(i) + D], keb[:, :], kr[:, :])
            P.inc(e.tensor_scalar(khb[:, slot(i):slot(i) + D],
                                  kpb[:, slot(i):slot(i) + D],
                                  dc["dk"][:, jj:jj + 1], None, ALU.mult),
                  Dv, f"kp{i}")
            P.wait(e, Tn, f"tq{i}")
            if i >= 2:
                P.wait(e, std["q"][i % 2], f"sq{i - 2}")
            P.inc(e.tensor_copy(qfmb[:, slot(i):slot(i) + D], psT[:, 0:D]),
                  Dv, f"eq{i}")
            P.wait(e, Tn, f"tqs{i}")
            P.inc(e.tensor_copy(qsfmb[:, slot(i):slot(i) + D], psT[:, D:2 * D]),
                  Dv, f"eqs{i}")
            P.wait(e, Tn, f"tkp{i}")
            P.inc(e.tensor_copy(kfmb[:, slot(i):slot(i) + D], psT[:, 0:D]),
                  Dv, f"ekp{i}")

    def b_gpsimd(e):
        for i in range(NT):
            P.wait(e, Dv, f"eq{i}")
            P.inc(e.dma_start(P.ap("q_fm")[:, :, i * 128:(i + 1) * 128]
                              .rearrange("c p j -> p c j"),
                              qfmb[:, slot(i):slot(i) + D]
                              .rearrange("p (c j) -> p c j", c=KC)),
                  std["q"][i % 2], f"sq{i}", 16)
            P.wait(e, Dv, f"eqs{i}")
            P.inc(e.dma_start(P.ap("qs_fm")[:, :, i * 128:(i + 1) * 128]
                              .rearrange("c p j -> p c j"),
                              qsfmb[:, slot(i):slot(i) + D]
                              .rearrange("p (c j) -> p c j", c=KC)),
                  std["qs"][i % 2], f"sqs{i}", 16)
            P.wait(e, Dv, f"ekp{i}")
            P.inc(e.dma_start(P.ap("kphi_fm")[:, :, i * 128:(i + 1) * 128]
                              .rearrange("c p j -> p c j"),
                              kfmb[:, slot(i):slot(i) + D]
                              .rearrange("p (c j) -> p c j", c=KC)),
                  std["kp"][i % 2], f"skp{i}", 16)
            P.wait(e, Dv, f"kp{i}")
            P.inc(e.dma_start(P.ap("khat_tm")[i * 128:(i + 1) * 128, :],
                              khb[:, slot(i):slot(i) + D]),
                  std["kh"][i % 2], f"skh{i}", 16)
            P.wait(e, Dv, f"vb{i}")
            P.inc(e.dma_start(P.ap("v_tm")[i * 128:(i + 1) * 128, :],
                              vbf[:, slot(i):slot(i) + D]),
                  std["v"][i % 2], f"sv{i}", 16)

    P.twopass([("sync", b_sync), ("scalar", b_scalar), ("vector", b_vector),
               ("tensor", b_tensor), ("gpsimd", b_gpsimd)])



def ph_scan0(P, es):
    nc = P.nc
    A, Dv, Tn = (P.sem(n) for n in ["A", "Dv", "Tn"])
    lds = [[P.dsem(2 * k), P.dsem(2 * k + 1)] for k in range(5)]
    strd = [P.dsem(10), P.dsem(11)]
    dc = emit_decay_consts(P, es)
    mT = es.enter_context(nc.sbuf_tensor(f"s0_mT{P.phase_id}", [128, 4 * 512], F32))
    M = es.enter_context(nc.sbuf_tensor(f"s0_M{P.phase_id}", [128, B_CORE * KC * D], F32))
    Mb = es.enter_context(nc.sbuf_tensor(f"s0_Mb{P.phase_id}", [128, B_CORE * KC * D], BF16))
    ld14 = P.dsem(22)
    ld13 = P.dsem(23)
    with contextlib.ExitStack() as hes:
        m1 = hes.enter_context(nc.sbuf_tensor(f"s0_m1{P.phase_id}", [128, 4 * 512], F32))
        m2 = hes.enter_context(nc.sbuf_tensor(f"s0_m2{P.phase_id}", [128, 4 * 512], F32))

        def mb_sync(e):
            P.inc(e.dma_start(m1[:, :].rearrange("p (j t) -> p j t", j=4),
                              P.ap("c_m1").rearrange("(j p) t -> p j t", p=128)),
                  ld14, "lm1", 16)
            P.inc(e.dma_start(m2[:, :].rearrange("p (j t) -> p j t", j=4),
                              P.ap("c_m2neg").rearrange("(j p) t -> p j t", p=128)),
                  ld13, "lm2", 16)

        def mb_vec(e):
            P.wait(e, ld14, "lm1")
            P.wait(e, ld13, "lm2")
            P.wait(e, A, dc["done"])
            e.tensor_scalar(m1[:, :], m1[:, :], dc["ld_col"], None, ALU.mult)
            P.inc(e.tensor_add(m1[:, :], m1[:, :], m2[:, :]), Dv, "marg")
            e.memset(M[:, :], 0.0)
            e.memset(Mb[:, :], 0.0)

        def mb_scalar(e):
            P.wait(e, Dv, "marg")
            e.activation(mT[:, :], m1[:, :], AF.Exp)

        P.twopass([("sync", mb_sync), ("vector", mb_vec), ("scalar", mb_scalar)])

    qsf = es.enter_context(nc.sbuf_tensor(f"s0_qs{P.phase_id}", [128, 2 * KC * CH], BF16))
    qf = es.enter_context(nc.sbuf_tensor(f"s0_q{P.phase_id}", [128, 2 * KC * CH], BF16))
    kf = es.enter_context(nc.sbuf_tensor(f"s0_k{P.phase_id}", [128, 2 * KC * CH], BF16))
    kht = es.enter_context(nc.sbuf_tensor(f"s0_kh{P.phase_id}", [128, 2 * 4 * D], BF16))
    vt = es.enter_context(nc.sbuf_tensor(f"s0_v{P.phase_id}", [128, 2 * 4 * D], BF16))
    absf = es.enter_context(nc.sbuf_tensor(f"s0_ab{P.phase_id}", [128, 4 * CH], BF16))
    rbf = es.enter_context(nc.sbuf_tensor(f"s0_rb{P.phase_id}", [128, 2 * D], BF16))
    rfm = es.enter_context(nc.sbuf_tensor(f"s0_rf{P.phase_id}", [128, 2 * D], BF16))
    psA = es.enter_context(nc.psum_tensor(f"s0_pa{P.phase_id}", [128, 2 * 512], F32))
    psR = es.enter_context(nc.psum_tensor(f"s0_pr{P.phase_id}", [128, 2 * D], F32))
    psT = es.enter_context(nc.psum_tensor(f"s0_pt{P.phase_id}", [128, D], BF16))
    NBC = B_CORE * NCH


    def b_sync(e):
        for idx in range(NBC):
            b, c = idx // NCH, idx % NCH
            off = b * S + c * CH
            sl = (idx % 2)
            if idx >= 2:
                P.wait(e, Tn, f"dm{idx - 2}.7")
            for k, (name, fm) in enumerate([("qs_fm", True), ("q_fm", True),
                                            ("kphi_fm", True), ("khat_tm", False),
                                            ("v_tm", False)]):
                if fm:
                    dst = [qsf, qf, kf][k][:, sl * KC * CH:(sl + 1) * KC * CH]
                    P.inc(e.dma_start(dst.rearrange("p (c j) -> p c j", c=KC),
                                      P.ap(name)[:, :, off:off + CH]
                                      .rearrange("c p j -> p c j")),
                          lds[k][idx % 2], f"l{k}.{idx}", 16)
                else:
                    dst = [None, None, None, kht, vt][k][:, sl * 4 * D:(sl + 1) * 4 * D]
                    P.inc(e.dma_start(dst.rearrange("p (s d) -> p s d", s=4),
                                      P.ap(name)[off:off + CH, :]
                                      .rearrange("(s p) d -> p s d", p=128)),
                          lds[k][idx % 2], f"l{k}.{idx}", 16)

    def b_vector(e):
        for idx in range(NBC):
            b, c = idx // NCH, idx % NCH
            sl = (idx % 2)
            for js in range(4):
                P.wait(e, Tn, f"aw{idx}.{js}")
                P.inc(e.tensor_mul(absf[:, js * CH:(js + 1) * CH],
                                   psA[:, (js % 2) * 512:(js % 2) * 512 + 512],
                                   mT[:, js * CH:(js + 1) * CH]), Dv, f"ab{idx}.{js}")
            for jt in range(4):
                P.wait(e, Tn, f"rd{idx}.{jt}")
                P.inc(e.tensor_copy(rbf[:, (jt % 2) * D:(jt % 2) * D + D],
                                    psR[:, (jt % 2) * D:(jt % 2) * D + D]),
                      Dv, f"re{idx}.{jt}")
                P.wait(e, Tn, f"rt{idx}.{jt}")
                if idx * 4 + jt >= 2:
                    pj = idx * 4 + jt - 2
                    P.wait(e, strd[jt % 2], f"sr{pj // 4}.{pj % 4}")
                P.inc(e.tensor_copy(rfm[:, (jt % 2) * D:(jt % 2) * D + D],
                                    psT[:, 0:D]), Dv, f"rf{idx}.{jt}")
            for di in range(KC):
                P.wait(e, Tn, f"dm{idx}.{di}")
                ins = None
                for n in range(2):
                    ins = e.scalar_tensor_tensor(
                        M[:, b * KC * D + di * D + n * 512:
                          b * KC * D + di * D + n * 512 + 512],
                        M[:, b * KC * D + di * D + n * 512:
                          b * KC * D + di * D + n * 512 + 512],
                        dc["dC"], psA[:, n * 512:n * 512 + 512],
                        ALU.mult, ALU.add)
                P.inc(ins, Dv, f"mu{idx}.{di}")
                P.inc(e.tensor_copy(Mb[:, b * KC * D + di * D:
                                       b * KC * D + (di + 1) * D],
                                    M[:, b * KC * D + di * D:
                                      b * KC * D + (di + 1) * D]),
                      Dv, f"mb{idx}.{di}")

    def b_tensor2(e):
        for idx in range(NBC):
            b, c = idx // NCH, idx % NCH
            sl = (idx % 2)
            qso = sl * KC * CH
            kho = sl * 4 * D
            for k in range(5):
                P.wait(e, lds[k][idx % 2], f"l{k}.{idx}")
            for js in range(4):
                if js >= 2:
                    P.wait(e, Dv, f"ab{idx}.{js - 2}")
                elif idx >= 1:
                    P.wait(e, Dv, f"mu{idx - 1}.7")
                ins = None
                for kc in range(KC):
                    ins = e.matmul(psA[:, (js % 2) * 512:(js % 2) * 512 + 512],
                                   kf[:, qso + kc * CH + js * 128:
                                      qso + kc * CH + js * 128 + 128],
                                   qf[:, qso + kc * CH:qso + kc * CH + CH],
                                   start=(kc == 0), stop=(kc == KC - 1))
                P.inc(ins, Tn, f"aw{idx}.{js}")
            if c != 0:
                P.wait(e, Dv, f"mb{idx - 1}.7")
            P.wait(e, Dv, f"ab{idx}.3")
            for jt in range(4):
                if jt >= 2:
                    P.wait(e, Dv, f"re{idx}.{jt - 2}")
                elif idx >= 1:
                    P.wait(e, Dv, f"re{idx - 1}.{jt + 2}")
                ins = None
                ngrp = 0
                for n in range(2):
                    for kc in range(KC):
                        ins = e.matmul(psR[:, (jt % 2) * D + n * 512:
                                           (jt % 2) * D + n * 512 + 512],
                                       qsf[:, qso + kc * CH + jt * 128:
                                           qso + kc * CH + jt * 128 + 128],
                                       Mb[:, b * KC * D + kc * D + n * 512:
                                          b * KC * D + kc * D + n * 512 + 512],
                                       start=(kc == 0), stop=False)
                    for js in range(4):
                        ins = e.matmul(psR[:, (jt % 2) * D + n * 512:
                                           (jt % 2) * D + n * 512 + 512],
                                       absf[:, js * CH + jt * 128:
                                            js * CH + jt * 128 + 128],
                                       vt[:, kho + js * D + n * 512:
                                          kho + js * D + n * 512 + 512],
                                       start=False, stop=(js == 3))
                P.inc(ins, Tn, f"rd{idx}.{jt}")
                P.wait(e, Dv, f"re{idx}.{jt}")
                if idx * 4 + jt >= 1:
                    pj = idx * 4 + jt - 1
                    P.wait(e, Dv, f"rf{pj // 4}.{pj % 4}")
                ins = None
                for cc in range(KC):
                    ins = e.transpose(psT[:, cc * 128:(cc + 1) * 128],
                                      rbf[:, (jt % 2) * D + cc * 128:
                                          (jt % 2) * D + (cc + 1) * 128],
                                      P.identb)
                P.inc(ins, Tn, f"rt{idx}.{jt}")
            for di in range(KC):
                if di >= 1:
                    P.wait(e, Dv, f"mu{idx}.{di - 1}")
                ins = None
                for n in range(2):
                    for js in range(4):
                        ins = e.matmul(psA[:, n * 512:n * 512 + 512],
                                       kht[:, kho + js * D + di * 128:
                                           kho + js * D + di * 128 + 128],
                                       vt[:, kho + js * D + n * 512:
                                          kho + js * D + n * 512 + 512],
                                       start=(js == 0), stop=(js == 3))
                P.inc(ins, Tn, f"dm{idx}.{di}")

    def b_gpsimd(e):
        for idx in range(NBC):
            b, c = idx // NCH, idx % NCH
            off = b * S + c * CH
            for jt in range(4):
                P.wait(e, Dv, f"rf{idx}.{jt}")
                P.inc(e.dma_start(P.ap("read_fm")[:, :, off + jt * 128:
                                                  off + (jt + 1) * 128]
                                  .rearrange("c p j -> p c j"),
                                  rfm[:, (jt % 2) * D:(jt % 2) * D + D]
                                  .rearrange("p (c j) -> p c j", c=KC)),
                      strd[jt % 2], f"sr{idx}.{jt}", 16)

    P.twopass([("sync", b_sync), ("vector", b_vector), ("tensor", b_tensor2),
               ("gpsimd", b_gpsimd)])



def ph_proj_tm(P, es, src_fm, wname, out_tm, bias=None, w_transposed=False,
               extra_tm=(), out_bf16=False):
    nc = P.nc
    A, Dv, Tn = (P.sem(n) for n in ["A", "Dv", "Tn"])
    lds = [P.dsem(0), P.dsem(1)]
    eld = [[P.dsem(2 + 2 * k), P.dsem(3 + 2 * k)] for k in range(len(extra_tm))]
    sto = [P.dsem(10), P.dsem(11)]
    wsb = es.enter_context(nc.sbuf_tensor(f"pj_w{P.phase_id}", [128, KC * D], BF16))
    if w_transposed:
        load_cast_transpose_weights(P, es, [(P.ap(wname), wsb[:, :])])
    else:
        load_cast_weights(P, es, [(P.ap(wname), wsb[:, :])])
    bb = bias_bcast(P, es, [bias])[bias] if bias else None
    xfm = es.enter_context(nc.sbuf_tensor(f"pj_x{P.phase_id}", [128, 2 * D], BF16))
    ext = [es.enter_context(nc.sbuf_tensor(f"pj{P.phase_id}_e{k}", [128, 2 * D], F32))
           for k in range(len(extra_tm))]
    ob = es.enter_context(nc.sbuf_tensor(f"pj_o{P.phase_id}", [128, 2 * D],
                                         BF16 if out_bf16 else F32))
    psP = es.enter_context(nc.psum_tensor(f"pj_ps{P.phase_id}", [128, 2 * D], F32))

    def slot(i):
        return (i % 2) * D

    def b_sync(e):
        for i in range(NT):
            if i >= 2:
                P.wait(e, Tn, f"pj{i - 2}")
            P.inc(e.dma_start(xfm[:, slot(i):slot(i) + D]
                              .rearrange("p (c j) -> p c j", c=KC),
                              P.ap(src_fm)[:, :, i * 128:(i + 1) * 128]
                              .rearrange("c p j -> p c j")),
                  lds[i % 2], f"lx{i}", 16)
            for k, nm in enumerate(extra_tm):
                if i >= 2:
                    P.wait(e, Dv, f"ev{i - 2}")
                P.inc(e.dma_start(ext[k][:, slot(i):slot(i) + D],
                                  P.ap(nm)[i * 128:(i + 1) * 128, :]),
                      eld[k][i % 2], f"le{k}.{i}", 16)

    def b_tensor(e):
        for i in range(NT):
            P.wait(e, lds[i % 2], f"lx{i}")
            if i >= 2:
                P.wait(e, Dv, f"ev{i - 2}")
            ins = None
            for n in range(2):
                for kc in range(KC):
                    ins = e.matmul(psP[:, slot(i) + n * 512:slot(i) + n * 512 + 512],
                                   xfm[:, slot(i) + kc * 128:slot(i) + kc * 128 + 128],
                                   wsb[:, kc * D + n * 512:kc * D + n * 512 + 512],
                                   start=(kc == 0), stop=(kc == KC - 1))
            P.inc(ins, Tn, f"pj{i}")

    def b_vector(e):
        for i in range(NT):
            P.wait(e, Tn, f"pj{i}")
            if i >= 2:
                P.wait(e, sto[i % 2], f"so{i - 2}")
            for k in range(len(extra_tm)):
                P.wait(e, eld[k][i % 2], f"le{k}.{i}")
            src = psP[:, slot(i):slot(i) + D]
            if bias is not None:
                e.tensor_add(ob[:, slot(i):slot(i) + D], src, bb)
                src = ob[:, slot(i):slot(i) + D]
            for k in range(len(extra_tm)):
                e.tensor_add(ob[:, slot(i):slot(i) + D], src,
                             ext[k][:, slot(i):slot(i) + D])
                src = ob[:, slot(i):slot(i) + D]
            if bias is None and not extra_tm:
                e.tensor_copy(ob[:, slot(i):slot(i) + D], src)
            P.inc(e.tensor_copy(P.jD[:, 2:3], P.jD[:, 3:4]), Dv, f"ev{i}")

    def b_gpsimd(e):
        for i in range(NT):
            P.wait(e, Dv, f"ev{i}")
            P.inc(e.dma_start(P.ap(out_tm)[i * 128:(i + 1) * 128, :],
                              ob[:, slot(i):slot(i) + D]),
                  sto[i % 2], f"so{i}", 16)

    P.twopass([("sync", b_sync), ("tensor", b_tensor), ("vector", b_vector),
               ("gpsimd", b_gpsimd)])



def ph_shared0(P, es):
    nc = P.nc
    A, Dv, Tn = (P.sem(n) for n in ["A", "Dv", "Tn"])
    lds = [P.dsem(0), P.dsem(1)]
    sto = [P.dsem(10), P.dsem(11)]
    F2 = 2048
    wg = es.enter_context(nc.sbuf_tensor(f"sh_wg{P.phase_id}", [128, KC * F2], BF16))
    wu = es.enter_context(nc.sbuf_tensor(f"sh_wu{P.phase_id}", [128, KC * F2], BF16))
    wd = es.enter_context(nc.sbuf_tensor(f"sh_wd{P.phase_id}", [128, (F2 // 128) * D], BF16))
    load_cast_transpose_weights(P, es, [(P.ap("L0_ffn_w_gate"), wg[:, :]),
                                        (P.ap("L0_ffn_w_up"), wu[:, :]),
                                        (P.ap("L0_ffn_w_down"), wd[:, :])])
    TC = 512
    NTC = T // TC
    xfm = es.enter_context(nc.sbuf_tensor(f"sh_x{P.phase_id}", [128, 2 * KC * TC], BF16))
    hg = es.enter_context(nc.sbuf_tensor(f"sh_hg{P.phase_id}", [128, TC], F32))
    hfm = es.enter_context(nc.sbuf_tensor(f"sh_h{P.phase_id}", [128, (F2 // 128) * TC], BF16))
    ob = es.enter_context(nc.sbuf_tensor(f"sh_o{P.phase_id}", [128, 2 * 2 * D], F32))
    psG = es.enter_context(nc.psum_tensor(f"sh_pg{P.phase_id}", [128, TC], F32))
    psU = es.enter_context(nc.psum_tensor(f"sh_pu{P.phase_id}", [128, TC], F32))
    psD = es.enter_context(nc.psum_tensor(f"sh_pd{P.phase_id}", [128, 2 * D], F32))
    NF = F2 // 128

    def b_sync(e):
        for i in range(NTC):
            if i >= 2:
                P.wait(e, Tn, f"gu{i - 2}.{NF - 1}")
            P.inc(e.dma_start(xfm[:, (i % 2) * KC * TC:(i % 2 + 1) * KC * TC]
                              .rearrange("p (c j) -> p c j", c=KC),
                              P.ap("xn_fm")[:, :, i * TC:(i + 1) * TC]
                              .rearrange("c p j -> p c j")),
                  lds[i % 2], f"lx{i}", 16)

    def b_tensor(e):
        for i in range(NTC):
            P.wait(e, lds[i % 2], f"lx{i}")
            xo = (i % 2) * KC * TC
            for j in range(NF):
                if j >= 1:
                    P.wait(e, Dv, f"hm{i}.{j - 1}")
                elif i >= 1:
                    P.wait(e, Dv, f"hm{i - 1}.{NF - 1}")
                ins = None
                for kc in range(KC):
                    ins = e.matmul(psG[:, :],
                                   wg[:, kc * F2 + j * 128:kc * F2 + (j + 1) * 128],
                                   xfm[:, xo + kc * TC:xo + kc * TC + TC],
                                   start=(kc == 0), stop=(kc == KC - 1))
                for kc in range(KC):
                    ins = e.matmul(psU[:, :],
                                   wu[:, kc * F2 + j * 128:kc * F2 + (j + 1) * 128],
                                   xfm[:, xo + kc * TC:xo + kc * TC + TC],
                                   start=(kc == 0), stop=(kc == KC - 1))
                P.inc(ins, Tn, f"gu{i}.{j}")
            for ts in range(4):
                P.wait(e, Dv, f"hm{i}.{NF - 1}")
                if ts >= 2:
                    P.wait(e, Dv, f"de{i}.{ts - 2}")
                elif i >= 1:
                    P.wait(e, Dv, f"de{i - 1}.{ts + 2}")
                ins = None
                for n in range(2):
                    for kc in range(NF):
                        ins = e.matmul(psD[:, (ts % 2) * D + n * 512:
                                           (ts % 2) * D + n * 512 + 512],
                                       hfm[:, kc * TC + ts * 128:
                                           kc * TC + ts * 128 + 128],
                                       wd[:, kc * D + n * 512:kc * D + n * 512 + 512],
                                       start=(kc == 0), stop=(kc == NF - 1))
                P.inc(ins, Tn, f"dn{i}.{ts}")

    def b_scalar(e):
        for i in range(NTC):
            for j in range(NF):
                P.wait(e, Tn, f"gu{i}.{j}")
                P.inc(e.activation(hg[:, :], psG[:, :], AF.Silu), A, f"sl{i}.{j}")

    def b_vector(e):
        for i in range(NTC):
            for j in range(NF):
                P.wait(e, A, f"sl{i}.{j}")
                P.inc(e.tensor_mul(hfm[:, j * TC:(j + 1) * TC], hg[:, :],
                                   psU[:, :]), Dv, f"hm{i}.{j}")
            for ts in range(4):
                P.wait(e, Tn, f"dn{i}.{ts}")
                gl = i * 4 + ts
                if gl >= 4:
                    P.wait(e, sto[gl % 2], f"so{(gl - 4) // 4}.{(gl - 4) % 4}")
                P.inc(e.tensor_copy(ob[:, (gl % 4) * D:(gl % 4) * D + D],
                                    psD[:, (ts % 2) * D:(ts % 2) * D + D]),
                      Dv, f"de{i}.{ts}")

    def b_gpsimd(e):
        for i in range(NTC):
            for ts in range(4):
                P.wait(e, Dv, f"de{i}.{ts}")
                gl = i * 4 + ts
                P.inc(e.dma_start(P.ap("ffn0_tm")[gl * 128:(gl + 1) * 128, :],
                                  ob[:, (gl % 4) * D:(gl % 4) * D + D]),
                      sto[gl % 2], f"so{i}.{ts}", 16)

    P.twopass([("sync", b_sync), ("tensor", b_tensor), ("scalar", b_scalar),
               ("vector", b_vector), ("gpsimd", b_gpsimd)])



def ph_ffn0e(P, es):
    nc = P.nc
    A, Dv, Tn = (P.sem(n) for n in ["A", "Dv", "Tn"])
    lds = [P.dsem(0), P.dsem(1)]
    sto = [P.dsem(10 + k) for k in range(8)]
    NE = 16
    F2 = 4096
    ND = 2048 // 128
    wgu = es.enter_context(nc.sbuf_tensor(f"fe_wgu{P.phase_id}", [128, KC * F2], BF16))
    wd = es.enter_context(nc.sbuf_tensor(f"fe_wd{P.phase_id}", [128, ND * D], BF16))
    ssc = es.enter_context(nc.sbuf_tensor(f"fe_sc{P.phase_id}", [128, NT * NE], F32))
    TC = 512
    NTC = T // TC
    xtm = es.enter_context(nc.sbuf_tensor(f"fe_x{P.phase_id}", [128, 2 * 4 * D], BF16))
    xsb = es.enter_context(nc.sbuf_tensor(f"fe_xs{P.phase_id}", [128, 4 * D], BF16))
    xfm = es.enter_context(nc.sbuf_tensor(f"fe_xf{P.phase_id}", [128, KC * TC], BF16))
    hg = es.enter_context(nc.sbuf_tensor(f"fe_hg{P.phase_id}", [128, TC], F32))
    hfm = es.enter_context(nc.sbuf_tensor(f"fe_h{P.phase_id}", [128, (F2 // 2 // 128) * TC], BF16))
    ob = es.enter_context(nc.sbuf_tensor(f"fe_o{P.phase_id}", [128, 4 * D], F32))
    psG = es.enter_context(nc.psum_tensor(f"fe_pg{P.phase_id}", [128, TC], F32))
    psU = es.enter_context(nc.psum_tensor(f"fe_pu{P.phase_id}", [128, TC], F32))
    psD = es.enter_context(nc.psum_tensor(f"fe_pd{P.phase_id}", [128, 2 * D], F32))
    psT = es.enter_context(nc.psum_tensor(f"fe_pt{P.phase_id}", [128, D], BF16))
    NF = F2 // 2 // 128

    ld22 = P.dsem(22)
    with nc.Block() as blk:
        @blk.sync
        def _(e):
            e.dma_start(ssc[:, :].rearrange("p (i e) -> p i e", i=NT),
                        P.ap("scores_tm").rearrange("(i p) e -> p i e", p=128))\
                .then_inc(ld22.h, 16)
            e.wait_ge(ld22.h, ld22.v + 16)
    ld22.v += 16

    for ei in range(NE):
        with contextlib.ExitStack() as wes:
            load_cast_weights(P, wes,
                              [(P.ap("L0_ffn_gate_up_proj")[ei], wgu[:, :]),
                               (P.ap("L0_ffn_down_proj")[ei], wd[:, :])],
                              tag=f"e{ei}")

        def b_sync(e, ei=ei):
            for i in range(NTC):
                if i >= 2:
                    P.wait(e, Tn, f"e{ei}gu{i - 2}.{NF - 1}")
                elif ei >= 1 and i == 0:
                    P.wait(e, Tn, f"e{ei - 1}gu{NTC - 2}.{NF - 1}")
                elif ei >= 1 and i == 1:
                    P.wait(e, Tn, f"e{ei - 1}gu{NTC - 1}.{NF - 1}")
                P.inc(e.dma_start(xtm[:, (i % 2) * 4 * D:(i % 2) * 4 * D + 4 * D]
                                  .rearrange("p (s d) -> p s d", s=4),
                                  P.ap("xn_tm")[i * TC:(i + 1) * TC, :]
                                  .rearrange("(s p) d -> p s d", p=128)),
                      lds[i % 2], f"e{ei}lx{i}", 16)

        def b_vector(e, ei=ei):
            for i in range(NTC):
                P.wait(e, lds[i % 2], f"e{ei}lx{i}")
                for su in range(4):
                    if i > 0:
                        P.wait(e, Tn, f"e{ei}tr{i - 1}.{su}")
                    elif ei > 0:
                        P.wait(e, Tn, f"e{ei - 1}tr{NTC - 1}.{su}")
                    P.inc(e.tensor_scalar(xsb[:, su * D:(su + 1) * D],
                                          xtm[:, (i % 2) * 4 * D + su * D:
                                              (i % 2) * 4 * D + (su + 1) * D],
                                          ssc[:, (i * 4 + su) * NE + ei:
                                              (i * 4 + su) * NE + ei + 1],
                                          None, ALU.mult), Dv, f"e{ei}xs{i}.{su}")
                    P.wait(e, Tn, f"e{ei}tr{i}.{su}")
                    P.inc(e.tensor_copy(
                        xfm[:, :].rearrange("p (c j) -> p c j", c=KC)
                        [:, :, su * 128:(su + 1) * 128],
                        psT[:, :].rearrange("p (c j) -> p c j", c=KC)),
                        Dv, f"e{ei}xf{i}.{su}")
                for j in range(NF):
                    P.wait(e, A, f"e{ei}sl{i}.{j}")
                    P.inc(e.tensor_mul(hfm[:, j * TC:(j + 1) * TC], hg[:, :],
                                       psU[:, :]), Dv, f"e{ei}hm{i}.{j}")
                for ts in range(4):
                    P.wait(e, Tn, f"e{ei}dn{i}.{ts}")
                    if i >= 1:
                        P.wait(e, sto[((i - 1) * 4 + ts) % 8],
                               f"e{ei}so{i - 1}.{ts}")
                    elif ei >= 1:
                        P.wait(e, sto[((NTC - 1) * 4 + ts) % 8],
                               f"e{ei - 1}so{NTC - 1}.{ts}")
                    P.inc(e.tensor_copy(ob[:, ts * D:(ts + 1) * D],
                                        psD[:, (ts % 2) * D:(ts % 2) * D + D]),
                          Dv, f"e{ei}de{i}.{ts}")

        def b_tensor(e, ei=ei):
            for i in range(NTC):
                for su in range(4):
                    P.wait(e, Dv, f"e{ei}xs{i}.{su}")
                    if i > 0 or su > 0 or ei > 0:
                        P.wait(e, Dv, f"e{ei}xf{i}.{su - 1}" if su > 0 else
                               (f"e{ei}xf{i - 1}.3" if i > 0 else
                                f"e{ei - 1}xf{NTC - 1}.3"))
                    ins = None
                    for c in range(KC):
                        ins = e.transpose(psT[:, c * 128:(c + 1) * 128],
                                          xsb[:, su * D + c * 128:
                                              su * D + (c + 1) * 128], P.identb)
                    P.inc(ins, Tn, f"e{ei}tr{i}.{su}")
                P.wait(e, Dv, f"e{ei}xf{i}.3")
                for j in range(NF):
                    if j >= 1:
                        P.wait(e, Dv, f"e{ei}hm{i}.{j - 1}")
                    elif i >= 1:
                        P.wait(e, Dv, f"e{ei}hm{i - 1}.{NF - 1}")
                    elif ei >= 1:
                        P.wait(e, Dv, f"e{ei - 1}hm{NTC - 1}.{NF - 1}")
                    ins = None
                    for kc in range(KC):
                        ins = e.matmul(psG[:, :],
                                       wgu[:, kc * F2 + j * 128:
                                           kc * F2 + (j + 1) * 128],
                                       xfm[:, kc * TC:(kc + 1) * TC],
                                       start=(kc == 0), stop=(kc == KC - 1))
                    for kc in range(KC):
                        ins = e.matmul(psU[:, :],
                                       wgu[:, kc * F2 + 2048 + j * 128:
                                           kc * F2 + 2048 + (j + 1) * 128],
                                       xfm[:, kc * TC:(kc + 1) * TC],
                                       start=(kc == 0), stop=(kc == KC - 1))
                    P.inc(ins, Tn, f"e{ei}gu{i}.{j}")
                for ts in range(4):
                    P.wait(e, Dv, f"e{ei}hm{i}.{NF - 1}")
                    if ts >= 2:
                        P.wait(e, Dv, f"e{ei}de{i}.{ts - 2}")
                    elif i >= 1:
                        P.wait(e, Dv, f"e{ei}de{i - 1}.{ts + 2}")
                    elif ei >= 1:
                        P.wait(e, Dv, f"e{ei - 1}de{NTC - 1}.{ts + 2}")
                    ins = None
                    for n in range(2):
                        for kc in range(ND):
                            ins = e.matmul(psD[:, (ts % 2) * D + n * 512:
                                               (ts % 2) * D + n * 512 + 512],
                                           hfm[:, kc * TC + ts * 128:
                                               kc * TC + ts * 128 + 128],
                                           wd[:, kc * D + n * 512:
                                              kc * D + n * 512 + 512],
                                           start=(kc == 0), stop=(kc == ND - 1))
                    P.inc(ins, Tn, f"e{ei}dn{i}.{ts}")

        def b_scalar(e, ei=ei):
            for i in range(NTC):
                for j in range(NF):
                    P.wait(e, Tn, f"e{ei}gu{i}.{j}")
                    P.inc(e.activation(hg[:, :], psG[:, :], AF.Silu),
                          A, f"e{ei}sl{i}.{j}")

        def b_gpsimd(e, ei=ei):
            for i in range(NTC):
                for ts in range(4):
                    P.wait(e, Dv, f"e{ei}de{i}.{ts}")
                    gl = i * 4 + ts
                    if ei >= 1:
                        P.wait(e, sto[gl % 8], f"e{ei - 1}so{i}.{ts}")
                    elif gl >= 8:
                        P.wait(e, sto[gl % 8], f"e{ei}so{i - 2}.{ts}")
                    if ei == 0:
                        ins = e.dma_start(
                            P.ap("ffn0e_tm")[gl * 128:(gl + 1) * 128, :],
                            ob[:, ts * D:(ts + 1) * D])
                    else:
                        ins = e.dma_start(
                            P.ap("ffn0e_tm")[gl * 128:(gl + 1) * 128, :],
                            ob[:, ts * D:(ts + 1) * D], accum_op=ALU.add)
                    P.inc(ins, sto[gl % 8], f"e{ei}so{i}.{ts}", 16)

        P.twopass([("sync", b_sync), ("vector", b_vector), ("tensor", b_tensor),
                   ("scalar", b_scalar), ("gpsimd", b_gpsimd)])



def ph_add(P, es, srcs, out):
    nc = P.nc
    Dv = P.sem("Dv")
    lds = [[P.dsem(2 * k), P.dsem(2 * k + 1)] for k in range(len(srcs))]
    sto = [P.dsem(10), P.dsem(11)]
    bufs = [es.enter_context(nc.sbuf_tensor(f"ad{P.phase_id}_{k}", [128, 2 * D], F32))
            for k in range(len(srcs))]
    ob = es.enter_context(nc.sbuf_tensor(f"ad_o{P.phase_id}", [128, 2 * D], F32))

    def slot(i):
        return (i % 2) * D

    def b_sync(e):
        for i in range(NT):
            for k, nm in enumerate(srcs):
                if i >= 2:
                    P.wait(e, Dv, f"ao{i - 2}")
                P.inc(e.dma_start(bufs[k][:, slot(i):slot(i) + D],
                                  P.ap(nm)[i * 128:(i + 1) * 128, :]),
                      lds[k][i % 2], f"l{k}.{i}", 16)

    def b_vector(e):
        for i in range(NT):
            for k in range(len(srcs)):
                P.wait(e, lds[k][i % 2], f"l{k}.{i}")
            if i >= 2:
                P.wait(e, sto[i % 2], f"s{i - 2}")
            e.tensor_add(ob[:, slot(i):slot(i) + D], bufs[0][:, slot(i):slot(i) + D],
                         bufs[1][:, slot(i):slot(i) + D])
            for k in range(2, len(srcs)):
                e.tensor_add(ob[:, slot(i):slot(i) + D],
                             ob[:, slot(i):slot(i) + D],
                             bufs[k][:, slot(i):slot(i) + D])
            P.inc(e.tensor_copy(P.jD[:, 2:3], P.jD[:, 3:4]), Dv, f"ao{i}")

    def b_gpsimd(e):
        for i in range(NT):
            P.wait(e, Dv, f"ao{i}")
            P.inc(e.dma_start(P.ap(out)[i * 128:(i + 1) * 128, :],
                              ob[:, slot(i):slot(i) + D]),
                  sto[i % 2], f"s{i}", 16)

    P.twopass([("sync", b_sync), ("vector", b_vector), ("gpsimd", b_gpsimd)])



def ph_qkv1(P, es):
    nc = P.nc
    A, Dv, Tn = (P.sem(n) for n in ["A", "Dv", "Tn"])
    lds = [P.dsem(0), P.dsem(1)]
    std = {"pq": [P.dsem(10), P.dsem(11)], "pkf": [P.dsem(12), P.dsem(13)],
           "pkt": [P.dsem(14), P.dsem(15)], "v": [P.dsem(16), P.dsem(17)]}
    wq = es.enter_context(nc.sbuf_tensor(f"q1_wq{P.phase_id}", [128, KC * D], BF16))
    wk = es.enter_context(nc.sbuf_tensor(f"q1_wk{P.phase_id}", [128, KC * D], BF16))
    wv = es.enter_context(nc.sbuf_tensor(f"q1_wv{P.phase_id}", [128, KC * D], BF16))
    load_cast_transpose_weights(P, es, [(P.ap("L1_attn_Wq"), wq[:, :]),
                                        (P.ap("L1_attn_Wk"), wk[:, :]),
                                        (P.ap("L1_attn_Wv"), wv[:, :])])
    xfm = es.enter_context(nc.sbuf_tensor(f"q1_x{P.phase_id}", [128, 2 * D], BF16))
    pqb = es.enter_context(nc.sbuf_tensor(f"q1_pq{P.phase_id}", [128, 2 * D], BF16))
    pkb = es.enter_context(nc.sbuf_tensor(f"q1_pk{P.phase_id}", [128, 2 * D], BF16))
    vbf = es.enter_context(nc.sbuf_tensor(f"q1_vb{P.phase_id}", [128, 2 * D], BF16))
    w1 = es.enter_context(nc.sbuf_tensor(f"q1_t1{P.phase_id}", [128, D], F32))
    wm = es.enter_context(nc.sbuf_tensor(f"q1_tm{P.phase_id}", [128, 2 * D], F32))
    wr = es.enter_context(nc.sbuf_tensor(f"q1_tr{P.phase_id}", [128, 2 * D], F32))
    we = es.enter_context(nc.sbuf_tensor(f"q1_te{P.phase_id}", [128, 2 * D], F32))
    qfmb = es.enter_context(nc.sbuf_tensor(f"q1_qf{P.phase_id}", [128, 2 * D], BF16))
    kfmb = es.enter_context(nc.sbuf_tensor(f"q1_kf{P.phase_id}", [128, 2 * D], BF16))
    psP = es.enter_context(nc.psum_tensor(f"q1_ps{P.phase_id}", [128, 3 * D], F32))
    psT = es.enter_context(nc.psum_tensor(f"q1_pt{P.phase_id}", [128, 2 * D], BF16))

    def slot(i):
        return (i % 2) * D

    def b_sync(e):
        for i in range(NT):
            if i >= 2:
                P.wait(e, Tn, f"pjv{i - 2}")
            P.inc(e.dma_start(xfm[:, slot(i):slot(i) + D]
                              .rearrange("p (c j) -> p c j", c=KC),
                              P.ap("xn1_fm")[:, :, i * 128:(i + 1) * 128]
                              .rearrange("c p j -> p c j")),
                  lds[i % 2], f"lx{i}", 16)

    def b_tensor(e):
        for i in range(NT):
            P.wait(e, lds[i % 2], f"lx{i}")
            if i >= 1:
                P.wait(e, Dv, f"vb{i - 1}")
            for wi, (wsb, tag) in enumerate([(wq, "q"), (wk, "k"), (wv, "v")]):
                ins = None
                for n in range(2):
                    for kc in range(KC):
                        ins = e.matmul(psP[:, wi * D + n * 512:wi * D + n * 512 + 512],
                                       xfm[:, slot(i) + kc * 128:slot(i) + kc * 128 + 128],
                                       wsb[:, kc * D + n * 512:kc * D + n * 512 + 512],
                                       start=(kc == 0), stop=(kc == KC - 1))
                P.inc(ins, Tn, f"pj{tag}{i}")
            P.wait(e, Dv, f"fq{i}")
            if i >= 1:
                P.wait(e, Dv, f"eq{i - 1}")
            ins = None
            for c in range(KC):
                ins = e.transpose(psT[:, c * 128:(c + 1) * 128],
                                  pqb[:, slot(i) + c * 128:slot(i) + (c + 1) * 128],
                                  P.identb)
            P.inc(ins, Tn, f"tq{i}")
            P.wait(e, Dv, f"fk{i}")
            if i >= 1:
                P.wait(e, Dv, f"ek{i - 1}")
            ins = None
            for c in range(KC):
                ins = e.transpose(psT[:, D + c * 128:D + (c + 1) * 128],
                                  pkb[:, slot(i) + c * 128:slot(i) + (c + 1) * 128],
                                  P.identb)
            P.inc(ins, Tn, f"tk{i}")

    def b_scalar(e):
        for i in range(NT):
            P.wait(e, Dv, f"mq{i}")
            P.inc(e.activation(we[:, 0:D], wm[:, 0:D], AF.Exp), A, f"xq{i}")
            P.wait(e, Dv, f"mk{i}")
            P.inc(e.activation(we[:, D:2 * D], wm[:, D:2 * D], AF.Exp), A, f"xk{i}")

    def b_vector(e):
        for i in range(NT):
            P.wait(e, Tn, f"pjq{i}")
            e.tensor_copy(w1[:, :], psP[:, 0:D])
            e.tensor_relu(wr[:, 0:D], w1[:, :])
            P.inc(e.tensor_scalar(wm[:, 0:D], w1[:, :], 0.0, None, ALU.min),
                  Dv, f"mq{i}")
            P.wait(e, Tn, f"pjk{i}")
            e.tensor_copy(w1[:, :], psP[:, D:2 * D])
            e.tensor_relu(wr[:, D:2 * D], w1[:, :])
            P.inc(e.tensor_scalar(wm[:, D:2 * D], w1[:, :], 0.0, None, ALU.min),
                  Dv, f"mk{i}")
            P.wait(e, Tn, f"pjv{i}")
            if i >= 2:
                P.wait(e, std["v"][i % 2], f"sv{i - 2}")
            P.inc(e.tensor_copy(vbf[:, slot(i):slot(i) + D], psP[:, 2 * D:3 * D]),
                  Dv, f"vb{i}")
            P.wait(e, A, f"xq{i}")
            if i >= 2:
                P.wait(e, Tn, f"tq{i - 2}")
            P.inc(e.tensor_add(pqb[:, slot(i):slot(i) + D], we[:, 0:D],
                               wr[:, 0:D]), Dv, f"fq{i}")
            P.wait(e, A, f"xk{i}")
            if i >= 2:
                P.wait(e, Tn, f"tk{i - 2}")
                P.wait(e, std["pkt"][i % 2], f"skt{i - 2}")
            P.inc(e.tensor_add(pkb[:, slot(i):slot(i) + D], we[:, D:2 * D],
                               wr[:, D:2 * D]), Dv, f"fk{i}")
            P.wait(e, Tn, f"tq{i}")
            if i >= 2:
                P.wait(e, std["pq"][i % 2], f"sq{i - 2}")
            P.inc(e.tensor_copy(qfmb[:, slot(i):slot(i) + D], psT[:, 0:D]),
                  Dv, f"eq{i}")
            P.wait(e, Tn, f"tk{i}")
            if i >= 2:
                P.wait(e, std["pkf"][i % 2], f"skf{i - 2}")
            P.inc(e.tensor_copy(kfmb[:, slot(i):slot(i) + D], psT[:, D:2 * D]),
                  Dv, f"ek{i}")

    def b_gpsimd(e):
        for i in range(NT):
            P.wait(e, Dv, f"eq{i}")
            P.inc(e.dma_start(P.ap("phiq_fm")[:, :, i * 128:(i + 1) * 128]
                              .rearrange("c p j -> p c j"),
                              qfmb[:, slot(i):slot(i) + D]
                              .rearrange("p (c j) -> p c j", c=KC)),
                  std["pq"][i % 2], f"sq{i}", 16)
            P.wait(e, Dv, f"ek{i}")
            P.inc(e.dma_start(P.ap("phik_fm")[:, :, i * 128:(i + 1) * 128]
                              .rearrange("c p j -> p c j"),
                              kfmb[:, slot(i):slot(i) + D]
                              .rearrange("p (c j) -> p c j", c=KC)),
                  std["pkf"][i % 2], f"skf{i}", 16)
            P.wait(e, Dv, f"fk{i}")
            P.inc(e.dma_start(P.ap("phik_tm")[i * 128:(i + 1) * 128, :],
                              pkb[:, slot(i):slot(i) + D]),
                  std["pkt"][i % 2], f"skt{i}", 16)
            P.wait(e, Dv, f"vb{i}")
            P.inc(e.dma_start(P.ap("v1_tm")[i * 128:(i + 1) * 128, :],
                              vbf[:, slot(i):slot(i) + D]),
                  std["v"][i % 2], f"sv{i}", 16)

    P.twopass([("sync", b_sync), ("scalar", b_scalar), ("vector", b_vector),
               ("tensor", b_tensor), ("gpsimd", b_gpsimd)])



def ph_scan1(P, es):
    nc = P.nc
    A, Dv, Tn, xD = (P.sem(n) for n in ["A", "Dv", "Tn", "xD"])
    lds = [[P.dsem(2 * k), P.dsem(2 * k + 1)] for k in range(4)]
    strd = [P.dsem(10), P.dsem(11)]
    ld22 = P.dsem(22)
    NH = 16
    DH = 64
    W = DH + 1
    msk = es.enter_context(nc.sbuf_tensor(f"s1_mk{P.phase_id}", [128, 4 * CH], F32))
    Sg = es.enter_context(nc.sbuf_tensor(f"s1_S{P.phase_id}", [128, B_CORE * NH * W], F32))
    Sb = es.enter_context(nc.sbuf_tensor(f"s1_Sb{P.phase_id}", [128, B_CORE * NH * W], BF16))
    pqf = es.enter_context(nc.sbuf_tensor(f"s1_pq{P.phase_id}", [128, 2 * KC * CH], BF16))
    pkf = es.enter_context(nc.sbuf_tensor(f"s1_pk{P.phase_id}", [128, 2 * KC * CH], BF16))
    pkt = es.enter_context(nc.sbuf_tensor(f"s1_kt{P.phase_id}", [128, 2 * 4 * D], BF16))
    vt = es.enter_context(nc.sbuf_tensor(f"s1_v{P.phase_id}", [128, 2 * 4 * D], BF16))
    vaug = es.enter_context(nc.sbuf_tensor(f"s1_va{P.phase_id}", [128, 4 * NH * W], BF16))
    absf = es.enter_context(nc.sbuf_tensor(f"s1_ab{P.phase_id}", [128, 4 * CH], BF16))
    ytm = es.enter_context(nc.sbuf_tensor(f"s1_y{P.phase_id}", [128, 4 * D], BF16))
    yfm = es.enter_context(nc.sbuf_tensor(f"s1_yf{P.phase_id}", [128, 2 * D], BF16))
    nrw = es.enter_context(nc.sbuf_tensor(f"s1_nw{P.phase_id}", [128, 8], F32))
    psA = es.enter_context(nc.psum_tensor(f"s1_pa{P.phase_id}", [128, 4 * 512], F32))
    psR = es.enter_context(nc.psum_tensor(f"s1_pr{P.phase_id}", [128, 2 * 512], F32))
    psD = es.enter_context(nc.psum_tensor(f"s1_pd{P.phase_id}", [128, W], F32))
    psT = es.enter_context(nc.psum_tensor(f"s1_pt{P.phase_id}", [128, D], BF16))
    NBC = B_CORE * NCH

    with nc.Block() as blk:
        @blk.sync
        def _(e):
            e.dma_start(msk[:, :].rearrange("p (j t) -> p j t", j=4),
                        P.ap("c_mask1").rearrange("(j p) t -> p j t", p=128))\
                .then_inc(ld22.h, 16)

        @blk.vector
        def _(e):
            e.wait_ge(ld22.h, ld22.v + 16)
            e.memset(Sg[:, :], 0.0)
            e.memset(Sb[:, :], 0.0)
            e.memset(vaug[:, :], 1.0)
    ld22.v += 16

    def hq(h):
        return h // 2, (h % 2) * DH

    def b_sync(e):
        for idx in range(NBC):
            if idx >= 2:
                P.wait(e, Tn, f"ds{idx - 2}.15")
            for k, (name, fm) in enumerate([("phiq_fm", True), ("phik_fm", True),
                                            ("phik_tm", False), ("v1_tm", False)]):
                b, c = idx // NCH, idx % NCH
                off = b * S + c * CH
                sl = idx % 2
                if fm:
                    dst = [pqf, pkf][k][:, sl * KC * CH:(sl + 1) * KC * CH]
                    P.inc(e.dma_start(dst.rearrange("p (c j) -> p c j", c=KC),
                                      P.ap(name)[:, :, off:off + CH]
                                      .rearrange("c p j -> p c j")),
                          lds[k][idx % 2], f"l{k}.{idx}", 16)
                else:
                    dst = [None, None, pkt, vt][k][:, sl * 4 * D:(sl + 1) * 4 * D]
                    P.inc(e.dma_start(dst.rearrange("p (s d) -> p s d", s=4),
                                      P.ap(name)[off:off + CH, :]
                                      .rearrange("(s p) d -> p s d", p=128)),
                          lds[k][idx % 2], f"l{k}.{idx}", 16)

    def b_vector(e):
        for idx in range(NBC):
            sl = idx % 2
            P.wait(e, lds[3][idx % 2], f"l3.{idx}")
            if idx >= 1:
                P.wait(e, Tn, f"ds{idx - 1}.15")
            ins = None
            for su in range(4):
                for h in range(NH):
                    ins = e.tensor_copy(vaug[:, su * NH * W + h * W:
                                            su * NH * W + h * W + DH],
                                        vt[:, sl * 4 * D + su * D + h * DH:
                                           sl * 4 * D + su * D + (h + 1) * DH])
            P.inc(ins, Dv, f"va{idx}")
            for h in range(NH):
                for js in range(4):
                    P.wait(e, Tn, f"aw{idx}.{h}.{js}")
                    P.inc(e.tensor_mul(absf[:, js * CH:(js + 1) * CH],
                                       psA[:, js * 512:(js + 1) * 512],
                                       msk[:, js * CH:(js + 1) * CH]),
                          Dv, f"ab{idx}.{h}.{js}")
                for jt in range(4):
                    P.wait(e, Tn, f"rd{idx}.{h}.{jt}")
                    ro = ((h * 4 + jt) % 2) * 512
                    P.selfsync(e, e.tensor_scalar(nrw[:, 0:1],
                                                  psR[:, ro + DH:ro + W],
                                                  1e-5, None, ALU.add), xD)
                    P.selfsync(e, e.reciprocal(nrw[:, 1:2], nrw[:, 0:1]), xD)
                    P.inc(e.tensor_scalar(ytm[:, jt * D + h * DH:jt * D + (h + 1) * DH],
                                          psR[:, ro:ro + DH], nrw[:, 1:2],
                                          None, ALU.mult), Dv, f"yn{idx}.{h}.{jt}")
                P.wait(e, Tn, f"ds{idx}.{h}")
                bo = (idx // NCH) * NH * W
                hb = (h % 2) * DH
                P.selfsync(e, e.tensor_add(Sg[hb:hb + DH, bo + h * W:bo + (h + 1) * W],
                                           Sg[hb:hb + DH, bo + h * W:bo + (h + 1) * W],
                                           psD[hb:hb + DH, :]), xD)
                P.selfsync(e, e.tensor_copy(Sb[hb:hb + DH, bo + h * W:bo + (h + 1) * W],
                                            Sg[hb:hb + DH, bo + h * W:bo + (h + 1) * W]),
                           xD)
                P.inc(e.tensor_copy(P.jD[:, 2:3], P.jD[:, 3:4]), Dv, f"su{idx}.{h}")
            P.selfsync(e, e.tensor_copy(P.jD[:, 2:3], P.jD[:, 3:4]), xD)
            P.inc(e.tensor_copy(P.jD[:, 1:2], P.jD[:, 0:1]), Dv, f"ynf{idx}")
            for jt in range(4):
                P.wait(e, Tn, f"yt{idx}.{jt}")
                gl = idx * 4 + jt
                if gl >= 2:
                    P.wait(e, strd[gl % 2], f"sy{(gl - 2) // 4}.{(gl - 2) % 4}")
                P.inc(e.tensor_copy(yfm[:, (gl % 2) * D:(gl % 2) * D + D],
                                    psT[:, :]), Dv, f"ye{idx}.{jt}")

    def b_tensor(e):
        for idx in range(NBC):
            b, c = idx // NCH, idx % NCH
            sl = idx % 2
            qo = sl * KC * CH
            ko = sl * 4 * D
            for k in range(4):
                P.wait(e, lds[k][idx % 2], f"l{k}.{idx}")
            P.wait(e, Dv, f"va{idx}")
            for h in range(NH):
                hc, hr = hq(h)
                for js in range(4):
                    if h > 0:
                        P.wait(e, Dv, f"ab{idx}.{h - 1}.{js}")
                    elif idx >= 1:
                        P.wait(e, Dv, f"ab{idx - 1}.{NH - 1}.{js}")
                    P.inc(e.matmul(psA[:, js * 512:(js + 1) * 512],
                                   pkf[hr:hr + DH, qo + hc * CH + js * 128:
                                       qo + hc * CH + js * 128 + 128],
                                   pqf[hr:hr + DH, qo + hc * CH:qo + hc * CH + CH],
                                   start=True, stop=True), Tn, f"aw{idx}.{h}.{js}")
                P.wait(e, Dv, f"ab{idx}.{h}.3")
                if c != 0:
                    P.wait(e, Dv, f"su{idx - 1}.{h}")
                for jt in range(4):
                    ro = ((h * 4 + jt) % 2) * 512
                    gl = idx * NH * 4 + h * 4 + jt
                    if gl >= 2:
                        pg = gl - 2
                        pidx, ph, pjt = pg // (NH * 4), \
                            (pg % (NH * 4)) // 4, pg % 4
                        P.wait(e, Dv, f"yn{pidx}.{ph}.{pjt}")
                    ins = e.matmul(psR[:, ro:ro + W],
                                   pqf[hr:hr + DH, qo + hc * CH + jt * 128:
                                       qo + hc * CH + jt * 128 + 128],
                                   Sb[hr:hr + DH, b * NH * W + h * W:
                                      b * NH * W + (h + 1) * W],
                                   start=True, stop=False)
                    for js in range(4):
                        ins = e.matmul(psR[:, ro:ro + W],
                                       absf[:, js * CH + jt * 128:
                                            js * CH + jt * 128 + 128],
                                       vaug[:, js * NH * W + h * W:
                                            js * NH * W + (h + 1) * W],
                                       start=False, stop=(js == 3))
                    P.inc(ins, Tn, f"rd{idx}.{h}.{jt}")
                if idx >= 1 or h >= 1:
                    P.wait(e, Dv, f"su{idx}.{h - 1}" if h >= 1 else
                           f"su{idx - 1}.{NH - 1}")
                ins = None
                hb2 = (h % 2) * DH
                for js in range(4):
                    ins = e.matmul(psD[hb2:hb2 + DH, :],
                                   pkt[:, ko + js * D + h * DH:
                                       ko + js * D + (h + 1) * DH],
                                   vaug[:, js * NH * W + h * W:
                                        js * NH * W + (h + 1) * W],
                                   start=(js == 0), stop=(js == 3),
                                   tile_position=(0, hb2))
                P.inc(ins, Tn, f"ds{idx}.{h}")
            for jt in range(4):
                P.wait(e, Dv, f"ynf{idx}")
                gl = idx * 4 + jt
                if gl >= 1:
                    P.wait(e, Dv, f"ye{(gl - 1) // 4}.{(gl - 1) % 4}")
                ins = None
                for cc in range(KC):
                    ins = e.transpose(psT[:, cc * 128:(cc + 1) * 128],
                                      ytm[:, jt * D + cc * 128:
                                          jt * D + (cc + 1) * 128],
                                      P.identb)
                P.inc(ins, Tn, f"yt{idx}.{jt}")

    def b_gpsimd(e):
        for idx in range(NBC):
            b, c = idx // NCH, idx % NCH
            off = b * S + c * CH
            for jt in range(4):
                P.wait(e, Dv, f"ye{idx}.{jt}")
                gl = idx * 4 + jt
                P.inc(e.dma_start(P.ap("y1_fm")[:, :, off + jt * 128:
                                                off + (jt + 1) * 128]
                                  .rearrange("c p j -> p c j"),
                                  yfm[:, (gl % 2) * D:(gl % 2) * D + D]
                                  .rearrange("p (c j) -> p c j", c=KC)),
                      strd[gl % 2], f"sy{idx}.{jt}", 16)

    P.twopass([("sync", b_sync), ("vector", b_vector), ("tensor", b_tensor),
               ("gpsimd", b_gpsimd)])



def ph_qkvmsa(P, es):
    nc = P.nc
    A, Dv, Tn = (P.sem(n) for n in ["A", "Dv", "Tn"])
    lds = [P.dsem(0), P.dsem(1)]
    stq = [P.dsem(10), P.dsem(11)]
    stv = [P.dsem(12), P.dsem(13)]
    D3 = 3 * D
    wq = es.enter_context(nc.sbuf_tensor(f"qm_w{P.phase_id}", [128, KC * D3], BF16))
    load_cast_weights(P, es, [(P.ap("L1_ffn_w_qkv"), wq[:, :])])
    TC = 512
    NTC = T // TC
    xfm = es.enter_context(nc.sbuf_tensor(f"qm_x{P.phase_id}", [128, 2 * KC * TC], BF16))
    qkb = es.enter_context(nc.sbuf_tensor(f"qm_qk{P.phase_id}", [128, 2 * 16 * TC], BF16))
    vbf = es.enter_context(nc.sbuf_tensor(f"qm_v{P.phase_id}", [128, 2 * 4 * D], BF16))
    psQ = es.enter_context(nc.psum_tensor(f"qm_pq{P.phase_id}", [128, 2 * TC], F32))
    psV = es.enter_context(nc.psum_tensor(f"qm_pv{P.phase_id}", [128, 2 * D], F32))

    def b_sync(e):
        for i in range(NTC):
            if i >= 2:
                P.wait(e, Tn, f"v{i - 2}.3")
            P.inc(e.dma_start(xfm[:, (i % 2) * KC * TC:(i % 2 + 1) * KC * TC]
                              .rearrange("p (c j) -> p c j", c=KC),
                              P.ap("xln_fm")[:, :, i * TC:(i + 1) * TC]
                              .rearrange("c p j -> p c j")),
                  lds[i % 2], f"lx{i}", 16)

    def b_tensor(e):
        for i in range(NTC):
            P.wait(e, lds[i % 2], f"lx{i}")
            xo = (i % 2) * KC * TC
            for f in range(16):
                if f >= 2:
                    P.wait(e, Dv, f"qe{i}.{f - 2}")
                elif i >= 1:
                    P.wait(e, Dv, f"qe{i - 1}.{f + 14}")
                ins = None
                for kc in range(KC):
                    ins = e.matmul(psQ[:, (f % 2) * TC:(f % 2) * TC + TC],
                                   wq[:, kc * D3 + f * 128:kc * D3 + (f + 1) * 128],
                                   xfm[:, xo + kc * TC:xo + kc * TC + TC],
                                   start=(kc == 0), stop=(kc == KC - 1))
                P.inc(ins, Tn, f"qk{i}.{f}")
            for su in range(4):
                if su >= 2:
                    P.wait(e, Dv, f"ve{i}.{su - 2}")
                elif i >= 1:
                    P.wait(e, Dv, f"ve{i - 1}.{su + 2}")
                ins = None
                for n in range(2):
                    for kc in range(KC):
                        ins = e.matmul(psV[:, (su % 2) * D + n * 512:
                                           (su % 2) * D + n * 512 + 512],
                                       xfm[:, xo + kc * TC + su * 128:
                                           xo + kc * TC + su * 128 + 128],
                                       wq[:, kc * D3 + 2048 + n * 512:
                                          kc * D3 + 2048 + n * 512 + 512],
                                       start=(kc == 0), stop=(kc == KC - 1))
                P.inc(ins, Tn, f"v{i}.{su}")

    def b_vector(e):
        for i in range(NTC):
            for f in range(16):
                P.wait(e, Tn, f"qk{i}.{f}")
                if i >= 2 and f == 0:
                    P.wait(e, stq[i % 2], f"sq{i - 2}")
                P.inc(e.tensor_copy(qkb[:, (i % 2) * 16 * TC + f * TC:
                                        (i % 2) * 16 * TC + (f + 1) * TC],
                                    psQ[:, (f % 2) * TC:(f % 2) * TC + TC]),
                      Dv, f"qe{i}.{f}")
            for su in range(4):
                P.wait(e, Tn, f"v{i}.{su}")
                gl = i * 4 + su
                if gl >= 8:
                    P.wait(e, stv[(gl // 4) % 2], f"sv{(gl - 8) // 4}.{(gl - 8) % 4}")
                P.inc(e.tensor_copy(vbf[:, (i % 2) * 4 * D + su * D:
                                        (i % 2) * 4 * D + (su + 1) * D],
                                    psV[:, (su % 2) * D:(su % 2) * D + D]),
                      Dv, f"ve{i}.{su}")

    def b_gpsimd(e):
        for i in range(NTC):
            P.wait(e, Dv, f"qe{i}.15")
            P.inc(e.dma_start(P.ap("qk_fm")[:, :, i * TC:(i + 1) * TC]
                              .rearrange("c p j -> p c j"),
                              qkb[:, (i % 2) * 16 * TC:(i % 2 + 1) * 16 * TC]
                              .rearrange("p (c j) -> p c j", c=16)),
                  stq[i % 2], f"sq{i}", 16)
            for su in range(4):
                P.wait(e, Dv, f"ve{i}.{su}")
                gl = i * 4 + su
                P.inc(e.dma_start(P.ap("vm_tm")[gl * 128:(gl + 1) * 128, :],
                                  vbf[:, (i % 2) * 4 * D + su * D:
                                      (i % 2) * 4 * D + (su + 1) * D]),
                      stv[(gl // 4) % 2], f"sv{i}.{su}", 16)

    P.twopass([("sync", b_sync), ("tensor", b_tensor), ("vector", b_vector),
               ("gpsimd", b_gpsimd)])



def ph_msa(P, es):
    nc = P.nc
    A, Dv, Tn, xA, xD = (P.sem(n) for n in ["A", "Dv", "Tn", "xA", "xD"])
    ldk = [P.dsem(0), P.dsem(1)]
    ldv = [P.dsem(2), P.dsem(3)]
    ldq = [P.dsem(4), P.dsem(5)]
    sty = [P.dsem(10), P.dsem(11)]
    NH, DH = 16, 64
    SK = S // 512
    NQ = S // 128
    ksb = es.enter_context(nc.sbuf_tensor(f"ms_k{P.phase_id}", [128, KC * S], BF16))
    vsb = es.enter_context(nc.sbuf_tensor(f"ms_v{P.phase_id}", [128, (S // 128) * D], BF16))
    qsb = es.enter_context(nc.sbuf_tensor(f"ms_q{P.phase_id}", [128, 2 * KC * 128], BF16))
    Pb = es.enter_context(nc.sbuf_tensor(f"ms_P{P.phase_id}", [128, S], BF16))
    PTb = es.enter_context(nc.sbuf_tensor(f"ms_PT{P.phase_id}", [128, S], BF16))
    ybuf = es.enter_context(nc.sbuf_tensor(f"ms_y{P.phase_id}", [128, D], BF16))
    yfm = es.enter_context(nc.sbuf_tensor(f"ms_yf{P.phase_id}", [128, 2 * D], BF16))
    nrw = es.enter_context(nc.sbuf_tensor(f"ms_nw{P.phase_id}", [128, 8], F32))
    psS = es.enter_context(nc.psum_tensor(f"ms_ps{P.phase_id}", [128, S], F32))
    psT = es.enter_context(nc.psum_tensor(f"ms_pt{P.phase_id}", [128, S], BF16))
    psV = es.enter_context(nc.psum_tensor(f"ms_pv{P.phase_id}", [128, DH], F32))
    psY = es.enter_context(nc.psum_tensor(f"ms_py{P.phase_id}", [128, D], BF16))

    def b_sync(e):
        for b in range(B_CORE):
            if b >= 1:
                P.wait(e, Tn, f"pv{b - 1}.{NQ - 1}.{NH - 1}")
            P.inc(e.dma_start(ksb[:, :].rearrange("p (c j) -> p c j", c=KC),
                              P.ap("qk_fm")[KC:2 * KC, :, b * S:(b + 1) * S]
                              .rearrange("c p j -> p c j")),
                  ldk[b % 2], f"lk{b}", 16)
            P.inc(e.dma_start(vsb[:, :].rearrange("p (s d) -> p s d", s=S // 128),
                              P.ap("vm_tm")[b * S:(b + 1) * S, :]
                              .rearrange("(s p) d -> p s d", p=128)),
                  ldv[b % 2], f"lv{b}", 16)
            for qt in range(NQ):
                g = b * NQ + qt
                if g >= 2:
                    P.wait(e, Tn, f"pv{(g - 2) // NQ}.{(g - 2) % NQ}.{NH - 1}")
                P.inc(e.dma_start(qsb[:, (g % 2) * KC * 128:(g % 2 + 1) * KC * 128]
                                  .rearrange("p (c j) -> p c j", c=KC),
                                  P.ap("qk_fm")[0:KC, :, b * S + qt * 128:
                                                b * S + (qt + 1) * 128]
                                  .rearrange("c p j -> p c j")),
                      ldq[g % 2], f"lq{b}.{qt}", 16)

    def b_tensor(e):
        for b in range(B_CORE):
            P.wait(e, ldk[b % 2], f"lk{b}")
            P.wait(e, ldv[b % 2], f"lv{b}")
            for qt in range(NQ):
                g = b * NQ + qt
                P.wait(e, ldq[g % 2], f"lq{b}.{qt}")
                qo = (g % 2) * KC * 128
                for h in range(NH):
                    hc, hr = h // 2, (h % 2) * DH
                    if h >= 1:
                        P.wait(e, A, f"ex{b}.{qt}.{h - 1}")
                    elif g >= 1:
                        pb, pq = (g - 1) // NQ, (g - 1) % NQ
                        P.wait(e, A, f"ex{pb}.{pq}.{NH - 1}")
                    ins = None
                    for kb in range(SK):
                        ins = e.matmul(psS[:, kb * 512:(kb + 1) * 512],
                                       qsb[hr:hr + DH, qo + hc * 128:
                                           qo + (hc + 1) * 128],
                                       ksb[hr:hr + DH, hc * S + kb * 512:
                                           hc * S + kb * 512 + 512],
                                       start=True, stop=True)
                    P.inc(ins, Tn, f"sc{b}.{qt}.{h}")
                    P.wait(e, A, f"ex{b}.{qt}.{h}")
                    if h >= 1:
                        P.wait(e, Dv, f"pte{b}.{qt}.{h - 1}")
                    elif g >= 1:
                        pb, pq = (g - 1) // NQ, (g - 1) % NQ
                        P.wait(e, Dv, f"pte{pb}.{pq}.{NH - 1}")
                    ins = None
                    for kb in range(S // 128):
                        ins = e.transpose(psT[:, kb * 128:(kb + 1) * 128],
                                          Pb[:, kb * 128:(kb + 1) * 128],
                                          P.identb)
                    P.inc(ins, Tn, f"pt{b}.{qt}.{h}")
                    P.wait(e, Dv, f"pte{b}.{qt}.{h}")
                    if h >= 1:
                        P.wait(e, Dv, f"yn{b}.{qt}.{h - 1}")
                    elif g >= 1:
                        P.wait(e, Dv, f"yn{(g - 1) // NQ}.{(g - 1) % NQ}.{NH - 1}")
                    ins = None
                    for kb in range(S // 128):
                        ins = e.matmul(psV[:, :],
                                       PTb[:, kb * 128:(kb + 1) * 128],
                                       vsb[:, kb * D + h * DH:kb * D + (h + 1) * DH],
                                       start=(kb == 0), stop=(kb == S // 128 - 1))
                    P.inc(ins, Tn, f"pv{b}.{qt}.{h}")
                P.wait(e, Dv, f"ynf{b}.{qt}")
                if g >= 1:
                    P.wait(e, Dv, f"ye{(g - 1) // NQ}.{(g - 1) % NQ}")
                ins = None
                for cc in range(KC):
                    ins = e.transpose(psY[:, cc * 128:(cc + 1) * 128],
                                      ybuf[:, cc * 128:(cc + 1) * 128], P.identb)
                P.inc(ins, Tn, f"yt{b}.{qt}")

    def b_scalar(e):
        for b in range(B_CORE):
            for qt in range(NQ):
                for h in range(NH):
                    g = b * NQ + qt
                    P.wait(e, Tn, f"sc{b}.{qt}.{h}")
                    P.wait(e, Dv, f"nb{b}.{qt}.{h}")
                    if h >= 1:
                        P.wait(e, Tn, f"pt{b}.{qt}.{h - 1}")
                    elif g >= 1:
                        P.wait(e, Tn, f"pt{(g - 1) // NQ}.{(g - 1) % NQ}.{NH - 1}")
                    P.selfsync(e, e.activation(Pb[:, :], psS[:, :], AF.Exp,
                                               scale=0.125, bias=nrw[:, 2:3],
                                               accum_out=nrw[:, 3:4]), xA)
                    P.incflush(e, e.activation(nrw[:, 4:5], nrw[:, 3:4], AF.Copy),
                               "A", A, f"ex{b}.{qt}.{h}")

    def b_vector(e):
        for b in range(B_CORE):
            for qt in range(NQ):
                g = b * NQ + qt
                for h in range(NH):
                    P.wait(e, Tn, f"sc{b}.{qt}.{h}")
                    P.selfsync(e, e.reduce_max(nrw[:, 0:1], psS[:, :], axis=AX.X),
                               xD)
                    P.selfsync(e, e.tensor_scalar(nrw[:, 2:3], nrw[:, 0:1], -0.125,
                                                  None, ALU.mult), xD)
                    P.inc(e.tensor_copy(P.jD[:, 2:3], P.jD[:, 3:4]),
                          Dv, f"nb{b}.{qt}.{h}")
                    P.wait(e, Tn, f"pt{b}.{qt}.{h}")
                    P.inc(e.tensor_copy(PTb[:, :], psT[:, :]),
                          Dv, f"pte{b}.{qt}.{h}")
                    P.wait(e, A, f"ex{b}.{qt}.{h}")
                    P.selfsync(e, e.reciprocal(nrw[:, 5:6], nrw[:, 4:5]), xD)
                    P.wait(e, Tn, f"pv{b}.{qt}.{h}")
                    P.inc(e.tensor_scalar(ybuf[:, h * DH:(h + 1) * DH], psV[:, :],
                                          nrw[:, 5:6], None, ALU.mult),
                          Dv, f"yn{b}.{qt}.{h}")
                P.selfsync(e, e.tensor_copy(P.jD[:, 2:3], P.jD[:, 3:4]), xD)
                P.inc(e.tensor_copy(P.jD[:, 1:2], P.jD[:, 0:1]), Dv, f"ynf{b}.{qt}")
                P.wait(e, Tn, f"yt{b}.{qt}")
                if g >= 2:
                    P.wait(e, sty[g % 2], f"sy{(g - 2) // NQ}.{(g - 2) % NQ}")
                P.inc(e.tensor_copy(yfm[:, (g % 2) * D:(g % 2) * D + D], psY[:, :]),
                      Dv, f"ye{b}.{qt}")

    def b_gpsimd(e):
        for b in range(B_CORE):
            for qt in range(NQ):
                g = b * NQ + qt
                P.wait(e, Dv, f"ye{b}.{qt}")
                P.inc(e.dma_start(P.ap("ym_fm")[:, :, g * 128:(g + 1) * 128]
                                  .rearrange("c p j -> p c j"),
                                  yfm[:, (g % 2) * D:(g % 2) * D + D]
                                  .rearrange("p (c j) -> p c j", c=KC)),
                      sty[g % 2], f"sy{b}.{qt}", 16)

    P.twopass([("sync", b_sync), ("tensor", b_tensor), ("scalar", b_scalar),
               ("vector", b_vector), ("gpsimd", b_gpsimd)])



def ph_moe8(P, es):
    nc = P.nc
    A, Dv, Tn = (P.sem(n) for n in ["A", "Dv", "Tn"])
    lds = [P.dsem(0), P.dsem(1)]
    sto = [P.dsem(10 + k) for k in range(8)]
    ld22 = P.dsem(22)
    NE = 8
    HID = 4096
    NHID = HID // 128
    w1 = es.enter_context(nc.sbuf_tensor(f"m8_w1{P.phase_id}", [128, KC * HID], BF16))
    w2 = es.enter_context(nc.sbuf_tensor(f"m8_w2{P.phase_id}", [128, NHID * D], BF16))
    msk = es.enter_context(nc.sbuf_tensor(f"m8_mk{P.phase_id}", [128, NT * NE], F32))
    TC = 512
    NTC = T // TC
    xfm = es.enter_context(nc.sbuf_tensor(f"m8_x{P.phase_id}", [128, 2 * KC * TC], BF16))
    hfm = es.enter_context(nc.sbuf_tensor(f"m8_h{P.phase_id}", [128, NHID * TC], BF16))
    ob = es.enter_context(nc.sbuf_tensor(f"m8_o{P.phase_id}", [128, 4 * D], F32))
    psH = es.enter_context(nc.psum_tensor(f"m8_ph{P.phase_id}", [128, 2 * TC], F32))
    psY = es.enter_context(nc.psum_tensor(f"m8_py{P.phase_id}", [128, 2 * D], F32))

    with nc.Block() as blk:
        @blk.sync
        def _(e):
            e.dma_start(msk[:, :].rearrange("p (i e) -> p i e", i=NT),
                        P.ap("mask8_tm").rearrange("(i p) e -> p i e", p=128))\
                .then_inc(ld22.h, 16)
            e.wait_ge(ld22.h, ld22.v + 16)
    ld22.v += 16

    for ei in range(NE):
        with contextlib.ExitStack() as wes:
            load_cast_weights(P, wes, [(P.ap("L1_ffn_w1")[ei], w1[:, :]),
                                       (P.ap("L1_ffn_w2")[ei], w2[:, :])],
                              tag=f"m{ei}")

        def b_sync(e, ei=ei):
            for i in range(NTC):
                if i >= 2:
                    P.wait(e, Tn, f"e{ei}h{i - 2}.{NHID - 1}")
                elif ei >= 1:
                    P.wait(e, Tn, f"e{ei - 1}h{NTC - 2 + i}.{NHID - 1}")
                P.inc(e.dma_start(xfm[:, (i % 2) * KC * TC:(i % 2 + 1) * KC * TC]
                                  .rearrange("p (c j) -> p c j", c=KC),
                                  P.ap("xg_fm")[:, :, i * TC:(i + 1) * TC]
                                  .rearrange("c p j -> p c j")),
                      lds[i % 2], f"e{ei}lx{i}", 16)

        def b_tensor(e, ei=ei):
            for i in range(NTC):
                P.wait(e, lds[i % 2], f"e{ei}lx{i}")
                xo = (i % 2) * KC * TC
                for j in range(NHID):
                    if j >= 2:
                        P.wait(e, A, f"e{ei}g{i}.{j - 2}")
                    elif i >= 1:
                        P.wait(e, A, f"e{ei}g{i - 1}.{NHID - 2 + j}")
                    elif ei >= 1:
                        P.wait(e, A, f"e{ei - 1}g{NTC - 1}.{NHID - 2 + j}")
                    ins = None
                    for kc in range(KC):
                        ins = e.matmul(psH[:, (j % 2) * TC:(j % 2) * TC + TC],
                                       w1[:, kc * HID + j * 128:
                                          kc * HID + (j + 1) * 128],
                                       xfm[:, xo + kc * TC:xo + kc * TC + TC],
                                       start=(kc == 0), stop=(kc == KC - 1))
                    P.inc(ins, Tn, f"e{ei}h{i}.{j}")
                for ts in range(4):
                    P.wait(e, A, f"e{ei}g{i}.{NHID - 1}")
                    if ts >= 2:
                        P.wait(e, A, f"e{ei}ye{i}.{ts - 2}")
                    elif i >= 1:
                        P.wait(e, A, f"e{ei}ye{i - 1}.{ts + 2}")
                    elif ei >= 1:
                        P.wait(e, A, f"e{ei - 1}ye{NTC - 1}.{ts + 2}")
                    ins = None
                    for n in range(2):
                        for kc in range(NHID):
                            ins = e.matmul(psY[:, (ts % 2) * D + n * 512:
                                               (ts % 2) * D + n * 512 + 512],
                                           hfm[:, kc * TC + ts * 128:
                                               kc * TC + ts * 128 + 128],
                                           w2[:, kc * D + n * 512:
                                              kc * D + n * 512 + 512],
                                           start=(kc == 0), stop=(kc == NHID - 1))
                    P.inc(ins, Tn, f"e{ei}y{i}.{ts}")

        def b_scalar(e, ei=ei):
            for i in range(NTC):
                for j in range(NHID):
                    P.wait(e, Tn, f"e{ei}h{i}.{j}")
                    P.inc(e.activation(hfm[:, j * TC:(j + 1) * TC],
                                       psH[:, (j % 2) * TC:(j % 2) * TC + TC],
                                       AF.Gelu_apprx_tanh), A, f"e{ei}g{i}.{j}")
                for ts in range(4):
                    P.wait(e, Tn, f"e{ei}y{i}.{ts}")
                    gl = i * 4 + ts
                    if i >= 1:
                        P.wait(e, sto[((i - 1) * 4 + ts) % 8],
                               f"e{ei}so{i - 1}.{ts}")
                    elif ei >= 1:
                        P.wait(e, sto[((NTC - 1) * 4 + ts) % 8],
                               f"e{ei - 1}so{NTC - 1}.{ts}")
                    P.inc(e.activation(ob[:, ts * D:(ts + 1) * D],
                                       psY[:, (ts % 2) * D:(ts % 2) * D + D],
                                       AF.Copy,
                                       scale=msk[:, (i * 4 + ts) * NE + ei:
                                                 (i * 4 + ts) * NE + ei + 1]),
                          A, f"e{ei}ye{i}.{ts}")

        def b_gpsimd(e, ei=ei):
            for i in range(NTC):
                for ts in range(4):
                    P.wait(e, A, f"e{ei}ye{i}.{ts}")
                    gl = i * 4 + ts
                    if ei >= 1:
                        P.wait(e, sto[gl % 8], f"e{ei - 1}so{i}.{ts}")
                    elif gl >= 8:
                        P.wait(e, sto[gl % 8], f"e{ei}so{i - 2}.{ts}")
                    if ei == 0:
                        ins = e.dma_start(
                            P.ap("moe_tm")[gl * 128:(gl + 1) * 128, :],
                            ob[:, ts * D:(ts + 1) * D])
                    else:
                        ins = e.dma_start(
                            P.ap("moe_tm")[gl * 128:(gl + 1) * 128, :],
                            ob[:, ts * D:(ts + 1) * D], accum_op=ALU.add)
                    P.inc(ins, sto[gl % 8], f"e{ei}so{i}.{ts}", 16)

        P.twopass([("sync", b_sync), ("tensor", b_tensor), ("scalar", b_scalar),
                   ("gpsimd", b_gpsimd)])


def _ph_qkv0(P, es):
    ph_qkv0(P, es)


def _ph_scan0(P, es):
    ph_scan0(P, es)


def _ph_wo0(P, es):
    ph_proj_tm(P, es, "read_fm", "L0_attn_Wo", "attn0_tm", bias="L0_attn_bo")


PHASE_DEFS = [
    ("ln0", _ph_ln0,
     ["x0", "L0_ffn_w_router", "c_ident_b"],
     ["xn_tm", "xn_fm", "scores_tm"]),
    ("qkv0", _ph_qkv0,
     ["xn_fm", "L0_attn_Wq", "L0_attn_bq", "L0_attn_Wk", "L0_attn_bk",
      "L0_attn_Wv", "L0_attn_bv", "L0_attn_decay_param", "c_trel", "c_trel2",
      "c_ident_b"],
     ["q_fm", "qs_fm", "kphi_fm", "khat_tm", "v_tm"]),
    ("scan0", _ph_scan0,
     ["q_fm", "qs_fm", "kphi_fm", "khat_tm", "v_tm", "L0_attn_decay_param",
      "c_m1", "c_m2neg", "c_trel", "c_trel2", "c_ident_b"],
     ["read_fm"]),
    ("wo0", _ph_wo0,
     ["read_fm", "L0_attn_Wo", "L0_attn_bo", "c_ident_b"],
     ["attn0_tm"]),
    ("shared0", ph_shared0,
     ["xn_fm", "L0_ffn_w_gate", "L0_ffn_w_up", "L0_ffn_w_down", "c_ident_b"],
     ["ffn0_tm"]),
    ("ffn0e", ph_ffn0e,
     ["xn_tm", "scores_tm", "L0_ffn_gate_up_proj", "L0_ffn_down_proj",
      "c_ident_b"],
     ["ffn0e_tm"]),
    ("resid0", lambda P, es: ph_add(P, es, ["x0", "attn0_tm", "ffn0_tm",
                                            "ffn0e_tm"], "x1_tm"),
     ["x0", "attn0_tm", "ffn0_tm", "ffn0e_tm", "c_ident_b"],
     ["x1_tm"]),
    ("ln1", lambda P, es: ph_ln(P, es, "x1_tm", "xn1_fm", out_f32="xn1f_tm"),
     ["x1_tm", "c_ident_b"],
     ["xn1_fm", "xn1f_tm"]),
    ("qkv1", ph_qkv1,
     ["xn1_fm", "L1_attn_Wq", "L1_attn_Wk", "L1_attn_Wv", "c_ident_b"],
     ["phiq_fm", "phik_fm", "phik_tm", "v1_tm"]),
    ("scan1", ph_scan1,
     ["phiq_fm", "phik_fm", "phik_tm", "v1_tm", "c_mask1", "c_ident_b"],
     ["y1_fm"]),
    ("wo1", lambda P, es: ph_proj_tm(P, es, "y1_fm", "L1_attn_Wo", "attn1_tm",
                                     bias="L1_attn_bo", w_transposed=True),
     ["y1_fm", "L1_attn_Wo", "L1_attn_bo", "c_ident_b"],
     ["attn1_tm"]),
    ("vitln1", lambda P, es: ph_ln(P, es, "xn1f_tm", "xln_fm",
                                   lnw=("L1_ffn_ln1_w", "L1_ffn_ln1_b")),
     ["xn1f_tm", "L1_ffn_ln1_w", "L1_ffn_ln1_b", "c_ident_b"],
     ["xln_fm"]),
    ("qkvmsa", ph_qkvmsa,
     ["xln_fm", "L1_ffn_w_qkv", "c_ident_b"],
     ["qk_fm", "vm_tm"]),
    ("msa", ph_msa,
     ["qk_fm", "vm_tm", "c_ident_b"],
     ["ym_fm"]),
    ("msaproj", lambda P, es: ph_proj_tm(P, es, "ym_fm", "L1_ffn_w_proj", "u_tm",
                                         extra_tm=("xn1f_tm",)),
     ["ym_fm", "L1_ffn_w_proj", "xn1f_tm", "c_ident_b"],
     ["u_tm"]),
    ("ln2g", lambda P, es: ph_ln(P, es, "u_tm", "xg_fm",
                                 lnw=("L1_ffn_ln2_w", "L1_ffn_ln2_b"),
                                 router=("L1_ffn_gate_w", 8, "mask8_tm", False,
                                         False, "L1_ffn_gate_b")),
     ["u_tm", "L1_ffn_ln2_w", "L1_ffn_ln2_b", "L1_ffn_gate_w", "L1_ffn_gate_b",
      "c_ident_b"],
     ["xg_fm", "mask8_tm"]),
    ("moe8", ph_moe8,
     ["xg_fm", "mask8_tm", "L1_ffn_w1", "L1_ffn_w2", "c_ident_b"],
     ["moe_tm"]),
    ("residout", lambda P, es: ph_add(P, es, ["x1_tm", "attn1_tm", "u_tm",
                                              "moe_tm"], "OUT"),
     ["x1_tm", "attn1_tm", "u_tm", "moe_tm", "c_ident_b"],
     ["OUT"]),
]







# version 05 (final)
# speedup vs baseline: 1.6582x; 1.6431x over previous
import contextlib
import numpy as np
import ml_dtypes
import concourse.bass as bass
import concourse.mybir as mybir
from concourse.bass_utils import run_bass_kernel_spmd

F32 = mybir.dt.float32
BF16 = mybir.dt.bfloat16
AF = mybir.ActivationFunctionType
ALU = mybir.AluOpType
AX = mybir.AxisListType
BF = ml_dtypes.bfloat16

NCORES = 8
B_CORE = 2
S = 2048
T = B_CORE * S
D = 1024
NT = T // 128
KC = D // 128
CH = 512
NCH = S // CH
NEG = -1.0e30
ROUTE_DBG = False
DBG_SC = False

TENSORS = {
    "x0": ([T, D], F32),
    "L0_attn_Wq": ([D, D], F32), "L0_attn_bq": ([1, D], F32),
    "L0_attn_Wk": ([D, D], F32), "L0_attn_bk": ([1, D], F32),
    "L0_attn_Wv": ([D, D], F32), "L0_attn_bv": ([1, D], F32),
    "L0_attn_Wo": ([D, D], F32), "L0_attn_bo": ([1, D], F32),
    "L0_attn_decay_param": ([1, 1], F32),
    "L0_ffn_w_router": ([16, D], F32),
    "L0_ffn_gate_up_proj": ([16, D, 4096], F32),
    "L0_ffn_down_proj": ([16, 2048, D], F32),
    "L0_ffn_w_gate": ([2048, D], F32),
    "L0_ffn_w_up": ([2048, D], F32),
    "L0_ffn_w_down": ([D, 2048], F32),
    "L1_attn_Wq": ([D, D], F32), "L1_attn_Wk": ([D, D], F32),
    "L1_attn_Wv": ([D, D], F32), "L1_attn_Wo": ([D, D], F32),
    "L1_attn_bo": ([1, D], F32),
    "L1_ffn_ln1_w": ([1, D], F32), "L1_ffn_ln1_b": ([1, D], F32),
    "L1_ffn_w_qkv": ([D, 3 * D], F32),
    "L1_ffn_w_proj": ([D, D], F32),
    "L1_ffn_ln2_w": ([1, D], F32), "L1_ffn_ln2_b": ([1, D], F32),
    "L1_ffn_gate_w": ([D, 8], F32), "L1_ffn_gate_b": ([1, 8], F32),
    "L1_ffn_w1": ([8, D, 4096], F32),
    "L1_ffn_w2": ([8, 4096, D], F32),
    "c_ident_b": ([128, 128], BF16),
    "c_m1": ([CH, CH], F32),
    "c_m2neg": ([CH, CH], F32),
    "c_mask1": ([CH, CH], F32),
    "c_trel": ([128, CH // 128], F32),
    "c_trel2": ([128, CH // 128], F32),
    "xn_tm": ([T, D], BF16),
    "xn_fm": ([KC, 128, T], BF16),
    "scores_tm": ([T, 16], F32),
    "q_fm": ([KC, 128, T], BF16),
    "qs_fm": ([KC, 128, T], BF16),
    "kphi_fm": ([KC, 128, T], BF16),
    "khat_tm": ([T, D], BF16),
    "v_tm": ([T, D], BF16),
    "read_fm": ([KC, 128, T], BF16),
    "attn0_tm": ([T, D], F32),
    "ffn0_tm": ([T, D], F32),
    "ffn0e_tm": ([T, D], F32),
    "x1_tm": ([T, D], F32),
    "xn1f_tm": ([T, D], F32),
    "xn1_fm": ([KC, 128, T], BF16),
    "phiq_fm": ([KC, 128, T], BF16),
    "phik_fm": ([KC, 128, T], BF16),
    "phik_tm": ([T, D], BF16),
    "v1_tm": ([T, D], BF16),
    "y1_fm": ([KC, 128, T], BF16),
    "attn1_tm": ([T, D], F32),
    "xln_fm": ([KC, 128, T], BF16),
    "qk_fm": ([2 * KC, 128, T], BF16),
    "vm_tm": ([T, D], BF16),
    "ym_fm": ([KC, 128, T], BF16),
    "u_tm": ([T, D], F32),
    "xg_fm": ([KC, 128, T], BF16),
    "mask8_tm": ([T, 8], F32),
    "moe_tm": ([T, D], F32),
    "OUT": ([T, D], F32),
}

NP_DT = {F32: np.float32, BF16: BF}


class Sem:
    def __init__(self, h):
        self.h = h
        self.v = 0


class _DummyInstr:
    def then_inc(self, h, n):
        return self

    def annotate(self, a):
        return self


class _DummyEngine:
    def __getattr__(self, name):
        def f(*a, **k):
            return _DummyInstr()
        return f


class Ctx:
    NDSEM = 28

    def __init__(self, nc, es, kinds):
        self.nc = nc
        self.es = es
        self.kinds = kinds
        self._aps = {}
        self.sems = {}
        for nm in ["A", "Dv", "Tn", "Pl", "Sp", "xA", "xD", "xT"]:
            self.sems[nm] = Sem(es.enter_context(nc.semaphore(nm)))
        self.dsems = [Sem(es.enter_context(nc.semaphore(f"d{i}")))
                      for i in range(self.NDSEM)]
        self.identb = None
        self.marks = {}
        self._dry = True
        self.phase_id = 0

    def ap(self, name):
        if name not in self._aps:
            shape, dt = TENSORS[name]
            kind = {"in": "ExternalInput", "out": "ExternalOutput",
                    "tmp": "Internal"}[self.kinds[name]]
            self._aps[name] = self.nc.dram_tensor(name, shape, dt, kind=kind).ap()
        return self._aps[name]

    def sem(self, nm):
        return self.sems[nm]

    def dsem(self, i):
        return self.dsems[i]

    def inc(self, ins, sem, label=None, n=1):
        sem.v += n
        if label is not None:
            key = (self.phase_id, label)
            old = self.marks.get(key)
            assert old is None or old[0] is sem, f"mark {label} sem mismatch"
            self.marks[key] = (sem, sem.v)
        ins.then_inc(sem.h, n)
        return ins

    def wait(self, e, sem, label):
        key = (self.phase_id, label)
        if key not in self.marks:
            assert self._dry, f"unknown mark {label} in emit pass"
            v = 0
        else:
            msem, v = self.marks[key]
            assert msem is sem, f"wait on mark {label}: wrong semaphore"
        e.wait_ge(sem.h, v)

    def incflush(self, e, ins, which, sem, label, n=1):
        xs = self.sems["xA"] if which == "A" else self.sems["xD"]
        self.selfsync(e, ins, xs)
        if which == "A":
            f = e.activation(self.jA[:, 1:2], self.jA[:, 0:1], AF.Copy)
        else:
            f = e.tensor_copy(self.jD[:, 1:2], self.jD[:, 0:1])
        return self.inc(f, sem, label, n)

    def selfsync(self, e, ins, xsem):
        ins.then_inc(xsem.h, 1)
        xsem.v += 1
        e.wait_ge(xsem.h, xsem.v)
        return ins

    def twopass(self, bodies):
        allsems = list(self.sems.values()) + self.dsems
        snap = [(s, s.v) for s in allsems]
        finals = {id(s): v for s, v in snap}
        self._dry = True
        dummy = _DummyEngine()
        for _, fn in bodies:
            for s, v in snap:
                s.v = v
            fn(dummy)
            for s, v in snap:
                finals[id(s)] = max(finals[id(s)], s.v)
        self._dry = False
        with self.nc.Block() as blk:
            for name, fn in bodies:
                for s, v in snap:
                    s.v = v
                getattr(blk, name)(fn)
        for s, v in snap:
            s.v = finals[id(s)]


def _ph_prelude(P):
    nc = P.nc
    if P.identb is not None:
        return
    P.jA = P.es.enter_context(nc.sbuf_tensor(f"junkA{P.phase_id}", [128, 4], F32))[:, :]
    P.jD = P.es.enter_context(nc.sbuf_tensor(f"junkD{P.phase_id}", [128, 4], F32))[:, :]
    identb = P.es.enter_context(nc.sbuf_tensor(f"c_identb_sb{P.phase_id}", [128, 128], BF16))
    ld = P.dsem(27)
    with nc.Block() as blk:
        @blk.sync
        def _(e):
            e.dma_start(identb[:, :], P.ap("c_ident_b")).then_inc(ld.h, 16)

        @blk.vector
        def _(e):
            e.memset(P.jD, 0.0)

        @blk.scalar
        def _(e):
            e.memzero(P.jA)

        @blk.tensor
        def _(e):
            e.wait_ge(ld.h, ld.v + 16)
    ld.v += 16
    P.identb = identb[:, :]


def run_phase(P, fn, *args, **kwargs):
    _ph_prelude(P)
    P.phase_id += 1
    with contextlib.ExitStack() as es:
        fn(P, es, *args, **kwargs)
    with P.nc.Block() as blk:
        @blk.sync
        def _(e):
            for ds in P.dsems:
                e.wait_ge(ds.h, ds.v)



def load_cast_weights(P, es, specs, tag=""):
    nc = P.nc
    Dv = P.sem("Dv")
    lds = [P.dsem(22 + k) for k in range(4)]
    hes = contextlib.ExitStack()
    stage = hes.enter_context(nc.sbuf_tensor(f"wc_stage{P.phase_id}{tag}", [128, 4 * 512], F32))
    pieces = []
    for dram, sb in specs:
        K, N = dram.shape
        kp = min(K, 128)
        for kc in range(max(1, K // 128)):
            for j in range(0, N, 512):
                w = min(512, N - j)
                pieces.append((dram[kc * kp:kc * kp + kp, j:j + w],
                               sb[0:kp, kc * N + j: kc * N + j + w]))

    def b_sync(e):
        for i, (dsl, ssl) in enumerate(pieces):
            if i >= 4:
                P.wait(e, Dv, f"wcv{i - 4}")
            kp, w = dsl.shape
            P.inc(e.dma_start(stage[0:kp, (i % 4) * 512:(i % 4) * 512 + w], dsl),
                  lds[i % 4], f"wcl{i}", 16)

    def b_vec(e):
        for i, (dsl, ssl) in enumerate(pieces):
            P.wait(e, lds[i % 4], f"wcl{i}")
            kp, w = dsl.shape
            P.inc(e.tensor_copy(ssl, stage[0:kp, (i % 4) * 512:(i % 4) * 512 + w]),
                  Dv, f"wcv{i}")

    P.twopass([("sync", b_sync), ("vector", b_vec)])
    hes.close()


def load_cast_transpose_weights(P, es, specs, tag=""):
    nc = P.nc
    Dv, Tn = P.sem("Dv"), P.sem("Tn")
    lds = [P.dsem(22), P.dsem(23)]
    KMAX = max(d.shape[1] for d, _ in specs)
    hes = contextlib.ExitStack()
    stage = hes.enter_context(nc.sbuf_tensor(f"wt_stage{P.phase_id}{tag}", [128, 2 * 512], F32))
    stb = hes.enter_context(nc.sbuf_tensor(f"wt_stb{P.phase_id}{tag}", [128, 2 * KMAX], BF16))
    pst = hes.enter_context(nc.psum_tensor(f"wt_ps{P.phase_id}{tag}", [128, 2 * KMAX], BF16))
    rows = []
    for dram, sb in specs:
        M, K = dram.shape
        for m in range(0, M, 128):
            rows.append((dram[m:m + 128, :], sb, M, m, K))

    def b_sync(e):
        ip = 0
        for r, (dsl, sb, M, m, K) in enumerate(rows):
            for j in range(0, K, 512):
                if ip >= 2:
                    P.wait(e, Dv, f"wtc{ip - 2}")
                P.inc(e.dma_start(stage[:, (ip % 2) * 512:(ip % 2) * 512 + 512],
                                  dsl[:, j:j + 512]), lds[ip % 2], f"wtl{ip}", 16)
                ip += 1

    def b_tensor(e):
        ip = 0
        for r, (dsl, sb, M, m, K) in enumerate(rows):
            ip += K // 512
            P.wait(e, Dv, f"wtc{ip - 1}")
            if r >= 2:
                K2 = rows[r - 2][4]
                P.wait(e, Dv, f"wte{r - 2}.{K2 // 128 - 1}")
            ins = None
            for kc in range(K // 128):
                ins = e.transpose(pst[:, (r % 2) * KMAX + kc * 128:
                                      (r % 2) * KMAX + (kc + 1) * 128],
                                  stb[:, (r % 2) * KMAX + kc * 128:
                                      (r % 2) * KMAX + (kc + 1) * 128],
                                  P.identb)
            P.inc(ins, Tn, f"wtt{r}")

    def b_vec(e):
        ip = 0
        for r, (dsl, sb, M, m, K) in enumerate(rows):
            for j in range(0, K, 512):
                P.wait(e, lds[ip % 2], f"wtl{ip}")
                if r >= 2:
                    P.wait(e, Tn, f"wtt{r - 1}")
                P.inc(e.tensor_copy(stb[:, (r % 2) * KMAX + j:
                                        (r % 2) * KMAX + j + 512],
                                    stage[:, (ip % 2) * 512:(ip % 2) * 512 + 512]),
                      Dv, f"wtc{ip}")
                ip += 1
            P.wait(e, Tn, f"wtt{r}")
            for kc in range(K // 128):
                P.inc(e.tensor_copy(sb[:, kc * M + m: kc * M + m + 128],
                                    pst[:, (r % 2) * KMAX + kc * 128:
                                        (r % 2) * KMAX + (kc + 1) * 128]),
                      Dv, f"wte{r}.{kc}")

    P.twopass([("sync", b_sync), ("tensor", b_tensor), ("vector", b_vec)])
    hes.close()


def bias_bcast(P, es, names, width=None, tag=""):
    nc = P.nc
    Dv, Tn = P.sem("Dv"), P.sem("Tn")
    ld = P.dsem(22)
    widths = {nm: (width or TENSORS[nm][0][1]) for nm in names}
    tot = sum(widths.values())
    bbc = es.enter_context(nc.sbuf_tensor(f"bb_bc{P.phase_id}{tag}", [128, tot], F32))
    hes = contextlib.ExitStack()
    brow = hes.enter_context(nc.sbuf_tensor(f"bb_row{P.phase_id}{tag}", [1, tot], F32))
    ones = hes.enter_context(nc.sbuf_tensor(f"bb_ones{P.phase_id}{tag}", [1, 128], F32))
    psb = hes.enter_context(nc.psum_tensor(f"bb_ps{P.phase_id}{tag}", [128, 512], F32))
    offs = {}
    o = 0
    for nm in names:
        offs[nm] = o
        o += widths[nm]
    segs = []
    for nm in names:
        w = widths[nm]
        for j in range(0, w, 512):
            segs.append((offs[nm] + j, min(512, w - j)))

    def b_sync(e):
        for i, nm in enumerate(names):
            P.inc(e.dma_start(brow[:, offs[nm]:offs[nm] + widths[nm]], P.ap(nm)),
                  ld, f"bbl{i}", 16)

    def b_vec(e):
        P.inc(e.memset(ones[:, :], 1.0), Dv, "bbones")
        for i, (ro, w) in enumerate(segs):
            P.wait(e, Tn, f"bbm{i}")
            P.inc(e.tensor_copy(bbc[:, ro:ro + w], psb[:, 0:w]), Dv, f"bbe{i}")

    def b_tensor(e):
        P.wait(e, Dv, "bbones")
        for i, nm in enumerate(names):
            P.wait(e, ld, f"bbl{i}")
        for i, (ro, w) in enumerate(segs):
            if i >= 1:
                P.wait(e, Dv, f"bbe{i - 1}")
            P.inc(e.matmul(psb[:, 0:w], ones[:, :], brow[:, ro:ro + w],
                           start=True, stop=True), Tn, f"bbm{i}")

    P.twopass([("sync", b_sync), ("vector", b_vec), ("tensor", b_tensor)])
    hes.close()
    return {nm: bbc[:, offs[nm]:offs[nm] + widths[nm]] for nm in names}



def ph_ln(P, es, src, out_fm, out_tm=None, out_f32=None, lnw=None, router=None):
    nc = P.nc
    A, Dv, Tn, xA, xD = (P.sem(n) for n in ["A", "Dv", "Tn", "xA", "xD"])
    lds = [P.dsem(0), P.dsem(1)]
    s_tm = [P.dsem(10), P.dsem(11)]
    s_f32 = [P.dsem(12), P.dsem(13)]
    s_fm = [P.dsem(14), P.dsem(15)]
    s_sc = [P.dsem(16), P.dsem(17)]
    E = router[1] if router else 0
    use_sig = router[3] if router else False
    assert not (router and use_sig and router[5]), "sigmoid+bias router unsupported"

    xa = es.enter_context(nc.sbuf_tensor(f"ln_x{P.phase_id}", [128, 2 * D], F32))
    xc = es.enter_context(nc.sbuf_tensor(f"ln_xc{P.phase_id}", [128, D], F32))
    sq = es.enter_context(nc.sbuf_tensor(f"ln_sq{P.phase_id}", [128, D], F32))
    xnf = es.enter_context(nc.sbuf_tensor(f"ln_xnf{P.phase_id}", [128, 2 * D], F32))
    xnb = es.enter_context(nc.sbuf_tensor(f"ln_xnb{P.phase_id}", [128, 2 * D], BF16))
    lob = es.enter_context(nc.sbuf_tensor(f"ln_lob{P.phase_id}", [128, 2 * D], BF16))
    fmb = es.enter_context(nc.sbuf_tensor(f"ln_fmb{P.phase_id}", [128, 2 * D], BF16))
    lofm = es.enter_context(nc.sbuf_tensor(f"ln_lofm{P.phase_id}", [128, 2 * D], BF16))
    stats = es.enter_context(nc.sbuf_tensor(f"ln_stats{P.phase_id}", [128, 8], F32))
    pst = es.enter_context(nc.psum_tensor(f"ln_pst{P.phase_id}", [128, 2 * D], BF16))
    if router:
        wrt = es.enter_context(nc.sbuf_tensor(f"ln_wrt{P.phase_id}", [128, 2 * KC * E], BF16))
        psr = es.enter_context(nc.psum_tensor(f"ln_psr{P.phase_id}", [128, E], F32))
        sc = es.enter_context(nc.sbuf_tensor(f"ln_sc{P.phase_id}", [128, 2 * 4 * E], F32))
        lgb = es.enter_context(nc.sbuf_tensor(f"ln_lgb{P.phase_id}", [128, 2 * 8], F32)) \
            if router[5] else None
    wbc = bias_bcast(P, es, list(lnw)) if lnw else None
    gb = bias_bcast(P, es, [router[5]], width=E, tag="g")[router[5]] \
        if (router and router[5]) else None
    if router:
        _router_w_prep(P, es, router, wrt)
    xsrc = P.ap(src)

    def slot(i):
        return (i % 2) * D

    def emit_exp(e, j):
        P.wait(e, Tn, f"mmr{j}")
        if j >= 2:
            P.wait(e, s_sc[j % 2], f"ssc{j - 2}")
        P.selfsync(e, e.activation(sc[:, (j % 2) * 4 * E + 2 * E:
                                      (j % 2) * 4 * E + 3 * E],
                                   psr[:, 0:E], AF.Exp, scale=-1.0), xA)
        P.inc(e.activation(stats[:, 7:8], stats[:, 7:8], AF.Copy), A, f"ex{j}")

    def b_sync(e):
        for i in range(NT):
            if i >= 2:
                P.wait(e, A, f"ln{i - 2}")
            P.inc(e.dma_start(xa[:, slot(i):slot(i) + D],
                              xsrc[i * 128:(i + 1) * 128, :]),
                  lds[i % 2], f"lx{i}", 16)

    def b_scalar(e):
        for i in range(NT):
            P.wait(e, lds[i % 2], f"lx{i}")
            if i >= 2:
                P.wait(e, Tn, f"tr{i - 2}")
                if out_tm:
                    P.wait(e, s_tm[i % 2], f"stm{i - 2}")
                if out_f32:
                    P.wait(e, s_f32[i % 2], f"sf32{i - 2}")
                if lnw:
                    P.wait(e, Dv, f"lo{i - 2}")
            xi = xa[:, slot(i):slot(i) + D]
            P.selfsync(e, e.activation(sq[:, :], xi, AF.Copy,
                                       accum_out=stats[:, 0:1]), xA)
            P.selfsync(e, e.activation(stats[:, 1:2], stats[:, 0:1], AF.Copy,
                                       scale=-1.0 / D), xA)
            e.activation(xc[:, :], xi, AF.Identity, bias=stats[:, 1:2])
            P.selfsync(e, e.activation(sq[:, :], xc[:, :], AF.Square,
                                       accum_out=stats[:, 2:3]), xA)
            P.selfsync(e, e.activation(stats[:, 3:4], stats[:, 2:3], AF.Ln,
                                       scale=1.0 / D, bias=np.float32(1e-5)), xA)
            P.selfsync(e, e.activation(stats[:, 4:5], stats[:, 3:4], AF.Exp,
                                       scale=-0.5), xA)
            e.activation(xnf[:, slot(i):slot(i) + D], xc[:, :], AF.Copy,
                         scale=stats[:, 4:5])
            if lnw:
                P.inc(e.activation(xnf[:, slot(i):slot(i) + 1],
                                   xnf[:, slot(i):slot(i) + 1], AF.Copy),
                      A, f"ln{i}")
            else:
                P.inc(e.activation(xnb[:, slot(i):slot(i) + D],
                                   xnf[:, slot(i):slot(i) + D], AF.Copy),
                      A, f"ln{i}")
            if router and use_sig and i >= 1:
                emit_exp(e, i - 1)
        if router and use_sig:
            emit_exp(e, NT - 1)

    def b_vector(e):
        for i in range(NT + 1):
            if i < NT:
                P.wait(e, A, f"ln{i}")
                if lnw:
                    if i >= 2:
                        if out_tm:
                            P.wait(e, s_tm[i % 2], f"stm{i - 2}")
                        if out_f32:
                            P.wait(e, s_f32[i % 2], f"sf32{i - 2}")
                        P.wait(e, Tn, f"tr{i - 2}")
                    e.tensor_mul(xnf[:, slot(i):slot(i) + D],
                                 xnf[:, slot(i):slot(i) + D], wbc[lnw[0]])
                    e.tensor_add(xnf[:, slot(i):slot(i) + D],
                                 xnf[:, slot(i):slot(i) + D], wbc[lnw[1]])
                    e.tensor_copy(xnb[:, slot(i):slot(i) + D],
                                  xnf[:, slot(i):slot(i) + D])
                if router:
                    P.inc(e.tensor_sub(lob[:, slot(i):slot(i) + D],
                                       xnf[:, slot(i):slot(i) + D],
                                       xnb[:, slot(i):slot(i) + D]), Dv, f"lo{i}")
                else:
                    P.inc(e.tensor_copy(lob[:, 0:1], xnb[:, slot(i):slot(i) + 1]),
                          Dv, f"lo{i}")
            if i >= 1:
                j = i - 1
                P.wait(e, Tn, f"tr{j}")
                if j >= 2:
                    P.wait(e, s_fm[j % 2], f"sfm{j - 2}")
                if router:
                    e.tensor_copy(fmb[:, slot(j):slot(j) + D], pst[:, 0:D])
                    P.inc(e.tensor_copy(lofm[:, slot(j):slot(j) + D],
                                        pst[:, D:2 * D]), Dv, f"ev{j}")
                else:
                    P.inc(e.tensor_copy(fmb[:, slot(j):slot(j) + D], pst[:, 0:D]),
                          Dv, f"ev{j}")
            if router and i >= 1:
                j = i - 1
                P.wait(e, Tn, f"mmr{j}")
                if j >= 2:
                    P.wait(e, s_sc[j % 2], f"ssc{j - 2}")
                base = (j % 2) * 4 * E
                if router[5]:
                    P.selfsync(e, e.tensor_add(lgb[:, (j % 2) * 8:(j % 2) * 8 + E],
                                               psr[:, 0:E], gb), xD)
                    lsrc = lgb[:, (j % 2) * 8:(j % 2) * 8 + E]
                else:
                    lsrc = psr[:, 0:E]
                if DBG_SC:
                    e.tensor_copy(sc[:, base + 3 * E:base + 4 * E], psr[:, 0:E])
                P.selfsync(e, e.reduce_max(stats[:, 5:6], lsrc, axis=AX.X), xD)
                if use_sig:
                    P.wait(e, A, f"ex{j}")
                    P.selfsync(e, e.tensor_scalar(sc[:, base + E:base + 2 * E],
                                                  sc[:, base + 2 * E:base + 3 * E],
                                                  1.0, None, ALU.add), xD)
                    P.selfsync(e, e.reciprocal(sc[:, base + 3 * E:base + 4 * E],
                                               sc[:, base + E:base + 2 * E]), xD)
                    P.selfsync(e, e.tensor_scalar(sc[:, base:base + E], lsrc,
                                                  stats[:, 5:6], None,
                                                  ALU.is_equal), xD)
                    P.selfsync(e, e.tensor_mul(sc[:, base:base + E],
                                               sc[:, base:base + E],
                                               sc[:, base + 3 * E:base + 4 * E]),
                               xD)
                    P.inc(e.tensor_copy(stats[:, 6:7], stats[:, 5:6]),
                          Dv, f"sco{j}")
                else:
                    P.selfsync(e, e.tensor_scalar(sc[:, base:base + E], lsrc,
                                                  stats[:, 5:6], None,
                                                  ALU.is_equal), xD)
                    P.inc(e.tensor_copy(stats[:, 6:7], stats[:, 5:6]),
                          Dv, f"sco{j}")

    def b_tensor(e):
        for i in range(NT + 1):
            if i < NT:
                P.wait(e, Dv, f"lo{i}")
                if i >= 1:
                    P.wait(e, Dv, f"ev{i - 1}")
                ins = None
                for c in range(KC):
                    ins = e.transpose(pst[:, c * 128:(c + 1) * 128],
                                      xnb[:, slot(i) + c * 128:
                                          slot(i) + (c + 1) * 128], P.identb)
                if router:
                    for c in range(KC):
                        ins = e.transpose(pst[:, D + c * 128:D + (c + 1) * 128],
                                          lob[:, slot(i) + c * 128:
                                              slot(i) + (c + 1) * 128], P.identb)
                P.inc(ins, Tn, f"tr{i}")
            if router and i >= 1:
                j = i - 1
                P.wait(e, Dv, f"ev{j}")
                if j >= 1:
                    P.wait(e, Dv, f"sco{j - 1}")
                    if use_sig:
                        P.wait(e, A, f"ex{j - 1}")
                n = 0
                ins = None
                for kc in range(KC):
                    for hl in range(3):
                        lh = lofm[:, slot(j) + kc * 128:slot(j) + kc * 128 + 128] \
                            if hl == 1 else \
                            fmb[:, slot(j) + kc * 128:slot(j) + kc * 128 + 128]
                        rh = wrt[:, (2 * kc + (1 if hl == 2 else 0)) * E:
                                 (2 * kc + (1 if hl == 2 else 0) + 1) * E]
                        ins = e.matmul(psr[:, 0:E], lh, rh, start=(n == 0),
                                       stop=(n == 3 * KC - 1))
                        n += 1
                P.inc(ins, Tn, f"mmr{j}")

    def b_gpsimd(e):
        for i in range(NT):
            P.wait(e, A, f"ln{i}")
            if lnw:
                P.wait(e, Dv, f"lo{i}")
            if out_tm:
                P.inc(e.dma_start(P.ap(out_tm)[i * 128:(i + 1) * 128, :],
                                  xnb[:, slot(i):slot(i) + D]),
                      s_tm[i % 2], f"stm{i}", 16)
            if out_f32:
                P.inc(e.dma_start(P.ap(out_f32)[i * 128:(i + 1) * 128, :],
                                  xnf[:, slot(i):slot(i) + D]),
                      s_f32[i % 2], f"sf32{i}", 16)
            P.wait(e, Dv, f"ev{i}")
            P.inc(e.dma_start(P.ap(out_fm)[:, :, i * 128:(i + 1) * 128]
                              .rearrange("c p j -> p c j"),
                              fmb[:, slot(i):slot(i) + D]
                              .rearrange("p (c j) -> p c j", c=KC)),
                  s_fm[i % 2], f"sfm{i}", 16)
            if router:
                P.wait(e, Dv, f"sco{i}")
                wE = 4 * E if DBG_SC else E
                P.inc(e.dma_start(P.ap(router[2])[i * 128:(i + 1) * 128, :],
                                  sc[:, (i % 2) * 4 * E:(i % 2) * 4 * E + wE]),
                      s_sc[i % 2], f"ssc{i}", 16)

    P.twopass([("sync", b_sync), ("scalar", b_scalar), ("vector", b_vector),
               ("tensor", b_tensor), ("gpsimd", b_gpsimd)])


def _router_w_prep(P, es, router, wrt):
    nc = P.nc
    wname, E, w_is_ExD = router[0], router[1], router[4]
    Dv, Tn = P.sem("Dv"), P.sem("Tn")
    ld = P.dsem(22)
    if w_is_ExD:
        hes = contextlib.ExitStack()
        wst = hes.enter_context(nc.sbuf_tensor(f"rw_st{P.phase_id}", [E, D], F32))
        whi = hes.enter_context(nc.sbuf_tensor(f"rw_hi{P.phase_id}", [E, D], BF16))
        wlo = hes.enter_context(nc.sbuf_tensor(f"rw_lo{P.phase_id}", [E, D], BF16))
        pse = hes.enter_context(nc.psum_tensor(f"rw_ps{P.phase_id}", [128, 2 * E], BF16))

        def b_sync(e):
            P.inc(e.dma_start(wst[:, :], P.ap(wname)), ld, "rwl", 16)

        def b_vec(e):
            P.wait(e, ld, "rwl")
            e.tensor_copy(whi[:, :], wst[:, :])
            P.inc(e.tensor_sub(wlo[:, :], wst[:, :], whi[:, :]), Dv, "rws")
            for kc in range(KC):
                P.wait(e, Tn, f"rwt{kc}")
                e.tensor_copy(wrt[:, 2 * kc * E:(2 * kc + 1) * E], pse[:, 0:E])
                P.inc(e.tensor_copy(wrt[:, (2 * kc + 1) * E:(2 * kc + 2) * E],
                                    pse[:, E:2 * E]), Dv, f"rwe{kc}")

        def b_tensor(e):
            P.wait(e, Dv, "rws")
            for kc in range(KC):
                if kc >= 1:
                    P.wait(e, Dv, f"rwe{kc - 1}")
                e.transpose(pse[:, 0:E], whi[0:E, kc * 128:(kc + 1) * 128],
                            P.identb[0:E, 0:E])
                P.inc(e.transpose(pse[:, E:2 * E],
                                  wlo[0:E, kc * 128:(kc + 1) * 128],
                                  P.identb[0:E, 0:E]), Tn, f"rwt{kc}")

        P.twopass([("sync", b_sync), ("vector", b_vec), ("tensor", b_tensor)])
        hes.close()
    else:
        wst = es.enter_context(nc.sbuf_tensor(f"rw_st{P.phase_id}", [128, KC * E], F32))

        def b_sync(e):
            P.inc(e.dma_start(wst[:, :].rearrange("p (c e) -> p c e", c=KC),
                              P.ap(wname).rearrange("(c p) e -> p c e", p=128)),
                  ld, "rwl", 16)

        def b_vec(e):
            P.wait(e, ld, "rwl")
            for kc in range(KC):
                e.tensor_copy(wrt[:, 2 * kc * E:(2 * kc + 1) * E],
                              wst[:, kc * E:(kc + 1) * E])
            for kc in range(KC):
                e.tensor_sub(wrt[:, (2 * kc + 1) * E:(2 * kc + 2) * E],
                             wst[:, kc * E:(kc + 1) * E],
                             wrt[:, 2 * kc * E:(2 * kc + 1) * E])

        P.twopass([("sync", b_sync), ("vector", b_vec)])



def _ph_ln0(P, es):
    ph_ln(P, es, "x0", "xn_fm", out_tm="xn_tm",
          router=("L0_ffn_w_router", 16, "scores_tm", True, True, None))



def make_constants():
    s = np.arange(CH, dtype=np.float32)[:, None]
    t = np.arange(CH, dtype=np.float32)[None, :]
    c = {
        "c_ident_b": np.eye(128, dtype=BF),
        "c_m1": (t - 1.0 - s).astype(np.float32),
        "c_m2neg": np.where(s < t, 0.0, NEG).astype(np.float32),
        "c_mask1": (s <= t).astype(np.float32),
        "c_trel": (np.arange(128, dtype=np.float32)[:, None]
                   + 128.0 * np.arange(CH // 128, dtype=np.float32)[None, :]),
    }
    c["c_trel2"] = (CH - 1.0) - c["c_trel"]
    ss = np.arange(128, dtype=np.float32)[:, None]
    tt = np.arange(128, dtype=np.float32)[None, :]
    c["c_ltri"] = (ss < tt).astype(np.float32)
    e1 = np.arange(16, dtype=np.float32)[:, None]
    e2 = np.arange(16, dtype=np.float32)[None, :]
    c["c_ltri16"] = (e1 < e2).astype(np.float32)
    c["c_iota"] = np.arange(128, dtype=np.float32)[:, None]
    c["c_eids"] = np.broadcast_to(np.arange(16, dtype=np.float32)[None, :],
                                  (128, 16)).copy()
    c["c_tstart"] = np.broadcast_to(512.0 * np.arange(32, dtype=np.float32)[None, :],
                                    (16, 32)).copy()
    return c


def shard_inputs(inputs):
    vals = {}
    x = np.ascontiguousarray(np.asarray(inputs["L0_attn_x"], dtype=np.float32))
    vals["x0"] = [np.ascontiguousarray(x[2 * i:2 * i + 2].reshape(T, D))
                  for i in range(NCORES)]
    for name in TENSORS:
        if name in ("x0", "OUT") or name.startswith("c_"):
            continue
        if name in inputs:
            shape, dt = TENSORS[name]
            a = np.ascontiguousarray(np.asarray(inputs[name],
                                                dtype=NP_DT[dt]).reshape(shape))
            vals[name] = [a] * NCORES
    for name, a in make_constants().items():
        shape, dt = TENSORS[name]
        a = np.ascontiguousarray(a.astype(NP_DT[dt]).reshape(shape))
        vals[name] = [a] * NCORES
    return vals


def build_launch(phase_items, later_reads):
    all_writes = set()
    written = set()
    kinds = {}
    for _, _, rd, wr in phase_items:
        for r in rd:
            if r not in written:
                kinds[r] = "in"
        for w in wr:
            written.add(w)
            all_writes.add(w)
    for w in all_writes:
        if w == "OUT" or w in later_reads:
            kinds[w] = "out"
        elif kinds.get(w) != "in":
            kinds[w] = "tmp"
        else:
            raise RuntimeError(f"tensor {w} both external input and written")
    nc = bass.Bass("TRN2", target_bir_lowering=False)
    es = contextlib.ExitStack()
    P = Ctx(nc, es, kinds)
    for name, builder, rd, wr in phase_items:
        run_phase(P, builder)
    es.close()
    ext_in = [n for n, k in kinds.items() if k == "in"]
    ext_out = [n for n, k in kinds.items() if k == "out"]
    return nc, ext_in, ext_out


def run_pipeline(vals, groups, phase_defs=None, keep_all=False):
    defs = {p[0]: p for p in (phase_defs or PHASE_DEFS)}
    for gi, g in enumerate(groups):
        items = [defs[n] for n in g]
        later = set()
        for g2 in groups[gi + 1:]:
            for n2 in g2:
                later.update(defs[n2][2])
        later.add("OUT")
        if keep_all:
            later.update(TENSORS.keys())
        nc, ext_in, ext_out = build_launch(items, later)
        in_maps = [{n: vals[n][c] for n in ext_in} for c in range(NCORES)]
        res = run_bass_kernel_spmd(nc, in_maps, core_ids=list(range(NCORES)))
        for n in ext_out:
            vals[n] = [np.asarray(res.results[c][n]) for c in range(NCORES)]
    return vals


FUSED = True


def kernel(**inputs):
    vals = shard_inputs(inputs)
    if FUSED:
        groups = [[p[0] for p in PHASE_DEFS]]
    else:
        groups = [[p[0]] for p in PHASE_DEFS]
    run_pipeline(vals, groups)
    out = np.stack([vals["OUT"][c].reshape(B_CORE, S, D) for c in range(NCORES)])
    return np.ascontiguousarray(out.reshape(16, S, D))



def emit_decay_consts(P, es):
    nc = P.nc
    A, Dv, Tn, xA, xD = (P.sem(n) for n in ["A", "Dv", "Tn", "xA", "xD"])
    ld = P.dsem(22)
    ld2 = P.dsem(23)
    ld3 = P.dsem(24)
    NJ = CH // 128
    cols = es.enter_context(nc.sbuf_tensor(f"dc_cols{P.phase_id}", [128, 2 * NJ + 2], F32))
    hes = contextlib.ExitStack()
    dst = hes.enter_context(nc.sbuf_tensor(f"dc_st{P.phase_id}", [1, 8], F32))
    trl = hes.enter_context(nc.sbuf_tensor(f"dc_trl{P.phase_id}", [128, 2 * NJ], F32))
    ones1 = hes.enter_context(nc.sbuf_tensor(f"dc_ones{P.phase_id}", [1, 128], F32))
    psc = hes.enter_context(nc.psum_tensor(f"dc_ps{P.phase_id}", [128, 1], F32))

    def b_sync(e):
        P.inc(e.dma_start(dst[:, 0:1], P.ap("L0_attn_decay_param")), ld, "dcl", 16)
        P.inc(e.dma_start(trl[:, 0:NJ], P.ap("c_trel")), ld2, "dct1", 16)
        P.inc(e.dma_start(trl[:, NJ:2 * NJ], P.ap("c_trel2")), ld3, "dct2", 16)

    def b_scalar(e):
        P.wait(e, ld, "dcl")
        P.selfsync(e, e.activation(dst[:, 1:2], dst[:, 0:1], AF.Exp, scale=-1.0), xA)
        P.incflush(e, e.activation(dst[:, 2:3], dst[:, 1:2], AF.Identity,
                                   bias=np.float32(1.0)), "A", A, "dcs")
        P.wait(e, Dv, "dcr")
        P.incflush(e, e.activation(dst[:, 4:5], dst[:, 3:4], AF.Ln), "A", A, "dcln")
        P.wait(e, Dv, "dcb")
        P.wait(e, ld2, "dct1")
        P.wait(e, ld3, "dct2")
        P.selfsync(e, e.activation(cols[:, 1:1 + NJ], trl[:, 0:NJ], AF.Exp,
                                   scale=cols[:, 0:1]), xA)
        P.selfsync(e, e.activation(cols[:, 1 + NJ:1 + 2 * NJ], trl[:, NJ:2 * NJ],
                                   AF.Exp, scale=cols[:, 0:1]), xA)
        P.incflush(e, e.activation(cols[:, 1 + 2 * NJ:2 + 2 * NJ], cols[:, 0:1],
                                   AF.Exp, scale=float(CH)), "A", A, "dcd")

    def b_vector(e):
        P.wait(e, A, "dcs")
        P.incflush(e, e.reciprocal(dst[:, 3:4], dst[:, 2:3]), "D", Dv, "dcr")
        P.incflush(e, e.memset(ones1[:, :], 1.0), "D", Dv, "dco")
        P.wait(e, Tn, "dcm")
        P.incflush(e, e.tensor_copy(cols[:, 0:1], psc[:, 0:1]), "D", Dv, "dcb")

    def b_tensor(e):
        P.wait(e, A, "dcln")
        P.wait(e, Dv, "dco")
        P.inc(e.matmul(psc[:, 0:1], ones1[:, :], dst[:, 4:5],
                       start=True, stop=True), Tn, "dcm")

    P.twopass([("sync", b_sync), ("scalar", b_scalar), ("vector", b_vector),
               ("tensor", b_tensor)])
    hes.close()
    return {"ld_col": cols[:, 0:1], "dq": cols[:, 1:1 + NJ],
            "dk": cols[:, 1 + NJ:1 + 2 * NJ],
            "dC": cols[:, 1 + 2 * NJ:2 + 2 * NJ], "done": "dcd"}



def ph_qkv0(P, es):
    nc = P.nc
    A, Dv, Tn = (P.sem(n) for n in ["A", "Dv", "Tn"])
    lds = [P.dsem(0), P.dsem(1)]
    std = {"qs": [P.dsem(10), P.dsem(11)], "q": [P.dsem(12), P.dsem(13)],
           "kp": [P.dsem(14), P.dsem(15)], "kh": [P.dsem(16), P.dsem(17)],
           "v": [P.dsem(18), P.dsem(19)]}
    wq = es.enter_context(nc.sbuf_tensor(f"q0_wq{P.phase_id}", [128, KC * D], BF16))
    wk = es.enter_context(nc.sbuf_tensor(f"q0_wk{P.phase_id}", [128, KC * D], BF16))
    wv = es.enter_context(nc.sbuf_tensor(f"q0_wv{P.phase_id}", [128, KC * D], BF16))
    load_cast_weights(P, es, [(P.ap("L0_attn_Wq"), wq[:, :]),
                              (P.ap("L0_attn_Wk"), wk[:, :]),
                              (P.ap("L0_attn_Wv"), wv[:, :])])
    bb = bias_bcast(P, es, ["L0_attn_bq", "L0_attn_bk", "L0_attn_bv"])
    dc = emit_decay_consts(P, es)

    xfm = es.enter_context(nc.sbuf_tensor(f"q0_x{P.phase_id}", [128, 2 * D], BF16))
    qbf = es.enter_context(nc.sbuf_tensor(f"q0_qb{P.phase_id}", [128, 2 * D], BF16))
    qsb = es.enter_context(nc.sbuf_tensor(f"q0_qs{P.phase_id}", [128, 2 * D], BF16))
    kpb = es.enter_context(nc.sbuf_tensor(f"q0_kp{P.phase_id}", [128, 2 * D], BF16))
    khb = es.enter_context(nc.sbuf_tensor(f"q0_kh{P.phase_id}", [128, 2 * D], BF16))
    vbf = es.enter_context(nc.sbuf_tensor(f"q0_vb{P.phase_id}", [128, 2 * D], BF16))
    k1 = es.enter_context(nc.sbuf_tensor(f"q0_k1{P.phase_id}", [128, D], F32))
    km = es.enter_context(nc.sbuf_tensor(f"q0_km{P.phase_id}", [128, D], F32))
    kr = es.enter_context(nc.sbuf_tensor(f"q0_kr{P.phase_id}", [128, D], F32))
    keb = es.enter_context(nc.sbuf_tensor(f"q0_ke{P.phase_id}", [128, D], F32))
    qfmb = es.enter_context(nc.sbuf_tensor(f"q0_qf{P.phase_id}", [128, 2 * D], BF16))
    qsfmb = es.enter_context(nc.sbuf_tensor(f"q0_qsf{P.phase_id}", [128, 2 * D], BF16))
    kfmb = es.enter_context(nc.sbuf_tensor(f"q0_kf{P.phase_id}", [128, 2 * D], BF16))
    psP = es.enter_context(nc.psum_tensor(f"q0_ps{P.phase_id}", [128, 3 * D], F32))
    psT = es.enter_context(nc.psum_tensor(f"q0_pt{P.phase_id}", [128, 2 * D], BF16))

    def slot(i):
        return (i % 2) * D

    def jsub(i):
        return (i % (S // 128)) % NCH

    def b_sync(e):
        for i in range(NT):
            if i >= 2:
                P.wait(e, Tn, f"pjv{i - 2}")
            P.inc(e.dma_start(xfm[:, slot(i):slot(i) + D]
                              .rearrange("p (c j) -> p c j", c=KC),
                              P.ap("xn_fm")[:, :, i * 128:(i + 1) * 128]
                              .rearrange("c p j -> p c j")),
                  lds[i % 2], f"lx{i}", 16)

    def b_tensor(e):
        for i in range(NT):
            P.wait(e, lds[i % 2], f"lx{i}")
            if i >= 1:
                P.wait(e, Dv, f"vb{i - 1}")
                P.wait(e, Dv, f"km{i - 1}")
            for wi, (wsb, tag) in enumerate([(wq, "q"), (wk, "k"), (wv, "v")]):
                ins = None
                for n in range(2):
                    for kc in range(KC):
                        ins = e.matmul(psP[:, wi * D + n * 512:wi * D + n * 512 + 512],
                                       xfm[:, slot(i) + kc * 128:slot(i) + kc * 128 + 128],
                                       wsb[:, kc * D + n * 512:kc * D + n * 512 + 512],
                                       start=(kc == 0), stop=(kc == KC - 1))
                P.inc(ins, Tn, f"pj{tag}{i}")
            P.wait(e, Dv, f"qs{i}")
            if i >= 1:
                P.wait(e, Dv, f"eq{i - 1}")
            ins = None
            for c in range(KC):
                ins = e.transpose(psT[:, c * 128:(c + 1) * 128],
                                  qbf[:, slot(i) + c * 128:slot(i) + (c + 1) * 128],
                                  P.identb)
            P.inc(ins, Tn, f"tq{i}")
            if i >= 1:
                P.wait(e, Dv, f"eqs{i - 1}")
            ins = None
            for c in range(KC):
                ins = e.transpose(psT[:, D + c * 128:D + (c + 1) * 128],
                                  qsb[:, slot(i) + c * 128:slot(i) + (c + 1) * 128],
                                  P.identb)
            P.inc(ins, Tn, f"tqs{i}")
            P.wait(e, Dv, f"kp{i}")
            P.wait(e, Dv, f"eq{i}")
            ins = None
            for c in range(KC):
                ins = e.transpose(psT[:, c * 128:(c + 1) * 128],
                                  kpb[:, slot(i) + c * 128:slot(i) + (c + 1) * 128],
                                  P.identb)
            P.inc(ins, Tn, f"tkp{i}")

    def b_scalar(e):
        for i in range(NT):
            P.wait(e, Dv, f"km{i}")
            if i >= 1:
                P.wait(e, Dv, f"kp{i - 1}")
            P.inc(e.activation(keb[:, :], km[:, :], AF.Exp), A, f"ke{i}")

    def b_vector(e):
        j = None
        for i in range(NT):
            jj = jsub(i)
            P.wait(e, Tn, f"pjq{i}")
            if i >= 2:
                P.wait(e, Tn, f"tq{i - 2}")
                P.wait(e, Tn, f"tqs{i - 2}")
            e.tensor_add(qbf[:, slot(i):slot(i) + D], psP[:, 0:D],
                         bb["L0_attn_bq"])
            P.inc(e.tensor_scalar(qsb[:, slot(i):slot(i) + D],
                                  qbf[:, slot(i):slot(i) + D],
                                  dc["dq"][:, jj:jj + 1], None, ALU.mult),
                  Dv, f"qs{i}")
            P.wait(e, Tn, f"pjk{i}")
            e.tensor_add(k1[:, :], psP[:, D:2 * D], bb["L0_attn_bk"])
            e.tensor_relu(kr[:, :], k1[:, :])
            P.inc(e.tensor_scalar(km[:, :], k1[:, :], 0.0, None, ALU.min),
                  Dv, f"km{i}")
            P.wait(e, Tn, f"pjv{i}")
            if i >= 2:
                P.wait(e, std["v"][i % 2], f"sv{i - 2}")
                P.wait(e, std["kh"][i % 2], f"skh{i - 2}")
                P.wait(e, std["kp"][i % 2], f"skp{i - 2}")
            P.inc(e.tensor_add(vbf[:, slot(i):slot(i) + D], psP[:, 2 * D:3 * D],
                               bb["L0_attn_bv"]), Dv, f"vb{i}")
            P.wait(e, A, f"ke{i}")
            if i >= 2:
                P.wait(e, Tn, f"tkp{i - 2}")
            e.tensor_add(kpb[:, slot(i):slot(i) + D], keb[:, :], kr[:, :])
            P.inc(e.tensor_scalar(khb[:, slot(i):slot(i) + D],
                                  kpb[:, slot(i):slot(i) + D],
                                  dc["dk"][:, jj:jj + 1], None, ALU.mult),
                  Dv, f"kp{i}")
            P.wait(e, Tn, f"tq{i}")
            if i >= 2:
                P.wait(e, std["q"][i % 2], f"sq{i - 2}")
            P.inc(e.tensor_copy(qfmb[:, slot(i):slot(i) + D], psT[:, 0:D]),
                  Dv, f"eq{i}")
            P.wait(e, Tn, f"tqs{i}")
            P.inc(e.tensor_copy(qsfmb[:, slot(i):slot(i) + D], psT[:, D:2 * D]),
                  Dv, f"eqs{i}")
            P.wait(e, Tn, f"tkp{i}")
            P.inc(e.tensor_copy(kfmb[:, slot(i):slot(i) + D], psT[:, 0:D]),
                  Dv, f"ekp{i}")

    def b_gpsimd(e):
        for i in range(NT):
            P.wait(e, Dv, f"eq{i}")
            P.inc(e.dma_start(P.ap("q_fm")[:, :, i * 128:(i + 1) * 128]
                              .rearrange("c p j -> p c j"),
                              qfmb[:, slot(i):slot(i) + D]
                              .rearrange("p (c j) -> p c j", c=KC)),
                  std["q"][i % 2], f"sq{i}", 16)
            P.wait(e, Dv, f"eqs{i}")
            P.inc(e.dma_start(P.ap("qs_fm")[:, :, i * 128:(i + 1) * 128]
                              .rearrange("c p j -> p c j"),
                              qsfmb[:, slot(i):slot(i) + D]
                              .rearrange("p (c j) -> p c j", c=KC)),
                  std["qs"][i % 2], f"sqs{i}", 16)
            P.wait(e, Dv, f"ekp{i}")
            P.inc(e.dma_start(P.ap("kphi_fm")[:, :, i * 128:(i + 1) * 128]
                              .rearrange("c p j -> p c j"),
                              kfmb[:, slot(i):slot(i) + D]
                              .rearrange("p (c j) -> p c j", c=KC)),
                  std["kp"][i % 2], f"skp{i}", 16)
            P.wait(e, Dv, f"kp{i}")
            P.inc(e.dma_start(P.ap("khat_tm")[i * 128:(i + 1) * 128, :],
                              khb[:, slot(i):slot(i) + D]),
                  std["kh"][i % 2], f"skh{i}", 16)
            P.wait(e, Dv, f"vb{i}")
            P.inc(e.dma_start(P.ap("v_tm")[i * 128:(i + 1) * 128, :],
                              vbf[:, slot(i):slot(i) + D]),
                  std["v"][i % 2], f"sv{i}", 16)

    P.twopass([("sync", b_sync), ("scalar", b_scalar), ("vector", b_vector),
               ("tensor", b_tensor), ("gpsimd", b_gpsimd)])



def ph_scan0(P, es):
    nc = P.nc
    A, Dv, Tn = (P.sem(n) for n in ["A", "Dv", "Tn"])
    lds = [[P.dsem(2 * k), P.dsem(2 * k + 1)] for k in range(5)]
    strd = [P.dsem(10), P.dsem(11)]
    dc = emit_decay_consts(P, es)
    mT = es.enter_context(nc.sbuf_tensor(f"s0_mT{P.phase_id}", [128, 4 * 512], F32))
    M = es.enter_context(nc.sbuf_tensor(f"s0_M{P.phase_id}", [128, B_CORE * KC * D], F32))
    Mb = es.enter_context(nc.sbuf_tensor(f"s0_Mb{P.phase_id}", [128, B_CORE * KC * D], BF16))
    ld14 = P.dsem(22)
    ld13 = P.dsem(23)
    with contextlib.ExitStack() as hes:
        m1 = hes.enter_context(nc.sbuf_tensor(f"s0_m1{P.phase_id}", [128, 4 * 512], F32))
        m2 = hes.enter_context(nc.sbuf_tensor(f"s0_m2{P.phase_id}", [128, 4 * 512], F32))

        def mb_sync(e):
            P.inc(e.dma_start(m1[:, :].rearrange("p (j t) -> p j t", j=4),
                              P.ap("c_m1").rearrange("(j p) t -> p j t", p=128)),
                  ld14, "lm1", 16)
            P.inc(e.dma_start(m2[:, :].rearrange("p (j t) -> p j t", j=4),
                              P.ap("c_m2neg").rearrange("(j p) t -> p j t", p=128)),
                  ld13, "lm2", 16)

        def mb_vec(e):
            P.wait(e, ld14, "lm1")
            P.wait(e, ld13, "lm2")
            P.wait(e, A, dc["done"])
            e.tensor_scalar(m1[:, :], m1[:, :], dc["ld_col"], None, ALU.mult)
            P.inc(e.tensor_add(m1[:, :], m1[:, :], m2[:, :]), Dv, "marg")
            e.memset(M[:, :], 0.0)
            e.memset(Mb[:, :], 0.0)

        def mb_scalar(e):
            P.wait(e, Dv, "marg")
            e.activation(mT[:, :], m1[:, :], AF.Exp)

        P.twopass([("sync", mb_sync), ("vector", mb_vec), ("scalar", mb_scalar)])

    qsf = es.enter_context(nc.sbuf_tensor(f"s0_qs{P.phase_id}", [128, 2 * KC * CH], BF16))
    qf = es.enter_context(nc.sbuf_tensor(f"s0_q{P.phase_id}", [128, 2 * KC * CH], BF16))
    kf = es.enter_context(nc.sbuf_tensor(f"s0_k{P.phase_id}", [128, 2 * KC * CH], BF16))
    kht = es.enter_context(nc.sbuf_tensor(f"s0_kh{P.phase_id}", [128, 2 * 4 * D], BF16))
    vt = es.enter_context(nc.sbuf_tensor(f"s0_v{P.phase_id}", [128, 2 * 4 * D], BF16))
    absf = es.enter_context(nc.sbuf_tensor(f"s0_ab{P.phase_id}", [128, 4 * CH], BF16))
    rbf = es.enter_context(nc.sbuf_tensor(f"s0_rb{P.phase_id}", [128, 2 * D], BF16))
    rfm = es.enter_context(nc.sbuf_tensor(f"s0_rf{P.phase_id}", [128, 2 * D], BF16))
    psA = es.enter_context(nc.psum_tensor(f"s0_pa{P.phase_id}", [128, 2 * 512], F32))
    psR = es.enter_context(nc.psum_tensor(f"s0_pr{P.phase_id}", [128, 2 * D], F32))
    psT = es.enter_context(nc.psum_tensor(f"s0_pt{P.phase_id}", [128, D], BF16))
    NBC = B_CORE * NCH


    def b_sync(e):
        for idx in range(NBC):
            b, c = idx // NCH, idx % NCH
            off = b * S + c * CH
            sl = (idx % 2)
            if idx >= 2:
                P.wait(e, Tn, f"dm{idx - 2}.7")
            for k, (name, fm) in enumerate([("qs_fm", True), ("q_fm", True),
                                            ("kphi_fm", True), ("khat_tm", False),
                                            ("v_tm", False)]):
                if fm:
                    dst = [qsf, qf, kf][k][:, sl * KC * CH:(sl + 1) * KC * CH]
                    P.inc(e.dma_start(dst.rearrange("p (c j) -> p c j", c=KC),
                                      P.ap(name)[:, :, off:off + CH]
                                      .rearrange("c p j -> p c j")),
                          lds[k][idx % 2], f"l{k}.{idx}", 16)
                else:
                    dst = [None, None, None, kht, vt][k][:, sl * 4 * D:(sl + 1) * 4 * D]
                    P.inc(e.dma_start(dst.rearrange("p (s d) -> p s d", s=4),
                                      P.ap(name)[off:off + CH, :]
                                      .rearrange("(s p) d -> p s d", p=128)),
                          lds[k][idx % 2], f"l{k}.{idx}", 16)

    def b_vector(e):
        for idx in range(NBC):
            b, c = idx // NCH, idx % NCH
            sl = (idx % 2)
            for js in range(4):
                P.wait(e, Tn, f"aw{idx}.{js}")
                P.inc(e.tensor_mul(absf[:, js * CH:(js + 1) * CH],
                                   psA[:, (js % 2) * 512:(js % 2) * 512 + 512],
                                   mT[:, js * CH:(js + 1) * CH]), Dv, f"ab{idx}.{js}")
            for jt in range(4):
                P.wait(e, Tn, f"rd{idx}.{jt}")
                P.inc(e.tensor_copy(rbf[:, (jt % 2) * D:(jt % 2) * D + D],
                                    psR[:, (jt % 2) * D:(jt % 2) * D + D]),
                      Dv, f"re{idx}.{jt}")
                P.wait(e, Tn, f"rt{idx}.{jt}")
                if idx * 4 + jt >= 2:
                    pj = idx * 4 + jt - 2
                    P.wait(e, strd[jt % 2], f"sr{pj // 4}.{pj % 4}")
                P.inc(e.tensor_copy(rfm[:, (jt % 2) * D:(jt % 2) * D + D],
                                    psT[:, 0:D]), Dv, f"rf{idx}.{jt}")
            for di in range(KC):
                P.wait(e, Tn, f"dm{idx}.{di}")
                ins = None
                for n in range(2):
                    ins = e.scalar_tensor_tensor(
                        M[:, b * KC * D + di * D + n * 512:
                          b * KC * D + di * D + n * 512 + 512],
                        M[:, b * KC * D + di * D + n * 512:
                          b * KC * D + di * D + n * 512 + 512],
                        dc["dC"], psA[:, n * 512:n * 512 + 512],
                        ALU.mult, ALU.add)
                P.inc(ins, Dv, f"mu{idx}.{di}")
                P.inc(e.tensor_copy(Mb[:, b * KC * D + di * D:
                                       b * KC * D + (di + 1) * D],
                                    M[:, b * KC * D + di * D:
                                      b * KC * D + (di + 1) * D]),
                      Dv, f"mb{idx}.{di}")

    def b_tensor2(e):
        for idx in range(NBC):
            b, c = idx // NCH, idx % NCH
            sl = (idx % 2)
            qso = sl * KC * CH
            kho = sl * 4 * D
            for k in range(5):
                P.wait(e, lds[k][idx % 2], f"l{k}.{idx}")
            for js in range(4):
                if js >= 2:
                    P.wait(e, Dv, f"ab{idx}.{js - 2}")
                elif idx >= 1:
                    P.wait(e, Dv, f"mu{idx - 1}.7")
                ins = None
                for kc in range(KC):
                    ins = e.matmul(psA[:, (js % 2) * 512:(js % 2) * 512 + 512],
                                   kf[:, qso + kc * CH + js * 128:
                                      qso + kc * CH + js * 128 + 128],
                                   qf[:, qso + kc * CH:qso + kc * CH + CH],
                                   start=(kc == 0), stop=(kc == KC - 1))
                P.inc(ins, Tn, f"aw{idx}.{js}")
            if c != 0:
                P.wait(e, Dv, f"mb{idx - 1}.7")
            P.wait(e, Dv, f"ab{idx}.3")
            for jt in range(4):
                if jt >= 2:
                    P.wait(e, Dv, f"re{idx}.{jt - 2}")
                elif idx >= 1:
                    P.wait(e, Dv, f"re{idx - 1}.{jt + 2}")
                ins = None
                ngrp = 0
                for n in range(2):
                    for kc in range(KC):
                        ins = e.matmul(psR[:, (jt % 2) * D + n * 512:
                                           (jt % 2) * D + n * 512 + 512],
                                       qsf[:, qso + kc * CH + jt * 128:
                                           qso + kc * CH + jt * 128 + 128],
                                       Mb[:, b * KC * D + kc * D + n * 512:
                                          b * KC * D + kc * D + n * 512 + 512],
                                       start=(kc == 0), stop=False)
                    for js in range(4):
                        ins = e.matmul(psR[:, (jt % 2) * D + n * 512:
                                           (jt % 2) * D + n * 512 + 512],
                                       absf[:, js * CH + jt * 128:
                                            js * CH + jt * 128 + 128],
                                       vt[:, kho + js * D + n * 512:
                                          kho + js * D + n * 512 + 512],
                                       start=False, stop=(js == 3))
                P.inc(ins, Tn, f"rd{idx}.{jt}")
                P.wait(e, Dv, f"re{idx}.{jt}")
                if idx * 4 + jt >= 1:
                    pj = idx * 4 + jt - 1
                    P.wait(e, Dv, f"rf{pj // 4}.{pj % 4}")
                ins = None
                for cc in range(KC):
                    ins = e.transpose(psT[:, cc * 128:(cc + 1) * 128],
                                      rbf[:, (jt % 2) * D + cc * 128:
                                          (jt % 2) * D + (cc + 1) * 128],
                                      P.identb)
                P.inc(ins, Tn, f"rt{idx}.{jt}")
            for di in range(KC):
                if di >= 1:
                    P.wait(e, Dv, f"mu{idx}.{di - 1}")
                ins = None
                for n in range(2):
                    for js in range(4):
                        ins = e.matmul(psA[:, n * 512:n * 512 + 512],
                                       kht[:, kho + js * D + di * 128:
                                           kho + js * D + di * 128 + 128],
                                       vt[:, kho + js * D + n * 512:
                                          kho + js * D + n * 512 + 512],
                                       start=(js == 0), stop=(js == 3))
                P.inc(ins, Tn, f"dm{idx}.{di}")

    def b_gpsimd(e):
        for idx in range(NBC):
            b, c = idx // NCH, idx % NCH
            off = b * S + c * CH
            for jt in range(4):
                P.wait(e, Dv, f"rf{idx}.{jt}")
                P.inc(e.dma_start(P.ap("read_fm")[:, :, off + jt * 128:
                                                  off + (jt + 1) * 128]
                                  .rearrange("c p j -> p c j"),
                                  rfm[:, (jt % 2) * D:(jt % 2) * D + D]
                                  .rearrange("p (c j) -> p c j", c=KC)),
                      strd[jt % 2], f"sr{idx}.{jt}", 16)

    P.twopass([("sync", b_sync), ("vector", b_vector), ("tensor", b_tensor2),
               ("gpsimd", b_gpsimd)])



def ph_proj_tm(P, es, src_fm, wname, out_tm, bias=None, w_transposed=False,
               extra_tm=(), out_bf16=False):
    nc = P.nc
    A, Dv, Tn = (P.sem(n) for n in ["A", "Dv", "Tn"])
    lds = [P.dsem(0), P.dsem(1)]
    eld = [[P.dsem(2 + 2 * k), P.dsem(3 + 2 * k)] for k in range(len(extra_tm))]
    sto = [P.dsem(10), P.dsem(11)]
    wsb = es.enter_context(nc.sbuf_tensor(f"pj_w{P.phase_id}", [128, KC * D], BF16))
    if w_transposed:
        load_cast_transpose_weights(P, es, [(P.ap(wname), wsb[:, :])])
    else:
        load_cast_weights(P, es, [(P.ap(wname), wsb[:, :])])
    bb = bias_bcast(P, es, [bias])[bias] if bias else None
    xfm = es.enter_context(nc.sbuf_tensor(f"pj_x{P.phase_id}", [128, 2 * D], BF16))
    ext = [es.enter_context(nc.sbuf_tensor(f"pj{P.phase_id}_e{k}", [128, 2 * D], F32))
           for k in range(len(extra_tm))]
    ob = es.enter_context(nc.sbuf_tensor(f"pj_o{P.phase_id}", [128, 2 * D],
                                         BF16 if out_bf16 else F32))
    psP = es.enter_context(nc.psum_tensor(f"pj_ps{P.phase_id}", [128, 2 * D], F32))

    def slot(i):
        return (i % 2) * D

    def b_sync(e):
        for i in range(NT):
            if i >= 2:
                P.wait(e, Tn, f"pj{i - 2}")
            P.inc(e.dma_start(xfm[:, slot(i):slot(i) + D]
                              .rearrange("p (c j) -> p c j", c=KC),
                              P.ap(src_fm)[:, :, i * 128:(i + 1) * 128]
                              .rearrange("c p j -> p c j")),
                  lds[i % 2], f"lx{i}", 16)
            for k, nm in enumerate(extra_tm):
                if i >= 2:
                    P.wait(e, Dv, f"ev{i - 2}")
                P.inc(e.dma_start(ext[k][:, slot(i):slot(i) + D],
                                  P.ap(nm)[i * 128:(i + 1) * 128, :]),
                      eld[k][i % 2], f"le{k}.{i}", 16)

    def b_tensor(e):
        for i in range(NT):
            P.wait(e, lds[i % 2], f"lx{i}")
            if i >= 2:
                P.wait(e, Dv, f"ev{i - 2}")
            ins = None
            for n in range(2):
                for kc in range(KC):
                    ins = e.matmul(psP[:, slot(i) + n * 512:slot(i) + n * 512 + 512],
                                   xfm[:, slot(i) + kc * 128:slot(i) + kc * 128 + 128],
                                   wsb[:, kc * D + n * 512:kc * D + n * 512 + 512],
                                   start=(kc == 0), stop=(kc == KC - 1))
            P.inc(ins, Tn, f"pj{i}")

    def b_vector(e):
        for i in range(NT):
            P.wait(e, Tn, f"pj{i}")
            if i >= 2:
                P.wait(e, sto[i % 2], f"so{i - 2}")
            for k in range(len(extra_tm)):
                P.wait(e, eld[k][i % 2], f"le{k}.{i}")
            src = psP[:, slot(i):slot(i) + D]
            if bias is not None:
                e.tensor_add(ob[:, slot(i):slot(i) + D], src, bb)
                src = ob[:, slot(i):slot(i) + D]
            for k in range(len(extra_tm)):
                e.tensor_add(ob[:, slot(i):slot(i) + D], src,
                             ext[k][:, slot(i):slot(i) + D])
                src = ob[:, slot(i):slot(i) + D]
            if bias is None and not extra_tm:
                e.tensor_copy(ob[:, slot(i):slot(i) + D], src)
            P.inc(e.tensor_copy(P.jD[:, 2:3], P.jD[:, 3:4]), Dv, f"ev{i}")

    def b_gpsimd(e):
        for i in range(NT):
            P.wait(e, Dv, f"ev{i}")
            P.inc(e.dma_start(P.ap(out_tm)[i * 128:(i + 1) * 128, :],
                              ob[:, slot(i):slot(i) + D]),
                  sto[i % 2], f"so{i}", 16)

    P.twopass([("sync", b_sync), ("tensor", b_tensor), ("vector", b_vector),
               ("gpsimd", b_gpsimd)])



def ph_shared0(P, es):
    nc = P.nc
    A, Dv, Tn = (P.sem(n) for n in ["A", "Dv", "Tn"])
    lds = [P.dsem(0), P.dsem(1)]
    sto = [P.dsem(10), P.dsem(11)]
    F2 = 2048
    wg = es.enter_context(nc.sbuf_tensor(f"sh_wg{P.phase_id}", [128, KC * F2], BF16))
    wu = es.enter_context(nc.sbuf_tensor(f"sh_wu{P.phase_id}", [128, KC * F2], BF16))
    wd = es.enter_context(nc.sbuf_tensor(f"sh_wd{P.phase_id}", [128, (F2 // 128) * D], BF16))
    load_cast_transpose_weights(P, es, [(P.ap("L0_ffn_w_gate"), wg[:, :]),
                                        (P.ap("L0_ffn_w_up"), wu[:, :]),
                                        (P.ap("L0_ffn_w_down"), wd[:, :])])
    TC = 512
    NTC = T // TC
    xfm = es.enter_context(nc.sbuf_tensor(f"sh_x{P.phase_id}", [128, 2 * KC * TC], BF16))
    hg = es.enter_context(nc.sbuf_tensor(f"sh_hg{P.phase_id}", [128, TC], F32))
    hfm = es.enter_context(nc.sbuf_tensor(f"sh_h{P.phase_id}", [128, (F2 // 128) * TC], BF16))
    ob = es.enter_context(nc.sbuf_tensor(f"sh_o{P.phase_id}", [128, 2 * 2 * D], F32))
    psG = es.enter_context(nc.psum_tensor(f"sh_pg{P.phase_id}", [128, TC], F32))
    psU = es.enter_context(nc.psum_tensor(f"sh_pu{P.phase_id}", [128, TC], F32))
    psD = es.enter_context(nc.psum_tensor(f"sh_pd{P.phase_id}", [128, 2 * D], F32))
    NF = F2 // 128

    def b_sync(e):
        for i in range(NTC):
            if i >= 2:
                P.wait(e, Tn, f"gu{i - 2}.{NF - 1}")
            P.inc(e.dma_start(xfm[:, (i % 2) * KC * TC:(i % 2 + 1) * KC * TC]
                              .rearrange("p (c j) -> p c j", c=KC),
                              P.ap("xn_fm")[:, :, i * TC:(i + 1) * TC]
                              .rearrange("c p j -> p c j")),
                  lds[i % 2], f"lx{i}", 16)

    def b_tensor(e):
        for i in range(NTC):
            P.wait(e, lds[i % 2], f"lx{i}")
            xo = (i % 2) * KC * TC
            for j in range(NF):
                if j >= 1:
                    P.wait(e, Dv, f"hm{i}.{j - 1}")
                elif i >= 1:
                    P.wait(e, Dv, f"hm{i - 1}.{NF - 1}")
                ins = None
                for kc in range(KC):
                    ins = e.matmul(psG[:, :],
                                   wg[:, kc * F2 + j * 128:kc * F2 + (j + 1) * 128],
                                   xfm[:, xo + kc * TC:xo + kc * TC + TC],
                                   start=(kc == 0), stop=(kc == KC - 1))
                for kc in range(KC):
                    ins = e.matmul(psU[:, :],
                                   wu[:, kc * F2 + j * 128:kc * F2 + (j + 1) * 128],
                                   xfm[:, xo + kc * TC:xo + kc * TC + TC],
                                   start=(kc == 0), stop=(kc == KC - 1))
                P.inc(ins, Tn, f"gu{i}.{j}")
            for ts in range(4):
                P.wait(e, Dv, f"hm{i}.{NF - 1}")
                if ts >= 2:
                    P.wait(e, Dv, f"de{i}.{ts - 2}")
                elif i >= 1:
                    P.wait(e, Dv, f"de{i - 1}.{ts + 2}")
                ins = None
                for n in range(2):
                    for kc in range(NF):
                        ins = e.matmul(psD[:, (ts % 2) * D + n * 512:
                                           (ts % 2) * D + n * 512 + 512],
                                       hfm[:, kc * TC + ts * 128:
                                           kc * TC + ts * 128 + 128],
                                       wd[:, kc * D + n * 512:kc * D + n * 512 + 512],
                                       start=(kc == 0), stop=(kc == NF - 1))
                P.inc(ins, Tn, f"dn{i}.{ts}")

    def b_scalar(e):
        for i in range(NTC):
            for j in range(NF):
                P.wait(e, Tn, f"gu{i}.{j}")
                P.inc(e.activation(hg[:, :], psG[:, :], AF.Silu), A, f"sl{i}.{j}")

    def b_vector(e):
        for i in range(NTC):
            for j in range(NF):
                P.wait(e, A, f"sl{i}.{j}")
                P.inc(e.tensor_mul(hfm[:, j * TC:(j + 1) * TC], hg[:, :],
                                   psU[:, :]), Dv, f"hm{i}.{j}")
            for ts in range(4):
                P.wait(e, Tn, f"dn{i}.{ts}")
                gl = i * 4 + ts
                if gl >= 4:
                    P.wait(e, sto[gl % 2], f"so{(gl - 4) // 4}.{(gl - 4) % 4}")
                P.inc(e.tensor_copy(ob[:, (gl % 4) * D:(gl % 4) * D + D],
                                    psD[:, (ts % 2) * D:(ts % 2) * D + D]),
                      Dv, f"de{i}.{ts}")

    def b_gpsimd(e):
        for i in range(NTC):
            for ts in range(4):
                P.wait(e, Dv, f"de{i}.{ts}")
                gl = i * 4 + ts
                P.inc(e.dma_start(P.ap("ffn0_tm")[gl * 128:(gl + 1) * 128, :],
                                  ob[:, (gl % 4) * D:(gl % 4) * D + D]),
                      sto[gl % 2], f"so{i}.{ts}", 16)

    P.twopass([("sync", b_sync), ("tensor", b_tensor), ("scalar", b_scalar),
               ("vector", b_vector), ("gpsimd", b_gpsimd)])



def ph_ffn0e(P, es):
    nc = P.nc
    A, Dv, Tn = (P.sem(n) for n in ["A", "Dv", "Tn"])
    lds = [P.dsem(0), P.dsem(1)]
    sto = [P.dsem(10 + k) for k in range(8)]
    NE = 16
    F2 = 4096
    ND = 2048 // 128
    wgu = es.enter_context(nc.sbuf_tensor(f"fe_wgu{P.phase_id}", [128, KC * F2], BF16))
    wd = es.enter_context(nc.sbuf_tensor(f"fe_wd{P.phase_id}", [128, ND * D], BF16))
    ssc = es.enter_context(nc.sbuf_tensor(f"fe_sc{P.phase_id}", [128, NT * NE], F32))
    TC = 512
    NTC = T // TC
    xtm = es.enter_context(nc.sbuf_tensor(f"fe_x{P.phase_id}", [128, 2 * 4 * D], BF16))
    xsb = es.enter_context(nc.sbuf_tensor(f"fe_xs{P.phase_id}", [128, 4 * D], BF16))
    xfm = es.enter_context(nc.sbuf_tensor(f"fe_xf{P.phase_id}", [128, KC * TC], BF16))
    hg = es.enter_context(nc.sbuf_tensor(f"fe_hg{P.phase_id}", [128, TC], F32))
    hfm = es.enter_context(nc.sbuf_tensor(f"fe_h{P.phase_id}", [128, (F2 // 2 // 128) * TC], BF16))
    ob = es.enter_context(nc.sbuf_tensor(f"fe_o{P.phase_id}", [128, 4 * D], F32))
    psG = es.enter_context(nc.psum_tensor(f"fe_pg{P.phase_id}", [128, TC], F32))
    psU = es.enter_context(nc.psum_tensor(f"fe_pu{P.phase_id}", [128, TC], F32))
    psD = es.enter_context(nc.psum_tensor(f"fe_pd{P.phase_id}", [128, 2 * D], F32))
    psT = es.enter_context(nc.psum_tensor(f"fe_pt{P.phase_id}", [128, D], BF16))
    NF = F2 // 2 // 128

    ld22 = P.dsem(22)
    with nc.Block() as blk:
        @blk.sync
        def _(e):
            e.dma_start(ssc[:, :].rearrange("p (i e) -> p i e", i=NT),
                        P.ap("scores_tm").rearrange("(i p) e -> p i e", p=128))\
                .then_inc(ld22.h, 16)
            e.wait_ge(ld22.h, ld22.v + 16)
    ld22.v += 16

    for ei in range(NE):
        with contextlib.ExitStack() as wes:
            load_cast_weights(P, wes,
                              [(P.ap("L0_ffn_gate_up_proj")[ei], wgu[:, :]),
                               (P.ap("L0_ffn_down_proj")[ei], wd[:, :])],
                              tag=f"e{ei}")

        def b_sync(e, ei=ei):
            for i in range(NTC):
                if i >= 2:
                    P.wait(e, Tn, f"e{ei}gu{i - 2}.{NF - 1}")
                elif ei >= 1 and i == 0:
                    P.wait(e, Tn, f"e{ei - 1}gu{NTC - 2}.{NF - 1}")
                elif ei >= 1 and i == 1:
                    P.wait(e, Tn, f"e{ei - 1}gu{NTC - 1}.{NF - 1}")
                P.inc(e.dma_start(xtm[:, (i % 2) * 4 * D:(i % 2) * 4 * D + 4 * D]
                                  .rearrange("p (s d) -> p s d", s=4),
                                  P.ap("xn_tm")[i * TC:(i + 1) * TC, :]
                                  .rearrange("(s p) d -> p s d", p=128)),
                      lds[i % 2], f"e{ei}lx{i}", 16)

        def b_vector(e, ei=ei):
            for i in range(NTC):
                P.wait(e, lds[i % 2], f"e{ei}lx{i}")
                for su in range(4):
                    if i > 0:
                        P.wait(e, Tn, f"e{ei}tr{i - 1}.{su}")
                    elif ei > 0:
                        P.wait(e, Tn, f"e{ei - 1}tr{NTC - 1}.{su}")
                    P.inc(e.tensor_scalar(xsb[:, su * D:(su + 1) * D],
                                          xtm[:, (i % 2) * 4 * D + su * D:
                                              (i % 2) * 4 * D + (su + 1) * D],
                                          ssc[:, (i * 4 + su) * NE + ei:
                                              (i * 4 + su) * NE + ei + 1],
                                          None, ALU.mult), Dv, f"e{ei}xs{i}.{su}")
                    P.wait(e, Tn, f"e{ei}tr{i}.{su}")
                    P.inc(e.tensor_copy(
                        xfm[:, :].rearrange("p (c j) -> p c j", c=KC)
                        [:, :, su * 128:(su + 1) * 128],
                        psT[:, :].rearrange("p (c j) -> p c j", c=KC)),
                        Dv, f"e{ei}xf{i}.{su}")
                for j in range(NF):
                    P.wait(e, A, f"e{ei}sl{i}.{j}")
                    P.inc(e.tensor_mul(hfm[:, j * TC:(j + 1) * TC], hg[:, :],
                                       psU[:, :]), Dv, f"e{ei}hm{i}.{j}")
                for ts in range(4):
                    P.wait(e, Tn, f"e{ei}dn{i}.{ts}")
                    if i >= 1:
                        P.wait(e, sto[((i - 1) * 4 + ts) % 8],
                               f"e{ei}so{i - 1}.{ts}")
                    elif ei >= 1:
                        P.wait(e, sto[((NTC - 1) * 4 + ts) % 8],
                               f"e{ei - 1}so{NTC - 1}.{ts}")
                    P.inc(e.tensor_copy(ob[:, ts * D:(ts + 1) * D],
                                        psD[:, (ts % 2) * D:(ts % 2) * D + D]),
                          Dv, f"e{ei}de{i}.{ts}")

        def b_tensor(e, ei=ei):
            for i in range(NTC):
                for su in range(4):
                    P.wait(e, Dv, f"e{ei}xs{i}.{su}")
                    if i > 0 or su > 0 or ei > 0:
                        P.wait(e, Dv, f"e{ei}xf{i}.{su - 1}" if su > 0 else
                               (f"e{ei}xf{i - 1}.3" if i > 0 else
                                f"e{ei - 1}xf{NTC - 1}.3"))
                    ins = None
                    for c in range(KC):
                        ins = e.transpose(psT[:, c * 128:(c + 1) * 128],
                                          xsb[:, su * D + c * 128:
                                              su * D + (c + 1) * 128], P.identb)
                    P.inc(ins, Tn, f"e{ei}tr{i}.{su}")
                P.wait(e, Dv, f"e{ei}xf{i}.3")
                for j in range(NF):
                    if j >= 1:
                        P.wait(e, Dv, f"e{ei}hm{i}.{j - 1}")
                    elif i >= 1:
                        P.wait(e, Dv, f"e{ei}hm{i - 1}.{NF - 1}")
                    elif ei >= 1:
                        P.wait(e, Dv, f"e{ei - 1}hm{NTC - 1}.{NF - 1}")
                    ins = None
                    for kc in range(KC):
                        ins = e.matmul(psG[:, :],
                                       wgu[:, kc * F2 + j * 128:
                                           kc * F2 + (j + 1) * 128],
                                       xfm[:, kc * TC:(kc + 1) * TC],
                                       start=(kc == 0), stop=(kc == KC - 1))
                    for kc in range(KC):
                        ins = e.matmul(psU[:, :],
                                       wgu[:, kc * F2 + 2048 + j * 128:
                                           kc * F2 + 2048 + (j + 1) * 128],
                                       xfm[:, kc * TC:(kc + 1) * TC],
                                       start=(kc == 0), stop=(kc == KC - 1))
                    P.inc(ins, Tn, f"e{ei}gu{i}.{j}")
                for ts in range(4):
                    P.wait(e, Dv, f"e{ei}hm{i}.{NF - 1}")
                    if ts >= 2:
                        P.wait(e, Dv, f"e{ei}de{i}.{ts - 2}")
                    elif i >= 1:
                        P.wait(e, Dv, f"e{ei}de{i - 1}.{ts + 2}")
                    elif ei >= 1:
                        P.wait(e, Dv, f"e{ei - 1}de{NTC - 1}.{ts + 2}")
                    ins = None
                    for n in range(2):
                        for kc in range(ND):
                            ins = e.matmul(psD[:, (ts % 2) * D + n * 512:
                                               (ts % 2) * D + n * 512 + 512],
                                           hfm[:, kc * TC + ts * 128:
                                               kc * TC + ts * 128 + 128],
                                           wd[:, kc * D + n * 512:
                                              kc * D + n * 512 + 512],
                                           start=(kc == 0), stop=(kc == ND - 1))
                    P.inc(ins, Tn, f"e{ei}dn{i}.{ts}")

        def b_scalar(e, ei=ei):
            for i in range(NTC):
                for j in range(NF):
                    P.wait(e, Tn, f"e{ei}gu{i}.{j}")
                    P.inc(e.activation(hg[:, :], psG[:, :], AF.Silu),
                          A, f"e{ei}sl{i}.{j}")

        def b_gpsimd(e, ei=ei):
            for i in range(NTC):
                for ts in range(4):
                    P.wait(e, Dv, f"e{ei}de{i}.{ts}")
                    gl = i * 4 + ts
                    if ei >= 1:
                        P.wait(e, sto[gl % 8], f"e{ei - 1}so{i}.{ts}")
                    elif gl >= 8:
                        P.wait(e, sto[gl % 8], f"e{ei}so{i - 2}.{ts}")
                    if ei == 0:
                        ins = e.dma_start(
                            P.ap("ffn0e_tm")[gl * 128:(gl + 1) * 128, :],
                            ob[:, ts * D:(ts + 1) * D])
                    else:
                        ins = e.dma_start(
                            P.ap("ffn0e_tm")[gl * 128:(gl + 1) * 128, :],
                            ob[:, ts * D:(ts + 1) * D], accum_op=ALU.add)
                    P.inc(ins, sto[gl % 8], f"e{ei}so{i}.{ts}", 16)

        P.twopass([("sync", b_sync), ("vector", b_vector), ("tensor", b_tensor),
                   ("scalar", b_scalar), ("gpsimd", b_gpsimd)])



def ph_add(P, es, srcs, out):
    nc = P.nc
    Dv = P.sem("Dv")
    lds = [[P.dsem(2 * k), P.dsem(2 * k + 1)] for k in range(len(srcs))]
    sto = [P.dsem(10), P.dsem(11)]
    bufs = [es.enter_context(nc.sbuf_tensor(f"ad{P.phase_id}_{k}", [128, 2 * D], F32))
            for k in range(len(srcs))]
    ob = es.enter_context(nc.sbuf_tensor(f"ad_o{P.phase_id}", [128, 2 * D], F32))

    def slot(i):
        return (i % 2) * D

    def b_sync(e):
        for i in range(NT):
            for k, nm in enumerate(srcs):
                if i >= 2:
                    P.wait(e, Dv, f"ao{i - 2}")
                P.inc(e.dma_start(bufs[k][:, slot(i):slot(i) + D],
                                  P.ap(nm)[i * 128:(i + 1) * 128, :]),
                      lds[k][i % 2], f"l{k}.{i}", 16)

    def b_vector(e):
        for i in range(NT):
            for k in range(len(srcs)):
                P.wait(e, lds[k][i % 2], f"l{k}.{i}")
            if i >= 2:
                P.wait(e, sto[i % 2], f"s{i - 2}")
            e.tensor_add(ob[:, slot(i):slot(i) + D], bufs[0][:, slot(i):slot(i) + D],
                         bufs[1][:, slot(i):slot(i) + D])
            for k in range(2, len(srcs)):
                e.tensor_add(ob[:, slot(i):slot(i) + D],
                             ob[:, slot(i):slot(i) + D],
                             bufs[k][:, slot(i):slot(i) + D])
            P.inc(e.tensor_copy(P.jD[:, 2:3], P.jD[:, 3:4]), Dv, f"ao{i}")

    def b_gpsimd(e):
        for i in range(NT):
            P.wait(e, Dv, f"ao{i}")
            P.inc(e.dma_start(P.ap(out)[i * 128:(i + 1) * 128, :],
                              ob[:, slot(i):slot(i) + D]),
                  sto[i % 2], f"s{i}", 16)

    P.twopass([("sync", b_sync), ("vector", b_vector), ("gpsimd", b_gpsimd)])



def ph_qkv1(P, es):
    nc = P.nc
    A, Dv, Tn = (P.sem(n) for n in ["A", "Dv", "Tn"])
    lds = [P.dsem(0), P.dsem(1)]
    std = {"pq": [P.dsem(10), P.dsem(11)], "pkf": [P.dsem(12), P.dsem(13)],
           "pkt": [P.dsem(14), P.dsem(15)], "v": [P.dsem(16), P.dsem(17)]}
    wq = es.enter_context(nc.sbuf_tensor(f"q1_wq{P.phase_id}", [128, KC * D], BF16))
    wk = es.enter_context(nc.sbuf_tensor(f"q1_wk{P.phase_id}", [128, KC * D], BF16))
    wv = es.enter_context(nc.sbuf_tensor(f"q1_wv{P.phase_id}", [128, KC * D], BF16))
    load_cast_transpose_weights(P, es, [(P.ap("L1_attn_Wq"), wq[:, :]),
                                        (P.ap("L1_attn_Wk"), wk[:, :]),
                                        (P.ap("L1_attn_Wv"), wv[:, :])])
    xfm = es.enter_context(nc.sbuf_tensor(f"q1_x{P.phase_id}", [128, 2 * D], BF16))
    pqb = es.enter_context(nc.sbuf_tensor(f"q1_pq{P.phase_id}", [128, 2 * D], BF16))
    pkb = es.enter_context(nc.sbuf_tensor(f"q1_pk{P.phase_id}", [128, 2 * D], BF16))
    vbf = es.enter_context(nc.sbuf_tensor(f"q1_vb{P.phase_id}", [128, 2 * D], BF16))
    w1 = es.enter_context(nc.sbuf_tensor(f"q1_t1{P.phase_id}", [128, D], F32))
    wm = es.enter_context(nc.sbuf_tensor(f"q1_tm{P.phase_id}", [128, 2 * D], F32))
    wr = es.enter_context(nc.sbuf_tensor(f"q1_tr{P.phase_id}", [128, 2 * D], F32))
    we = es.enter_context(nc.sbuf_tensor(f"q1_te{P.phase_id}", [128, 2 * D], F32))
    qfmb = es.enter_context(nc.sbuf_tensor(f"q1_qf{P.phase_id}", [128, 2 * D], BF16))
    kfmb = es.enter_context(nc.sbuf_tensor(f"q1_kf{P.phase_id}", [128, 2 * D], BF16))
    psP = es.enter_context(nc.psum_tensor(f"q1_ps{P.phase_id}", [128, 3 * D], F32))
    psT = es.enter_context(nc.psum_tensor(f"q1_pt{P.phase_id}", [128, 2 * D], BF16))

    def slot(i):
        return (i % 2) * D

    def b_sync(e):
        for i in range(NT):
            if i >= 2:
                P.wait(e, Tn, f"pjv{i - 2}")
            P.inc(e.dma_start(xfm[:, slot(i):slot(i) + D]
                              .rearrange("p (c j) -> p c j", c=KC),
                              P.ap("xn1_fm")[:, :, i * 128:(i + 1) * 128]
                              .rearrange("c p j -> p c j")),
                  lds[i % 2], f"lx{i}", 16)

    def b_tensor(e):
        for i in range(NT):
            P.wait(e, lds[i % 2], f"lx{i}")
            if i >= 1:
                P.wait(e, Dv, f"vb{i - 1}")
            for wi, (wsb, tag) in enumerate([(wq, "q"), (wk, "k"), (wv, "v")]):
                ins = None
                for n in range(2):
                    for kc in range(KC):
                        ins = e.matmul(psP[:, wi * D + n * 512:wi * D + n * 512 + 512],
                                       xfm[:, slot(i) + kc * 128:slot(i) + kc * 128 + 128],
                                       wsb[:, kc * D + n * 512:kc * D + n * 512 + 512],
                                       start=(kc == 0), stop=(kc == KC - 1))
                P.inc(ins, Tn, f"pj{tag}{i}")
            P.wait(e, Dv, f"fq{i}")
            if i >= 1:
                P.wait(e, Dv, f"eq{i - 1}")
            ins = None
            for c in range(KC):
                ins = e.transpose(psT[:, c * 128:(c + 1) * 128],
                                  pqb[:, slot(i) + c * 128:slot(i) + (c + 1) * 128],
                                  P.identb)
            P.inc(ins, Tn, f"tq{i}")
            P.wait(e, Dv, f"fk{i}")
            if i >= 1:
                P.wait(e, Dv, f"ek{i - 1}")
            ins = None
            for c in range(KC):
                ins = e.transpose(psT[:, D + c * 128:D + (c + 1) * 128],
                                  pkb[:, slot(i) + c * 128:slot(i) + (c + 1) * 128],
                                  P.identb)
            P.inc(ins, Tn, f"tk{i}")

    def b_scalar(e):
        for i in range(NT):
            P.wait(e, Dv, f"mq{i}")
            P.inc(e.activation(we[:, 0:D], wm[:, 0:D], AF.Exp), A, f"xq{i}")
            P.wait(e, Dv, f"mk{i}")
            P.inc(e.activation(we[:, D:2 * D], wm[:, D:2 * D], AF.Exp), A, f"xk{i}")

    def b_vector(e):
        for i in range(NT):
            P.wait(e, Tn, f"pjq{i}")
            e.tensor_copy(w1[:, :], psP[:, 0:D])
            e.tensor_relu(wr[:, 0:D], w1[:, :])
            P.inc(e.tensor_scalar(wm[:, 0:D], w1[:, :], 0.0, None, ALU.min),
                  Dv, f"mq{i}")
            P.wait(e, Tn, f"pjk{i}")
            e.tensor_copy(w1[:, :], psP[:, D:2 * D])
            e.tensor_relu(wr[:, D:2 * D], w1[:, :])
            P.inc(e.tensor_scalar(wm[:, D:2 * D], w1[:, :], 0.0, None, ALU.min),
                  Dv, f"mk{i}")
            P.wait(e, Tn, f"pjv{i}")
            if i >= 2:
                P.wait(e, std["v"][i % 2], f"sv{i - 2}")
            P.inc(e.tensor_copy(vbf[:, slot(i):slot(i) + D], psP[:, 2 * D:3 * D]),
                  Dv, f"vb{i}")
            P.wait(e, A, f"xq{i}")
            if i >= 2:
                P.wait(e, Tn, f"tq{i - 2}")
            P.inc(e.tensor_add(pqb[:, slot(i):slot(i) + D], we[:, 0:D],
                               wr[:, 0:D]), Dv, f"fq{i}")
            P.wait(e, A, f"xk{i}")
            if i >= 2:
                P.wait(e, Tn, f"tk{i - 2}")
                P.wait(e, std["pkt"][i % 2], f"skt{i - 2}")
            P.inc(e.tensor_add(pkb[:, slot(i):slot(i) + D], we[:, D:2 * D],
                               wr[:, D:2 * D]), Dv, f"fk{i}")
            P.wait(e, Tn, f"tq{i}")
            if i >= 2:
                P.wait(e, std["pq"][i % 2], f"sq{i - 2}")
            P.inc(e.tensor_copy(qfmb[:, slot(i):slot(i) + D], psT[:, 0:D]),
                  Dv, f"eq{i}")
            P.wait(e, Tn, f"tk{i}")
            if i >= 2:
                P.wait(e, std["pkf"][i % 2], f"skf{i - 2}")
            P.inc(e.tensor_copy(kfmb[:, slot(i):slot(i) + D], psT[:, D:2 * D]),
                  Dv, f"ek{i}")

    def b_gpsimd(e):
        for i in range(NT):
            P.wait(e, Dv, f"eq{i}")
            P.inc(e.dma_start(P.ap("phiq_fm")[:, :, i * 128:(i + 1) * 128]
                              .rearrange("c p j -> p c j"),
                              qfmb[:, slot(i):slot(i) + D]
                              .rearrange("p (c j) -> p c j", c=KC)),
                  std["pq"][i % 2], f"sq{i}", 16)
            P.wait(e, Dv, f"ek{i}")
            P.inc(e.dma_start(P.ap("phik_fm")[:, :, i * 128:(i + 1) * 128]
                              .rearrange("c p j -> p c j"),
                              kfmb[:, slot(i):slot(i) + D]
                              .rearrange("p (c j) -> p c j", c=KC)),
                  std["pkf"][i % 2], f"skf{i}", 16)
            P.wait(e, Dv, f"fk{i}")
            P.inc(e.dma_start(P.ap("phik_tm")[i * 128:(i + 1) * 128, :],
                              pkb[:, slot(i):slot(i) + D]),
                  std["pkt"][i % 2], f"skt{i}", 16)
            P.wait(e, Dv, f"vb{i}")
            P.inc(e.dma_start(P.ap("v1_tm")[i * 128:(i + 1) * 128, :],
                              vbf[:, slot(i):slot(i) + D]),
                  std["v"][i % 2], f"sv{i}", 16)

    P.twopass([("sync", b_sync), ("scalar", b_scalar), ("vector", b_vector),
               ("tensor", b_tensor), ("gpsimd", b_gpsimd)])



def ph_scan1(P, es):
    nc = P.nc
    A, Dv, Tn, xD = (P.sem(n) for n in ["A", "Dv", "Tn", "xD"])
    lds = [[P.dsem(2 * k), P.dsem(2 * k + 1)] for k in range(4)]
    strd = [P.dsem(10), P.dsem(11)]
    ld22 = P.dsem(22)
    NH = 16
    DH = 64
    W = DH + 1
    msk = es.enter_context(nc.sbuf_tensor(f"s1_mk{P.phase_id}", [128, 4 * CH], F32))
    Sg = es.enter_context(nc.sbuf_tensor(f"s1_S{P.phase_id}", [128, B_CORE * NH * W], F32))
    Sb = es.enter_context(nc.sbuf_tensor(f"s1_Sb{P.phase_id}", [128, B_CORE * NH * W], BF16))
    pqf = es.enter_context(nc.sbuf_tensor(f"s1_pq{P.phase_id}", [128, 2 * KC * CH], BF16))
    pkf = es.enter_context(nc.sbuf_tensor(f"s1_pk{P.phase_id}", [128, 2 * KC * CH], BF16))
    pkt = es.enter_context(nc.sbuf_tensor(f"s1_kt{P.phase_id}", [128, 2 * 4 * D], BF16))
    vt = es.enter_context(nc.sbuf_tensor(f"s1_v{P.phase_id}", [128, 2 * 4 * D], BF16))
    vaug = es.enter_context(nc.sbuf_tensor(f"s1_va{P.phase_id}", [128, 4 * NH * W], BF16))
    absf = es.enter_context(nc.sbuf_tensor(f"s1_ab{P.phase_id}", [128, 4 * CH], BF16))
    ytm = es.enter_context(nc.sbuf_tensor(f"s1_y{P.phase_id}", [128, 4 * D], BF16))
    yfm = es.enter_context(nc.sbuf_tensor(f"s1_yf{P.phase_id}", [128, 2 * D], BF16))
    nrw = es.enter_context(nc.sbuf_tensor(f"s1_nw{P.phase_id}", [128, 8], F32))
    psA = es.enter_context(nc.psum_tensor(f"s1_pa{P.phase_id}", [128, 4 * 512], F32))
    psR = es.enter_context(nc.psum_tensor(f"s1_pr{P.phase_id}", [128, 2 * 512], F32))
    psD = es.enter_context(nc.psum_tensor(f"s1_pd{P.phase_id}", [128, W], F32))
    psT = es.enter_context(nc.psum_tensor(f"s1_pt{P.phase_id}", [128, D], BF16))
    NBC = B_CORE * NCH

    with nc.Block() as blk:
        @blk.sync
        def _(e):
            e.dma_start(msk[:, :].rearrange("p (j t) -> p j t", j=4),
                        P.ap("c_mask1").rearrange("(j p) t -> p j t", p=128))\
                .then_inc(ld22.h, 16)

        @blk.vector
        def _(e):
            e.wait_ge(ld22.h, ld22.v + 16)
            e.memset(Sg[:, :], 0.0)
            e.memset(Sb[:, :], 0.0)
            e.memset(vaug[:, :], 1.0)
    ld22.v += 16

    def hq(h):
        return h // 2, (h % 2) * DH

    def b_sync(e):
        for idx in range(NBC):
            if idx >= 2:
                P.wait(e, Tn, f"ds{idx - 2}.15")
            for k, (name, fm) in enumerate([("phiq_fm", True), ("phik_fm", True),
                                            ("phik_tm", False), ("v1_tm", False)]):
                b, c = idx // NCH, idx % NCH
                off = b * S + c * CH
                sl = idx % 2
                if fm:
                    dst = [pqf, pkf][k][:, sl * KC * CH:(sl + 1) * KC * CH]
                    P.inc(e.dma_start(dst.rearrange("p (c j) -> p c j", c=KC),
                                      P.ap(name)[:, :, off:off + CH]
                                      .rearrange("c p j -> p c j")),
                          lds[k][idx % 2], f"l{k}.{idx}", 16)
                else:
                    dst = [None, None, pkt, vt][k][:, sl * 4 * D:(sl + 1) * 4 * D]
                    P.inc(e.dma_start(dst.rearrange("p (s d) -> p s d", s=4),
                                      P.ap(name)[off:off + CH, :]
                                      .rearrange("(s p) d -> p s d", p=128)),
                          lds[k][idx % 2], f"l{k}.{idx}", 16)

    def b_vector(e):
        for idx in range(NBC):
            sl = idx % 2
            P.wait(e, lds[3][idx % 2], f"l3.{idx}")
            if idx >= 1:
                P.wait(e, Tn, f"ds{idx - 1}.15")
            ins = None
            for su in range(4):
                for h in range(NH):
                    ins = e.tensor_copy(vaug[:, su * NH * W + h * W:
                                            su * NH * W + h * W + DH],
                                        vt[:, sl * 4 * D + su * D + h * DH:
                                           sl * 4 * D + su * D + (h + 1) * DH])
            P.inc(ins, Dv, f"va{idx}")
            for h in range(NH):
                for js in range(4):
                    P.wait(e, Tn, f"aw{idx}.{h}.{js}")
                    P.inc(e.tensor_mul(absf[:, js * CH:(js + 1) * CH],
                                       psA[:, js * 512:(js + 1) * 512],
                                       msk[:, js * CH:(js + 1) * CH]),
                          Dv, f"ab{idx}.{h}.{js}")
                for jt in range(4):
                    P.wait(e, Tn, f"rd{idx}.{h}.{jt}")
                    ro = ((h * 4 + jt) % 2) * 512
                    P.selfsync(e, e.tensor_scalar(nrw[:, 0:1],
                                                  psR[:, ro + DH:ro + W],
                                                  1e-5, None, ALU.add), xD)
                    P.selfsync(e, e.reciprocal(nrw[:, 1:2], nrw[:, 0:1]), xD)
                    P.inc(e.tensor_scalar(ytm[:, jt * D + h * DH:jt * D + (h + 1) * DH],
                                          psR[:, ro:ro + DH], nrw[:, 1:2],
                                          None, ALU.mult), Dv, f"yn{idx}.{h}.{jt}")
                P.wait(e, Tn, f"ds{idx}.{h}")
                bo = (idx // NCH) * NH * W
                hb = (h % 2) * DH
                P.selfsync(e, e.tensor_add(Sg[hb:hb + DH, bo + h * W:bo + (h + 1) * W],
                                           Sg[hb:hb + DH, bo + h * W:bo + (h + 1) * W],
                                           psD[hb:hb + DH, :]), xD)
                P.selfsync(e, e.tensor_copy(Sb[hb:hb + DH, bo + h * W:bo + (h + 1) * W],
                                            Sg[hb:hb + DH, bo + h * W:bo + (h + 1) * W]),
                           xD)
                P.inc(e.tensor_copy(P.jD[:, 2:3], P.jD[:, 3:4]), Dv, f"su{idx}.{h}")
            P.selfsync(e, e.tensor_copy(P.jD[:, 2:3], P.jD[:, 3:4]), xD)
            P.inc(e.tensor_copy(P.jD[:, 1:2], P.jD[:, 0:1]), Dv, f"ynf{idx}")
            for jt in range(4):
                P.wait(e, Tn, f"yt{idx}.{jt}")
                gl = idx * 4 + jt
                if gl >= 2:
                    P.wait(e, strd[gl % 2], f"sy{(gl - 2) // 4}.{(gl - 2) % 4}")
                P.inc(e.tensor_copy(yfm[:, (gl % 2) * D:(gl % 2) * D + D],
                                    psT[:, :]), Dv, f"ye{idx}.{jt}")

    def b_tensor(e):
        for idx in range(NBC):
            b, c = idx // NCH, idx % NCH
            sl = idx % 2
            qo = sl * KC * CH
            ko = sl * 4 * D
            for k in range(4):
                P.wait(e, lds[k][idx % 2], f"l{k}.{idx}")
            P.wait(e, Dv, f"va{idx}")
            for h in range(NH):
                hc, hr = hq(h)
                for js in range(4):
                    if h > 0:
                        P.wait(e, Dv, f"ab{idx}.{h - 1}.{js}")
                    elif idx >= 1:
                        P.wait(e, Dv, f"ab{idx - 1}.{NH - 1}.{js}")
                    P.inc(e.matmul(psA[:, js * 512:(js + 1) * 512],
                                   pkf[hr:hr + DH, qo + hc * CH + js * 128:
                                       qo + hc * CH + js * 128 + 128],
                                   pqf[hr:hr + DH, qo + hc * CH:qo + hc * CH + CH],
                                   start=True, stop=True), Tn, f"aw{idx}.{h}.{js}")
                P.wait(e, Dv, f"ab{idx}.{h}.3")
                if c != 0:
                    P.wait(e, Dv, f"su{idx - 1}.{h}")
                for jt in range(4):
                    ro = ((h * 4 + jt) % 2) * 512
                    gl = idx * NH * 4 + h * 4 + jt
                    if gl >= 2:
                        pg = gl - 2
                        pidx, ph, pjt = pg // (NH * 4), \
                            (pg % (NH * 4)) // 4, pg % 4
                        P.wait(e, Dv, f"yn{pidx}.{ph}.{pjt}")
                    ins = e.matmul(psR[:, ro:ro + W],
                                   pqf[hr:hr + DH, qo + hc * CH + jt * 128:
                                       qo + hc * CH + jt * 128 + 128],
                                   Sb[hr:hr + DH, b * NH * W + h * W:
                                      b * NH * W + (h + 1) * W],
                                   start=True, stop=False)
                    for js in range(4):
                        ins = e.matmul(psR[:, ro:ro + W],
                                       absf[:, js * CH + jt * 128:
                                            js * CH + jt * 128 + 128],
                                       vaug[:, js * NH * W + h * W:
                                            js * NH * W + (h + 1) * W],
                                       start=False, stop=(js == 3))
                    P.inc(ins, Tn, f"rd{idx}.{h}.{jt}")
                if idx >= 1 or h >= 1:
                    P.wait(e, Dv, f"su{idx}.{h - 1}" if h >= 1 else
                           f"su{idx - 1}.{NH - 1}")
                ins = None
                hb2 = (h % 2) * DH
                for js in range(4):
                    ins = e.matmul(psD[hb2:hb2 + DH, :],
                                   pkt[:, ko + js * D + h * DH:
                                       ko + js * D + (h + 1) * DH],
                                   vaug[:, js * NH * W + h * W:
                                        js * NH * W + (h + 1) * W],
                                   start=(js == 0), stop=(js == 3),
                                   tile_position=(0, hb2))
                P.inc(ins, Tn, f"ds{idx}.{h}")
            for jt in range(4):
                P.wait(e, Dv, f"ynf{idx}")
                gl = idx * 4 + jt
                if gl >= 1:
                    P.wait(e, Dv, f"ye{(gl - 1) // 4}.{(gl - 1) % 4}")
                ins = None
                for cc in range(KC):
                    ins = e.transpose(psT[:, cc * 128:(cc + 1) * 128],
                                      ytm[:, jt * D + cc * 128:
                                          jt * D + (cc + 1) * 128],
                                      P.identb)
                P.inc(ins, Tn, f"yt{idx}.{jt}")

    def b_gpsimd(e):
        for idx in range(NBC):
            b, c = idx // NCH, idx % NCH
            off = b * S + c * CH
            for jt in range(4):
                P.wait(e, Dv, f"ye{idx}.{jt}")
                gl = idx * 4 + jt
                P.inc(e.dma_start(P.ap("y1_fm")[:, :, off + jt * 128:
                                                off + (jt + 1) * 128]
                                  .rearrange("c p j -> p c j"),
                                  yfm[:, (gl % 2) * D:(gl % 2) * D + D]
                                  .rearrange("p (c j) -> p c j", c=KC)),
                      strd[gl % 2], f"sy{idx}.{jt}", 16)

    P.twopass([("sync", b_sync), ("vector", b_vector), ("tensor", b_tensor),
               ("gpsimd", b_gpsimd)])



def ph_qkvmsa(P, es):
    nc = P.nc
    A, Dv, Tn = (P.sem(n) for n in ["A", "Dv", "Tn"])
    lds = [P.dsem(0), P.dsem(1)]
    stq = [P.dsem(10), P.dsem(11)]
    stv = [P.dsem(12), P.dsem(13)]
    D3 = 3 * D
    wq = es.enter_context(nc.sbuf_tensor(f"qm_w{P.phase_id}", [128, KC * D3], BF16))
    load_cast_weights(P, es, [(P.ap("L1_ffn_w_qkv"), wq[:, :])])
    TC = 512
    NTC = T // TC
    xfm = es.enter_context(nc.sbuf_tensor(f"qm_x{P.phase_id}", [128, 2 * KC * TC], BF16))
    qkb = es.enter_context(nc.sbuf_tensor(f"qm_qk{P.phase_id}", [128, 2 * 16 * TC], BF16))
    vbf = es.enter_context(nc.sbuf_tensor(f"qm_v{P.phase_id}", [128, 2 * 4 * D], BF16))
    psQ = es.enter_context(nc.psum_tensor(f"qm_pq{P.phase_id}", [128, 2 * TC], F32))
    psV = es.enter_context(nc.psum_tensor(f"qm_pv{P.phase_id}", [128, 2 * D], F32))

    def b_sync(e):
        for i in range(NTC):
            if i >= 2:
                P.wait(e, Tn, f"v{i - 2}.3")
            P.inc(e.dma_start(xfm[:, (i % 2) * KC * TC:(i % 2 + 1) * KC * TC]
                              .rearrange("p (c j) -> p c j", c=KC),
                              P.ap("xln_fm")[:, :, i * TC:(i + 1) * TC]
                              .rearrange("c p j -> p c j")),
                  lds[i % 2], f"lx{i}", 16)

    def b_tensor(e):
        for i in range(NTC):
            P.wait(e, lds[i % 2], f"lx{i}")
            xo = (i % 2) * KC * TC
            for f in range(16):
                if f >= 2:
                    P.wait(e, Dv, f"qe{i}.{f - 2}")
                elif i >= 1:
                    P.wait(e, Dv, f"qe{i - 1}.{f + 14}")
                ins = None
                for kc in range(KC):
                    ins = e.matmul(psQ[:, (f % 2) * TC:(f % 2) * TC + TC],
                                   wq[:, kc * D3 + f * 128:kc * D3 + (f + 1) * 128],
                                   xfm[:, xo + kc * TC:xo + kc * TC + TC],
                                   start=(kc == 0), stop=(kc == KC - 1))
                P.inc(ins, Tn, f"qk{i}.{f}")
            for su in range(4):
                if su >= 2:
                    P.wait(e, Dv, f"ve{i}.{su - 2}")
                elif i >= 1:
                    P.wait(e, Dv, f"ve{i - 1}.{su + 2}")
                ins = None
                for n in range(2):
                    for kc in range(KC):
                        ins = e.matmul(psV[:, (su % 2) * D + n * 512:
                                           (su % 2) * D + n * 512 + 512],
                                       xfm[:, xo + kc * TC + su * 128:
                                           xo + kc * TC + su * 128 + 128],
                                       wq[:, kc * D3 + 2048 + n * 512:
                                          kc * D3 + 2048 + n * 512 + 512],
                                       start=(kc == 0), stop=(kc == KC - 1))
                P.inc(ins, Tn, f"v{i}.{su}")

    def b_vector(e):
        for i in range(NTC):
            for f in range(16):
                P.wait(e, Tn, f"qk{i}.{f}")
                if i >= 2 and f == 0:
                    P.wait(e, stq[i % 2], f"sq{i - 2}")
                P.inc(e.tensor_copy(qkb[:, (i % 2) * 16 * TC + f * TC:
                                        (i % 2) * 16 * TC + (f + 1) * TC],
                                    psQ[:, (f % 2) * TC:(f % 2) * TC + TC]),
                      Dv, f"qe{i}.{f}")
            for su in range(4):
                P.wait(e, Tn, f"v{i}.{su}")
                gl = i * 4 + su
                if gl >= 8:
                    P.wait(e, stv[(gl // 4) % 2], f"sv{(gl - 8) // 4}.{(gl - 8) % 4}")
                P.inc(e.tensor_copy(vbf[:, (i % 2) * 4 * D + su * D:
                                        (i % 2) * 4 * D + (su + 1) * D],
                                    psV[:, (su % 2) * D:(su % 2) * D + D]),
                      Dv, f"ve{i}.{su}")

    def b_gpsimd(e):
        for i in range(NTC):
            P.wait(e, Dv, f"qe{i}.15")
            P.inc(e.dma_start(P.ap("qk_fm")[:, :, i * TC:(i + 1) * TC]
                              .rearrange("c p j -> p c j"),
                              qkb[:, (i % 2) * 16 * TC:(i % 2 + 1) * 16 * TC]
                              .rearrange("p (c j) -> p c j", c=16)),
                  stq[i % 2], f"sq{i}", 16)
            for su in range(4):
                P.wait(e, Dv, f"ve{i}.{su}")
                gl = i * 4 + su
                P.inc(e.dma_start(P.ap("vm_tm")[gl * 128:(gl + 1) * 128, :],
                                  vbf[:, (i % 2) * 4 * D + su * D:
                                      (i % 2) * 4 * D + (su + 1) * D]),
                      stv[(gl // 4) % 2], f"sv{i}.{su}", 16)

    P.twopass([("sync", b_sync), ("tensor", b_tensor), ("vector", b_vector),
               ("gpsimd", b_gpsimd)])



def ph_msa(P, es):
    nc = P.nc
    A, Dv, Tn, xA, xD = (P.sem(n) for n in ["A", "Dv", "Tn", "xA", "xD"])
    ldk = [P.dsem(0), P.dsem(1)]
    ldv = [P.dsem(2), P.dsem(3)]
    ldq = [P.dsem(4), P.dsem(5)]
    sty = [P.dsem(10), P.dsem(11)]
    NH, DH = 16, 64
    SK = S // 512
    NQ = S // 128
    ksb = es.enter_context(nc.sbuf_tensor(f"ms_k{P.phase_id}", [128, KC * S], BF16))
    vsb = es.enter_context(nc.sbuf_tensor(f"ms_v{P.phase_id}", [128, (S // 128) * D], BF16))
    qsb = es.enter_context(nc.sbuf_tensor(f"ms_q{P.phase_id}", [128, 2 * KC * 128], BF16))
    Pb = es.enter_context(nc.sbuf_tensor(f"ms_P{P.phase_id}", [128, S], BF16))
    PTb = es.enter_context(nc.sbuf_tensor(f"ms_PT{P.phase_id}", [128, S], BF16))
    ybuf = es.enter_context(nc.sbuf_tensor(f"ms_y{P.phase_id}", [128, D], BF16))
    yfm = es.enter_context(nc.sbuf_tensor(f"ms_yf{P.phase_id}", [128, 2 * D], BF16))
    nrw = es.enter_context(nc.sbuf_tensor(f"ms_nw{P.phase_id}", [128, 8], F32))
    psS = es.enter_context(nc.psum_tensor(f"ms_ps{P.phase_id}", [128, S], F32))
    psT = es.enter_context(nc.psum_tensor(f"ms_pt{P.phase_id}", [128, S], BF16))
    psV = es.enter_context(nc.psum_tensor(f"ms_pv{P.phase_id}", [128, DH], F32))
    psY = es.enter_context(nc.psum_tensor(f"ms_py{P.phase_id}", [128, D], BF16))

    def b_sync(e):
        for b in range(B_CORE):
            if b >= 1:
                P.wait(e, Tn, f"pv{b - 1}.{NQ - 1}.{NH - 1}")
            P.inc(e.dma_start(ksb[:, :].rearrange("p (c j) -> p c j", c=KC),
                              P.ap("qk_fm")[KC:2 * KC, :, b * S:(b + 1) * S]
                              .rearrange("c p j -> p c j")),
                  ldk[b % 2], f"lk{b}", 16)
            P.inc(e.dma_start(vsb[:, :].rearrange("p (s d) -> p s d", s=S // 128),
                              P.ap("vm_tm")[b * S:(b + 1) * S, :]
                              .rearrange("(s p) d -> p s d", p=128)),
                  ldv[b % 2], f"lv{b}", 16)
            for qt in range(NQ):
                g = b * NQ + qt
                if g >= 2:
                    P.wait(e, Tn, f"pv{(g - 2) // NQ}.{(g - 2) % NQ}.{NH - 1}")
                P.inc(e.dma_start(qsb[:, (g % 2) * KC * 128:(g % 2 + 1) * KC * 128]
                                  .rearrange("p (c j) -> p c j", c=KC),
                                  P.ap("qk_fm")[0:KC, :, b * S + qt * 128:
                                                b * S + (qt + 1) * 128]
                                  .rearrange("c p j -> p c j")),
                      ldq[g % 2], f"lq{b}.{qt}", 16)

    def b_tensor(e):
        for b in range(B_CORE):
            P.wait(e, ldk[b % 2], f"lk{b}")
            P.wait(e, ldv[b % 2], f"lv{b}")
            for qt in range(NQ):
                g = b * NQ + qt
                P.wait(e, ldq[g % 2], f"lq{b}.{qt}")
                qo = (g % 2) * KC * 128
                for h in range(NH):
                    hc, hr = h // 2, (h % 2) * DH
                    if h >= 1:
                        P.wait(e, A, f"ex{b}.{qt}.{h - 1}")
                    elif g >= 1:
                        pb, pq = (g - 1) // NQ, (g - 1) % NQ
                        P.wait(e, A, f"ex{pb}.{pq}.{NH - 1}")
                    ins = None
                    for kb in range(SK):
                        ins = e.matmul(psS[:, kb * 512:(kb + 1) * 512],
                                       qsb[hr:hr + DH, qo + hc * 128:
                                           qo + (hc + 1) * 128],
                                       ksb[hr:hr + DH, hc * S + kb * 512:
                                           hc * S + kb * 512 + 512],
                                       start=True, stop=True)
                    P.inc(ins, Tn, f"sc{b}.{qt}.{h}")
                    P.wait(e, A, f"ex{b}.{qt}.{h}")
                    if h >= 1:
                        P.wait(e, Dv, f"pte{b}.{qt}.{h - 1}")
                    elif g >= 1:
                        pb, pq = (g - 1) // NQ, (g - 1) % NQ
                        P.wait(e, Dv, f"pte{pb}.{pq}.{NH - 1}")
                    ins = None
                    for kb in range(S // 128):
                        ins = e.transpose(psT[:, kb * 128:(kb + 1) * 128],
                                          Pb[:, kb * 128:(kb + 1) * 128],
                                          P.identb)
                    P.inc(ins, Tn, f"pt{b}.{qt}.{h}")
                    P.wait(e, Dv, f"pte{b}.{qt}.{h}")
                    if h >= 1:
                        P.wait(e, Dv, f"yn{b}.{qt}.{h - 1}")
                    elif g >= 1:
                        P.wait(e, Dv, f"yn{(g - 1) // NQ}.{(g - 1) % NQ}.{NH - 1}")
                    ins = None
                    for kb in range(S // 128):
                        ins = e.matmul(psV[:, :],
                                       PTb[:, kb * 128:(kb + 1) * 128],
                                       vsb[:, kb * D + h * DH:kb * D + (h + 1) * DH],
                                       start=(kb == 0), stop=(kb == S // 128 - 1))
                    P.inc(ins, Tn, f"pv{b}.{qt}.{h}")
                P.wait(e, Dv, f"ynf{b}.{qt}")
                if g >= 1:
                    P.wait(e, Dv, f"ye{(g - 1) // NQ}.{(g - 1) % NQ}")
                ins = None
                for cc in range(KC):
                    ins = e.transpose(psY[:, cc * 128:(cc + 1) * 128],
                                      ybuf[:, cc * 128:(cc + 1) * 128], P.identb)
                P.inc(ins, Tn, f"yt{b}.{qt}")

    def b_scalar(e):
        for b in range(B_CORE):
            for qt in range(NQ):
                for h in range(NH):
                    g = b * NQ + qt
                    P.wait(e, Tn, f"sc{b}.{qt}.{h}")
                    P.wait(e, Dv, f"nb{b}.{qt}.{h}")
                    if h >= 1:
                        P.wait(e, Tn, f"pt{b}.{qt}.{h - 1}")
                    elif g >= 1:
                        P.wait(e, Tn, f"pt{(g - 1) // NQ}.{(g - 1) % NQ}.{NH - 1}")
                    P.selfsync(e, e.activation(Pb[:, :], psS[:, :], AF.Exp,
                                               scale=0.125, bias=nrw[:, 2:3],
                                               accum_out=nrw[:, 3:4]), xA)
                    P.incflush(e, e.activation(nrw[:, 4:5], nrw[:, 3:4], AF.Copy),
                               "A", A, f"ex{b}.{qt}.{h}")

    def b_vector(e):
        for b in range(B_CORE):
            for qt in range(NQ):
                g = b * NQ + qt
                for h in range(NH):
                    P.wait(e, Tn, f"sc{b}.{qt}.{h}")
                    P.selfsync(e, e.reduce_max(nrw[:, 0:1], psS[:, :], axis=AX.X),
                               xD)
                    P.selfsync(e, e.tensor_scalar(nrw[:, 2:3], nrw[:, 0:1], -0.125,
                                                  None, ALU.mult), xD)
                    P.inc(e.tensor_copy(P.jD[:, 2:3], P.jD[:, 3:4]),
                          Dv, f"nb{b}.{qt}.{h}")
                    P.wait(e, Tn, f"pt{b}.{qt}.{h}")
                    P.inc(e.tensor_copy(PTb[:, :], psT[:, :]),
                          Dv, f"pte{b}.{qt}.{h}")
                    P.wait(e, A, f"ex{b}.{qt}.{h}")
                    P.selfsync(e, e.reciprocal(nrw[:, 5:6], nrw[:, 4:5]), xD)
                    P.wait(e, Tn, f"pv{b}.{qt}.{h}")
                    P.inc(e.tensor_scalar(ybuf[:, h * DH:(h + 1) * DH], psV[:, :],
                                          nrw[:, 5:6], None, ALU.mult),
                          Dv, f"yn{b}.{qt}.{h}")
                P.selfsync(e, e.tensor_copy(P.jD[:, 2:3], P.jD[:, 3:4]), xD)
                P.inc(e.tensor_copy(P.jD[:, 1:2], P.jD[:, 0:1]), Dv, f"ynf{b}.{qt}")
                P.wait(e, Tn, f"yt{b}.{qt}")
                if g >= 2:
                    P.wait(e, sty[g % 2], f"sy{(g - 2) // NQ}.{(g - 2) % NQ}")
                P.inc(e.tensor_copy(yfm[:, (g % 2) * D:(g % 2) * D + D], psY[:, :]),
                      Dv, f"ye{b}.{qt}")

    def b_gpsimd(e):
        for b in range(B_CORE):
            for qt in range(NQ):
                g = b * NQ + qt
                P.wait(e, Dv, f"ye{b}.{qt}")
                P.inc(e.dma_start(P.ap("ym_fm")[:, :, g * 128:(g + 1) * 128]
                                  .rearrange("c p j -> p c j"),
                                  yfm[:, (g % 2) * D:(g % 2) * D + D]
                                  .rearrange("p (c j) -> p c j", c=KC)),
                      sty[g % 2], f"sy{b}.{qt}", 16)

    P.twopass([("sync", b_sync), ("tensor", b_tensor), ("scalar", b_scalar),
               ("vector", b_vector), ("gpsimd", b_gpsimd)])



def ph_moe8(P, es):
    nc = P.nc
    A, Dv, Tn = (P.sem(n) for n in ["A", "Dv", "Tn"])
    lds = [P.dsem(0), P.dsem(1)]
    sto = [P.dsem(10 + k) for k in range(8)]
    ld22 = P.dsem(22)
    NE = 8
    HID = 4096
    NHID = HID // 128
    w1 = es.enter_context(nc.sbuf_tensor(f"m8_w1{P.phase_id}", [128, KC * HID], BF16))
    w2 = es.enter_context(nc.sbuf_tensor(f"m8_w2{P.phase_id}", [128, NHID * D], BF16))
    msk = es.enter_context(nc.sbuf_tensor(f"m8_mk{P.phase_id}", [128, NT * NE], F32))
    TC = 512
    NTC = T // TC
    xfm = es.enter_context(nc.sbuf_tensor(f"m8_x{P.phase_id}", [128, 2 * KC * TC], BF16))
    hfm = es.enter_context(nc.sbuf_tensor(f"m8_h{P.phase_id}", [128, NHID * TC], BF16))
    ob = es.enter_context(nc.sbuf_tensor(f"m8_o{P.phase_id}", [128, 4 * D], F32))
    psH = es.enter_context(nc.psum_tensor(f"m8_ph{P.phase_id}", [128, 2 * TC], F32))
    psY = es.enter_context(nc.psum_tensor(f"m8_py{P.phase_id}", [128, 2 * D], F32))

    with nc.Block() as blk:
        @blk.sync
        def _(e):
            e.dma_start(msk[:, :].rearrange("p (i e) -> p i e", i=NT),
                        P.ap("mask8_tm").rearrange("(i p) e -> p i e", p=128))\
                .then_inc(ld22.h, 16)
            e.wait_ge(ld22.h, ld22.v + 16)
    ld22.v += 16

    for ei in range(NE):
        with contextlib.ExitStack() as wes:
            load_cast_weights(P, wes, [(P.ap("L1_ffn_w1")[ei], w1[:, :]),
                                       (P.ap("L1_ffn_w2")[ei], w2[:, :])],
                              tag=f"m{ei}")

        def b_sync(e, ei=ei):
            for i in range(NTC):
                if i >= 2:
                    P.wait(e, Tn, f"e{ei}h{i - 2}.{NHID - 1}")
                elif ei >= 1:
                    P.wait(e, Tn, f"e{ei - 1}h{NTC - 2 + i}.{NHID - 1}")
                P.inc(e.dma_start(xfm[:, (i % 2) * KC * TC:(i % 2 + 1) * KC * TC]
                                  .rearrange("p (c j) -> p c j", c=KC),
                                  P.ap("xg_fm")[:, :, i * TC:(i + 1) * TC]
                                  .rearrange("c p j -> p c j")),
                      lds[i % 2], f"e{ei}lx{i}", 16)

        def b_tensor(e, ei=ei):
            for i in range(NTC):
                P.wait(e, lds[i % 2], f"e{ei}lx{i}")
                xo = (i % 2) * KC * TC
                for j in range(NHID):
                    if j >= 2:
                        P.wait(e, A, f"e{ei}g{i}.{j - 2}")
                    elif i >= 1:
                        P.wait(e, A, f"e{ei}g{i - 1}.{NHID - 2 + j}")
                    elif ei >= 1:
                        P.wait(e, A, f"e{ei - 1}g{NTC - 1}.{NHID - 2 + j}")
                    ins = None
                    for kc in range(KC):
                        ins = e.matmul(psH[:, (j % 2) * TC:(j % 2) * TC + TC],
                                       w1[:, kc * HID + j * 128:
                                          kc * HID + (j + 1) * 128],
                                       xfm[:, xo + kc * TC:xo + kc * TC + TC],
                                       start=(kc == 0), stop=(kc == KC - 1))
                    P.inc(ins, Tn, f"e{ei}h{i}.{j}")
                for ts in range(4):
                    P.wait(e, A, f"e{ei}g{i}.{NHID - 1}")
                    if ts >= 2:
                        P.wait(e, A, f"e{ei}ye{i}.{ts - 2}")
                    elif i >= 1:
                        P.wait(e, A, f"e{ei}ye{i - 1}.{ts + 2}")
                    elif ei >= 1:
                        P.wait(e, A, f"e{ei - 1}ye{NTC - 1}.{ts + 2}")
                    ins = None
                    for n in range(2):
                        for kc in range(NHID):
                            ins = e.matmul(psY[:, (ts % 2) * D + n * 512:
                                               (ts % 2) * D + n * 512 + 512],
                                           hfm[:, kc * TC + ts * 128:
                                               kc * TC + ts * 128 + 128],
                                           w2[:, kc * D + n * 512:
                                              kc * D + n * 512 + 512],
                                           start=(kc == 0), stop=(kc == NHID - 1))
                    P.inc(ins, Tn, f"e{ei}y{i}.{ts}")

        def b_scalar(e, ei=ei):
            for i in range(NTC):
                for j in range(NHID):
                    P.wait(e, Tn, f"e{ei}h{i}.{j}")
                    P.inc(e.activation(hfm[:, j * TC:(j + 1) * TC],
                                       psH[:, (j % 2) * TC:(j % 2) * TC + TC],
                                       AF.Gelu_apprx_tanh), A, f"e{ei}g{i}.{j}")
                for ts in range(4):
                    P.wait(e, Tn, f"e{ei}y{i}.{ts}")
                    gl = i * 4 + ts
                    if i >= 1:
                        P.wait(e, sto[((i - 1) * 4 + ts) % 8],
                               f"e{ei}so{i - 1}.{ts}")
                    elif ei >= 1:
                        P.wait(e, sto[((NTC - 1) * 4 + ts) % 8],
                               f"e{ei - 1}so{NTC - 1}.{ts}")
                    P.inc(e.activation(ob[:, ts * D:(ts + 1) * D],
                                       psY[:, (ts % 2) * D:(ts % 2) * D + D],
                                       AF.Copy,
                                       scale=msk[:, (i * 4 + ts) * NE + ei:
                                                 (i * 4 + ts) * NE + ei + 1]),
                          A, f"e{ei}ye{i}.{ts}")

        def b_gpsimd(e, ei=ei):
            for i in range(NTC):
                for ts in range(4):
                    P.wait(e, A, f"e{ei}ye{i}.{ts}")
                    gl = i * 4 + ts
                    if ei >= 1:
                        P.wait(e, sto[gl % 8], f"e{ei - 1}so{i}.{ts}")
                    elif gl >= 8:
                        P.wait(e, sto[gl % 8], f"e{ei}so{i - 2}.{ts}")
                    if ei == 0:
                        ins = e.dma_start(
                            P.ap("moe_tm")[gl * 128:(gl + 1) * 128, :],
                            ob[:, ts * D:(ts + 1) * D])
                    else:
                        ins = e.dma_start(
                            P.ap("moe_tm")[gl * 128:(gl + 1) * 128, :],
                            ob[:, ts * D:(ts + 1) * D], accum_op=ALU.add)
                    P.inc(ins, sto[gl % 8], f"e{ei}so{i}.{ts}", 16)

        P.twopass([("sync", b_sync), ("tensor", b_tensor), ("scalar", b_scalar),
                   ("gpsimd", b_gpsimd)])



I32 = mybir.dt.int32
NT0S = 23
NT1S = 15
NSLOT0 = NT0S * 512
NSLOT1 = NT1S * 512

SP_TENSORS = {
    "gup_bf": ([16 * D, 4096], BF16),
    "dwn_bf": ([16 * 2048, D], BF16),
    "w1_bf": ([8 * D, 4096], BF16),
    "w2_bf": ([8 * 4096, D], BF16),
    "dest0": ([128, NT], I32),
    "widx0": ([1, 32], F32),
    "dest1": ([128, NT], I32),
    "widx1": ([1, 32], F32),
    "xs0_s": ([NSLOT0, D], BF16),
    "xg1_s": ([NSLOT1, D], BF16),
    "y0_s": ([NSLOT0, D], F32),
    "y1_s": ([NSLOT1, D], F32),
    "xg_tm": ([T, D], BF16),
    "c_ltri": ([128, 128], F32),
    "c_ltri16": ([16, 16], F32),
    "c_iota": ([128, 1], F32),
    "c_eids": ([128, 16], F32),
    "c_tstart": ([16, 32], F32),
    "rdbg": ([128, 80], F32),
}
TENSORS.update(SP_TENSORS)


def ph_wconv(P, es):
    nc = P.nc
    Dv = P.sem("Dv")
    lds = [P.dsem(0 + k) for k in range(4)]
    sto = [P.dsem(10 + k) for k in range(4)]
    stage = es.enter_context(nc.sbuf_tensor(f"wv_st{P.phase_id}",
                                            [128, 4 * 512], F32))
    obuf = es.enter_context(nc.sbuf_tensor(f"wv_o{P.phase_id}",
                                           [128, 4 * 512], BF16))
    pieces = []
    for src, dst, rows in [("L0_ffn_gate_up_proj", "gup_bf", 16 * 1024),
                           ("L0_ffn_down_proj", "dwn_bf", 16 * 2048),
                           ("L1_ffn_w1", "w1_bf", 8 * 1024),
                           ("L1_ffn_w2", "w2_bf", 8 * 4096)]:
        sap = P.ap(src)
        sap2 = sap.rearrange("e m f -> (e m) f")
        dap = P.ap(dst)
        N = dap.shape[1]
        for r in range(0, rows, 128):
            for j in range(0, N, 512):
                w = min(512, N - j)
                pieces.append((sap2[r:r + 128, j:j + w], dap[r:r + 128, j:j + w]))

    def b_sync(e):
        for i, (si, di) in enumerate(pieces):
            if i >= 4:
                P.wait(e, Dv, f"wv{i - 4}")
            P.inc(e.dma_start(stage[:, (i % 4) * 512:(i % 4) * 512 + si.shape[1]],
                              si), lds[i % 4], f"wl{i}", 16)

    def b_vec(e):
        for i, (si, di) in enumerate(pieces):
            P.wait(e, lds[i % 4], f"wl{i}")
            if i >= 4:
                P.wait(e, sto[i % 4], f"ws{i - 4}")
            P.inc(e.tensor_copy(obuf[:, (i % 4) * 512:(i % 4) * 512 + si.shape[1]],
                                stage[:, (i % 4) * 512:(i % 4) * 512 + si.shape[1]]),
                  Dv, f"wv{i}")

    def b_gpsimd(e):
        for i, (si, di) in enumerate(pieces):
            P.wait(e, Dv, f"wv{i}")
            P.inc(e.dma_start(di,
                              obuf[:, (i % 4) * 512:(i % 4) * 512 + si.shape[1]]),
                  sto[i % 4], f"ws{i}", 16)

    P.twopass([("sync", b_sync), ("vector", b_vec), ("gpsimd", b_gpsimd)])




def ph_route(P, es, scores_name, E, dest_name, widx_name, from_mask):
    nc = P.nc
    A, Dv, Tn, xD = (P.sem(n) for n in ["A", "Dv", "Tn", "xD"])
    lds = [P.dsem(0), P.dsem(1)]
    sto = [P.dsem(10), P.dsem(11)]
    lc = [P.dsem(22), P.dsem(23)]
    pid = P.phase_id
    sc = es.enter_context(nc.sbuf_tensor(f"rt_sc{pid}", [128, 2 * 16], F32))
    oh = es.enter_context(nc.sbuf_tensor(f"rt_oh{pid}", [128, NT * 16], F32))
    posb = es.enter_context(nc.sbuf_tensor(f"rt_pos{pid}", [128, NT], F32))
    basev = es.enter_context(nc.sbuf_tensor(f"rt_bv{pid}", [1, 16], F32))
    ltri = es.enter_context(nc.sbuf_tensor(f"rt_lt{pid}", [128, 128], F32))
    lt16 = es.enter_context(nc.sbuf_tensor(f"rt_l6{pid}", [16, 16], F32))
    tst = es.enter_context(nc.sbuf_tensor(f"rt_ts{pid}", [16, 32], F32))
    onesc = es.enter_context(nc.sbuf_tensor(f"rt_on{pid}", [128, 1], F32))
    ones1 = es.enter_context(nc.sbuf_tensor(f"rt_o1{pid}", [1, 128], F32))
    frow = es.enter_context(nc.sbuf_tensor(f"rt_fr{pid}", [1, 16], F32))
    prow = es.enter_context(nc.sbuf_tensor(f"rt_pr{pid}", [1, 16], F32))
    pirow = es.enter_context(nc.sbuf_tensor(f"rt_pi{pid}", [1, 16], I32))
    psrow = es.enter_context(nc.sbuf_tensor(f"rt_psr{pid}", [1, 16], F32))
    pcol = es.enter_context(nc.sbuf_tensor(f"rt_pc{pid}", [16, 1], F32))
    cmpb = es.enter_context(nc.sbuf_tensor(f"rt_cm{pid}", [16, 32], F32))
    wrow = es.enter_context(nc.sbuf_tensor(f"rt_wr{pid}", [1, 32], F32))
    tmp = es.enter_context(nc.sbuf_tensor(f"rt_tp{pid}", [128, 16], F32))
    dcol = es.enter_context(nc.sbuf_tensor(f"rt_dc{pid}", [128, 2], F32))
    dsti = es.enter_context(nc.sbuf_tensor(f"rt_di{pid}", [128, NT], I32))
    psB = es.enter_context(nc.psum_tensor(f"rt_pb{pid}", [1, 16], F32))
    psC = es.enter_context(nc.psum_tensor(f"rt_pcs{pid}", [1, 16], F32))
    psP = es.enter_context(nc.psum_tensor(f"rt_pp{pid}", [128, 16], F32))
    psQ = es.enter_context(nc.psum_tensor(f"rt_pq{pid}", [128, 16], F32))
    psX = es.enter_context(nc.psum_tensor(f"rt_px{pid}", [128, 32], F32))
    thr = es.enter_context(nc.sbuf_tensor(f"rt_th{pid}", [16, 8], F32))
    fcol = es.enter_context(nc.sbuf_tensor(f"rt_fc{pid}", [16, 1], F32))
    cmp2 = es.enter_context(nc.sbuf_tensor(f"rt_c2{pid}", [16, 8], F32))
    pcol2 = es.enter_context(nc.sbuf_tensor(f"rt_pc2{pid}", [16, 1], F32))

    def b_sync(e):
        P.inc(e.dma_start(ltri[:, :], P.ap("c_ltri")), lc[0], "l1", 16)
        P.inc(e.dma_start(lt16[:, :], P.ap("c_ltri16")), lc[1], "l2", 16)
        P.inc(e.dma_start(tst[:, :], P.ap("c_tstart")), P.dsem(24), "l3", 16)
        P.inc(e.dma_start(thr[:, :], P.ap("c_tstart")[:, 0:8]), P.dsem(25),
              "l5", 16)
        for i in range(NT):
            if i >= 2:
                P.wait(e, Dv, f"oh{i - 2}")
            P.inc(e.dma_start(sc[:, (i % 2) * 16:(i % 2) * 16 + E],
                              P.ap(scores_name)[i * 128:(i + 1) * 128, :]),
                  lds[i % 2], f"ls{i}", 16)

    def b_vector(e):
        e.memset(onesc[:, :], 1.0)
        e.memset(oh[:, :], 0.0)
        P.selfsync(e, e.memset(basev[:, :], 0.0), xD)
        P.inc(e.memset(ones1[:, :], 1.0), Dv, "bv0")
        for i in range(NT):
            P.wait(e, lds[i % 2], f"ls{i}")
            if from_mask:
                P.selfsync(e, e.tensor_copy(oh[:, i * 16:i * 16 + E],
                                            sc[:, (i % 2) * 16:(i % 2) * 16 + E]),
                           xD)
            else:
                P.selfsync(e, e.tensor_scalar(oh[:, i * 16:i * 16 + E],
                                              sc[:, (i % 2) * 16:(i % 2) * 16 + E],
                                              0.0, None, ALU.is_gt), xD)
            P.inc(e.tensor_copy(P.jD[:, 2:3], P.jD[:, 3:4]), Dv, f"oh{i}")
            P.wait(e, Tn, f"pp{i}")
            P.selfsync(e, e.tensor_mul(tmp[:, :], psP[:, :],
                                       oh[:, i * 16:(i + 1) * 16]), xD)
            P.selfsync(e, e.reduce_sum(dcol[:, 0:1], tmp[:, :], axis=AX.X), xD)
            P.selfsync(e, e.tensor_mul(tmp[:, :], psQ[:, :],
                                       oh[:, i * 16:(i + 1) * 16]), xD)
            P.selfsync(e, e.reduce_sum(dcol[:, 1:2], tmp[:, :], axis=AX.X), xD)
            P.selfsync(e, e.tensor_add(posb[:, i:i + 1], dcol[:, 0:1],
                                       dcol[:, 1:2]), xD)
            P.inc(e.tensor_copy(P.jD[:, 2:3], P.jD[:, 3:4]), Dv, f"pos{i}")
            P.wait(e, Tn, f"cs{i}")
            P.selfsync(e, e.tensor_add(basev[:, :], basev[:, :], psC[:, :]), xD)
            P.inc(e.tensor_copy(P.jD[:, 2:3], P.jD[:, 3:4]), Dv, f"bv{i + 1}")
        P.selfsync(e, e.tensor_copy(frow[:, :], basev[:, :]), xD)
        P.inc(e.tensor_copy(P.jD[:, 2:3], P.jD[:, 3:4]), Dv, "pad")
        P.wait(e, Tn, "trp")
        P.selfsync(e, e.tensor_copy(fcol[:, :], psX[0:16, 0:1]), xD)
        P.wait(e, P.dsem(25), "l5")
        P.selfsync(e, e.tensor_scalar(cmp2[:, :], thr[:, :], fcol[:, :],
                                      None, ALU.is_lt), xD)
        P.selfsync(e, e.reduce_sum(pcol[:, :], cmp2[:, :], axis=AX.X), xD)
        P.selfsync(e, e.tensor_scalar(pcol[:, :], pcol[:, :], 512.0, None,
                                      ALU.mult), xD)
        P.inc(e.tensor_copy(P.jD[:, 2:3], P.jD[:, 3:4]), Dv, "pcl")
        P.wait(e, Tn, "pst")
        P.selfsync(e, e.tensor_copy(psrow[:, :], psB[:, :]), xD)
        P.inc(e.tensor_copy(P.jD[:, 2:3], P.jD[:, 3:4]), Dv, "pse")
        P.wait(e, Tn, "trs")
        P.selfsync(e, e.tensor_copy(pcol[:, :], psX[0:16, 0:1]), xD)
        P.selfsync(e, e.tensor_scalar(cmpb[:, :], tst[:, :], pcol[:, :],
                                      None, ALU.is_ge), xD)
        P.inc(e.tensor_copy(P.jD[:, 2:3], P.jD[:, 3:4]), Dv, "cmp")
        P.wait(e, Tn, "wix")
        P.selfsync(e, e.tensor_scalar(wrow[:, :], psX[0:1, 0:32], -1.0,
                                      float(E - 1), ALU.add, ALU.min), xD)
        P.inc(e.tensor_copy(P.jD[:, 2:3], P.jD[:, 3:4]), Dv, "wie")
        P.wait(e, Tn, "pbc")
        for i in range(NT):
            P.selfsync(e, e.tensor_mul(tmp[:, :], psP[:, :],
                                       oh[:, i * 16:(i + 1) * 16]), xD)
            P.selfsync(e, e.reduce_sum(dcol[:, 0:1], tmp[:, :], axis=AX.X), xD)
            P.selfsync(e, e.tensor_add(dcol[:, 1:2], dcol[:, 0:1],
                                       posb[:, i:i + 1]), xD)
            P.selfsync(e, e.tensor_copy(dsti[:, i:i + 1], dcol[:, 1:2]), xD)
        P.inc(e.tensor_copy(P.jD[:, 2:3], P.jD[:, 3:4]), Dv, "dst")

    def b_tensor(e):
        P.wait(e, lc[0], "l1")
        P.wait(e, lc[1], "l2")
        P.wait(e, P.dsem(24), "l3")
        P.wait(e, Dv, "bv0")
        for i in range(NT):
            P.wait(e, Dv, f"oh{i}")
            P.wait(e, Dv, f"bv{i}")
            if i >= 1:
                P.wait(e, Dv, f"pos{i - 1}")
            e.matmul(psP[:, :], ltri[:, :], oh[:, i * 16:(i + 1) * 16],
                     start=True, stop=True)
            P.inc(e.matmul(psQ[:, :], ones1[:, :], basev[:, :],
                           start=True, stop=True), Tn, f"pp{i}")
            if i >= 1:
                P.wait(e, Dv, f"bv{i}")
            P.inc(e.matmul(psC[:, :], onesc[:, :], oh[:, i * 16:(i + 1) * 16],
                           start=True, stop=True), Tn, f"cs{i}")
        P.wait(e, Dv, "pad")
        P.inc(e.transpose(psX[0:16, 0:1], frow[:, :], ones1[0:1, 0:1]), Tn, "trp")
        P.wait(e, Dv, "pcl")
        P.inc(e.matmul(psB[:, :], pcol[:, :], lt16[:, :], start=True, stop=True),
              Tn, "pst")
        P.wait(e, Dv, "pse")
        P.inc(e.transpose(psX[0:16, 0:1], psrow[:, :], ones1[0:1, 0:1]),
              Tn, "trs")
        P.wait(e, Dv, "cmp")
        P.inc(e.matmul(psX[0:1, 0:32], onesc[0:16, 0:1], cmpb[:, :],
                       start=True, stop=True), Tn, "wix")
        P.wait(e, Dv, "pse")
        P.wait(e, Dv, "wie")
        P.inc(e.matmul(psP[:, :], ones1[:, :], psrow[:, :], start=True, stop=True),
              Tn, "pbc")

    def b_gpsimd(e):
        P.wait(e, Dv, "dst")
        P.inc(e.dma_start(P.ap(dest_name), dsti[:, :]), sto[0], "sdst", 16)
        P.wait(e, Dv, "wie")
        P.inc(e.dma_start(P.ap(widx_name), wrow[:, :]), sto[1], "swix", 16)
        if ROUTE_DBG:
            P.inc(e.dma_start(P.ap("rdbg")[:, 0:NT], posb[:, :]),
                  P.dsem(17), "d1", 16)
            P.inc(e.dma_start(P.ap("rdbg")[0:1, 32:48], frow[:, :]),
                  P.dsem(18), "d2", 16)
            P.inc(e.dma_start(P.ap("rdbg")[0:1, 48:64], prow[:, :]),
                  P.dsem(19), "d3", 16)
            P.inc(e.dma_start(P.ap("rdbg")[0:1, 64:80], psrow[:, :]),
                  P.dsem(20), "d4", 16)

    P.twopass([("sync", b_sync), ("vector", b_vector),
               ("tensor", b_tensor), ("gpsimd", b_gpsimd)])


def ph_scatter(P, es, src_tm, scores_name, dest_name, out_name):
    nc = P.nc
    Dv, xD = P.sem("Dv"), P.sem("xD")
    lds = [P.dsem(0), P.dsem(1)]
    lsc = [P.dsem(2), P.dsem(3)]
    ldd = P.dsem(22)
    sto = [P.dsem(10), P.dsem(11)]
    pid = P.phase_id
    xa = es.enter_context(nc.sbuf_tensor(f"st_x{pid}", [128, 2 * D], BF16))
    sc = es.enter_context(nc.sbuf_tensor(f"st_s{pid}", [128, 2 * 16], F32))
    xs = es.enter_context(nc.sbuf_tensor(f"st_xs{pid}", [128, 2 * D], BF16))
    dsti = es.enter_context(nc.sbuf_tensor(f"st_di{pid}", [128, NT], I32))
    srow = es.enter_context(nc.sbuf_tensor(f"st_sr{pid}", [128, 2], F32))

    def slot(i):
        return (i % 2) * D

    def b_sync(e):
        P.inc(e.dma_start(dsti[:, :], P.ap(dest_name)), ldd, "ld", 16)
        for i in range(NT):
            if i >= 2:
                P.wait(e, Dv, f"xs{i - 2}")
            P.inc(e.dma_start(xa[:, slot(i):slot(i) + D],
                              P.ap(src_tm)[i * 128:(i + 1) * 128, :]),
                  lds[i % 2], f"lx{i}", 16)
            if scores_name:
                P.inc(e.dma_start(sc[:, (i % 2) * 16:(i % 2) * 16 + 16],
                                  P.ap(scores_name)[i * 128:(i + 1) * 128, :]),
                      lsc[i % 2], f"lsc{i}", 16)

    def b_vector(e):
        for i in range(NT):
            P.wait(e, lds[i % 2], f"lx{i}")
            if i >= 2:
                P.wait(e, sto[i % 2], f"so{i - 2}")
            if scores_name:
                P.wait(e, lsc[i % 2], f"lsc{i}")
                P.selfsync(e, e.reduce_sum(srow[:, 0:1],
                                           sc[:, (i % 2) * 16:(i % 2) * 16 + 16],
                                           axis=AX.X), xD)
                P.inc(e.tensor_scalar(xs[:, slot(i):slot(i) + D],
                                      xa[:, slot(i):slot(i) + D],
                                      srow[:, 0:1], None, ALU.mult),
                      Dv, f"xs{i}")
            else:
                P.inc(e.tensor_copy(xs[:, slot(i):slot(i) + D],
                                    xa[:, slot(i):slot(i) + D]), Dv, f"xs{i}")

    def b_gpsimd(e):
        P.wait(e, ldd, "ld")
        for i in range(NT):
            P.wait(e, Dv, f"xs{i}")
            from concourse.bass_core_rust import IndirectOffsetOnAxis
            P.inc(e.indirect_dma_start(
                P.ap(out_name), IndirectOffsetOnAxis(dsti[:, i:i + 1], 0),
                xs[:, slot(i):slot(i) + D], None),
                sto[i % 2], f"so{i}", 16)

    P.twopass([("sync", b_sync), ("vector", b_vector), ("gpsimd", b_gpsimd)])


def ph_gatherback(P, es, ybuf_name, dest_name, out_tm):
    nc = P.nc
    Dv = P.sem("Dv")
    ldd = P.dsem(16)
    gth = [P.dsem(12), P.dsem(13)]
    sto = [P.dsem(10), P.dsem(11)]
    pid = P.phase_id
    dsti = es.enter_context(nc.sbuf_tensor(f"gb_di{pid}", [128, NT], I32))
    ga = es.enter_context(nc.sbuf_tensor(f"gb_g{pid}", [128, 2 * D], F32))

    def b_gpsimd(e):
        from concourse.bass_core_rust import IndirectOffsetOnAxis
        P.inc(e.dma_start(dsti[:, :], P.ap(dest_name)), ldd, "ld", 16)
        e.wait_ge(ldd.h, ldd.v)
        for i in range(NT):
            if i >= 2:
                P.wait(e, sto[i % 2], f"so{i - 2}")
            P.inc(e.indirect_dma_start(
                ga[:, (i % 2) * D:(i % 2) * D + D], None,
                P.ap(ybuf_name), IndirectOffsetOnAxis(dsti[:, i:i + 1], 0)),
                gth[i % 2], f"g{i}", 16)

    def b_sync(e):
        for i in range(NT):
            P.wait(e, gth[i % 2], f"g{i}")
            P.inc(e.dma_start(P.ap(out_tm)[i * 128:(i + 1) * 128, :],
                              ga[:, (i % 2) * D:(i % 2) * D + D]),
                  sto[i % 2], f"so{i}", 16)

    P.twopass([("gpsimd", b_gpsimd), ("sync", b_sync)])


def ph_sparse_moe(P, es, layer):
    nc = P.nc
    A, Dv, Tn, xD = (P.sem(n) for n in ["A", "Dv", "Tn", "xD"])
    lds = [P.dsem(0), P.dsem(1)]
    ldw = [P.dsem(12), P.dsem(13)]
    ldw2 = [P.dsem(14), P.dsem(15)]
    sto = [P.dsem(10), P.dsem(11)]
    lw = P.dsem(22)
    pid = P.phase_id
    NTL = NT0S if layer == 0 else NT1S
    xin = "xs0_s" if layer == 0 else "xg1_s"
    yout = "y0_s" if layer == 0 else "y1_s"
    w1n = "gup_bf" if layer == 0 else "w1_bf"
    w2n = "dwn_bf" if layer == 0 else "w2_bf"
    widxn = "widx0" if layer == 0 else "widx1"
    W1N = 4096
    W1R = D
    W2R = 2048 if layer == 0 else 4096
    NW2 = W2R // 128
    NF = 16 if layer == 0 else 32
    TC = 512
    wsb1 = es.enter_context(nc.sbuf_tensor(f"sm_w1{pid}", [128, KC * W1N], BF16))
    wsb2 = es.enter_context(nc.sbuf_tensor(f"sm_w2{pid}", [128, NW2 * D], BF16))
    wrow = es.enter_context(nc.sbuf_tensor(f"sm_wr{pid}", [1, 32], F32))
    wbc = es.enter_context(nc.sbuf_tensor(f"sm_wb{pid}", [128, 32], F32))
    iot = es.enter_context(nc.sbuf_tensor(f"sm_io{pid}", [128, 1], F32))
    idx1 = es.enter_context(nc.sbuf_tensor(f"sm_i1{pid}", [128, NTL * KC], I32))
    idx2 = es.enter_context(nc.sbuf_tensor(f"sm_i2{pid}", [128, NTL * NW2], I32))
    idxf = es.enter_context(nc.sbuf_tensor(f"sm_if{pid}", [128, 2], F32))
    ones1 = es.enter_context(nc.sbuf_tensor(f"sm_o1{pid}", [1, 128], F32))
    xtm = es.enter_context(nc.sbuf_tensor(f"sm_x{pid}", [128, 2 * 4 * D], BF16))
    xfm = es.enter_context(nc.sbuf_tensor(f"sm_xf{pid}", [128, KC * TC], BF16))
    hg = es.enter_context(nc.sbuf_tensor(f"sm_hg{pid}", [128, TC], F32))
    hfm = es.enter_context(nc.sbuf_tensor(f"sm_h{pid}", [128, NF * TC], BF16))
    ob = es.enter_context(nc.sbuf_tensor(f"sm_ob{pid}", [128, 4 * D], F32))
    psG = es.enter_context(nc.psum_tensor(f"sm_pg{pid}", [128, TC], F32))
    psU = es.enter_context(nc.psum_tensor(f"sm_pu{pid}", [128, TC], F32))
    psD = es.enter_context(nc.psum_tensor(f"sm_pd{pid}", [128, 2 * D], F32))
    psT = es.enter_context(nc.psum_tensor(f"sm_pt{pid}", [128, D], BF16))
    psI = es.enter_context(nc.psum_tensor(f"sm_pi{pid}", [128, 32], F32))
    from concourse.bass_core_rust import IndirectOffsetOnAxis

    def p_sync(e):
        P.inc(e.dma_start(wrow[:, :], P.ap(widxn)), lw, "lw", 16)
        P.inc(e.dma_start(iot[:, :], P.ap("c_iota")), P.dsem(23), "lio", 16)

    def p_vec(e):
        P.inc(e.memset(ones1[:, :], 1.0), Dv, "vini")
        P.wait(e, Tn, "bc0")
        P.wait(e, P.dsem(23), "lio")
        P.selfsync(e, e.tensor_copy(wbc[:, :], psI[:, :]), xD)
        for j in range(NTL):
            for kc in range(KC):
                P.selfsync(e, e.tensor_scalar(idxf[:, 0:1], wbc[:, j:j + 1],
                                              float(W1R), iot[:, 0:1],
                                              ALU.mult, ALU.add), xD)
                P.selfsync(e, e.tensor_scalar(idxf[:, 1:2], idxf[:, 0:1],
                                              float(kc * 128), None, ALU.add), xD)
                P.selfsync(e, e.tensor_copy(idx1[:, j * KC + kc:j * KC + kc + 1],
                                            idxf[:, 1:2]), xD)
            for kc in range(NW2):
                P.selfsync(e, e.tensor_scalar(idxf[:, 0:1], wbc[:, j:j + 1],
                                              float(W2R), iot[:, 0:1],
                                              ALU.mult, ALU.add), xD)
                P.selfsync(e, e.tensor_scalar(idxf[:, 1:2], idxf[:, 0:1],
                                              float(kc * 128), None, ALU.add), xD)
                P.selfsync(e, e.tensor_copy(idx2[:, j * NW2 + kc:j * NW2 + kc + 1],
                                            idxf[:, 1:2]), xD)
        P.inc(e.tensor_copy(P.jD[:, 2:3], P.jD[:, 3:4]), Dv, "idx")

    def p_tensor(e):
        P.wait(e, lw, "lw")
        P.wait(e, Dv, "vini")
        P.inc(e.matmul(psI[:, :], ones1[:, :], wrow[:, :], start=True, stop=True),
              Tn, "bc0")

    P.twopass([("sync", p_sync), ("tensor", p_tensor), ("vector", p_vec)])

    def b_sync(e):
        for j in range(NTL):
            if j >= 2:
                P.wait(e, Tn, f"tr{j - 2}.3")
            P.inc(e.dma_start(xtm[:, (j % 2) * 4 * D:(j % 2 + 1) * 4 * D]
                              .rearrange("p (s d) -> p s d", s=4),
                              P.ap(xin)[j * TC:(j + 1) * TC, :]
                              .rearrange("(s p) d -> p s d", p=128)),
                  lds[j % 2], f"lx{j}", 16)

    def b_gpsimd2(e):
        P.wait(e, Dv, "idx")
        for j in range(NTL):
            if j >= 1:
                P.wait(e, A, f"g{j - 1}.{NF - 1}")
            for kc in range(KC):
                P.inc(e.indirect_dma_start(
                    wsb1[:, kc * W1N:(kc + 1) * W1N], None,
                    P.ap(w1n), IndirectOffsetOnAxis(idx1[:, j * KC + kc:
                                                         j * KC + kc + 1], 0)),
                    ldw[kc % 2], f"w1.{j}.{kc}", 16)
            if j >= 1:
                P.wait(e, Dv, f"de{j - 1}.3")
            for kc in range(NW2):
                P.inc(e.indirect_dma_start(
                    wsb2[:, kc * D:(kc + 1) * D], None,
                    P.ap(w2n), IndirectOffsetOnAxis(idx2[:, j * NW2 + kc:
                                                         j * NW2 + kc + 1], 0)),
                    ldw2[kc % 2], f"w2.{j}.{kc}", 16)
            if j >= 1:
                for ts in range(4):
                    P.wait(e, Dv, f"de{j - 1}.{ts}")
                    gl = (j - 1) * 4 + ts
                    P.inc(e.dma_start(P.ap(yout)[gl * 128:(gl + 1) * 128, :],
                                      ob[:, ts * D:(ts + 1) * D]),
                          sto[gl % 2], f"so{j - 1}.{ts}", 16)
        for ts in range(4):
            P.wait(e, Dv, f"de{NTL - 1}.{ts}")
            gl = (NTL - 1) * 4 + ts
            P.inc(e.dma_start(P.ap(yout)[gl * 128:(gl + 1) * 128, :],
                              ob[:, ts * D:(ts + 1) * D]),
                  sto[gl % 2], f"so{NTL - 1}.{ts}", 16)

    def b_tensor(e):
        P.wait(e, Dv, "idx")
        for j in range(NTL):
            P.wait(e, lds[j % 2], f"lx{j}")
            for su in range(4):
                if j >= 1 and su == 0:
                    P.wait(e, A, f"g{j - 1}.{NF - 1}")
                if j * 4 + su >= 1:
                    pj, ps = (j * 4 + su - 1) // 4, (j * 4 + su - 1) % 4
                    P.wait(e, Dv, f"xf{pj}.{ps}")
                ins = None
                for c in range(KC):
                    ins = e.transpose(psT[:, c * 128:(c + 1) * 128],
                                      xtm[:, (j % 2) * 4 * D + su * D + c * 128:
                                          (j % 2) * 4 * D + su * D + (c + 1) * 128],
                                      P.identb)
                P.inc(ins, Tn, f"tr{j}.{su}")
            P.wait(e, Dv, f"xf{j}.3")
            for f in range(NF):
                P.wait(e, ldw[0], f"w1.{j}.{KC - 2}")
                P.wait(e, ldw[1], f"w1.{j}.{KC - 1}")
                if f >= 1:
                    P.wait(e, Dv, f"hm{j}.{f - 1}") if layer == 0 else \
                        P.wait(e, A, f"g{j}.{f - 1}")
                elif j >= 1:
                    P.wait(e, Dv, f"hm{j - 1}.{NF - 1}") if layer == 0 else \
                        P.wait(e, A, f"g{j - 1}.{NF - 1}")
                if layer == 0:
                    ins = None
                    for kc in range(KC):
                        ins = e.matmul(psG[:, :],
                                       wsb1[:, kc * W1N + f * 128:
                                            kc * W1N + (f + 1) * 128],
                                       xfm[:, kc * TC:(kc + 1) * TC],
                                       start=(kc == 0), stop=(kc == KC - 1))
                    for kc in range(KC):
                        ins = e.matmul(psU[:, :],
                                       wsb1[:, kc * W1N + 2048 + f * 128:
                                            kc * W1N + 2048 + (f + 1) * 128],
                                       xfm[:, kc * TC:(kc + 1) * TC],
                                       start=(kc == 0), stop=(kc == KC - 1))
                    P.inc(ins, Tn, f"gu{j}.{f}")
                else:
                    ins = None
                    for kc in range(KC):
                        ins = e.matmul(psG[:, :] if f % 2 == 0 else psU[:, :],
                                       wsb1[:, kc * W1N + f * 128:
                                            kc * W1N + (f + 1) * 128],
                                       xfm[:, kc * TC:(kc + 1) * TC],
                                       start=(kc == 0), stop=(kc == KC - 1))
                    P.inc(ins, Tn, f"gu{j}.{f}")
            for ts in range(4):
                P.wait(e, Dv, f"hm{j}.{NF - 1}") if layer == 0 else \
                    P.wait(e, A, f"g{j}.{NF - 1}")
                P.wait(e, ldw2[0], f"w2.{j}.{NW2 - 2}")
                P.wait(e, ldw2[1], f"w2.{j}.{NW2 - 1}")
                if ts >= 2:
                    P.wait(e, Dv, f"de{j}.{ts - 2}")
                elif j >= 1:
                    P.wait(e, Dv, f"de{j - 1}.{ts + 2}")
                ins = None
                for n in range(2):
                    for kc in range(NW2):
                        ins = e.matmul(psD[:, (ts % 2) * D + n * 512:
                                           (ts % 2) * D + n * 512 + 512],
                                       hfm[:, kc * TC + ts * 128:
                                           kc * TC + ts * 128 + 128],
                                       wsb2[:, kc * D + n * 512:
                                            kc * D + n * 512 + 512],
                                       start=(kc == 0), stop=(kc == NW2 - 1))
                P.inc(ins, Tn, f"dn{j}.{ts}")

    def b_scalar(e):
        for j in range(NTL):
            for f in range(NF):
                P.wait(e, Tn, f"gu{j}.{f}")
                if layer == 0:
                    P.inc(e.activation(hg[:, :], psG[:, :], AF.Silu),
                          A, f"g{j}.{f}")
                else:
                    P.inc(e.activation(hfm[:, f * TC:(f + 1) * TC],
                                       psG[:, :] if f % 2 == 0 else psU[:, :],
                                       AF.Gelu_apprx_tanh), A, f"g{j}.{f}")

    def b_vector(e):
        for j in range(NTL):
            for su in range(4):
                P.wait(e, Tn, f"tr{j}.{su}")
                P.inc(e.tensor_copy(
                    xfm[:, :].rearrange("p (c t) -> p c t", c=KC)
                    [:, :, su * 128:(su + 1) * 128],
                    psT[:, :].rearrange("p (c t) -> p c t", c=KC)),
                    Dv, f"xf{j}.{su}")
            if layer == 0:
                for f in range(NF):
                    P.wait(e, A, f"g{j}.{f}")
                    P.inc(e.tensor_mul(hfm[:, f * TC:(f + 1) * TC], hg[:, :],
                                       psU[:, :]), Dv, f"hm{j}.{f}")
            for ts in range(4):
                P.wait(e, Tn, f"dn{j}.{ts}")
                gl = j * 4 + ts
                if gl >= 4:
                    P.wait(e, sto[gl % 2], f"so{(gl - 4) // 4}.{(gl - 4) % 4}")
                P.inc(e.tensor_copy(ob[:, ts * D:(ts + 1) * D],
                                    psD[:, (ts % 2) * D:(ts % 2) * D + D]),
                      Dv, f"de{j}.{ts}")

    P.twopass([("sync", b_sync), ("gpsimd", b_gpsimd2), ("tensor", b_tensor),
               ("scalar", b_scalar), ("vector", b_vector)])


def _ph_qkv0(P, es):
    ph_qkv0(P, es)


def _ph_scan0(P, es):
    ph_scan0(P, es)


def _ph_wo0(P, es):
    ph_proj_tm(P, es, "read_fm", "L0_attn_Wo", "attn0_tm", bias="L0_attn_bo")


PHASE_DEFS = [
    ("ln0", _ph_ln0,
     ["x0", "L0_ffn_w_router", "c_ident_b"],
     ["xn_tm", "xn_fm", "scores_tm"]),
    ("qkv0", _ph_qkv0,
     ["xn_fm", "L0_attn_Wq", "L0_attn_bq", "L0_attn_Wk", "L0_attn_bk",
      "L0_attn_Wv", "L0_attn_bv", "L0_attn_decay_param", "c_trel", "c_trel2",
      "c_ident_b"],
     ["q_fm", "qs_fm", "kphi_fm", "khat_tm", "v_tm"]),
    ("scan0", _ph_scan0,
     ["q_fm", "qs_fm", "kphi_fm", "khat_tm", "v_tm", "L0_attn_decay_param",
      "c_m1", "c_m2neg", "c_trel", "c_trel2", "c_ident_b"],
     ["read_fm"]),
    ("wo0", _ph_wo0,
     ["read_fm", "L0_attn_Wo", "L0_attn_bo", "c_ident_b"],
     ["attn0_tm"]),
    ("shared0", ph_shared0,
     ["xn_fm", "L0_ffn_w_gate", "L0_ffn_w_up", "L0_ffn_w_down", "c_ident_b"],
     ["ffn0_tm"]),
    ("wconv", ph_wconv,
     ["L0_ffn_gate_up_proj", "L0_ffn_down_proj", "L1_ffn_w1", "L1_ffn_w2",
      "c_ident_b"],
     ["gup_bf", "dwn_bf", "w1_bf", "w2_bf"]),
    ("route0", lambda P, es: ph_route(P, es, "scores_tm", 16, "dest0", "widx0",
                                      False),
     ["scores_tm", "c_ltri", "c_ltri16", "c_tstart", "c_ident_b"],
     ["dest0", "widx0"]),
    ("scat0", lambda P, es: ph_scatter(P, es, "xn_tm", "scores_tm", "dest0",
                                       "xs0_s"),
     ["xn_tm", "scores_tm", "dest0", "c_ident_b"],
     ["xs0_s"]),
    ("ffn0s", lambda P, es: ph_sparse_moe(P, es, 0),
     ["xs0_s", "widx0", "gup_bf", "dwn_bf", "c_iota", "c_ident_b"],
     ["y0_s"]),
    ("gb0", lambda P, es: ph_gatherback(P, es, "y0_s", "dest0", "ffn0e_tm"),
     ["y0_s", "dest0", "c_ident_b"],
     ["ffn0e_tm"]),
    ("resid0", lambda P, es: ph_add(P, es, ["x0", "attn0_tm", "ffn0_tm",
                                            "ffn0e_tm"], "x1_tm"),
     ["x0", "attn0_tm", "ffn0_tm", "ffn0e_tm", "c_ident_b"],
     ["x1_tm"]),
    ("ln1", lambda P, es: ph_ln(P, es, "x1_tm", "xn1_fm", out_f32="xn1f_tm"),
     ["x1_tm", "c_ident_b"],
     ["xn1_fm", "xn1f_tm"]),
    ("qkv1", ph_qkv1,
     ["xn1_fm", "L1_attn_Wq", "L1_attn_Wk", "L1_attn_Wv", "c_ident_b"],
     ["phiq_fm", "phik_fm", "phik_tm", "v1_tm"]),
    ("scan1", ph_scan1,
     ["phiq_fm", "phik_fm", "phik_tm", "v1_tm", "c_mask1", "c_ident_b"],
     ["y1_fm"]),
    ("wo1", lambda P, es: ph_proj_tm(P, es, "y1_fm", "L1_attn_Wo", "attn1_tm",
                                     bias="L1_attn_bo", w_transposed=True),
     ["y1_fm", "L1_attn_Wo", "L1_attn_bo", "c_ident_b"],
     ["attn1_tm"]),
    ("vitln1", lambda P, es: ph_ln(P, es, "xn1f_tm", "xln_fm",
                                   lnw=("L1_ffn_ln1_w", "L1_ffn_ln1_b")),
     ["xn1f_tm", "L1_ffn_ln1_w", "L1_ffn_ln1_b", "c_ident_b"],
     ["xln_fm"]),
    ("qkvmsa", ph_qkvmsa,
     ["xln_fm", "L1_ffn_w_qkv", "c_ident_b"],
     ["qk_fm", "vm_tm"]),
    ("msa", ph_msa,
     ["qk_fm", "vm_tm", "c_ident_b"],
     ["ym_fm"]),
    ("msaproj", lambda P, es: ph_proj_tm(P, es, "ym_fm", "L1_ffn_w_proj", "u_tm",
                                         extra_tm=("xn1f_tm",)),
     ["ym_fm", "L1_ffn_w_proj", "xn1f_tm", "c_ident_b"],
     ["u_tm"]),
    ("ln2g", lambda P, es: ph_ln(P, es, "u_tm", "xg_fm", out_tm="xg_tm",
                                 lnw=("L1_ffn_ln2_w", "L1_ffn_ln2_b"),
                                 router=("L1_ffn_gate_w", 8, "mask8_tm", False,
                                         False, "L1_ffn_gate_b")),
     ["u_tm", "L1_ffn_ln2_w", "L1_ffn_ln2_b", "L1_ffn_gate_w", "L1_ffn_gate_b",
      "c_ident_b"],
     ["xg_fm", "xg_tm", "mask8_tm"]),
    ("route1", lambda P, es: ph_route(P, es, "mask8_tm", 8, "dest1", "widx1",
                                      True),
     ["mask8_tm", "c_ltri", "c_ltri16", "c_tstart", "c_ident_b"],
     ["dest1", "widx1"]),
    ("scat1", lambda P, es: ph_scatter(P, es, "xg_tm", None, "dest1", "xg1_s"),
     ["xg_tm", "dest1", "c_ident_b"],
     ["xg1_s"]),
    ("moe8s", lambda P, es: ph_sparse_moe(P, es, 1),
     ["xg1_s", "widx1", "w1_bf", "w2_bf", "c_iota", "c_ident_b"],
     ["y1_s"]),
    ("gb1", lambda P, es: ph_gatherback(P, es, "y1_s", "dest1", "moe_tm"),
     ["y1_s", "dest1", "c_ident_b"],
     ["moe_tm"]),
    ("residout", lambda P, es: ph_add(P, es, ["x1_tm", "attn1_tm", "u_tm",
                                              "moe_tm"], "OUT"),
     ["x1_tm", "attn1_tm", "u_tm", "moe_tm", "c_ident_b"],
     ["OUT"]),
]







